# version 08
# speedup vs baseline: 1.2685x; 1.0249x over previous
import math
import types as _types
from contextlib import ExitStack


def _freeze(fn):
    if fn.__closure__ is None:
        return fn
    frozen = tuple(_types.CellType(c.cell_contents) for c in fn.__closure__)
    g = _types.FunctionType(fn.__code__, fn.__globals__, fn.__name__,
                            fn.__defaults__, frozen)
    g.__kwdefaults__ = fn.__kwdefaults__
    return g

import numpy as np
import ml_dtypes

import concourse.bass as bass
import concourse.mybir as mybir
from concourse.bass_utils import run_bass_kernel_spmd
from concourse.bass import IndirectOffsetOnAxis
from concourse.aot_env import allow_ici_all_to_all

F32 = mybir.dt.float32
BF16 = mybir.dt.bfloat16
I32 = mybir.dt.int32
AF = mybir.ActivationFunctionType
ALU = mybir.AluOpType
AX = mybir.AxisListType

NCORES = 8
T = 8192
D = 2048
TOWN = T // NCORES

PI = math.pi
TWO_PI = 2.0 * math.pi
INV_2PI = 1.0 / TWO_PI


class Prog:
    NQSEM = 8

    def __init__(self, name="prog"):
        self.nc = bass.Bass("TRN2", target_bir_lowering=False, debug=False)
        self.ctx = ExitStack()
        self.started = False
        self.inputs = {}
        self.outputs = {}
        self.name = name
        self.fused_io = None
        self.tag = ""
        self._scope_stack = []

    def din(self, name, shape, dtype=F32):
        if self.fused_io is not None:
            ap = self.fused_io.get((self.tag, name))
            if ap is not None:
                return ap
            qn = self.tag + "_in_" + name
            t = self.nc.dram_tensor(qn, list(shape), dtype, kind="ExternalInput").ap()
            self.inputs[qn] = t
            return t
        t = self.nc.dram_tensor(name, list(shape), dtype, kind="ExternalInput").ap()
        self.inputs[name] = t
        return t

    def dout(self, name, shape, dtype=F32):
        if self.fused_io is not None:
            ap = self.fused_io.get((self.tag, name))
            assert ap is not None, f"fused dout {self.tag}/{name} not wired"
            return ap
        t = self.nc.dram_tensor(name, list(shape), dtype, kind="ExternalOutput").ap()
        self.outputs[name] = t
        return t

    def dtmp(self, name, shape, dtype=F32):
        return self.nc.dram_tensor(self.tag + "_" + name if self.fused_io is not None
                                   else name, list(shape), dtype, kind="Internal").ap()

    def start(self):
        if self.started:
            return
        self.started = True
        self.const_values = []
        self._cst_d = self.nc.dram_tensor("cst", [128, 16], F32, kind="ExternalInput").ap()
        self.inputs["cst"] = self._cst_d
        E = self.ctx.enter_context
        self.sems = {}
        self.cnt = {}
        for k in ("v", "s", "t", "g", "cc"):
            self.sems[k] = E(self.nc.semaphore("sem_" + k))
            self.cnt[k] = 0
        for q in ("qs", "qg"):
            for i in range(self.NQSEM):
                k = f"{q}{i}"
                self.sems[k] = E(self.nc.semaphore("sem_" + k))
                self.cnt[k] = 0
        self.qrr = {"qs": 0, "qg": 0}
        self._pending = {}
        self._tcells = []
        self._t_last_inc = -1
        self.block = E(self.nc.Block())
        self.engmap = {"v": "vector", "s": "scalar", "t": "tensor", "g": "gpsimd",
                       "qs": "sync", "qg": "gpsimd", "cc": "gpsimd"}
        self.watermark = {}
        self.writer = {}
        self.readers = {}
        self.nops = 0
        self.global_extra = []
        self._cst_s = self.sb("cst_s", [128, 16])
        self.dma("qs", lambda e: e.dma_start(out=self._cst_s[:], in_=self._cst_d[:]),
                 writes=[self._cst_s])

    def constcell(self, value, parts=128):
        value = float(value)
        if value in self.const_values:
            j = self.const_values.index(value)
        else:
            j = len(self.const_values)
            assert j < 16
            self.const_values.append(value)
        return self._cst_s[0:parts, j:j + 1]

    def cst_host(self):
        a = np.zeros((128, 16), np.float32)
        for j, v in enumerate(self.const_values):
            a[:, j] = v
        return a

    def sb(self, name, shape, dtype=F32):
        ctx = self._scope_stack[-1] if self._scope_stack else self.ctx
        return ctx.enter_context(self.nc.sbuf_tensor(name, list(shape), dtype))

    def ps(self, name, shape, dtype=F32):
        ctx = self._scope_stack[-1] if self._scope_stack else self.ctx
        return ctx.enter_context(self.nc.psum_tensor(name, list(shape), dtype))

    def phase(self, tag):
        from contextlib import contextmanager

        @contextmanager
        def _cm():
            self.tag = tag
            sc = ExitStack()
            self._scope_stack.append(sc)
            try:
                yield
            except BaseException:
                self._scope_stack.pop()
                self.tag = ""
                raise
            self._scope_stack.pop()
            sc.close()
            self.fence()
            self.tag = ""
        return _cm()

    def maybe_finish(self):
        if self.fused_io is None:
            self.finish()

    def _resolve_t(self, idx):
        if idx > self._t_last_inc:
            cell = self._tcells[-1]
            if not cell["inc"]:
                cell["inc"] = True
                self.cnt["t"] += 1
            self._t_last_inc = len(self._tcells) - 1
        return ("t", self.cnt["t"])

    def _deps(self, reads, writes, extra, consumer_ename=None):
        need = {}

        def add(tok):
            if tok is None:
                return
            if tok[0] == "t#":
                if consumer_ename == "tensor":
                    return
                tok = self._resolve_t(tok[1])
            k, v = tok
            if need.get(k, 0) < v:
                need[k] = v

        def key(b):
            return b if isinstance(b, str) else id(b)
        for b in reads:
            add(self.writer.get(key(b)))
        for b in writes:
            add(self.writer.get(key(b)))
            for tok in self.readers.get(key(b), ()):
                add(tok)
        for tok in extra:
            add(tok)
        return need

    def fence(self):
        self.global_extra = self.barrier_token()

    def _emit(self, ek, semkey, inc, build, reads, writes, extra):
        build = _freeze(build)
        ename = self.engmap[ek]
        need = self._deps(reads, writes, list(extra) + self.global_extra,
                          consumer_ename=ename)
        sems = self.sems
        wm = self.watermark

        waits = []
        for k, v in need.items():
            key = (ename, k)
            if wm.get(key, 0) >= v:
                continue
            wm[key] = v
            waits.append((k, v))

        if ek == "t":
            cell = {"inc": False}
            self._tcells.append(cell)
            tsem = sems["t"]

            def go(e, cell=cell, waits=waits):
                for k, v in waits:
                    e.wait_ge(sems[k], v)
                ins = build(e)
                if cell["inc"]:
                    ins.then_inc(tsem, 1)

            self._pending.setdefault(ename, []).append(go)
            self.nops += 1
            tok = ("t#", len(self._tcells) - 1)
        else:
            def go(e, waits=waits):
                for k, v in waits:
                    e.wait_ge(sems[k], v)
                ins = build(e)
                ins.then_inc(sems[semkey], inc)

            self._pending.setdefault(ename, []).append(go)
            self.cnt[semkey] += inc
            self.nops += 1
            tok = (semkey, self.cnt[semkey])
        for b in writes:
            kb = b if isinstance(b, str) else id(b)
            self.writer[kb] = tok
            self.readers[kb] = []
        for b in reads:
            kb = b if isinstance(b, str) else id(b)
            self.readers.setdefault(kb, []).append(tok)
        return tok

    def op(self, ek, build, reads=(), writes=(), extra=()):
        assert ek in ("v", "s", "t", "g")
        return self._emit(ek, ek, 1, build, reads, writes, extra)

    def dma(self, q, build, reads=(), writes=(), extra=()):
        assert q in ("qs", "qg")
        i = self.qrr[q]
        self.qrr[q] = (i + 1) % self.NQSEM
        semkey = f"{q}{i}"
        extra = list(extra)
        if self.cnt[semkey] > 0:
            extra.append((semkey, self.cnt[semkey]))
        return self._emit(q, semkey, 16, build, reads, writes, extra)

    def collective(self, build, reads=(), writes=(), extra=()):
        extra = list(extra)
        if self.cnt["cc"] > 0:
            extra.append(("cc", self.cnt["cc"]))
        return self._emit("cc", "cc", 1, build, reads, writes, extra)

    def barrier_token(self, keys=None):
        if self._tcells and self._t_last_inc < len(self._tcells) - 1:
            self._resolve_t(len(self._tcells) - 1)
        keys = keys or list(self.cnt)
        return [(k, self.cnt[k]) for k in keys if self.cnt[k] > 0]

    def flush(self):
        for ename, closures in self._pending.items():
            def run_all(e, closures=closures):
                for c in closures:
                    c(e)
            getattr(self.block, ename)(run_all)
        self._pending = {}

    def finish(self):
        toks = self.barrier_token()
        sems = self.sems

        def go(e):
            for k, v in toks:
                key = ("gpsimd", k)
                if self.watermark.get(key, 0) >= v:
                    continue
                e.wait_ge(sems[k], v)

        self._pending.setdefault("gpsimd", []).append(go)
        self.flush()
        self.ctx.close()


def tt(p, out, a, b, alu, reads=None, writes=None, extra=()):
    return p.op("v", lambda e: e.tensor_tensor(out=out, in0=a, in1=b, op=alu),
                reads=reads, writes=writes, extra=extra)


def _bf(x):
    return np.ascontiguousarray(x, dtype=ml_dtypes.bfloat16)


def _f32(x):
    return np.ascontiguousarray(x, dtype=np.float32)


def build_phase_a(tw=TOWN, p=None):
    if p is None:
        p = Prog("phaseA")
    xs_d = p.din("xs", [D, tw + 1])
    wab_d = p.din("wab", [2 * D, 2 * D])
    mag2_d = p.dout("mag2", [D // 2, tw])
    theta_d = p.dout("theta", [D // 2, tw])
    r4_d = p.dout("r4", [D, tw])
    p.start()

    KC = 2 * D // 128
    CC = 2 * D // 128
    NT = tw // 512

    xb = [p.sb(f"xb{i}", [128, tw + 1], BF16) for i in range(D // 128)]
    for i in range(D // 128):
        p.dma("qg", lambda e, i=i: e.dma_start(out=xb[i][:], in_=xs_d[128 * i:128 * (i + 1), :]),
              writes=[xb[i]])

    NPS = 8 if tw >= 512 else 4
    pst = [p.ps(f"psA{i}", [128, 512 if tw >= 512 else tw]) for i in range(NPS)]
    wtile = [p.sb(f"wA{i}", [128, 512], BF16) for i in range(KC)]
    xv = [p.sb(f"xv{i}", [128, tw]) for i in range(D // 128)]
    rr = [p.sb(f"rr{i}", [128, tw]) for i in range(D // 128)]

    CG = 4
    for cg in range(CC // CG):
        for k in range(KC):
            wt = wtile[k]
            p.dma("qg", lambda e, k=k, cg=cg, wt=wt: e.dma_start(
                out=wt[:], in_=wab_d[128 * k:128 * (k + 1), 512 * cg:512 * (cg + 1)]),
                writes=[wt])
        for ci in range(CG):
            c = cg * CG + ci
            for nt in range(max(NT, 1)):
                nw = 512 if tw >= 512 else tw
                psu = pst[(ci * max(NT, 1) + nt) % NPS]
                for k in range(KC):
                    wt = wtile[k]
                    xi = k % (D // 128)
                    off = 0 if k < KC // 2 else 1
                    rhs = xb[xi][:, nt * nw + off: nt * nw + off + nw]
                    p.op("t", lambda e, psu=psu, wt=wt, rhs=rhs, k=k, ci=ci: e.matmul(
                        psu[:], wt[:, 128 * ci:128 * (ci + 1)], rhs,
                        start=(k == 0), stop=(k == KC - 1)),
                        reads=[wt, xb[xi]], writes=[psu])
                if c < CC // 2:
                    dst = xv[c]
                else:
                    dst = rr[c - CC // 2]
                seg = dst[:, nt * nw: (nt + 1) * nw]
                p.op("v", lambda e, psu=psu, seg=seg: e.tensor_scalar(
                    out=seg, in0=psu[:], scalar1=0.0, scalar2=None, op0=ALU.max),
                    reads=[psu], writes=[dst])
                p.op("s", lambda e, seg=seg: e.activation(out=seg, in_=seg, func=AF.Sigmoid),
                    reads=[dst], writes=[dst])

    for i in range(D // 128):
        p.op("v", lambda e, i=i: e.tensor_tensor(out=rr[i][:], in0=rr[i][:], in1=rr[i][:], op=ALU.mult),
             reads=[rr[i]], writes=[rr[i]])
        p.op("v", lambda e, i=i: e.tensor_tensor(out=rr[i][:], in0=rr[i][:], in1=rr[i][:], op=ALU.mult),
             reads=[rr[i]], writes=[rr[i]])
        p.dma("qs", lambda e, i=i: e.dma_start(out=r4_d[128 * i:128 * (i + 1), :], in_=rr[i][:]),
              reads=[rr[i]])

    NPAIR = D // 256
    for i in range(NPAIR):
        re, im = xv[i], xv[i + NPAIR]
        m2, t1, t2 = rr[(3 * i) % 16], rr[(3 * i + 1) % 16], rr[(3 * i + 2) % 16]
        tt(p, m2[:], re[:], re[:], ALU.mult, reads=[re], writes=[m2])
        tt(p, t1[:], im[:], im[:], ALU.mult, reads=[im], writes=[t1])
        tt(p, m2[:], m2[:], t1[:], ALU.add, reads=[m2, t1], writes=[m2])
        p.dma("qs", lambda e, i=i, m2=m2: e.dma_start(
            out=mag2_d[128 * i:128 * (i + 1), :], in_=m2[:]), reads=[m2])
        tt(p, t1[:], re[:], im[:], ALU.min, reads=[re, im], writes=[t1])
        tt(p, t2[:], re[:], im[:], ALU.max, reads=[re, im], writes=[t2])
        p.op("v", lambda e, t2=t2: e.reciprocal(out=t2[:], in_=t2[:]), reads=[t2], writes=[t2])
        tt(p, t1[:], t1[:], t2[:], ALU.mult, reads=[t1, t2], writes=[t1])
        p.op("s", lambda e, t1=t1: e.activation(out=t1[:], in_=t1[:], func=AF.Arctan),
             reads=[t1], writes=[t1])
        tt(p, t2[:], im[:], re[:], ALU.is_gt, reads=[im, re], writes=[t2])
        p.op("v", lambda e, t2=t2, re=re: e.tensor_scalar(
            out=re[:], in0=t2[:], scalar1=-2.0, scalar2=1.0, op0=ALU.mult, op1=ALU.add),
            reads=[t2], writes=[re])
        tt(p, t1[:], t1[:], re[:], ALU.mult, reads=[t1, re], writes=[t1])
        p.op("v", lambda e, t2=t2, t1=t1: e.scalar_tensor_tensor(
            out=t1[:], in0=t2[:], scalar=PI / 2, in1=t1[:], op0=ALU.mult, op1=ALU.add),
            reads=[t2, t1], writes=[t1])
        p.dma("qs", lambda e, i=i, t1=t1: e.dma_start(
            out=theta_d[128 * i:128 * (i + 1), :], in_=t1[:]), reads=[t1])

    p.maybe_finish()
    return p


def build_phase_b(tn=T, p=None):
    if p is None:
        p = Prog("phaseB")
    mag2_d = p.din("mag2", [128, tn])
    theta_d = p.din("theta", [128, tn])
    kvre_d = p.dout("kvre", [128, tn], BF16)
    kvim_d = p.dout("kvim", [128, tn], BF16)
    p.start()

    m2 = p.sb("s_m2", [128, tn])
    th = p.sb("s_th", [128, tn])
    bA = p.sb("s_bA", [128, tn])
    bB = p.sb("s_bB", [128, tn])
    bC = p.sb("s_bC", [128, tn])
    zz = p.sb("s_zz", [128, 1])
    zb = zz[:, 0:1].broadcast_to([128, tn])

    p.dma("qs", lambda e: e.dma_start(out=m2[:], in_=mag2_d[:]), writes=[m2])
    p.dma("qs", lambda e: e.dma_start(out=th[:], in_=theta_d[:]), writes=[th])
    p.op("v", lambda e: e.memset(zz[:], 0.0), writes=[zz])

    def scan(dst, src_, init, op0, reads, writes):
        p.op("v", lambda e: e.tensor_tensor_scan(out=dst, data0=src_, data1=zb,
                                                 initial=init, op0=op0, op1=ALU.add),
             reads=list(reads) + [zz], writes=writes)

    scan(bA[:], m2[:], 0.0, ALU.max, [m2], [bA])
    p.op("v", lambda e: e.reciprocal(out=bA[:], in_=bA[:]), reads=[bA], writes=[bA])
    tt(p, bA[:], bA[:], m2[:], ALU.mult, reads=[bA, m2], writes=[bA])
    scan(m2[:], bA[:], 1.0, ALU.mult, [bA], [m2])
    p.op("s", lambda e: e.activation(out=m2[:], in_=m2[:], func=AF.Sqrt),
         reads=[m2], writes=[m2])

    scan(bA[:], th[:], 0.0, ALU.add, [th], [bA])
    p.op("v", lambda e: e.tensor_scalar(out=th[:], in0=bA[:], scalar1=INV_2PI,
                                        scalar2=None, op0=ALU.mult),
         reads=[bA], writes=[th])
    ki = bB[:].bitcast(I32)
    p.op("v", lambda e: e.tensor_copy(out=ki, in_=th[:]), reads=[th], writes=[bB])
    p.op("v", lambda e: e.tensor_copy(out=bC[:], in_=ki), reads=[bB], writes=[bC])
    tt(p, th[:], th[:], bC[:], ALU.subtract, reads=[th, bC], writes=[th])
    p.op("v", lambda e: e.tensor_scalar(out=bB[:], in0=th[:], scalar1=0.5, scalar2=None,
                                        op0=ALU.is_gt), reads=[th], writes=[bB])
    tt(p, bC[:], bC[:], bB[:], ALU.add, reads=[bC, bB], writes=[bC])
    p.op("v", lambda e: e.tensor_scalar(out=bB[:], in0=th[:], scalar1=-0.5, scalar2=None,
                                        op0=ALU.is_lt), reads=[th], writes=[bB])
    tt(p, bC[:], bC[:], bB[:], ALU.subtract, reads=[bC, bB], writes=[bC])
    p.op("v", lambda e: e.scalar_tensor_tensor(out=bA[:], in0=bC[:], scalar=-TWO_PI,
                                               in1=bA[:], op0=ALU.mult, op1=ALU.add),
         reads=[bC, bA], writes=[bA])
    p.op("v", lambda e: e.tensor_scalar(out=bA[:], in0=bA[:], scalar1=PI, scalar2=-PI,
                                        op0=ALU.min, op1=ALU.max), reads=[bA], writes=[bA])
    p.op("s", lambda e: e.activation(out=bB[:], in_=bA[:], func=AF.Sin),
         reads=[bA], writes=[bB])
    p.op("s", lambda e: e.activation(out=bC[:], in_=bA[:], func=AF.Abs),
         reads=[bA], writes=[bC])
    p.op("s", lambda e: e.activation(out=bC[:], in_=bC[:], func=AF.Sin, scale=-1.0, bias=p.constcell(PI / 2)),
         reads=[bC, p._cst_s], writes=[bC])
    obre = th[:].bitcast(BF16)[:, 0:tn]
    obim = th[:].bitcast(BF16)[:, tn:2 * tn]
    tt(p, obre, m2[:], bC[:], ALU.mult, reads=[m2, bC], writes=[th])
    tt(p, obim, m2[:], bB[:], ALU.mult, reads=[m2, bB, th], writes=[th])
    p.dma("qs", lambda e: e.dma_start(out=kvre_d[:], in_=obre), reads=[th])
    p.dma("qs", lambda e: e.dma_start(out=kvim_d[:], in_=obim), reads=[th])

    p.maybe_finish()
    return p


def emit_matmul_fm(p, out_chunks, lhsT_tiles, rhs_tiles, nK, nM, nN, nw,
                   psum_pool, evac):
    for m in range(nM):
        for n in range(nN):
            psu = psum_pool[(m * nN + n) % len(psum_pool)]
            for k in range(nK):
                lt, lreads = lhsT_tiles(k, m)
                rt, rreads = rhs_tiles(k, n)
                p.op("t", lambda e, psu=psu, lt=lt, rt=rt, k=k: e.matmul(
                    psu[:], lt, rt, start=(k == 0), stop=(k == nK - 1)),
                    reads=list(lreads) + list(rreads), writes=[psu])
            evac(m, n, psu)


def emit_ln_fm(p, h_chunks, hb_chunks, ones_col, ones_b, psum_small, psum_bcast, scratch_rows,
               tw=TOWN, eps=1e-5, extra_first=()):
    nch = len(h_chunks)
    nd = nch * 128
    psum_s1, psum_s2 = psum_small
    psum_b1, psum_b2 = psum_bcast
    r_mu, r_ms, r_a, r_b = scratch_rows
    nw = min(tw, 512)
    ntt = (tw + nw - 1) // nw
    for h, hb in zip(h_chunks, hb_chunks):
        p.op("s", lambda e, h=h, hb=hb: e.activation(out=hb[:], in_=h[:], func=AF.Square),
             reads=[h], writes=[hb], extra=extra_first)
    for j in range(ntt):
        sl = slice(j * nw, (j + 1) * nw)
        for i, h in enumerate(h_chunks):
            p.op("t", lambda e, h=h, i=i, sl=sl: e.matmul(
                psum_s1[:], ones_col[:, 0:1], h[:, sl], start=(i == 0), stop=(i == nch - 1)),
                reads=[h, ones_col], writes=[psum_s1])
        for i, hb in enumerate(hb_chunks):
            p.op("t", lambda e, hb=hb, i=i, sl=sl: e.matmul(
                psum_s2[:], ones_b[:, 0:1], hb[:, sl], start=(i == 0), stop=(i == nch - 1)),
                reads=[hb, ones_b], writes=[psum_s2])
        p.op("v", lambda e, sl=sl: e.tensor_scalar(out=r_mu[0:1, sl], in0=psum_s1[:],
                                                   scalar1=1.0 / nd, scalar2=None, op0=ALU.mult),
             reads=[psum_s1], writes=[r_mu])
        p.op("v", lambda e, sl=sl: e.tensor_scalar(out=r_ms[0:1, sl], in0=psum_s2[:],
                                                   scalar1=1.0 / nd, scalar2=None, op0=ALU.mult),
             reads=[psum_s2], writes=[r_ms])
    tt(p, r_a[:], r_mu[:], r_mu[:], ALU.mult, reads=[r_mu], writes=[r_a])
    tt(p, r_ms[:], r_ms[:], r_a[:], ALU.subtract, reads=[r_ms, r_a], writes=[r_ms])
    p.op("v", lambda e: e.tensor_scalar(out=r_ms[:], in0=r_ms[:], scalar1=eps, scalar2=None,
                                        op0=ALU.add), reads=[r_ms], writes=[r_ms])
    p.op("s", lambda e: e.activation(out=r_ms[:], in_=r_ms[:], func=AF.Sqrt),
         reads=[r_ms], writes=[r_ms])
    p.op("v", lambda e: e.reciprocal(out=r_a[:], in_=r_ms[:]), reads=[r_ms], writes=[r_a])
    tt(p, r_b[:], r_mu[:], r_a[:], ALU.mult, reads=[r_mu, r_a], writes=[r_b])
    p.op("v", lambda e: e.tensor_scalar(out=r_b[:], in0=r_b[:], scalar1=-1.0, scalar2=None,
                                        op0=ALU.mult), reads=[r_b], writes=[r_b])
    ab = p.sb(f"ln_ab_{p.nops}", [128, tw])
    bb = p.sb(f"ln_bb_{p.nops}", [128, tw])
    onesrow = ones_col[0:1, 0:1].broadcast_to([1, 128])
    for j in range(ntt):
        sl = slice(j * nw, (j + 1) * nw)
        p.op("t", lambda e, sl=sl: e.matmul(psum_b1[:], onesrow, r_a[0:1, sl],
                                            start=True, stop=True),
             reads=[r_a, ones_col], writes=[psum_b1])
        p.op("t", lambda e, sl=sl: e.matmul(psum_b2[:], onesrow, r_b[0:1, sl],
                                            start=True, stop=True),
             reads=[r_b, ones_col], writes=[psum_b2])
        p.op("v", lambda e, sl=sl: e.tensor_copy(out=ab[:, sl], in_=psum_b1[:]),
             reads=[psum_b1], writes=[ab])
        p.op("v", lambda e, sl=sl: e.tensor_copy(out=bb[:, sl], in_=psum_b2[:]),
             reads=[psum_b2], writes=[bb])
    for h, hb in zip(h_chunks, hb_chunks):
        tt(p, h[:], h[:], ab[:], ALU.mult, reads=[h, ab], writes=[h])
        tt(p, h[:], h[:], bb[:], ALU.add, reads=[h, bb], writes=[h])
        p.op("v", lambda e, h=h, hb=hb: e.tensor_copy(out=hb[:], in_=h[:]),
             reads=[h], writes=[hb])


def build_phase_c(tw=TOWN, p=None):
    if p is None:
        p = Prog("phaseC")
    kv_d = p.din("kv", [D, tw], BF16)
    r4_d = p.din("r4", [D, tw])
    xs_d = p.din("xs", [D, tw])
    wv_d = p.din("wv", [D, D])
    ones_d = p.din("ones", [128, 1])
    h1_d = p.dout("h1", [D, tw])
    h1b_d = p.dout("h1b", [D, tw], BF16)
    p.start()

    NCH = D // 128
    kv = [p.sb(f"c_kv{i}", [128, tw], BF16) for i in range(NCH)]
    r4 = [p.sb(f"c_r4_{i}", [128, tw], BF16) for i in range(NCH)]
    h = [p.sb(f"c_h{i}", [128, tw]) for i in range(NCH)]
    hb = [p.sb(f"c_hb{i}", [128, tw], BF16) for i in range(NCH)]
    ones = p.sb("c_ones", [128, 1])
    p.dma("qs", lambda e: e.dma_start(out=ones[:], in_=ones_d[:]), writes=[ones])
    for i in range(NCH):
        p.dma("qs", lambda e, i=i: e.dma_start(out=kv[i][:], in_=kv_d[128 * i:128 * (i + 1), :]),
              writes=[kv[i]])
        p.dma("qg", lambda e, i=i: e.dma_start(out=r4[i][:], in_=r4_d[128 * i:128 * (i + 1), :]),
              writes=[r4[i]])
        p.dma("qs", lambda e, i=i: e.dma_start(out=h[i][:], in_=xs_d[128 * i:128 * (i + 1), :]),
              writes=[h[i]])

    NT = max(tw // 512, 1)
    nw = 512 if tw >= 512 else tw
    pst = [p.ps(f"psC{i}", [128, nw]) for i in range(4)]
    wtl = [p.sb(f"wC{i}", [128, 512], BF16) for i in range(D // 128)]
    CG = 4
    for cg in range(NCH // CG):
        for k in range(NCH):
            wt = wtl[k]
            p.dma("qg", lambda e, k=k, cg=cg, wt=wt: e.dma_start(
                out=wt[:], in_=wv_d[128 * k:128 * (k + 1), 512 * cg:512 * (cg + 1)]),
                writes=[wt])
        for ci in range(CG):
            c = cg * CG + ci
            for nt in range(NT):
                psu = pst[(ci * NT + nt) % 4]
                for k in range(NCH):
                    wt = wtl[k]
                    rhs = kv[k][:, nt * nw:(nt + 1) * nw]
                    p.op("t", lambda e, psu=psu, wt=wt, rhs=rhs, k=k, ci=ci: e.matmul(
                        psu[:], wt[:, 128 * ci:128 * (ci + 1)], rhs,
                        start=(k == 0), stop=(k == NCH - 1)),
                        reads=[wt, kv[k]], writes=[psu])
                seg_h = h[c][:, nt * nw:(nt + 1) * nw]
                seg_r = r4[c][:, nt * nw:(nt + 1) * nw]
                p.op("v", lambda e, psu=psu, seg_r=seg_r: e.tensor_tensor(
                    out=psu[:], in0=psu[:], in1=seg_r, op=ALU.mult),
                    reads=[psu, r4[c]], writes=[psu])
                p.op("v", lambda e, psu=psu, seg_h=seg_h: e.tensor_tensor(
                    out=seg_h, in0=seg_h, in1=psu[:], op=ALU.add),
                    reads=[psu, h[c]], writes=[h[c]])

    ones_b = p.sb("c_ones_b", [128, 1], BF16)
    p.op("v", lambda e: e.tensor_copy(out=ones_b[:], in_=ones[:]), reads=[ones], writes=[ones_b])
    ps_s = [p.ps("psLs1", [1, min(tw, 512)]), p.ps("psLs2", [1, min(tw, 512)])]
    ps_b = [p.ps("psLb1", [128, min(tw, 512)]), p.ps("psLb2", [128, min(tw, 512)])]
    rows = [p.sb(f"lnrow{i}", [1, tw]) for i in range(4)]
    emit_ln_fm(p, h, hb, ones, ones_b, ps_s, ps_b, rows, tw=tw)

    for i in range(NCH):
        p.dma("qs", lambda e, i=i: e.dma_start(out=h1_d[128 * i:128 * (i + 1), :], in_=h[i][:]),
              reads=[h[i]])
        p.dma("qs", lambda e, i=i: e.dma_start(out=h1b_d[128 * i:128 * (i + 1), :], in_=hb[i][:]),
              reads=[hb[i]])

    p.maybe_finish()
    return p


NEXP = 32
TOPK = 8
CAP = 2 * (T * TOPK // NEXP)


def build_phase_d(tw=TOWN, p=None):
    if p is None:
        p = Prog("phaseD")
    h1_d = p.din("h1", [D, tw])
    gw_d = p.din("gw", [D, NEXP])
    tri_d = p.din("tri", [128, 128])
    ones_d = p.din("ones", [128, 1])
    wmat_d = p.dout("wmat", [tw, NEXP])
    posl_d = p.dout("posl", [tw, NEXP])
    cnt_d = p.dout("cnt", [1, NEXP])
    p.start()

    NCH = D // 128
    NTCH = tw // 128
    h = [p.sb(f"d_h{i}", [128, tw]) for i in range(NCH)]
    gw = [p.sb(f"d_gw{i}", [128, NEXP]) for i in range(NCH)]
    tri = p.sb("d_tri", [128, 128])
    for i in range(NCH):
        p.dma("qs", lambda e, i=i: e.dma_start(out=h[i][:], in_=h1_d[128 * i:128 * (i + 1), :]), writes=[h[i]])
        p.dma("qs", lambda e, i=i: e.dma_start(out=gw[i][:], in_=gw_d[128 * i:128 * (i + 1), :]), writes=[gw[i]])
    p.dma("qs", lambda e: e.dma_start(out=tri[:], in_=tri_d[:]), writes=[tri])
    onesD = p.sb("d_ones", [128, 1])
    p.dma("qs", lambda e: e.dma_start(out=onesD[:], in_=ones_d[:]), writes=[onesD])

    psl = p.ps("d_psl", [128, NEXP])
    psc = p.ps("d_psc", [128, NEXP])
    pst = p.ps("d_pst", [1, NEXP])
    z = p.sb("d_z", [128, NEXP])
    mx = p.sb("d_mx", [128, TOPK])
    zap = p.sb("d_zap", [128, NEXP])
    wch = p.sb("d_wch", [128, NEXP])
    sel = p.sb("d_sel", [128, NEXP])
    ssum = p.sb("d_ssum", [128, 1])
    run = p.sb("d_run", [1, NEXP])
    posb = p.sb("d_posb", [128, NEXP])
    p.op("v", lambda e: e.memset(run[:], 0.0), writes=[run])

    for j in range(NTCH):
        for i in range(NCH):
            p.op("t", lambda e, i=i, j=j: e.matmul(
                psl[:], h[i][:, 128 * j:128 * (j + 1)], gw[i][:],
                start=(i == 0), stop=(i == NCH - 1)),
                reads=[h[i], gw[i]], writes=[psl])
        p.op("v", lambda e: e.reduce_max(out=ssum[:], in_=psl[:], axis=AX.X),
             reads=[psl], writes=[ssum])
        p.op("v", lambda e: e.tensor_scalar(out=z[:], in0=psl[:], scalar1=ssum[:, 0:1],
                                            scalar2=None, op0=ALU.subtract),
             reads=[psl, ssum], writes=[z])
        p.op("s", lambda e: e.activation(out=z[:], in_=z[:], func=AF.Exp),
             reads=[z], writes=[z])
        p.op("v", lambda e: e.max(out=mx[:], in_=z[:]), reads=[z], writes=[mx])
        p.op("v", lambda e: e.match_replace(out=zap[:], in_to_replace=mx[:], in_values=z[:],
                                            imm_value=0.0), reads=[z, mx], writes=[zap])
        tt(p, wch[:], z[:], zap[:], ALU.subtract, reads=[z, zap], writes=[wch])
        p.op("v", lambda e: e.reduce_sum(out=ssum[:], in_=wch[:], axis=AX.X),
             reads=[wch], writes=[ssum])
        p.op("v", lambda e: e.reciprocal(out=ssum[:], in_=ssum[:]), reads=[ssum], writes=[ssum])
        p.op("v", lambda e: e.tensor_scalar(out=wch[:], in0=wch[:], scalar1=ssum[:, 0:1],
                                            scalar2=None, op0=ALU.mult),
             reads=[wch, ssum], writes=[wch])
        p.dma("qs", lambda e, j=j: e.dma_start(out=wmat_d[128 * j:128 * (j + 1), :], in_=wch[:]),
              reads=[wch])
        p.op("v", lambda e: e.tensor_scalar(out=sel[:], in0=wch[:], scalar1=0.0, scalar2=None,
                                            op0=ALU.is_gt), reads=[wch], writes=[sel])
        p.op("t", lambda e: e.matmul(psc[:], tri[:], sel[:], start=True, stop=False),
             reads=[tri, sel], writes=[psc])
        p.op("t", lambda e: e.matmul(psc[:], onesD[0:1, 0:1].broadcast_to([1, 128]), run[:],
                                     start=False, stop=True),
             reads=[onesD, run], writes=[psc])
        p.op("v", lambda e: e.tensor_copy(out=posb[:], in_=psc[:]), reads=[psc], writes=[posb])
        p.dma("qs", lambda e, j=j: e.dma_start(out=posl_d[128 * j:128 * (j + 1), :], in_=posb[:]),
              reads=[posb])
        p.op("t", lambda e: e.matmul(pst[:], onesD[:, 0:1], sel[:], start=True, stop=True),
             reads=[onesD, sel], writes=[pst])
        p.op("v", lambda e: e.tensor_tensor(out=run[:], in0=run[:], in1=pst[:], op=ALU.add),
             reads=[run, pst], writes=[run])
    p.dma("qs", lambda e: e.dma_start(out=cnt_d[:], in_=run[:]), reads=[run])
    p.maybe_finish()
    return p


def build_phase_e(tn=T, tile=512, p=None):
    if p is None:
        p = Prog("phaseE")
    xall_d = p.din("xall", [D, tn], BF16)
    wg_d = p.din("wg", [4 * D, 768])
    wu_d = p.din("wu", [4 * D, 768])
    wd_d = p.din("wd", [4 * 768, D])
    wall_d = p.din("wall", [tn, NEXP])
    posall_d = p.din("posall", [tn, NEXP])
    cnt8_d = p.din("cnt8", [8, NEXP])
    cmask_d = p.din("cmask", [8, 8])
    ones_d = p.din("ones", [128, 1])
    eye_d = p.din("eye", [128, 128])
    esel_d = p.din("esel", [4, 1], I32)
    part_d = p.dout("part", [D, tn], BF16)
    we_dram = p.dtmp("we_dram", [NEXP, tn])
    my_we = p.dtmp("my_we", [4, tn])

    p.start()
    NCH = D // 128
    NI = 768 // 128
    NTT = tn // tile

    ones = p.sb("e_ones", [128, 1])
    onesb = p.sb("e_onesb", [128, 1], BF16)
    cnt8 = p.sb("e_cnt8", [8, NEXP])
    cmask = p.sb("e_cmask", [8, 8])
    p.dma("qs", lambda e: e.dma_start(out=ones[:], in_=ones_d[:]), writes=[ones])
    p.dma("qs", lambda e: e.dma_start(out=cnt8[:], in_=cnt8_d[:]), writes=[cnt8])
    p.dma("qs", lambda e: e.dma_start(out=cmask[:], in_=cmask_d[:]), writes=[cmask])
    eye = p.sb("e_eye", [128, 128])
    p.dma("qs", lambda e: e.dma_start(out=eye[:], in_=eye_d[:]), writes=[eye])
    p.op("v", lambda e: e.tensor_copy(out=onesb[:], in_=ones[:]), reads=[ones], writes=[onesb])

    sub = ExitStack()
    pbc = sub.enter_context(p.nc.psum_tensor("e_pbc", [128, NEXP], F32))
    ptr = sub.enter_context(p.nc.psum_tensor("e_ptr", [NEXP, 128], F32))
    wch = sub.enter_context(p.nc.sbuf_tensor("e_wch", [128, NEXP], F32))
    pch = sub.enter_context(p.nc.sbuf_tensor("e_pch", [128, NEXP], F32))
    wtr = sub.enter_context(p.nc.sbuf_tensor("e_wtr", [NEXP, 128], F32))
    for j in range(tn // 128):
        core = j // max(tn // 128 // 8, 1)
        p.dma("qs", lambda e, j=j: e.dma_start(out=wch[:], in_=wall_d[128 * j:128 * (j + 1), :]),
              writes=[wch])
        p.dma("qs", lambda e, j=j: e.dma_start(out=pch[:], in_=posall_d[128 * j:128 * (j + 1), :]),
              writes=[pch])
        p.op("t", lambda e, core=core: e.matmul(
            pbc[:], cmask[:, core:core + 1].broadcast_to([8, 128]), cnt8[:],
            start=True, stop=True),
             reads=[cmask, cnt8], writes=[pbc])
        p.op("v", lambda e: e.tensor_tensor(out=pch[:], in0=pch[:], in1=pbc[:], op=ALU.add),
             reads=[pch, pbc], writes=[pch])
        p.op("v", lambda e: e.tensor_scalar(out=pch[:], in0=pch[:], scalar1=float(CAP),
                                            scalar2=None, op0=ALU.is_lt),
             reads=[pch], writes=[pch])
        tt(p, wch[:], wch[:], pch[:], ALU.mult, reads=[wch, pch], writes=[wch])
        p.op("t", lambda e: e.transpose(ptr[:], wch[:], eye[:]),
             reads=[wch, eye], writes=[ptr])
        p.op("v", lambda e: e.tensor_copy(out=wtr[:], in_=ptr[:]), reads=[ptr], writes=[wtr])
        p.dma("qs", lambda e, j=j: e.dma_start(out=we_dram[:, 128 * j:128 * (j + 1)], in_=wtr[:]),
              reads=[wtr], writes=["we_dram"])
    esel_s = sub.enter_context(p.nc.sbuf_tensor("e_esel", [4, 1], I32))
    wesel = sub.enter_context(p.nc.sbuf_tensor("e_wesel", [4, tn], F32))
    p.dma("qs", lambda e: e.dma_start(out=esel_s[:], in_=esel_d[:]), writes=[esel_s])
    p.op("v", lambda e: e.memset(wesel[:], 0.0), writes=[wesel])
    p.dma("qg", lambda e: e.indirect_dma_start(
        out=wesel[:], out_offset=None, in_=we_dram[:],
        in_offset=IndirectOffsetOnAxis(ap=esel_s[:], axis=0),
        bounds_check=NEXP - 1, oob_is_err=False),
        reads=[esel_s, "we_dram"], writes=[wesel])
    p.dma("qs", lambda e: e.dma_start(out=my_we[:], in_=wesel[:]),
          reads=[wesel], writes=["my_we"])

    sub.close()
    p.fence()

    wg = [[p.sb(f"e_wg{b}_{k}", [128, 768], BF16) for k in range(NCH)] for b in range(2)]
    wu = [[p.sb(f"e_wu{b}_{k}", [128, 768], BF16) for k in range(NCH)] for b in range(2)]
    wd = [[p.sb(f"e_wd{b}_{k}", [128, D], BF16) for k in range(NI)] for b in range(2)]
    xt = [p.sb(f"e_xt{k}", [128, tile], BF16) for k in range(NCH)]
    yt = [p.sb(f"e_yt{k}", [128, tile], BF16) for k in range(2 * NI)]
    ot = [p.sb(f"e_ot{k}", [128, tile], BF16) for k in range(3)]
    wrow = p.sb("e_wrow", [1, tile], BF16)
    psg = [p.ps(f"e_psg{k}", [128, tile]) for k in range(2)]
    psu_ = [p.ps(f"e_psu{k}", [128, tile]) for k in range(2)]
    psD = [p.ps(f"e_psD{k}", [128, tile]) for k in range(3)]
    pwb = p.ps("e_pwb", [128, tile])
    wbc = p.sb("e_wbc", [128, tile], BF16)

    st_tok = {}
    iters = [(eloc, ttn) for eloc in range(4) for ttn in range(NTT)]

    def emit_weights(eloc):
        wgb, wub = wg[eloc % 2], wu[eloc % 2]
        for k in range(NCH):
            p.dma("qg", lambda e, k=k, eloc=eloc, wgb=wgb: e.dma_start(
                out=wgb[k][:], in_=wg_d[eloc * D + 128 * k: eloc * D + 128 * (k + 1), :]),
                writes=[wgb[k]])
            p.dma("qg", lambda e, k=k, eloc=eloc, wub=wub: e.dma_start(
                out=wub[k][:], in_=wu_d[eloc * D + 128 * k: eloc * D + 128 * (k + 1), :]),
                writes=[wub[k]])
        wdb = wd[eloc % 2]
        for k in range(NI):
            p.dma("qg", lambda e, k=k, eloc=eloc, wdb=wdb: e.dma_start(
                out=wdb[k][:], in_=wd_d[eloc * 768 + 128 * k: eloc * 768 + 128 * (k + 1), :]),
                writes=[wdb[k]])

    def emit_loads(it):
        eloc, ttn = it
        tsl = slice(ttn * tile, (ttn + 1) * tile)

        def xall_sl(k, tsl=tsl):
            if len(xall_d.shape) == 3:
                c = tsl.start // TOWN
                w0 = tsl.start % TOWN
                return xall_d[128 * k:128 * (k + 1), c, w0:w0 + (tsl.stop - tsl.start)]
            return xall_d[128 * k:128 * (k + 1), tsl]
        for k in range(NCH):
            p.dma("qs", lambda e, k=k, tsl=tsl: e.dma_start(
                out=xt[k][:], in_=xall_sl(k)), writes=[xt[k]])
        p.dma("qg", lambda e, eloc=eloc, tsl=tsl: e.dma_start(
            out=wrow[:], in_=my_we[eloc:eloc + 1, tsl]),
            reads=["my_we"], writes=[wrow])
        p.op("t", lambda e: e.matmul(pwb[:], onesb[0:1, 0:1].broadcast_to([1, 128]),
                                     wrow[:], start=True, stop=True),
             reads=[wrow, onesb], writes=[pwb])
        p.op("v", lambda e: e.tensor_copy(out=wbc[:], in_=pwb[:]), reads=[pwb], writes=[wbc])

    def emit_gu_pass(it, half):
        eloc, ttn = it
        wgb, wub = wg[eloc % 2], wu[eloc % 2]
        ybase = (ttn % 2) * NI
        for ii in range(2):
            i = half * 2 + ii
            for k in range(NCH):
                p.op("t", lambda e, i=i, ii=ii, k=k, wgb=wgb: e.matmul(
                    psg[ii][:], wgb[k][:, 128 * i:128 * (i + 1)], xt[k][:],
                    start=(k == 0), stop=(k == NCH - 1)),
                    reads=[wgb[k], xt[k]], writes=[psg[ii]])
            for k in range(NCH):
                p.op("t", lambda e, i=i, ii=ii, k=k, wub=wub: e.matmul(
                    psu_[ii][:], wub[k][:, 128 * i:128 * (i + 1)], xt[k][:],
                    start=(k == 0), stop=(k == NCH - 1)),
                    reads=[wub[k], xt[k]], writes=[psu_[ii]])

    def emit_gu_evac(it, half):
        eloc, ttn = it
        ybase = (ttn % 2) * NI
        for ii in range(2):
            i = half * 2 + ii
            y = yt[ybase + i]
            p.op("s", lambda e, ii=ii, y=y: e.activation(
                out=y[:], in_=psg[ii][:], func=AF.Silu),
                reads=[psg[ii]], writes=[y])
            p.op("v", lambda e, ii=ii, y=y: e.tensor_tensor(
                out=y[:], in0=y[:], in1=psu_[ii][:], op=ALU.mult),
                reads=[y, psu_[ii]], writes=[y])
            p.op("v", lambda e, y=y: e.tensor_tensor(
                out=y[:], in0=y[:], in1=wbc[:], op=ALU.mult),
                reads=[y, wbc], writes=[y])

    def emit_down_chunks(it, c0, c1):
        eloc, ttn = it
        tsl = slice(ttn * tile, (ttn + 1) * tile)
        ybase = (ttn % 2) * NI

        def part_sl(c, tsl=tsl):
            if len(part_d.shape) == 3:
                cb = tsl.start // TOWN
                w0 = tsl.start % TOWN
                return part_d[128 * c:128 * (c + 1), cb, w0:w0 + (tsl.stop - tsl.start)]
            return part_d[128 * c:128 * (c + 1), tsl]
        wdb = wd[eloc % 2]
        for c in range(c0, c1):
            psb = psD[c % 3]
            om = ot[c % 3]
            for k in range(NI):
                p.op("t", lambda e, c=c, k=k, psb=psb, ybase=ybase, wdb=wdb: e.matmul(
                    psb[:], wdb[k][:, 128 * c:128 * (c + 1)], yt[ybase + k][:],
                    start=(k == 0), stop=(k == NI - 1)),
                    reads=[wdb[k], yt[ybase + k]], writes=[psb])
            p.op("v", lambda e, psb=psb, om=om: e.tensor_copy(out=om[:], in_=psb[:]),
                 reads=[psb], writes=[om])
            aop = ALU.add if eloc > 0 else ALU.bypass
            key = (c, ttn)
            prev = st_tok.get(key)
            tok = p.dma("qg", lambda e, c=c, om=om, aop=aop: e.dma_start(
                out=part_sl(c), in_=om[:], accum_op=aop),
                reads=[om], extra=[prev] if prev else [])
            st_tok[key] = tok

    emit_weights(0)
    for n, it in enumerate(iters):
        eloc, ttn = it
        if ttn == 1 and eloc < 3:
            emit_weights(eloc + 1)
        emit_loads(it)
        prev = iters[n - 1] if n > 0 else None
        for half in range(3):
            emit_gu_pass(it, half)
            if prev is not None:
                th = 16 // 3
                c0 = half * th
                c1 = (half + 1) * th if half < 2 else 16
                emit_down_chunks(prev, c0, c1)
            emit_gu_evac(it, half)
    emit_down_chunks(iters[-1], 0, 16)
    p.maybe_finish()
    return p


def build_phase_f(tw=TOWN, p=None, nparts=8):
    if p is None:
        p = Prog("phaseF")
    pj_d = p.din("pj", [nparts * D, tw], BF16)
    h1_d = p.din("h1", [D, tw])
    ones_d = p.din("ones", [128, 1])
    h2_d = p.dout("h2", [D, tw])
    h2b_d = p.dout("h2b", [D, tw], BF16)
    p.start()
    NCH = D // 128
    ones = p.sb("f_ones", [128, 1])
    ones_b = p.sb("f_ones_b", [128, 1], BF16)
    p.dma("qs", lambda e: e.dma_start(out=ones[:], in_=ones_d[:]), writes=[ones])
    p.op("v", lambda e: e.tensor_copy(out=ones_b[:], in_=ones[:]), reads=[ones], writes=[ones_b])
    h = [p.sb(f"f_h{i}", [128, tw]) for i in range(NCH)]
    hb = [p.sb(f"f_hb{i}", [128, tw], BF16) for i in range(NCH)]
    pj = [p.sb(f"f_pj{i}", [128, tw], BF16) for i in range(max(nparts, 2))]
    acc = p.sb("f_acc", [128, tw])
    for i in range(NCH):
        p.dma("qs", lambda e, i=i: e.dma_start(out=h[i][:], in_=h1_d[128 * i:128 * (i + 1), :]),
              writes=[h[i]])
        for j in range(nparts):
            p.dma("qs", lambda e, i=i, j=j: e.dma_start(
                out=pj[j][:], in_=pj_d[j * D + 128 * i: j * D + 128 * (i + 1), :]),
                writes=[pj[j]])
        if nparts == 1:
            tt(p, h[i][:], h[i][:], pj[0][:], ALU.add, reads=[h[i], pj[0]], writes=[h[i]])
        else:
            tt(p, acc[:], pj[0][:], pj[1][:], ALU.add, reads=[pj[0], pj[1]], writes=[acc])
            for j in range(2, nparts):
                tt(p, acc[:], acc[:], pj[j][:], ALU.add, reads=[acc, pj[j]], writes=[acc])
            tt(p, h[i][:], h[i][:], acc[:], ALU.add, reads=[h[i], acc], writes=[h[i]])
    ps_s = [p.ps("f_psLs1", [1, min(tw, 512)]), p.ps("f_psLs2", [1, min(tw, 512)])]
    ps_b = [p.ps("f_psLb1", [128, min(tw, 512)]), p.ps("f_psLb2", [128, min(tw, 512)])]
    rows = [p.sb(f"f_lnrow{i}", [1, tw]) for i in range(4)]
    emit_ln_fm(p, h, hb, ones, ones_b, ps_s, ps_b, rows, tw=tw)
    for i in range(NCH):
        p.dma("qs", lambda e, i=i: e.dma_start(out=h2_d[128 * i:128 * (i + 1), :], in_=h[i][:]),
              reads=[h[i]])
        p.dma("qs", lambda e, i=i: e.dma_start(out=h2b_d[128 * i:128 * (i + 1), :], in_=hb[i][:]),
              reads=[hb[i]])
    p.maybe_finish()
    return p


NH = 64
HP = 64
NN = 128
CS = 128
INTER = 4096
CONV_DIM = INTER + 2 * NN
NXC = CONV_DIM // 128


def build_phase_g(tw=TOWN, p=None):
    if p is None:
        p = Prog("phaseG")
    hb_d = p.din("hb", [D, tw + 3], BF16)
    wpg_d = p.din("wpg", [D, INTER])
    wpx_d = p.din("wpx", [D, CONV_DIM])
    wpd_d = p.din("wpd", [D, NH])
    convw_d = p.din("convw", [CONV_DIM, 4])
    convbt_d = p.din("convbt", [1, CONV_DIM])
    onesr_d = p.din("onesr", [1, 512])
    dtb_d = p.din("dtb", [NH, 1])
    alog_d = p.din("alog", [NH, 1])
    deye_d = p.din("deye", [128, NH])
    eye_d = p.din("eye", [128, 128])
    ones_d = p.din("ones", [128, 1])
    gsil_d = p.dout("gsil", [INTER, tw], BF16)
    yt_d = p.dout("yt", [INTER, tw], BF16)
    cw_d = p.dout("cw", [(tw // CS) * NN, NH * CS], BF16)
    sacc_d = p.dout("sacc", [NN, NH * HP], BF16)
    espan_d = p.dout("espan", [tw // CS, NH])
    espantot_d = p.dout("espantot", [1, NH])
    ac_dram = p.dtmp("ac_dram", [NH, CS])

    p.start()
    NCH = D // 128
    NCHUNK = tw // CS
    NT = tw // 512
    ones = p.sb("g_ones", [128, 1])
    onesb = p.sb("g_onesb", [128, 1], BF16)
    onesrb = p.sb("g_onesrb", [1, 512], BF16)
    eyeb = p.sb("g_eyeb", [128, 128], BF16)
    eyef = p.sb("g_eyef", [128, 128])
    p.dma("qs", lambda e: e.dma_start(out=ones[:], in_=ones_d[:]), writes=[ones])
    p.dma("qs", lambda e: e.dma_start(out=eyef[:], in_=eye_d[:]), writes=[eyef])
    p.dma("qg", lambda e: e.dma_start(out=eyeb[:], in_=eye_d[:]), writes=[eyeb])
    p.dma("qg", lambda e: e.dma_start(out=onesrb[:], in_=onesr_d[:]), writes=[onesrb])
    p.op("v", lambda e: e.tensor_copy(out=onesb[:], in_=ones[:]), reads=[ones], writes=[onesb])
    K1 = ones[0:1, 0:1].broadcast_to([1, 128])
    K1b = onesb[0:1, 0:1].broadcast_to([1, 128])

    dtm = p.sb("g_dt", [NH, tw])
    dtbc = p.sb("g_dtbc", [NH, 1])
    sc_a = p.sb("g_sca", [NH, tw])
    acol = p.sb("g_acol", [NH, 1])
    acum = p.sb("g_acum", [NH, tw])
    zz1 = p.sb("g_zz1", [NH, 1])
    xtok = [p.sb(f"g_xtok{i}", [128, 4224], BF16) for i in range(NCHUNK)]
    bt = p.sb("g_bt", [128, tw], BF16)
    ct = p.sb("g_ct", [128, tw], BF16)
    subx = ExitStack()
    xbc = [subx.enter_context(p.nc.sbuf_tensor(f"g_xbc{i}", [128, tw + 3], BF16))
           for i in range(NXC)]
    sub1 = ExitStack()
    hb = [sub1.enter_context(p.nc.sbuf_tensor(f"g_hb{i}", [128, tw + 3], BF16))
          for i in range(NCH)]
    wt = [sub1.enter_context(p.nc.sbuf_tensor(f"g_wt{i}", [128, 512], BF16))
          for i in range(NCH)]
    gso = [sub1.enter_context(p.nc.sbuf_tensor(f"g_gso{i}", [128, 512], BF16))
           for i in range(4)]
    psA = [sub1.enter_context(p.nc.psum_tensor(f"g_psA{i}", [128, 512], F32))
           for i in range(4)]
    for i in range(NCH):
        p.dma("qs", lambda e, i=i: e.dma_start(out=hb[i][:], in_=hb_d[128 * i:128 * (i + 1), :]),
              writes=[hb[i]])

    for cg in range(INTER // 512):
        for k in range(NCH):
            p.dma("qg", lambda e, k=k, cg=cg: e.dma_start(
                out=wt[k][:], in_=wpg_d[128 * k:128 * (k + 1), 512 * cg:512 * (cg + 1)]),
                writes=[wt[k]])
        for ci in range(4):
            c = cg * 4 + ci
            for nt in range(NT):
                psu = psA[(ci * NT + nt) % 4]
                for k in range(NCH):
                    p.op("t", lambda e, psu=psu, k=k, ci=ci, nt=nt: e.matmul(
                        psu[:], wt[k][:, 128 * ci:128 * (ci + 1)],
                        hb[k][:, 3 + nt * 512: 3 + (nt + 1) * 512],
                        start=(k == 0), stop=(k == NCH - 1)),
                        reads=[wt[k], hb[k]], writes=[psu])
                g_ = gso[(ci * NT + nt) % 4]
                p.op("s", lambda e, psu=psu, g_=g_: e.activation(out=g_[:], in_=psu[:], func=AF.Silu),
                     reads=[psu], writes=[g_])
                p.dma("qs", lambda e, c=c, nt=nt, g_=g_: e.dma_start(
                    out=gsil_d[128 * c:128 * (c + 1), 512 * nt:512 * (nt + 1)], in_=g_[:]),
                    reads=[g_])

    xtiles = []
    o = 0
    while o < tw + 3:
        nn_ = min(512, tw + 3 - o)
        xtiles.append((o, nn_))
        o += nn_
    for cg in range((NXC + 3) // 4):
        c0 = cg * 4
        ncc = min(4, NXC - c0)
        cwc = 128 * ncc
        for k in range(NCH):
            p.dma("qg", lambda e, k=k, cg=cg, cwc=cwc: e.dma_start(
                out=wt[k][:, 0:cwc],
                in_=wpx_d[128 * k:128 * (k + 1), 512 * cg:512 * cg + cwc]),
                writes=[wt[k]])
        for ci in range(ncc):
            c = c0 + ci
            for (o0, nn_) in xtiles:
                psu = psA[ci % 4]
                for k in range(NCH):
                    p.op("t", lambda e, psu=psu, k=k, ci=ci, o0=o0, nn_=nn_: e.matmul(
                        psu[:, 0:nn_], wt[k][:, 128 * ci:128 * (ci + 1)],
                        hb[k][:, o0:o0 + nn_],
                        start=(k == 0), stop=(k == NCH - 1)),
                        reads=[wt[k], hb[k]], writes=[psu])
                p.op("v", lambda e, psu=psu, c=c, o0=o0, nn_=nn_: e.tensor_copy(
                    out=xbc[c][:, o0:o0 + nn_], in_=psu[:, 0:nn_]),
                    reads=[psu], writes=[xbc[c]])

    for k in range(NCH):
        p.dma("qg", lambda e, k=k: e.dma_start(
            out=wt[k][:, 0:NH], in_=wpd_d[128 * k:128 * (k + 1), :]), writes=[wt[k]])
    p.dma("qs", lambda e: e.dma_start(out=dtbc[:], in_=dtb_d[:]), writes=[dtbc])
    for nt in range(NT):
        psu = psA[nt % 4]
        for k in range(NCH):
            p.op("t", lambda e, psu=psu, k=k, nt=nt: e.matmul(
                psu[0:NH, :], wt[k][:, 0:NH], hb[k][:, 3 + nt * 512: 3 + (nt + 1) * 512],
                start=(k == 0), stop=(k == NCH - 1)),
                reads=[wt[k], hb[k]], writes=[psu])
        p.op("v", lambda e, psu=psu, nt=nt: e.tensor_scalar(
            out=dtm[0:NH, 512 * nt:512 * (nt + 1)], in0=psu[0:NH, :], scalar1=dtbc[:, 0:1],
            scalar2=None, op0=ALU.add), reads=[psu, dtbc], writes=[dtm])
    sub1.close()
    p.fence()

    p.op("s", lambda e: e.activation(out=sc_a[:], in_=dtm[:], func=AF.Abs),
         reads=[dtm], writes=[sc_a])
    p.op("s", lambda e: e.activation(out=sc_a[:], in_=sc_a[:], func=AF.Exp, scale=-1.0),
         reads=[sc_a], writes=[sc_a])
    p.op("s", lambda e: e.activation(out=sc_a[:], in_=sc_a[:], func=AF.Ln,
                                     bias=p.constcell(1.0, NH)),
         reads=[sc_a, p._cst_s], writes=[sc_a])
    p.op("v", lambda e: e.tensor_scalar(out=dtm[:], in0=dtm[:], scalar1=0.0, scalar2=None,
                                        op0=ALU.max), reads=[dtm], writes=[dtm])
    tt(p, dtm[:], dtm[:], sc_a[:], ALU.add, reads=[dtm, sc_a], writes=[dtm])
    p.dma("qs", lambda e: e.dma_start(out=acol[:], in_=alog_d[:]), writes=[acol])
    p.op("s", lambda e: e.activation(out=acol[:], in_=acol[:], func=AF.Exp),
         reads=[acol], writes=[acol])
    adt = sc_a
    p.op("v", lambda e: e.tensor_scalar(out=adt[:], in0=dtm[:], scalar1=acol[:, 0:1],
                                        scalar2=-1.0, op0=ALU.mult, op1=ALU.mult),
         reads=[dtm, acol], writes=[adt])
    p.op("v", lambda e: e.memset(zz1[:], 0.0), writes=[zz1])
    for c in range(NCHUNK):
        sl = slice(c * CS, (c + 1) * CS)
        p.op("v", lambda e, sl=sl: e.tensor_tensor_scan(
            out=acum[:, sl], data0=adt[:, sl], data1=zz1[:, 0:1].broadcast_to([NH, CS]),
            initial=0.0, op0=ALU.add, op1=ALU.add),
            reads=[adt, zz1], writes=[acum])

    sub2 = ExitStack()
    cwgt = sub2.enter_context(p.nc.sbuf_tensor("g_cwgt", [128, 4], F32))
    diag = [sub2.enter_context(p.nc.sbuf_tensor(f"g_diag{j}", [128, 128], BF16))
            for j in range(4)]
    convbt = sub2.enter_context(p.nc.sbuf_tensor("g_convbt", [1, CONV_DIM], BF16))
    psC = [sub2.enter_context(p.nc.psum_tensor(f"g_psC{i}", [128, 512], F32))
           for i in range(2)]
    p.dma("qg", lambda e: e.dma_start(out=convbt[:], in_=convbt_d[:]), writes=[convbt])
    for cc in range(NXC):
        p.dma("qs", lambda e, cc=cc: e.dma_start(
            out=cwgt[:], in_=convw_d[128 * cc:128 * (cc + 1), :]), writes=[cwgt])
        for j in range(4):
            p.op("v", lambda e, j=j: e.tensor_scalar(
                out=diag[j][:], in0=eyeb[:], scalar1=cwgt[:, j:j + 1], scalar2=None,
                op0=ALU.mult), reads=[eyeb, cwgt], writes=[diag[j]])
        if cc < 33:
            for tc in range(NCHUNK):
                psu = psC[tc % 2]
                for j in range(4):
                    p.op("t", lambda e, psu=psu, j=j, tc=tc, cc=cc: e.matmul(
                        psu[:, 0:128], xbc[cc][:, 128 * tc + j:128 * tc + j + 128],
                        diag[j][:], start=(j == 0), stop=False),
                        reads=[xbc[cc], diag[j]], writes=[psu])
                p.op("t", lambda e, psu=psu, cc=cc: e.matmul(
                    psu[:, 0:128], K1b, convbt[0:1, 128 * cc:128 * (cc + 1)],
                    start=False, stop=True),
                    reads=[onesb, convbt], writes=[psu])
                p.op("s", lambda e, psu=psu, tc=tc, cc=cc: e.activation(
                    out=xtok[tc][:, 128 * cc:128 * (cc + 1)],
                    in_=psu[:, 0:128], func=AF.Silu),
                    reads=[psu], writes=[xtok[tc]])
        if cc >= 32:
            dst = bt if cc == 32 else ct
            for nt in range(NT):
                psu = psC[nt % 2]
                for j in range(4):
                    p.op("t", lambda e, psu=psu, j=j, nt=nt, cc=cc: e.matmul(
                        psu[:], diag[j][:], xbc[cc][:, j + nt * 512: j + nt * 512 + 512],
                        start=(j == 0), stop=False),
                        reads=[xbc[cc], diag[j]], writes=[psu])
                p.op("t", lambda e, psu=psu, cc=cc: e.matmul(
                    psu[:], convbt[0:1, 128 * cc:128 * (cc + 1)],
                    onesrb[:], start=False, stop=True),
                    reads=[onesrb, convbt], writes=[psu])
                p.op("s", lambda e, psu=psu, dst=dst, nt=nt: e.activation(
                    out=dst[:, 512 * nt:512 * (nt + 1)], in_=psu[:], func=AF.Silu),
                    reads=[psu], writes=[dst])
    sub2.close()
    subx.close()
    p.fence()

    eyeD = [p.sb(f"g_eyeD{h}", [128, 128], BF16) for h in range(NH)]
    deye = p.sb("g_deye", [128, NH])
    p.dma("qs", lambda e: e.dma_start(out=deye[:], in_=deye_d[:]), writes=[deye])
    for h in range(NH):
        p.op("v", lambda e, h=h: e.tensor_scalar(
            out=eyeD[h][:], in0=eyeb[:], scalar1=deye[:, h:h + 1], scalar2=None,
            op0=ALU.mult), reads=[eyeb, deye], writes=[eyeD[h]])

    triu = p.sb("g_triu", [128, 128])
    zz128 = p.sb("g_zz128", [128, 1])
    p.op("v", lambda e: e.memset(zz128[:], 0.0), writes=[zz128])
    p.op("v", lambda e: e.tensor_tensor_scan(
        out=triu[:], data0=eyef[:], data1=zz128[:, 0:1].broadcast_to([128, 128]),
        initial=0.0, op0=ALU.max, op1=ALU.add), reads=[eyef, zz128], writes=[triu])

    srun = p.sb("g_srun", [128, NH * HP])
    srunb = p.sb("g_srunb", [128, NH * HP], BF16)
    p.op("v", lambda e: e.memset(srun[:], 0.0), writes=[srun])
    stmask = p.sb("g_stm", [128, 128], BF16)
    rowb = p.sb("g_rowb", [1, 512])
    espraw = p.sb("g_espraw", [1, NH])
    acT = p.sb("g_acT", [128, NH])
    wT = p.sb("g_wT", [128, NH], BF16)
    dtT = p.sb("g_dtT", [128, NH], BF16)
    xdt = p.sb("g_xdt", [128, NH * HP], BF16)
    xw = p.sb("g_xw", [128, NH * HP], BF16)
    mt = p.sb("g_mt", [128, NH * CS], BF16)
    cwb = [p.sb(f"g_cwb{i}", [128, NH * CS], BF16) for i in range(1)]
    expd = p.sb("g_expd", [128, 512])
    wrow = p.sb("g_wrow", [NH, CS])
    espT = p.sb("g_espT", [128, NH])
    esprow = p.sb("g_esprow", [1, NH])
    esptot = p.sb("g_esptot", [1, NH])
    yest = [p.sb(f"g_ye{i}", [64, 512], BF16) for i in range(2)]
    p.op("v", lambda e: e.memset(esptot[:], 1.0), writes=[esptot])

    psU = p.ps("g_psU", [128, 512])
    psT = p.ps("g_psT", [128, 128])
    psY = [p.ps(f"g_psY{i}", [64, 512]) for i in range(2)]
    psS = [p.ps(f"g_psS{i}", [128, 512]) for i in range(2)]

    for c in range(NCHUNK):
        sl = slice(c * CS, (c + 1) * CS)
        p.op("t", lambda e, sl=sl: e.matmul(psU[:, 0:128], bt[:, sl], ct[:, sl],
                                            start=True, stop=True),
             reads=[bt, ct], writes=[psU])
        p.op("v", lambda e: e.tensor_tensor(out=stmask[:], in0=psU[:, 0:128], in1=triu[:],
                                            op=ALU.mult), reads=[psU, triu], writes=[stmask])
        p.dma("qs", lambda e, sl=sl: e.dma_start(out=ac_dram[:], in_=acum[:, sl]),
              reads=[acum], writes=["ac_dram"])
        p.op("t", lambda e, sl=sl: e.transpose(psT[:, 0:NH], acum[:, sl], eyef[0:NH, 0:NH]),
             reads=[acum, eyef], writes=[psT])
        p.op("v", lambda e: e.tensor_copy(out=acT[:], in_=psT[:, 0:NH]),
             reads=[psT], writes=[acT])
        p.op("v", lambda e, sl=sl: e.tensor_scalar(
            out=wrow[:], in0=acum[:, sl], scalar1=acum[:, sl.stop - 1:sl.stop],
            scalar2=-1.0, op0=ALU.subtract, op1=ALU.mult), reads=[acum], writes=[wrow])
        p.op("s", lambda e: e.activation(out=wrow[:], in_=wrow[:], func=AF.Exp),
             reads=[wrow], writes=[wrow])
        p.op("t", lambda e: e.transpose(psT[:, 0:NH], wrow[:], eyef[0:NH, 0:NH]),
             reads=[wrow, eyef], writes=[psT])
        p.op("v", lambda e: e.tensor_copy(out=wT[:], in_=psT[:, 0:NH]),
             reads=[psT], writes=[wT])
        p.op("t", lambda e, sl=sl: e.transpose(psT[:, 0:NH], dtm[:, sl], eyef[0:NH, 0:NH]),
             reads=[dtm, eyef], writes=[psT])
        p.op("v", lambda e: e.tensor_copy(out=dtT[:], in_=psT[:, 0:NH]),
             reads=[psT], writes=[dtT])
        p.dma("qg", lambda e: e.dma_start(
            out=espraw[:], in_=ac_dram[:, CS - 1:CS].rearrange("h o -> o h"),
            allow_slow_non_contiguous=True),
            reads=["ac_dram"], writes=[espraw])
        p.op("s", lambda e: e.activation(out=esprow[:], in_=espraw[:], func=AF.Exp),
             reads=[espraw], writes=[esprow])
        p.dma("qs", lambda e, c=c: e.dma_start(out=espan_d[c:c + 1, :], in_=esprow[:]),
              reads=[esprow])
        tt(p, esptot[:], esptot[:], esprow[:], ALU.mult, reads=[esptot, esprow], writes=[esptot])
        p.op("t", lambda e: e.matmul(psT[:, 0:NH], K1, esprow[:], start=True, stop=True),
             reads=[ones, esprow], writes=[psT])
        p.op("v", lambda e: e.tensor_copy(out=espT[:], in_=psT[:, 0:NH]),
             reads=[psT], writes=[espT])
        p.op("v", lambda e, c=c: e.tensor_tensor(
            out=xdt[:].rearrange("t (h q) -> t h q", h=NH),
            in0=xtok[c][:, 0:INTER].rearrange("t (h q) -> t h q", h=NH),
            in1=dtT[:].unsqueeze(-1).broadcast_to([128, NH, HP]), op=ALU.mult),
            reads=[xtok[c], dtT], writes=[xdt])
        p.op("v", lambda e: e.tensor_tensor(
            out=xw[:].rearrange("t (h q) -> t h q", h=NH),
            in0=xdt[:].rearrange("t (h q) -> t h q", h=NH),
            in1=wT[:].unsqueeze(-1).broadcast_to([128, NH, HP]), op=ALU.mult),
            reads=[xdt, wT], writes=[xw])
        p.op("v", lambda e: e.tensor_copy(out=srunb[:], in_=srun[:]),
             reads=[srun], writes=[srunb])
        cw = cwb[0]
        for g4 in range(NH // 4):
            hs = slice(g4 * 4 * CS, (g4 + 1) * 4 * CS)
            p.dma("qs", lambda e, g4=g4: e.dma_start(
                out=rowb[:],
                in_=ac_dram[4 * g4:4 * (g4 + 1), :].rearrange("h l -> (h l)").unsqueeze(0)),
                reads=["ac_dram"], writes=[rowb])
            p.op("t", lambda e: e.matmul(psU[:], K1, rowb[:], start=True, stop=True),
                 reads=[ones, rowb], writes=[psU])
            p.op("v", lambda e, g4=g4: e.tensor_tensor(
                out=expd[:].rearrange("s (h l) -> s h l", h=4),
                in0=psU[:].rearrange("s (h l) -> s h l", h=4),
                in1=acT[:, 4 * g4:4 * g4 + 4].unsqueeze(-1).broadcast_to([128, 4, CS]),
                op=ALU.subtract), reads=[psU, acT], writes=[expd])
            p.op("v", lambda e: e.tensor_scalar(out=expd[:], in0=expd[:], scalar1=0.0,
                                                scalar2=None, op0=ALU.min),
                 reads=[expd], writes=[expd])
            p.op("s", lambda e, hs=hs: e.activation(out=mt[:, hs], in_=expd[:], func=AF.Exp),
                 reads=[expd], writes=[mt])
            p.op("v", lambda e, hs=hs: e.tensor_tensor(
                out=mt[:, hs].rearrange("s (h l) -> s h l", h=4),
                in0=mt[:, hs].rearrange("s (h l) -> s h l", h=4),
                in1=stmask[:].unsqueeze(1).broadcast_to([128, 4, 128]), op=ALU.mult),
                reads=[mt, stmask], writes=[mt])
            p.op("s", lambda e: e.activation(out=expd[:], in_=psU[:], func=AF.Exp),
                 reads=[psU], writes=[expd])
            p.op("v", lambda e, hs=hs, sl=sl, cw=cw: e.tensor_tensor(
                out=cw[:, hs].rearrange("n (h l) -> n h l", h=4),
                in0=expd[:].rearrange("n (h l) -> n h l", h=4),
                in1=ct[:, sl].unsqueeze(1).broadcast_to([128, 4, 128]), op=ALU.mult),
                reads=[expd, ct], writes=[cw])
        p.dma("qs", lambda e, c=c, cw=cw: e.dma_start(
            out=cw_d[NN * c:NN * (c + 1), :], in_=cw[:]), reads=[cw])
        for wv in range(16):
            psu = psY[wv % 2]
            ye = yest[wv % 2]
            for hh in range(4):
                h = wv * 4 + hh
                osl = slice(hh * 128, (hh + 1) * 128)
                p.op("t", lambda e, h=h, osl=osl, psu=psu: e.matmul(
                    psu[:, osl], xdt[:, HP * h:HP * (h + 1)], mt[:, CS * h:CS * (h + 1)],
                    start=True, stop=False), reads=[xdt, mt], writes=[psu])
                p.op("t", lambda e, h=h, osl=osl, psu=psu, c=c: e.matmul(
                    psu[:, osl], xtok[c][:, HP * h:HP * (h + 1)], eyeD[h][:],
                    start=False, stop=False), reads=[xtok[c], eyeD[h]], writes=[psu])
                p.op("t", lambda e, h=h, osl=osl, psu=psu: e.matmul(
                    psu[:, osl], srunb[:, HP * h:HP * (h + 1)], cw[:, CS * h:CS * (h + 1)],
                    start=False, stop=True), reads=[srunb, cw], writes=[psu])
            p.op("v", lambda e, psu=psu, ye=ye: e.tensor_copy(out=ye[:], in_=psu[:]),
                 reads=[psu], writes=[ye])
            p.dma("qs", lambda e, c=c, wv=wv, ye=ye: e.dma_start(
                out=yt_d[HP * 4 * wv: HP * 4 * (wv + 1), CS * c:CS * (c + 1)]
                    .rearrange("(h q) l -> q h l", q=HP),
                in_=ye[:].rearrange("q (h l) -> q h l", h=4)), reads=[ye])
        for wv in range(8):
            psu = psS[wv % 2]
            for hh in range(8):
                h = wv * 8 + hh
                osl = slice(hh * HP, (hh + 1) * HP)
                p.op("t", lambda e, h=h, osl=osl, psu=psu, c=c: e.matmul(
                    psu[:, osl], xtok[c][:, INTER:INTER + NN],
                    xw[:, HP * h:HP * (h + 1)],
                    start=True, stop=True), reads=[xtok[c], xw], writes=[psu])
            ssl = slice(wv * 512, (wv + 1) * 512)
            p.op("v", lambda e, ssl=ssl, wv=wv: e.tensor_tensor(
                out=srun[:, ssl].rearrange("n (h q) -> n h q", h=8),
                in0=srun[:, ssl].rearrange("n (h q) -> n h q", h=8),
                in1=espT[:, 8 * wv:8 * (wv + 1)].unsqueeze(-1).broadcast_to([128, 8, HP]),
                op=ALU.mult), reads=[srun, espT], writes=[srun])
            p.op("v", lambda e, ssl=ssl, psu=psu: e.tensor_tensor(
                out=srun[:, ssl], in0=srun[:, ssl], in1=psu[:], op=ALU.add),
                reads=[srun, psu], writes=[srun])

    p.op("v", lambda e: e.tensor_copy(out=srunb[:], in_=srun[:]), reads=[srun], writes=[srunb])
    p.dma("qs", lambda e: e.dma_start(out=sacc_d[:], in_=srunb[:]), reads=[srunb])
    p.dma("qs", lambda e: e.dma_start(out=espantot_d[:], in_=esptot[:]), reads=[esptot])
    p.maybe_finish()
    return p


def build_phase_h(tw=TOWN, p=None):
    if p is None:
        p = Prog("phaseH")
    yt_dio = p.din("ytin", [INTER, tw], BF16)
    cw_d = p.din("cw", [(tw // CS) * NN, NH * CS], BF16)
    gsil_d = p.din("gsil", [INTER, tw], BF16)
    h2_d = p.din("h2", [D, tw])
    sacc8_d = p.din("sacc8", [8 * NN, NH * HP], BF16)
    esptot8_d = p.din("esptot8", [1, 8 * NH])
    espmy_d = p.din("espmy", [1, (tw // CS) * NH])
    jmask_d = p.din("jmask", [1, 8])
    nw_d = p.din("nw", [128, INTER // 128])
    wop_d = p.din("wop", [INTER, D])
    ones_d = p.din("ones", [128, 1])
    h3_d = p.dout("h3", [D, tw])
    h3b_d = p.dout("h3b", [D, tw], BF16)

    p.start()
    NCH = D // 128
    NYC = INTER // 128
    NCHUNK = tw // CS
    NT = tw // 512
    ones = p.sb("h_ones", [128, 1])
    p.dma("qs", lambda e: e.dma_start(out=ones[:], in_=ones_d[:]), writes=[ones])
    ones_b = p.sb("h_onesb", [128, 1], BF16)
    p.op("v", lambda e: e.tensor_copy(out=ones_b[:], in_=ones[:]), reads=[ones], writes=[ones_b])
    K1 = ones[0:1, 0:1].broadcast_to([1, 128])

    jrow = p.sb("h_jrow", [1, 8])
    invr = p.sb("h_invr", [1, 8])
    erow = p.sb("h_erow", [1, NH])
    etot = p.sb("h_etot", [1, 8 * NH])
    mcol8 = p.sb("h_mcol8", [128, 8])
    espmy = p.sb("h_espmy", [1, (tw // CS) * NH])
    vrow = p.sb("h_vrow", [1, tw])
    rstdb = p.sb("h_rstdb", [128, tw])
    nwc = p.sb("h_nwc", [128, INTER // 128])
    sub0 = ExitStack()
    scur = sub0.enter_context(p.nc.sbuf_tensor("h_scur", [128, NH * HP], F32))
    scb = sub0.enter_context(p.nc.sbuf_tensor("h_scb", [128, NH * HP], BF16))
    ebc = sub0.enter_context(p.nc.sbuf_tensor("h_ebc", [128, NH], F32))
    p.dma("qs", lambda e: e.dma_start(out=jrow[:], in_=jmask_d[:]), writes=[jrow])
    p.dma("qs", lambda e: e.dma_start(out=etot[:], in_=esptot8_d[:]), writes=[etot])
    p.op("v", lambda e: e.tensor_scalar(out=invr[:], in0=jrow[:], scalar1=-1.0,
                                        scalar2=1.0, op0=ALU.mult, op1=ALU.add),
         reads=[jrow], writes=[invr])
    mps = sub0.enter_context(p.nc.psum_tensor("h_mps", [128, 128], F32))
    p.op("t", lambda e: e.matmul(mps[:, 0:8], K1, jrow[:], start=True, stop=True),
         reads=[ones, jrow], writes=[mps])
    p.op("v", lambda e: e.tensor_copy(out=mcol8[:], in_=mps[:, 0:8]), reads=[mps], writes=[mcol8])

    saccj = [sub0.enter_context(p.nc.sbuf_tensor(f"h_sacc{i}", [128, NH * HP], BF16))
             for i in range(2)]
    p.op("v", lambda e: e.memset(scur[:], 0.0), writes=[scur])
    for j in range(7):
        sj = saccj[j % 2]
        p.dma("qs", lambda e, j=j, sj=sj: e.dma_start(
            out=sj[:], in_=sacc8_d[NN * j:NN * (j + 1), :]), writes=[sj])
        p.op("v", lambda e, j=j: e.tensor_scalar(
            out=erow[:], in0=etot[0:1, NH * j:NH * (j + 1)], scalar1=jrow[0:1, j:j + 1],
            scalar2=None, op0=ALU.mult), reads=[etot, jrow], writes=[erow])
        p.op("v", lambda e, j=j: e.tensor_scalar(
            out=erow[:], in0=erow[:], scalar1=invr[0:1, j:j + 1],
            scalar2=None, op0=ALU.add), reads=[erow, invr], writes=[erow])
        p.op("t", lambda e: e.matmul(mps[:, 0:NH], K1, erow[:], start=True, stop=True),
             reads=[ones, erow], writes=[mps])
        p.op("v", lambda e: e.tensor_copy(out=ebc[:], in_=mps[:, 0:NH]),
             reads=[mps], writes=[ebc])
        p.op("v", lambda e: e.tensor_tensor(
            out=scur[:].rearrange("n (h q) -> n h q", h=NH),
            in0=scur[:].rearrange("n (h q) -> n h q", h=NH),
            in1=ebc[:].unsqueeze(-1).broadcast_to([128, NH, HP]), op=ALU.mult),
            reads=[scur, ebc], writes=[scur])
        p.op("v", lambda e, sj=sj, j=j: e.scalar_tensor_tensor(
            out=scur[:], in0=sj[:], scalar=mcol8[:, j:j + 1], in1=scur[:],
            op0=ALU.mult, op1=ALU.add), reads=[sj, mcol8, scur], writes=[scur])

    p.dma("qs", lambda e: e.dma_start(out=espmy[:], in_=espmy_d[:]), writes=[espmy])
    cwch = [sub0.enter_context(p.nc.sbuf_tensor(f"h_cw{i}", [128, NH * CS], BF16))
            for i in range(2)]
    psY = [sub0.enter_context(p.nc.psum_tensor(f"h_psY{i}", [64, 512], F32))
           for i in range(2)]
    yez = [sub0.enter_context(p.nc.sbuf_tensor(f"h_ye{i}", [64, 512], BF16))
           for i in range(2)]
    for c in range(NCHUNK):
        cwc = cwch[c % 2]
        p.dma("qs", lambda e, c=c, cwc=cwc: e.dma_start(
            out=cwc[:], in_=cw_d[NN * c:NN * (c + 1), :]), writes=[cwc])
        p.op("v", lambda e: e.tensor_copy(out=scb[:], in_=scur[:]), reads=[scur], writes=[scb])
        for wv in range(16):
            psu = psY[wv % 2]
            ye = yez[wv % 2]
            for hh in range(4):
                h = wv * 4 + hh
                osl = slice(hh * 128, (hh + 1) * 128)
                p.op("t", lambda e, h=h, osl=osl, psu=psu, cwc=cwc: e.matmul(
                    psu[:, osl], scb[:, HP * h:HP * (h + 1)], cwc[:, CS * h:CS * (h + 1)],
                    start=True, stop=True), reads=[scb, cwc], writes=[psu])
            p.op("v", lambda e, psu=psu, ye=ye: e.tensor_copy(out=ye[:], in_=psu[:]),
                 reads=[psu], writes=[ye])
            p.dma("qg", lambda e, c=c, wv=wv, ye=ye: e.dma_start(
                out=yt_dio[HP * 4 * wv: HP * 4 * (wv + 1), CS * c:CS * (c + 1)]
                    .rearrange("(h q) l -> q h l", q=HP),
                in_=ye[:].rearrange("q (h l) -> q h l", h=4),
                accum_op=ALU.add), reads=[ye], writes=[f"ya_{wv}_{c}"])
        p.op("v", lambda e, c=c: e.tensor_scalar(
            out=erow[:], in0=espmy[0:1, NH * c:NH * (c + 1)], scalar1=1.0, scalar2=None,
            op0=ALU.mult), reads=[espmy], writes=[erow])
        p.op("t", lambda e: e.matmul(mps[:, 0:NH], K1, erow[:], start=True, stop=True),
             reads=[ones, erow], writes=[mps])
        p.op("v", lambda e: e.tensor_copy(out=ebc[:], in_=mps[:, 0:NH]),
             reads=[mps], writes=[ebc])
        p.op("v", lambda e: e.tensor_tensor(
            out=scur[:].rearrange("n (h q) -> n h q", h=NH),
            in0=scur[:].rearrange("n (h q) -> n h q", h=NH),
            in1=ebc[:].unsqueeze(-1).broadcast_to([128, NH, HP]), op=ALU.mult),
            reads=[scur, ebc], writes=[scur])
    sub0.close()
    p.fence()

    yf = [p.sb(f"h_yf{i}", [128, tw], BF16) for i in range(INTER // 128)]
    sub1 = ExitStack()
    ytb = [sub1.enter_context(p.nc.sbuf_tensor(f"h_ytb{i}", [128, tw], BF16))
           for i in range(2)]
    ysq = sub1.enter_context(p.nc.sbuf_tensor("h_ysq", [128, tw], BF16))
    psV = [sub1.enter_context(p.nc.psum_tensor(f"h_psV{i}", [1, 512], F32))
           for i in range(NT)]
    for r in range(NYC):
        yb = ytb[r % 2]
        p.dma("qs", lambda e, r=r, yb=yb: e.dma_start(
            out=yb[:], in_=yt_dio[128 * r:128 * (r + 1), :]), writes=[yb])
        p.op("s", lambda e, yb=yb: e.activation(out=ysq[:], in_=yb[:], func=AF.Square),
             reads=[yb], writes=[ysq])
        for nt in range(NT):
            p.op("t", lambda e, nt=nt, r=r: e.matmul(
                psV[nt][:], ones_b[:, 0:1], ysq[:, 512 * nt:512 * (nt + 1)],
                start=(r == 0), stop=(r == NYC - 1)),
                reads=[ysq, ones_b], writes=[psV[nt]])
    for nt in range(NT):
        p.op("v", lambda e, nt=nt: e.tensor_scalar(
            out=vrow[0:1, 512 * nt:512 * (nt + 1)], in0=psV[nt][:],
            scalar1=1.0 / INTER, scalar2=1e-5, op0=ALU.mult, op1=ALU.add),
            reads=[psV[nt]], writes=[vrow])
    p.op("s", lambda e: e.activation(out=vrow[:], in_=vrow[:], func=AF.Sqrt),
         reads=[vrow], writes=[vrow])
    p.op("v", lambda e: e.reciprocal(out=vrow[:], in_=vrow[:]), reads=[vrow], writes=[vrow])
    psB = sub1.enter_context(p.nc.psum_tensor("h_psB", [128, 512], F32))
    for nt in range(NT):
        p.op("t", lambda e, nt=nt: e.matmul(
            psB[:], K1, vrow[0:1, 512 * nt:512 * (nt + 1)], start=True, stop=True),
            reads=[ones, vrow], writes=[psB])
        p.op("v", lambda e, nt=nt: e.tensor_copy(
            out=rstdb[:, 512 * nt:512 * (nt + 1)], in_=psB[:]), reads=[psB], writes=[rstdb])

    p.dma("qs", lambda e: e.dma_start(out=nwc[:], in_=nw_d[:]), writes=[nwc])
    gsb = [sub1.enter_context(p.nc.sbuf_tensor(f"h_gs{i}", [128, tw], BF16))
           for i in range(2)]
    for r in range(NYC):
        yb = ytb[r % 2]
        gs = gsb[r % 2]
        p.dma("qs", lambda e, r=r, yb=yb: e.dma_start(
            out=yb[:], in_=yt_dio[128 * r:128 * (r + 1), :]), writes=[yb])
        p.dma("qs", lambda e, r=r, gs=gs: e.dma_start(
            out=gs[:], in_=gsil_d[128 * r:128 * (r + 1), :]), writes=[gs])
        p.op("v", lambda e, r=r, yb=yb: e.scalar_tensor_tensor(
            out=yf[r][:], in0=yb[:], scalar=nwc[:, r:r + 1], in1=rstdb[:],
            op0=ALU.mult, op1=ALU.mult), reads=[yb, nwc, rstdb], writes=[yf[r]])
        p.op("v", lambda e, r=r, gs=gs: e.tensor_tensor(
            out=yf[r][:], in0=yf[r][:], in1=gs[:], op=ALU.mult),
            reads=[yf[r], gs], writes=[yf[r]])
    sub1.close()
    p.fence()

    h = [p.sb(f"h_h{i}", [128, tw]) for i in range(NCH)]
    hb = [p.sb(f"h_hb{i}", [128, tw], BF16) for i in range(NCH)]
    sub2 = ExitStack()
    wtl = [sub2.enter_context(p.nc.sbuf_tensor(f"h_wtl{i}", [128, 512], BF16))
           for i in range(16)]
    pst = [sub2.enter_context(p.nc.psum_tensor(f"h_pst{i}", [128, 512], F32))
           for i in range(8)]
    for i in range(NCH):
        p.dma("qs", lambda e, i=i: e.dma_start(out=h[i][:], in_=h2_d[128 * i:128 * (i + 1), :]),
              writes=[h[i]])
    CG = 4
    for cg in range(NCH // CG):
        for khalf in range(2):
            for kk in range(16):
                k = khalf * 16 + kk
                p.dma("qg", lambda e, k=k, kk=kk, cg=cg: e.dma_start(
                    out=wtl[kk][:], in_=wop_d[128 * k:128 * (k + 1), 512 * cg:512 * (cg + 1)]),
                    writes=[wtl[kk]])
            for ci in range(CG):
                for nt in range(NT):
                    psu = pst[ci * NT + nt]
                    for kk in range(16):
                        k = khalf * 16 + kk
                        p.op("t", lambda e, psu=psu, kk=kk, ci=ci, nt=nt, k=k, khalf=khalf: e.matmul(
                            psu[:], wtl[kk][:, 128 * ci:128 * (ci + 1)],
                            yf[k][:, 512 * nt:512 * (nt + 1)],
                            start=(khalf == 0 and kk == 0), stop=(khalf == 1 and kk == 15)),
                            reads=[wtl[kk], yf[k]], writes=[psu])
        for ci in range(CG):
            c = cg * CG + ci
            for nt in range(NT):
                psu = pst[ci * NT + nt]
                p.op("v", lambda e, psu=psu, c=c, nt=nt: e.tensor_tensor(
                    out=h[c][:, 512 * nt:512 * (nt + 1)],
                    in0=h[c][:, 512 * nt:512 * (nt + 1)], in1=psu[:], op=ALU.add),
                    reads=[psu, h[c]], writes=[h[c]])
    sub2.close()
    p.fence()
    ones_bb = ones_b
    ps_s = [p.ps("h_psLs1", [1, min(tw, 512)]), p.ps("h_psLs2", [1, min(tw, 512)])]
    ps_b = [p.ps("h_psLb1", [128, min(tw, 512)]), p.ps("h_psLb2", [128, min(tw, 512)])]
    rows = [p.sb(f"h_lnrow{i}", [1, tw]) for i in range(4)]
    emit_ln_fm(p, h, hb, ones, ones_bb, ps_s, ps_b, rows, tw=tw)
    for i in range(NCH):
        p.dma("qs", lambda e, i=i: e.dma_start(out=h3_d[128 * i:128 * (i + 1), :], in_=h[i][:]),
              reads=[h[i]])
        p.dma("qs", lambda e, i=i: e.dma_start(out=h3b_d[128 * i:128 * (i + 1), :], in_=hb[i][:]),
              reads=[hb[i]])
    p.maybe_finish()
    return p


NHEAD = 16
HD = 128


def build_phase_i(tw=TOWN, p=None):
    if p is None:
        p = Prog("phaseI")
    h3_d = p.din("h3", [D, tw])
    h3b_d = p.din("h3b", [D, tw], BF16)
    wgt_d = p.din("wgt", [D, NHEAD])
    wqkvt_d = p.din("wqkvt", [D, 3 * D])
    ones_d = p.din("ones", [128, 1])
    qt_d = p.dout("qt", [D, tw], BF16)
    ksh_d = p.dout("ksh", [tw, HD], BF16)
    vsh_d = p.dout("vsh", [tw, HD], BF16)
    kraw_d = p.dout("kraw", [NHEAD, tw])
    p.start()
    NCH = D // 128
    NT = tw // 512
    NTC = tw // 128
    ones = p.sb("i_ones", [128, 1])
    onesb = p.sb("i_onesb", [128, 1], BF16)
    p.dma("qs", lambda e: e.dma_start(out=ones[:], in_=ones_d[:]), writes=[ones])
    p.op("v", lambda e: e.tensor_copy(out=onesb[:], in_=ones[:]), reads=[ones], writes=[onesb])

    h3 = [p.sb(f"i_h3_{i}", [128, tw]) for i in range(NCH)]
    h3b = [p.sb(f"i_h3b{i}", [128, tw], BF16) for i in range(NCH)]
    for i in range(NCH):
        p.dma("qs", lambda e, i=i: e.dma_start(out=h3[i][:], in_=h3_d[128 * i:128 * (i + 1), :]),
              writes=[h3[i]])
        p.dma("qg", lambda e, i=i: e.dma_start(out=h3b[i][:], in_=h3b_d[128 * i:128 * (i + 1), :]),
              writes=[h3b[i]])

    wgt = p.sb("i_wgt", [128, NHEAD * NCH])
    for k in range(NCH):
        p.dma("qs", lambda e, k=k: e.dma_start(
            out=wgt[:, NHEAD * k:NHEAD * (k + 1)], in_=wgt_d[128 * k:128 * (k + 1), :]),
            writes=[wgt])
    ind = p.sb("i_ind", [128, NHEAD * NTC])
    inds = p.sb("i_inds", [128, NHEAD * NTC])
    psg = p.ps("i_psg", [128, NHEAD])
    mx = p.sb("i_mx", [128, 1])
    zs = p.sb("i_zs", [128, 1])
    ztmp = p.sb("i_ztmp", [128, NHEAD])
    for j in range(NTC):
        for k in range(NCH):
            p.op("t", lambda e, j=j, k=k: e.matmul(
                psg[:], h3[k][:, 128 * j:128 * (j + 1)], wgt[:, NHEAD * k:NHEAD * (k + 1)],
                start=(k == 0), stop=(k == NCH - 1)),
                reads=[h3[k], wgt], writes=[psg])
        p.op("v", lambda e: e.reduce_max(out=mx[:], in_=psg[:], axis=AX.X),
             reads=[psg], writes=[mx])
        isl = slice(NHEAD * j, NHEAD * (j + 1))
        p.op("v", lambda e, isl=isl: e.tensor_scalar(
            out=ind[:, isl], in0=psg[:], scalar1=mx[:, 0:1], scalar2=None, op0=ALU.is_ge),
            reads=[psg, mx], writes=[ind])
        p.op("v", lambda e: e.tensor_scalar(
            out=ztmp[:], in0=psg[:], scalar1=mx[:, 0:1], scalar2=None, op0=ALU.subtract),
            reads=[psg, mx], writes=[ztmp])
        p.op("s", lambda e: e.activation(out=ztmp[:], in_=ztmp[:], func=AF.Exp,
                                         accum_out=zs[:]),
             reads=[ztmp], writes=[ztmp, zs])
        p.op("v", lambda e: e.reciprocal(out=zs[:], in_=zs[:]), reads=[zs], writes=[zs])
        p.op("v", lambda e, isl=isl: e.tensor_scalar(
            out=inds[:, isl], in0=ind[:, isl], scalar1=zs[:, 0:1], scalar2=None, op0=ALU.mult),
            reads=[ind, zs], writes=[inds])

    wtk = [p.sb(f"i_wtk{i}", [128, 512], BF16) for i in range(NCH)]
    ps4 = [p.ps(f"i_ps4{i}", [128, 512]) for i in range(4)]
    qtile = [p.sb(f"i_qt{i}", [128, tw], BF16) for i in range(NCH)]
    for cg in range(D // 512):
        for k in range(NCH):
            p.dma("qg", lambda e, k=k, cg=cg: e.dma_start(
                out=wtk[k][:], in_=wqkvt_d[128 * k:128 * (k + 1), 512 * cg:512 * (cg + 1)]),
                writes=[wtk[k]])
        for ci in range(4):
            c = cg * 4 + ci
            for nt in range(NT):
                psu = ps4[(ci * NT + nt) % 4]
                for k in range(NCH):
                    p.op("t", lambda e, psu=psu, k=k, ci=ci, nt=nt: e.matmul(
                        psu[:], wtk[k][:, 128 * ci:128 * (ci + 1)],
                        h3b[k][:, 512 * nt:512 * (nt + 1)],
                        start=(k == 0), stop=(k == NCH - 1)),
                        reads=[wtk[k], h3b[k]], writes=[psu])
                p.op("v", lambda e, psu=psu, c=c, nt=nt: e.tensor_copy(
                    out=qtile[c][:, 512 * nt:512 * (nt + 1)], in_=psu[:]),
                    reads=[psu], writes=[qtile[c]])
                p.dma("qs", lambda e, c=c, nt=nt: e.dma_start(
                    out=qt_d[128 * c:128 * (c + 1), 512 * nt:512 * (nt + 1)],
                    in_=qtile[c][:, 512 * nt:512 * (nt + 1)]),
                    reads=[qtile[c]])
    qsq = p.sb("i_qsq", [128, tw], BF16)
    psr = p.ps("i_psr", [1, 512])
    rowtmp = p.sb("i_rowtmp", [1, 512])
    for h in range(NHEAD):
        p.op("s", lambda e, h=h: e.activation(out=qsq[:], in_=qtile[h][:], func=AF.Square),
             reads=[qtile[h]], writes=[qsq])
        for nt in range(NT):
            p.op("t", lambda e, nt=nt: e.matmul(
                psr[:], onesb[:, 0:1], qsq[:, 512 * nt:512 * (nt + 1)],
                start=True, stop=True), reads=[qsq, onesb], writes=[psr])
            p.op("s", lambda e: e.activation(out=rowtmp[:], in_=psr[:], func=AF.Sqrt),
                 reads=[psr], writes=[rowtmp])
            p.dma("qs", lambda e, h=h, nt=nt: e.dma_start(
                out=kraw_d[h:h + 1, 512 * nt:512 * (nt + 1)], in_=rowtmp[:]),
                reads=[rowtmp])
    tmp = p.sb("i_tmp", [128, HD], BF16)
    accs = [p.sb(f"i_acc{j}", [128, HD]) for j in range(NTC)]
    for part, seltile, outd in ((1, ind, ksh_d), (2, inds, vsh_d)):
        for cg in range(D // 512):
            for k in range(NCH):
                p.dma("qg", lambda e, k=k, cg=cg, part=part: e.dma_start(
                    out=wtk[k][:],
                    in_=wqkvt_d[128 * k:128 * (k + 1),
                                part * D + 512 * cg: part * D + 512 * (cg + 1)]),
                    writes=[wtk[k]])
            for j in range(NTC):
                psu = ps4[j % 4]
                for k in range(NCH):
                    p.op("t", lambda e, psu=psu, k=k, j=j: e.matmul(
                        psu[:], h3b[k][:, 128 * j:128 * (j + 1)],
                        wtk[k][:, 0:512], start=(k == 0), stop=(k == NCH - 1)),
                        reads=[wtk[k], h3b[k]], writes=[psu])
                acc = accs[j]
                for hh in range(4):
                    h = cg * 4 + hh
                    if cg == 0 and hh == 0:
                        p.op("v", lambda e, psu=psu, j=j, h=h, acc=acc,
                             seltile=seltile: e.tensor_scalar(
                            out=acc[:], in0=psu[:, 0:HD],
                            scalar1=seltile[:, NHEAD * j + h:NHEAD * j + h + 1],
                            scalar2=None, op0=ALU.mult),
                            reads=[psu, seltile], writes=[acc])
                    else:
                        p.op("v", lambda e, psu=psu, j=j, h=h, hh=hh, acc=acc,
                             seltile=seltile: e.tensor_scalar(
                            out=tmp[:], in0=psu[:, HD * hh:HD * (hh + 1)],
                            scalar1=seltile[:, NHEAD * j + h:NHEAD * j + h + 1],
                            scalar2=None, op0=ALU.mult),
                            reads=[psu, seltile], writes=[tmp])
                        p.op("v", lambda e, acc=acc: e.tensor_tensor(
                            out=acc[:], in0=acc[:], in1=tmp[:], op=ALU.add),
                            reads=[acc, tmp], writes=[acc])
        for j in range(NTC):
            acc = accs[j]
            p.op("v", lambda e, acc=acc: e.tensor_copy(out=tmp[:], in_=acc[:]),
                 reads=[acc], writes=[tmp])
            p.dma("qs", lambda e, j=j, outd=outd: e.dma_start(
                out=outd[128 * j:128 * (j + 1), :], in_=tmp[:]), reads=[tmp])
    p.maybe_finish()
    return p


def build_phase_j(tn=T, p=None):
    if p is None:
        p = Prog("phaseJ")
    q2_d = p.din("q2", [2 * HD, tn], BF16)
    ksh_d = p.din("ksh", [tn, HD], BF16)
    vsh_d = p.din("vsh", [tn, HD], BF16)
    kraw_d = p.din("kraw", [2, tn])
    dmask_d = p.din("dmask", [4 * 128, 512])
    ones_d = p.din("ones", [128, 1])
    y2_d = p.dout("y2", [2 * HD, tn], BF16)
    p.start()
    NQB = tn // 512
    ones = p.sb("j_ones", [128, 1])
    onesb = p.sb("j_onesb", [128, 1], BF16)
    p.dma("qs", lambda e: e.dma_start(out=ones[:], in_=ones_d[:]), writes=[ones])
    p.op("v", lambda e: e.tensor_copy(out=onesb[:], in_=ones[:]), reads=[ones], writes=[onesb])
    K1 = ones[0:1, 0:1].broadcast_to([1, 128])
    K1b = onesb[0:1, 0:1].broadcast_to([1, 128])

    ksht = p.sb("j_ksht", [HD, tn], BF16)
    p.dma("qs", lambda e: e.dma_start(out=ksht[:], in_=ksh_d[:], transpose=True),
          writes=[ksht])
    vtok = p.sb("j_vtok", [128, (tn // 128) * HD], BF16)
    for j in range(tn // 128):
        p.dma("qs", lambda e, j=j: e.dma_start(
            out=vtok[:, HD * j:HD * (j + 1)], in_=vsh_d[128 * j:128 * (j + 1), :]),
            writes=[vtok])
    dmask = [p.sb(f"j_dm{i}", [128, 512], BF16) for i in range(4)]
    for i in range(4):
        p.dma("qg", lambda e, i=i: e.dma_start(out=dmask[i][:], in_=dmask_d[128 * i:128 * (i + 1), :]),
              writes=[dmask[i]])
    q2 = p.sb("j_q2", [128, 2 * tn], BF16)
    for h in range(2):
        p.dma("qs", lambda e, h=h: e.dma_start(out=q2[:, tn * h:tn * (h + 1)],
                                               in_=q2_d[HD * h:HD * (h + 1), :]),
              writes=[q2])
    ksq = p.sb("j_ksq", [128, tn], BF16)
    knrow = p.sb("j_knrow", [1, tn])
    kn1 = p.sb("j_kn1", [1, 1])
    psB = p.ps("j_psB", [128, 512])
    p.op("s", lambda e: e.activation(out=ksq[:], in_=ksht[:], func=AF.Square),
         reads=[ksht], writes=[ksq])
    for nt in range(tn // 512):
        p.op("t", lambda e, nt=nt: e.matmul(psB[0:1, :], onesb[:, 0:1],
                                            ksq[:, 512 * nt:512 * (nt + 1)],
                                            start=True, stop=True),
             reads=[ksq, onesb], writes=[psB])
        p.op("v", lambda e, nt=nt: e.tensor_copy(out=knrow[0:1, 512 * nt:512 * (nt + 1)],
                                                 in_=psB[0:1, :]), reads=[psB], writes=[knrow])
    p.op("v", lambda e: e.reduce_max(out=kn1[:], in_=knrow[:], axis=AX.X),
         reads=[knrow], writes=[kn1])
    p.op("s", lambda e: e.activation(out=kn1[:], in_=kn1[:], func=AF.Sqrt),
         reads=[kn1], writes=[kn1])
    kr2 = p.sb("j_kr2", [1, 2 * tn])
    km2 = p.sb("j_km2", [1, 1])
    negc = p.sb("j_negc", [128, 1])
    for hh in range(2):
        p.dma("qs", lambda e, hh=hh: e.dma_start(out=kr2[0:1, tn * hh:tn * (hh + 1)],
                                                 in_=kraw_d[hh:hh + 1, :]), writes=[kr2])
    p.op("v", lambda e: e.reduce_max(out=km2[:], in_=kr2[:], axis=AX.X),
         reads=[kr2], writes=[km2])
    p.op("v", lambda e: e.tensor_tensor(out=km2[:], in0=km2[:], in1=kn1[:],
                                        op=ALU.mult), reads=[km2, kn1], writes=[km2])
    p.op("v", lambda e: e.tensor_scalar(out=km2[:], in0=km2[:],
                                        scalar1=-1.0 / math.sqrt(HD), scalar2=None,
                                        op0=ALU.mult), reads=[km2], writes=[km2])
    p.op("t", lambda e: e.matmul(psB[:, 0:1], K1, km2[:], start=True, stop=True),
         reads=[km2, ones], writes=[psB])
    p.op("v", lambda e: e.tensor_copy(out=negc[:], in_=psB[:, 0:1]),
         reads=[psB], writes=[negc])

    psS = [p.ps(f"j_psS{i}", [128, 512]) for i in range(3)]
    psO = [p.ps(f"j_psO{i}", [128, 512]) for i in range(2)]
    psZ = [p.ps(f"j_psZ{i}", [1, 512]) for i in range(2)]
    expb = [p.sb(f"j_expb{i}", [128, 512], BF16) for i in range(4)]
    ebsum = p.sb("j_ebsum", [128, 512])
    ebpair = p.sb("j_ebpair", [128, 512], BF16)
    ebsb = p.sb("j_ebsb", [128, 512], BF16)
    rcpb = p.sb("j_rcpb", [128, 512])
    yout = p.sb("j_yout", [128, 512], BF16)
    INV_SQ = 1.0 / math.sqrt(HD)

    for h in range(2):
        for qb in range(NQB):
            qsl = slice(512 * qb, 512 * (qb + 1))
            pO = psO[qb % 2]
            pZ = psZ[qb % 2]
            nkb = 4 * qb + 4
            def emit_qk(kb):
                pS = psS[kb % 3]
                eb = expb[kb % 4]
                p.op("t", lambda e, h=h, kb=kb, qsl=qsl, pS=pS: e.matmul(
                    pS[:], ksht[:, 128 * kb:128 * (kb + 1)],
                    q2[:, tn * h + qsl.start:tn * h + qsl.stop],
                    start=True, stop=True), reads=[ksht, q2], writes=[pS])
                p.op("s", lambda e, pS=pS, eb=eb: e.activation(
                    out=eb[:], in_=pS[:], func=AF.Exp, scale=INV_SQ,
                    bias=negc[:, 0:1]),
                    reads=[pS, negc], writes=[eb])
                jj = kb - 4 * qb
                if 0 <= jj < 4:
                    p.op("v", lambda e, eb=eb, jj=jj: e.tensor_tensor(
                        out=eb[:], in0=eb[:], in1=dmask[jj][:], op=ALU.mult),
                        reads=[eb, dmask[jj]], writes=[eb])
                if kb % 2 == 1:
                    ebprev = expb[(kb - 1) % 4]
                    p.op("v", lambda e, eb=eb, ebprev=ebprev: e.tensor_tensor(
                        out=ebpair[:], in0=ebprev[:], in1=eb[:], op=ALU.add),
                        reads=[ebprev, eb], writes=[ebpair])
                    if kb == 1:
                        p.op("v", lambda e: e.tensor_copy(out=ebsum[:], in_=ebpair[:]),
                             reads=[ebpair], writes=[ebsum])
                    else:
                        p.op("v", lambda e: e.tensor_tensor(
                            out=ebsum[:], in0=ebsum[:], in1=ebpair[:], op=ALU.add),
                            reads=[ebsum, ebpair], writes=[ebsum])

            def emit_pv(kb):
                eb = expb[kb % 4]
                p.op("t", lambda e, kb=kb, eb=eb, pO=pO: e.matmul(
                    pO[:], vtok[:, HD * kb:HD * (kb + 1)], eb[:],
                    start=(kb == 0), stop=(kb == nkb - 1)),
                    reads=[vtok, eb], writes=[pO])

            for kb in range(nkb):
                emit_qk(kb)
                if kb > 0:
                    emit_pv(kb - 1)
            emit_pv(nkb - 1)
            p.op("v", lambda e: e.tensor_copy(out=ebsb[:], in_=ebsum[:]),
                 reads=[ebsum], writes=[ebsb])
            p.op("t", lambda e, pZ=pZ: e.matmul(pZ[:], onesb[:, 0:1], ebsb[:],
                                                start=True, stop=True),
                 reads=[onesb, ebsb], writes=[pZ])
            p.op("v", lambda e, pZ=pZ: e.reciprocal(out=rcpb[0:1, :], in_=pZ[:]),
                 reads=[pZ], writes=[rcpb])
            p.op("t", lambda e: e.matmul(psB[:], K1, rcpb[0:1, :], start=True, stop=True),
                 reads=[rcpb, ones], writes=[psB])
            p.op("v", lambda e: e.tensor_copy(out=rcpb[:], in_=psB[:]),
                 reads=[psB], writes=[rcpb])
            p.op("v", lambda e, pO=pO: e.tensor_tensor(out=yout[:], in0=pO[:], in1=rcpb[:],
                                                       op=ALU.mult),
                 reads=[pO, rcpb], writes=[yout])
            def y2_sl(h, qsl=qsl):
                if len(y2_d.shape) == 3:
                    cb = qsl.start // (tn // 8)
                    w0 = qsl.start % (tn // 8)
                    return y2_d[HD * h:HD * (h + 1), cb, w0:w0 + (qsl.stop - qsl.start)]
                return y2_d[HD * h:HD * (h + 1), qsl]
            p.dma("qs", lambda e, h=h, qsl=qsl: e.dma_start(
                out=y2_sl(h), in_=yout[:]), reads=[yout])
    p.maybe_finish()
    return p


def build_phase_k(tw=TOWN, p=None):
    if p is None:
        p = Prog("phaseK")
    y_d = p.din("y", [D, tw], BF16)
    h3_d = p.din("h3", [D, tw])
    ones_d = p.din("ones", [128, 1])
    out_d = p.dout("outT", [D, tw])
    p.start()
    NCH = D // 128
    ones = p.sb("k_ones", [128, 1])
    ones_b = p.sb("k_onesb", [128, 1], BF16)
    p.dma("qs", lambda e: e.dma_start(out=ones[:], in_=ones_d[:]), writes=[ones])
    p.op("v", lambda e: e.tensor_copy(out=ones_b[:], in_=ones[:]), reads=[ones], writes=[ones_b])
    h = [p.sb(f"k_h{i}", [128, tw]) for i in range(NCH)]
    hb = [p.sb(f"k_hb{i}", [128, tw], BF16) for i in range(NCH)]
    yb = [p.sb(f"k_yb{i}", [128, tw], BF16) for i in range(NCH)]
    for i in range(NCH):
        p.dma("qs", lambda e, i=i: e.dma_start(out=h[i][:], in_=h3_d[128 * i:128 * (i + 1), :]),
              writes=[h[i]])
        p.dma("qg", lambda e, i=i: e.dma_start(out=yb[i][:], in_=y_d[128 * i:128 * (i + 1), :]),
              writes=[yb[i]])
        tt(p, h[i][:], h[i][:], yb[i][:], ALU.add, reads=[h[i], yb[i]], writes=[h[i]])
    ps_s = [p.ps("k_psLs1", [1, min(tw, 512)]), p.ps("k_psLs2", [1, min(tw, 512)])]
    ps_b = [p.ps("k_psLb1", [128, min(tw, 512)]), p.ps("k_psLb2", [128, min(tw, 512)])]
    rows = [p.sb(f"k_lnrow{i}", [1, tw]) for i in range(4)]
    emit_ln_fm(p, h, hb, ones, ones_b, ps_s, ps_b, rows, tw=tw)
    for i in range(NCH):
        p.dma("qs", lambda e, i=i: e.dma_start(out=out_d[128 * i:128 * (i + 1), :], in_=h[i][:]),
              reads=[h[i]])
    p.maybe_finish()
    return p


class GlueIdx:
    def __init__(self):
        self.cols = []

    def reg(self, fn):
        self.cols.append(fn)
        return len(self.cols) - 1

    def host(self, core):
        assert len(self.cols) <= 96, len(self.cols)
        a = np.full((128, 96), 2 ** 30, np.int32)
        for j, fn in enumerate(self.cols):
            v = np.asarray(fn(core), np.int32)
            a[:len(v), j] = v
        return a


def build_fused(devdump=False):
    p = Prog("fused")
    gidx = GlueIdx()
    RG = [list(range(NCORES))]
    nc = p.nc
    io = {}
    p.fused_io = io

    def internal(name, shape, dtype=F32):
        return nc.dram_tensor(name, list(shape), dtype, kind="Internal").ap()

    def output(name, shape, dtype=F32):
        t = nc.dram_tensor(name, list(shape), dtype, kind="ExternalOutput").ap()
        p.outputs[name] = t
        return t

    mag2_l = internal("x_mag2", [1024, TOWN]); theta_l = internal("x_theta", [1024, TOWN])
    mag2_o = internal("x_mag2o", [1024, TOWN]); theta_o = internal("x_thetao", [1024, TOWN])
    r4_l = internal("x_r4", [D, TOWN])
    kvre_s = internal("x_kvres", [1024, TOWN], BF16)
    kvim_s = internal("x_kvims", [1024, TOWN], BF16)
    kv_c = internal("x_kvc", [D, TOWN], BF16)
    h1_l = internal("x_h1", [D, TOWN]); h1b_l = internal("x_h1b", [D, TOWN], BF16)
    h1b_ag = internal("x_h1bag", [8 * D, TOWN], BF16)
    wmat_l = internal("x_wmat", [TOWN, NEXP]); posl_l = internal("x_posl", [TOWN, NEXP])
    cnt_l = internal("x_cnt", [1, NEXP])
    wall_ag = internal("x_wallag", [T, NEXP]); posall_ag = internal("x_posallag", [T, NEXP])
    cnt_ag = internal("x_cntag", [8, NEXP])
    rs_in = internal("x_rsin", [8 * D, TOWN], BF16)
    rs_out = internal("x_rsout", [D, TOWN], BF16)
    h2_l = internal("x_h2", [D, TOWN])
    hb_f = internal("x_hbf", [D, TOWN + 3], BF16)
    gsil_l = internal("x_gsil", [INTER, TOWN], BF16)
    yt_l = internal("x_yt", [INTER, TOWN], BF16)
    cwd_l = internal("x_cwd", [(TOWN // CS) * NN, NH * CS], BF16)
    sacc_l = internal("x_sacc", [NN, NH * HP], BF16)
    sacc_ag = internal("x_saccag", [8 * NN, NH * HP], BF16)
    espan_l = internal("x_espan", [TOWN // CS, NH])
    esptot_l = internal("x_esptot", [1, NH])
    esptot_ag = internal("x_esptotag", [8, NH])
    bnd_ag = internal("x_bndag", [8 * D, 4], BF16)
    h3_l = internal("x_h3", [D, TOWN]); h3b_l = internal("x_h3b", [D, TOWN], BF16)
    qt_l = internal("x_qt", [D, TOWN], BF16)
    q2_o = internal("x_q2o", [D, TOWN], BF16)
    ksh_ag = internal("x_kshag", [T, HD], BF16); vsh_ag = internal("x_vshag", [T, HD], BF16)
    ksh_l = internal("x_kshl", [TOWN, HD], BF16); vsh_l = internal("x_vshl", [TOWN, HD], BF16)
    kraw_l = internal("x_krawl", [NHEAD, TOWN])
    kraw_ag = internal("x_krawag", [8 * NHEAD, TOWN])
    kraw_m = internal("x_krawm", [2, T])
    y2_s = internal("x_y2s", [D, TOWN], BF16)
    ky_m = internal("x_kym", [D, TOWN], BF16)
    out_f = output("outT", [D, TOWN])

    io[("a", "mag2")] = mag2_l; io[("a", "theta")] = theta_l; io[("a", "r4")] = r4_l
    io[("b", "mag2")] = mag2_o[:].rearrange("(c r) t -> r c t", c=8)
    io[("b", "theta")] = theta_o[:].rearrange("(c r) t -> r c t", c=8)
    io[("b", "kvre")] = kvre_s[:].rearrange("(c r) t -> r c t", c=8)
    io[("b", "kvim")] = kvim_s[:].rearrange("(c r) t -> r c t", c=8)
    io[("c", "kv")] = kv_c; io[("c", "r4")] = r4_l
    io[("c", "h1")] = h1_l; io[("c", "h1b")] = h1b_l
    io[("d", "h1")] = h1_l
    io[("d", "wmat")] = wmat_l; io[("d", "posl")] = posl_l; io[("d", "cnt")] = cnt_l
    io[("e", "xall")] = h1b_ag[:].rearrange("(c r) t -> r c t", c=8)
    io[("e", "wall")] = wall_ag; io[("e", "posall")] = posall_ag; io[("e", "cnt8")] = cnt_ag
    io[("e", "part")] = rs_in[:].rearrange("(c r) t -> r c t", c=8)
    io[("f", "pj")] = rs_out; io[("f", "h1")] = h1_l
    io[("f", "h2")] = h2_l; io[("f", "h2b")] = hb_f[:, 3:TOWN + 3]
    io[("g", "hb")] = hb_f
    io[("g", "gsil")] = gsil_l; io[("g", "yt")] = yt_l; io[("g", "cw")] = cwd_l
    io[("g", "sacc")] = sacc_l; io[("g", "espan")] = espan_l; io[("g", "espantot")] = esptot_l
    io[("h", "ytin")] = yt_l; io[("h", "cw")] = cwd_l; io[("h", "gsil")] = gsil_l
    io[("h", "h2")] = h2_l; io[("h", "sacc8")] = sacc_ag
    io[("h", "esptot8")] = esptot_ag[:].rearrange("a b -> (a b)").unsqueeze(0)
    io[("h", "espmy")] = espan_l[:].rearrange("a b -> (a b)").unsqueeze(0)
    io[("h", "h3")] = h3_l; io[("h", "h3b")] = h3b_l
    io[("i", "h3")] = h3_l; io[("i", "h3b")] = h3b_l
    io[("i", "qt")] = qt_l; io[("i", "ksh")] = ksh_l; io[("i", "vsh")] = vsh_l
    io[("i", "kraw")] = kraw_l
    io[("j", "q2")] = q2_o[:].rearrange("(c r) t -> r c t", c=8)
    io[("j", "ksh")] = ksh_ag; io[("j", "vsh")] = vsh_ag
    io[("j", "kraw")] = kraw_m
    io[("j", "y2")] = y2_s[:].rearrange("(c r) t -> r c t", c=8)
    io[("k", "y")] = ky_m; io[("k", "h3")] = h3_l
    io[("k", "outT")] = out_f

    p.tag = "z"
    p.start()
    gidx_d = nc.dram_tensor("gidx", [128, 96], I32, kind="ExternalInput").ap()
    p.inputs["gidx"] = gidx_d
    gix = p.sb("z_gix", [128, 96], I32)
    p.dma("qs", lambda e: e.dma_start(out=gix[:], in_=gidx_d[:]), writes=[gix])
    gbuf = p.sb("z_gbuf", [128, 8], BF16)
    ixb = p.sb("z_ixb", [128, 1], I32)

    def ag(src_ap, out_ap, note):
        p.collective(lambda e: e.collective_compute(
            "AllGather", ALU.bypass, replica_groups=RG, ins=[src_ap], outs=[out_ap]))
        p.fence()

    def a2a(src_ap, out_ap, note):
        def go(e):
            with allow_ici_all_to_all():
                return e.collective_compute(
                    "AllToAll", ALU.bypass, replica_groups=RG, ins=[src_ap], outs=[out_ap])
        p.collective(go)
        p.fence()

    with p.phase("a"):
        build_phase_a(p=p)
    a2a(mag2_l, mag2_o, "mag2")
    a2a(theta_l, theta_o, "theta")
    with p.phase("b"):
        build_phase_b(p=p)
    a2a(kvre_s, kv_c[0:1024, :], "kvre")
    a2a(kvim_s, kv_c[1024:2048, :], "kvim")
    with p.phase("c"):
        build_phase_c(p=p)
    p.collective(lambda e: e.collective_compute(
        "AllGather", ALU.bypass, replica_groups=RG, ins=[h1b_l], outs=[h1b_ag]))
    with p.phase("d"):
        build_phase_d(p=p)
    ag(wmat_l, wall_ag, "wmat")
    ag(posl_l, posall_ag, "posl")
    ag(cnt_l, cnt_ag, "cnt")
    with p.phase("e"):
        build_phase_e(p=p)
    p.collective(lambda e: e.collective_compute(
        "ReduceScatter", ALU.add, replica_groups=RG, ins=[rs_in], outs=[rs_out]))
    p.fence()
    with p.phase("f"):
        build_phase_f(p=p, nparts=1)
    with p.phase("x3"):
        bl = p.sb("x3_bl", [128, 64], BF16)
        bnd_l = internal("x_bndl", [D, 4], BF16)
        p.op("v", lambda e: e.memset(bl[:], 0.0), writes=[bl])
        for ch in range(16):
            p.dma("qs", lambda e, ch=ch: e.dma_start(
                out=bl[:, 4 * ch:4 * ch + 3], in_=hb_f[128 * ch:128 * (ch + 1), TOWN:TOWN + 3]),
                writes=[bl])
        for ch in range(16):
            p.dma("qs", lambda e, ch=ch: e.dma_start(
                out=bnd_l[128 * ch:128 * (ch + 1), :], in_=bl[:, 4 * ch:4 * (ch + 1)]),
                reads=[bl])
        p.fence()
    ag(bnd_l, bnd_ag, "bnd")
    with p.phase("x4"):
        for ch in range(16):
            col = gidx.reg(lambda core, ch=ch: ((core - 1) * D + 128 * ch + np.arange(128))
                           if core > 0 else np.full(128, 2 ** 30))
            p.op("v", lambda e, col=col: e.tensor_copy(out=ixb[:], in_=gix[:, col:col + 1]),
                 reads=[gix], writes=[ixb])
            p.op("v", lambda e: e.memset(gbuf[0:128, 0:4], 0.0), writes=[gbuf])
            p.dma("qg", lambda e: e.indirect_dma_start(
                out=gbuf[0:128, 0:4],
                out_offset=None, in_=bnd_ag[:],
                in_offset=IndirectOffsetOnAxis(ap=ixb[:], axis=0),
                bounds_check=8 * D - 1, oob_is_err=False),
                reads=[ixb], writes=[gbuf])
            p.dma("qs", lambda e, ch=ch: e.dma_start(
                out=hb_f[128 * ch:128 * (ch + 1), 0:3], in_=gbuf[0:128, 0:3]),
                reads=[gbuf])
            p.fence()
    with p.phase("g"):
        build_phase_g(p=p)
    ag(sacc_l, sacc_ag, "sacc")
    ag(esptot_l, esptot_ag, "esptot")
    with p.phase("h"):
        build_phase_h(p=p)
    with p.phase("i"):
        build_phase_i(p=p)
    a2a(qt_l, q2_o, "qt")
    ag(ksh_l, ksh_ag, "ksh")
    ag(vsh_l, vsh_ag, "vsh")
    ag(kraw_l, kraw_ag, "kraw")
    with p.phase("x5"):
        krb = p.sb("x5_krb", [16, 1024])
        colk = gidx.reg(lambda core: np.array(
            [c * NHEAD + 2 * core + hloc for hloc in range(2) for c in range(8)], np.int32))
        p.op("v", lambda e: e.tensor_copy(out=ixb[0:16, :], in_=gix[0:16, colk:colk + 1]),
             reads=[gix], writes=[ixb])
        p.op("v", lambda e: e.memset(krb[:], 0.0), writes=[krb])
        p.dma("qg", lambda e: e.indirect_dma_start(
            out=krb[:], out_offset=None, in_=kraw_ag[:],
            in_offset=IndirectOffsetOnAxis(ap=ixb[0:16, :], axis=0),
            bounds_check=8 * NHEAD - 1, oob_is_err=False),
            reads=[ixb], writes=[krb])
        p.dma("qs", lambda e: e.dma_start(
            out=kraw_m[:].rearrange("h (c t) -> (h c) t", c=8), in_=krb[:]), reads=[krb])
        p.fence()
    with p.phase("j"):
        build_phase_j(p=p)
    a2a(y2_s, ky_m, "y2")
    with p.phase("k"):
        build_phase_k(p=p)
    p.finish()
    return p, gidx


FUSED = True


def _host_prep(inputs):
    x = _f32(inputs["L0_attn_x"][0])
    xT = _f32(x.T)
    xT_halo = np.concatenate([np.zeros((D, 1), np.float32), xT], axis=1)
    perm = np.concatenate([np.arange(0, D, 2), np.arange(1, D, 2)])
    WA = _f32(inputs["L0_attn_W_act"].T[:, perm])
    WR = _f32(inputs["L0_attn_W_rec"].T)
    WAB = np.concatenate([WA, WR], axis=1)
    WV = _f32(inputs["L0_attn_W_val"].T[perm, :])
    ones_col = np.ones((128, 1), np.float32)
    TRI = np.triu(np.ones((128, 128), np.float32), 1)
    CMASK = np.triu(np.ones((8, 8), np.float32), 1)
    EYE = np.eye(128, dtype=np.float32)
    ONESR = np.ones((1, 512), np.float32)
    ipw = _f32(inputs["L1_attn_in_proj_w"])
    DMASK = np.zeros((4 * 128, 512), np.float32)
    for jj in range(4):
        tk = np.arange(128)[:, None] + 128 * jj
        tq = np.arange(512)[None, :]
        DMASK[128 * jj:128 * (jj + 1)] = (tk <= tq).astype(np.float32)
    return dict(
        xT=xT, xT_halo=xT_halo, WAB=WAB, WV=WV, ones=ones_col, TRI=TRI,
        CMASK=CMASK, EYE=EYE, ONESR=ONESR, DMASK=DMASK,
        gw=_f32(inputs["L0_ffn_gate_w"]),
        wg_all=_f32(inputs["L0_ffn_w_gate"]), wu_all=_f32(inputs["L0_ffn_w_up"]),
        wd_all=_f32(inputs["L0_ffn_w_down"]),
        wpg=ipw[:, :INTER], wpx=ipw[:, INTER:INTER + CONV_DIM],
        wpd=ipw[:, INTER + CONV_DIM:],
        convw=_f32(inputs["L1_attn_conv_w"]),
        convbt=_f32(inputs["L1_attn_conv_b"])[None, :],
        dtb=_f32(inputs["L1_attn_dt_bias"])[:, None],
        alog=_f32(inputs["L1_attn_A_log"])[:, None],
        deye=_f32(np.repeat(inputs["L1_attn_D"][None, :], 128, axis=0)),
        nw=_f32(inputs["L1_attn_norm_weight"].reshape(INTER // 128, 128).T),
        wop=_f32(inputs["L1_attn_out_proj_w"]),
        wgt=_f32(inputs["L1_ffn_W_gate"].T),
        wqkvt=_f32(inputs["L1_ffn_W_qkv"].T))


def kernel_fused(**inputs):
    hp = _host_prep(inputs)
    p, gidx = build_fused()
    in_maps = []
    for i in range(NCORES):
        jm = np.zeros((1, 8), np.float32)
        jm[0, :i] = 1.0
        m = {
            "a_in_xs": np.ascontiguousarray(hp["xT_halo"][:, TOWN * i:TOWN * (i + 1) + 1]),
            "a_in_wab": hp["WAB"],
            "c_in_xs": np.ascontiguousarray(hp["xT"][:, TOWN * i:TOWN * (i + 1)]),
            "c_in_wv": hp["WV"], "c_in_ones": hp["ones"],
            "d_in_gw": hp["gw"], "d_in_tri": hp["TRI"], "d_in_ones": hp["ones"],
            "e_in_wg": hp["wg_all"][4 * i:4 * i + 4].reshape(4 * D, 768),
            "e_in_wu": hp["wu_all"][4 * i:4 * i + 4].reshape(4 * D, 768),
            "e_in_wd": hp["wd_all"][4 * i:4 * i + 4].reshape(4 * 768, D),
            "e_in_cmask": hp["CMASK"], "e_in_ones": hp["ones"], "e_in_eye": hp["EYE"],
            "e_in_esel": np.arange(4 * i, 4 * i + 4, dtype=np.int32)[:, None],
            "f_in_ones": hp["ones"],
            "g_in_wpg": hp["wpg"], "g_in_wpx": hp["wpx"], "g_in_wpd": hp["wpd"],
            "g_in_convw": hp["convw"], "g_in_convbt": hp["convbt"],
            "g_in_onesr": hp["ONESR"], "g_in_dtb": hp["dtb"], "g_in_alog": hp["alog"],
            "g_in_deye": hp["deye"], "g_in_eye": hp["EYE"], "g_in_ones": hp["ones"],
            "h_in_jmask": jm, "h_in_nw": hp["nw"], "h_in_wop": hp["wop"],
            "h_in_ones": hp["ones"],
            "i_in_wgt": hp["wgt"], "i_in_wqkvt": hp["wqkvt"], "i_in_ones": hp["ones"],
            "j_in_dmask": hp["DMASK"], "j_in_ones": hp["ones"],
            "k_in_ones": hp["ones"],
            "cst": p.cst_host(),
            "gidx": gidx.host(i),
        }
        in_maps.append(m)
    res = run_bass_kernel_spmd(p.nc, in_maps, core_ids=list(range(NCORES))).results
    outT = np.concatenate([np.asarray(r["outT"]) for r in res], axis=1)
    return np.ascontiguousarray(outT.T)[None]


def kernel(**inputs):
    if FUSED:
        return kernel_fused(**inputs)
    return kernel_unfused(**inputs)


def kernel_unfused(**inputs):
    x = _f32(inputs["L0_attn_x"][0])
    xT = _f32(x.T)
    xT_halo = np.concatenate([np.zeros((D, 1), np.float32), xT], axis=1)

    perm = np.concatenate([np.arange(0, D, 2), np.arange(1, D, 2)])
    WA = _f32(inputs["L0_attn_W_act"].T[:, perm])
    WR = _f32(inputs["L0_attn_W_rec"].T)
    WAB = np.concatenate([WA, WR], axis=1)
    WV = _f32(inputs["L0_attn_W_val"].T[perm, :])
    ones_col = np.ones((128, 1), np.float32)

    pa = build_phase_a()
    in_maps = [dict(xs=xT_halo[:, TOWN * i: TOWN * (i + 1) + 1], wab=WAB, cst=pa.cst_host())
               for i in range(NCORES)]
    ra = run_bass_kernel_spmd(pa.nc, in_maps, core_ids=list(range(NCORES))).results

    mag2 = np.concatenate([np.asarray(r["mag2"]) for r in ra], axis=1)
    theta = np.concatenate([np.asarray(r["theta"]) for r in ra], axis=1)
    r4 = np.concatenate([np.asarray(r["r4"]) for r in ra], axis=1)

    pb = build_phase_b()
    in_maps = [dict(mag2=mag2[128 * i:128 * (i + 1)], theta=theta[128 * i:128 * (i + 1)],
                    cst=pb.cst_host()) for i in range(NCORES)]
    rb = run_bass_kernel_spmd(pb.nc, in_maps, core_ids=list(range(NCORES))).results
    kvre = np.concatenate([np.asarray(r["kvre"]) for r in rb], axis=0)
    kvim = np.concatenate([np.asarray(r["kvim"]) for r in rb], axis=0)
    kv_perm = np.concatenate([kvre, kvim], axis=0)

    pc = build_phase_c()
    in_maps = [dict(kv=kv_perm[:, TOWN * i:TOWN * (i + 1)],
                    r4=r4[:, TOWN * i:TOWN * (i + 1)],
                    xs=xT[:, TOWN * i:TOWN * (i + 1)],
                    wv=WV, ones=ones_col, cst=pc.cst_host()) for i in range(NCORES)]
    rc = run_bass_kernel_spmd(pc.nc, in_maps, core_ids=list(range(NCORES))).results
    h1 = np.concatenate([np.asarray(r["h1"]) for r in rc], axis=1)
    h1b = np.concatenate([np.asarray(r["h1b"]) for r in rc], axis=1)

    TRI = np.triu(np.ones((128, 128), np.float32), 1)
    pd = build_phase_d()
    in_maps = [dict(h1=h1[:, TOWN * i:TOWN * (i + 1)], gw=_f32(inputs["L0_ffn_gate_w"]),
                    tri=TRI, ones=ones_col, cst=pd.cst_host()) for i in range(NCORES)]
    rd = run_bass_kernel_spmd(pd.nc, in_maps, core_ids=list(range(NCORES))).results
    wall = np.concatenate([np.asarray(r["wmat"]) for r in rd], axis=0)
    posall = np.concatenate([np.asarray(r["posl"]) for r in rd], axis=0)
    cnt8 = np.concatenate([np.asarray(r["cnt"]) for r in rd], axis=0)

    pe = build_phase_e()
    CMASK = np.triu(np.ones((8, 8), np.float32), 1)
    EYE = np.eye(128, dtype=np.float32)
    wg_all = _f32(inputs["L0_ffn_w_gate"])
    wu_all = _f32(inputs["L0_ffn_w_up"])
    wd_all = _f32(inputs["L0_ffn_w_down"])
    in_maps = []
    for i in range(NCORES):
        in_maps.append(dict(
            xall=np.ascontiguousarray(h1b), wg=wg_all[4 * i:4 * i + 4].reshape(4 * D, 768),
            wu=wu_all[4 * i:4 * i + 4].reshape(4 * D, 768),
            wd=wd_all[4 * i:4 * i + 4].reshape(4 * 768, D),
            wall=wall, posall=posall, cnt8=cnt8,
            esel=np.arange(4 * i, 4 * i + 4, dtype=np.int32)[:, None],
            cmask=CMASK, ones=ones_col, eye=EYE, cst=pe.cst_host()))
    re_ = run_bass_kernel_spmd(pe.nc, in_maps, core_ids=list(range(NCORES))).results
    parts = [np.asarray(r["part"]) for r in re_]

    pf = build_phase_f()
    in_maps = []
    for i in range(NCORES):
        pj = np.concatenate([parts[j][:, TOWN * i:TOWN * (i + 1)] for j in range(8)], axis=0)
        in_maps.append(dict(pj=pj, h1=h1[:, TOWN * i:TOWN * (i + 1)], ones=ones_col,
                            cst=pf.cst_host()))
    rf = run_bass_kernel_spmd(pf.nc, in_maps, core_ids=list(range(NCORES))).results
    h2 = np.concatenate([np.asarray(r["h2"]) for r in rf], axis=1)
    h2b = np.concatenate([np.asarray(r["h2b"]) for r in rf], axis=1)

    pg = build_phase_g()
    ipw = _f32(inputs["L1_attn_in_proj_w"])
    wpg_h, wpx_h, wpd_h = ipw[:, :INTER], ipw[:, INTER:INTER + CONV_DIM], ipw[:, INTER + CONV_DIM:]
    convw_h = _f32(inputs["L1_attn_conv_w"])
    convbt_h = _f32(inputs["L1_attn_conv_b"])[None, :]
    dtb_h = _f32(inputs["L1_attn_dt_bias"])[:, None]
    alog_h = _f32(inputs["L1_attn_A_log"])[:, None]
    deye_h = _f32(np.repeat(inputs["L1_attn_D"][None, :], 128, axis=0))
    EYE = np.eye(128, dtype=np.float32)
    ONESR = np.ones((1, 512), np.float32)
    h2b_halo = np.concatenate([np.zeros((D, 3), ml_dtypes.bfloat16), h2b], axis=1)
    in_maps = []
    for i in range(NCORES):
        in_maps.append(dict(
            hb=np.ascontiguousarray(h2b_halo[:, TOWN * i:TOWN * (i + 1) + 3]),
            wpg=wpg_h, wpx=wpx_h, wpd=wpd_h, convw=convw_h, convbt=convbt_h,
            onesr=ONESR, dtb=dtb_h, alog=alog_h, deye=deye_h, eye=EYE,
            ones=ones_col, cst=pg.cst_host()))
    rg = run_bass_kernel_spmd(pg.nc, in_maps, core_ids=list(range(NCORES))).results

    ph = build_phase_h()
    sacc8 = np.concatenate([np.asarray(r["sacc"]) for r in rg], axis=0)
    esptot8 = np.concatenate([np.asarray(r["espantot"]) for r in rg], axis=1)
    nw_h = _f32(inputs["L1_attn_norm_weight"].reshape(INTER // 128, 128).T)
    wop_h = _f32(inputs["L1_attn_out_proj_w"])
    in_maps = []
    for i in range(NCORES):
        jm = np.zeros((1, 8), np.float32)
        jm[0, :i] = 1.0
        in_maps.append(dict(
            ytin=np.asarray(rg[i]["yt"]), cw=np.asarray(rg[i]["cw"]),
            gsil=np.asarray(rg[i]["gsil"]), h2=h2[:, TOWN * i:TOWN * (i + 1)],
            sacc8=sacc8, esptot8=esptot8,
            espmy=np.asarray(rg[i]["espan"]).reshape(1, -1),
            jmask=jm, nw=nw_h, wop=wop_h, ones=ones_col, cst=ph.cst_host()))
    rh = run_bass_kernel_spmd(ph.nc, in_maps, core_ids=list(range(NCORES))).results
    h3 = np.concatenate([np.asarray(r["h3"]) for r in rh], axis=1)
    h3b = np.concatenate([np.asarray(r["h3b"]) for r in rh], axis=1)

    pi = build_phase_i()
    WGT = _f32(inputs["L1_ffn_W_gate"].T)
    WQKVT = _f32(inputs["L1_ffn_W_qkv"].T)
    in_maps = [dict(h3=h3[:, TOWN * i:TOWN * (i + 1)], h3b=h3b[:, TOWN * i:TOWN * (i + 1)],
                    wgt=WGT, wqkvt=WQKVT, ones=ones_col, cst=pi.cst_host())
               for i in range(NCORES)]
    ri = run_bass_kernel_spmd(pi.nc, in_maps, core_ids=list(range(NCORES))).results
    qt_full = np.concatenate([np.asarray(r["qt"]) for r in ri], axis=1)
    ksh_full = np.concatenate([np.asarray(r["ksh"]) for r in ri], axis=0)
    vsh_full = np.concatenate([np.asarray(r["vsh"]) for r in ri], axis=0)
    kraw_full = np.concatenate([np.asarray(r["kraw"]) for r in ri], axis=1)

    pj = build_phase_j()
    DMASK = np.zeros((4 * 128, 512), np.float32)
    for jj in range(4):
        tk = np.arange(128)[:, None] + 128 * jj
        tq = np.arange(512)[None, :]
        DMASK[128 * jj:128 * (jj + 1)] = (tk <= tq).astype(np.float32)
    in_maps = []
    for i in range(NCORES):
        q2 = np.concatenate([qt_full[128 * (2 * i):128 * (2 * i + 1)],
                             qt_full[128 * (2 * i + 1):128 * (2 * i + 2)]], axis=0)
        in_maps.append(dict(q2=np.ascontiguousarray(q2), ksh=ksh_full, vsh=vsh_full,
                            kraw=np.ascontiguousarray(kraw_full[2 * i:2 * i + 2]),
                            dmask=DMASK, ones=ones_col, cst=pj.cst_host()))
    rj = run_bass_kernel_spmd(pj.nc, in_maps, core_ids=list(range(NCORES))).results
    ycat = np.concatenate([np.asarray(r["y2"]) for r in rj], axis=0)

    pk = build_phase_k()
    in_maps = [dict(y=np.ascontiguousarray(ycat[:, TOWN * i:TOWN * (i + 1)]),
                    h3=h3[:, TOWN * i:TOWN * (i + 1)], ones=ones_col, cst=pk.cst_host())
               for i in range(NCORES)]
    rk = run_bass_kernel_spmd(pk.nc, in_maps, core_ids=list(range(NCORES))).results
    outT = np.concatenate([np.asarray(r["outT"]) for r in rk], axis=1)
    return np.ascontiguousarray(outT.T)[None]


if __name__ == "__main__":
    pass


# version 09
# speedup vs baseline: 1.2794x; 1.0086x over previous
import math
import types as _types
from contextlib import ExitStack


def _freeze(fn):
    if fn.__closure__ is None:
        return fn
    frozen = tuple(_types.CellType(c.cell_contents) for c in fn.__closure__)
    g = _types.FunctionType(fn.__code__, fn.__globals__, fn.__name__,
                            fn.__defaults__, frozen)
    g.__kwdefaults__ = fn.__kwdefaults__
    return g

import numpy as np
import ml_dtypes

import concourse.bass as bass
import concourse.mybir as mybir
from concourse.bass_utils import run_bass_kernel_spmd
from concourse.bass import IndirectOffsetOnAxis
from concourse.aot_env import allow_ici_all_to_all

F32 = mybir.dt.float32
BF16 = mybir.dt.bfloat16
I32 = mybir.dt.int32
AF = mybir.ActivationFunctionType
ALU = mybir.AluOpType
AX = mybir.AxisListType

NCORES = 8
T = 8192
D = 2048
TOWN = T // NCORES

PI = math.pi
TWO_PI = 2.0 * math.pi
INV_2PI = 1.0 / TWO_PI


class Prog:
    NQSEM = 8

    def __init__(self, name="prog"):
        self.nc = bass.Bass("TRN2", target_bir_lowering=False, debug=False)
        self.ctx = ExitStack()
        self.started = False
        self.inputs = {}
        self.outputs = {}
        self.name = name
        self.fused_io = None
        self.tag = ""
        self._scope_stack = []

    def din(self, name, shape, dtype=F32):
        if self.fused_io is not None:
            ap = self.fused_io.get((self.tag, name))
            if ap is not None:
                return ap
            qn = self.tag + "_in_" + name
            t = self.nc.dram_tensor(qn, list(shape), dtype, kind="ExternalInput").ap()
            self.inputs[qn] = t
            return t
        t = self.nc.dram_tensor(name, list(shape), dtype, kind="ExternalInput").ap()
        self.inputs[name] = t
        return t

    def dout(self, name, shape, dtype=F32):
        if self.fused_io is not None:
            ap = self.fused_io.get((self.tag, name))
            assert ap is not None, f"fused dout {self.tag}/{name} not wired"
            return ap
        t = self.nc.dram_tensor(name, list(shape), dtype, kind="ExternalOutput").ap()
        self.outputs[name] = t
        return t

    def dtmp(self, name, shape, dtype=F32):
        return self.nc.dram_tensor(self.tag + "_" + name if self.fused_io is not None
                                   else name, list(shape), dtype, kind="Internal").ap()

    def start(self):
        if self.started:
            return
        self.started = True
        self.const_values = []
        self._cst_d = self.nc.dram_tensor("cst", [128, 16], F32, kind="ExternalInput").ap()
        self.inputs["cst"] = self._cst_d
        E = self.ctx.enter_context
        self.sems = {}
        self.cnt = {}
        for k in ("v", "s", "t", "g", "cc"):
            self.sems[k] = E(self.nc.semaphore("sem_" + k))
            self.cnt[k] = 0
        for q in ("qs", "qg"):
            for i in range(self.NQSEM):
                k = f"{q}{i}"
                self.sems[k] = E(self.nc.semaphore("sem_" + k))
                self.cnt[k] = 0
        self.qrr = {"qs": 0, "qg": 0}
        self._pending = {}
        self._tcells = []
        self._t_last_inc = -1
        self.block = E(self.nc.Block())
        self.engmap = {"v": "vector", "s": "scalar", "t": "tensor", "g": "gpsimd",
                       "qs": "sync", "qg": "gpsimd", "cc": "gpsimd"}
        self.watermark = {}
        self.writer = {}
        self.readers = {}
        self.nops = 0
        self.global_extra = []
        self._cst_s = self.sb("cst_s", [128, 16])
        self.dma("qs", lambda e: e.dma_start(out=self._cst_s[:], in_=self._cst_d[:]),
                 writes=[self._cst_s])

    def constcell(self, value, parts=128):
        value = float(value)
        if value in self.const_values:
            j = self.const_values.index(value)
        else:
            j = len(self.const_values)
            assert j < 16
            self.const_values.append(value)
        return self._cst_s[0:parts, j:j + 1]

    def cst_host(self):
        a = np.zeros((128, 16), np.float32)
        for j, v in enumerate(self.const_values):
            a[:, j] = v
        return a

    def sb(self, name, shape, dtype=F32):
        ctx = self._scope_stack[-1] if self._scope_stack else self.ctx
        return ctx.enter_context(self.nc.sbuf_tensor(name, list(shape), dtype))

    def ps(self, name, shape, dtype=F32):
        ctx = self._scope_stack[-1] if self._scope_stack else self.ctx
        return ctx.enter_context(self.nc.psum_tensor(name, list(shape), dtype))

    def phase(self, tag):
        from contextlib import contextmanager

        @contextmanager
        def _cm():
            self.tag = tag
            sc = ExitStack()
            self._scope_stack.append(sc)
            try:
                yield
            except BaseException:
                self._scope_stack.pop()
                self.tag = ""
                raise
            self._scope_stack.pop()
            sc.close()
            self.fence()
            self.tag = ""
        return _cm()

    def maybe_finish(self):
        if self.fused_io is None:
            self.finish()

    def _resolve_t(self, idx):
        if idx > self._t_last_inc:
            cell = self._tcells[-1]
            if not cell["inc"]:
                cell["inc"] = True
                self.cnt["t"] += 1
            self._t_last_inc = len(self._tcells) - 1
        return ("t", self.cnt["t"])

    def _deps(self, reads, writes, extra, consumer_ename=None):
        need = {}

        def add(tok):
            if tok is None:
                return
            if tok[0] == "t#":
                if consumer_ename == "tensor":
                    return
                tok = self._resolve_t(tok[1])
            k, v = tok
            if need.get(k, 0) < v:
                need[k] = v

        def key(b):
            return b if isinstance(b, str) else id(b)
        for b in reads:
            add(self.writer.get(key(b)))
        for b in writes:
            add(self.writer.get(key(b)))
            for tok in self.readers.get(key(b), ()):
                add(tok)
        for tok in extra:
            add(tok)
        return need

    def fence(self):
        self.global_extra = self.barrier_token()

    def _emit(self, ek, semkey, inc, build, reads, writes, extra):
        build = _freeze(build)
        ename = self.engmap[ek]
        need = self._deps(reads, writes, list(extra) + self.global_extra,
                          consumer_ename=ename)
        sems = self.sems
        wm = self.watermark

        waits = []
        for k, v in need.items():
            key = (ename, k)
            if wm.get(key, 0) >= v:
                continue
            wm[key] = v
            waits.append((k, v))

        if ek == "t":
            cell = {"inc": False}
            self._tcells.append(cell)
            tsem = sems["t"]

            def go(e, cell=cell, waits=waits):
                for k, v in waits:
                    e.wait_ge(sems[k], v)
                ins = build(e)
                if cell["inc"]:
                    ins.then_inc(tsem, 1)

            self._pending.setdefault(ename, []).append(go)
            self.nops += 1
            tok = ("t#", len(self._tcells) - 1)
        else:
            def go(e, waits=waits):
                for k, v in waits:
                    e.wait_ge(sems[k], v)
                ins = build(e)
                ins.then_inc(sems[semkey], inc)

            self._pending.setdefault(ename, []).append(go)
            self.cnt[semkey] += inc
            self.nops += 1
            tok = (semkey, self.cnt[semkey])
        for b in writes:
            kb = b if isinstance(b, str) else id(b)
            self.writer[kb] = tok
            self.readers[kb] = []
        for b in reads:
            kb = b if isinstance(b, str) else id(b)
            self.readers.setdefault(kb, []).append(tok)
        return tok

    def op(self, ek, build, reads=(), writes=(), extra=()):
        assert ek in ("v", "s", "t", "g")
        return self._emit(ek, ek, 1, build, reads, writes, extra)

    def dma(self, q, build, reads=(), writes=(), extra=()):
        assert q in ("qs", "qg")
        i = self.qrr[q]
        self.qrr[q] = (i + 1) % self.NQSEM
        semkey = f"{q}{i}"
        extra = list(extra)
        if self.cnt[semkey] > 0:
            extra.append((semkey, self.cnt[semkey]))
        return self._emit(q, semkey, 16, build, reads, writes, extra)

    def collective(self, build, reads=(), writes=(), extra=()):
        extra = list(extra)
        if self.cnt["cc"] > 0:
            extra.append(("cc", self.cnt["cc"]))
        return self._emit("cc", "cc", 1, build, reads, writes, extra)

    def barrier_token(self, keys=None):
        if self._tcells and self._t_last_inc < len(self._tcells) - 1:
            self._resolve_t(len(self._tcells) - 1)
        keys = keys or list(self.cnt)
        return [(k, self.cnt[k]) for k in keys if self.cnt[k] > 0]

    def flush(self):
        for ename, closures in self._pending.items():
            def run_all(e, closures=closures):
                for c in closures:
                    c(e)
            getattr(self.block, ename)(run_all)
        self._pending = {}

    def finish(self):
        toks = self.barrier_token()
        sems = self.sems

        def go(e):
            for k, v in toks:
                key = ("gpsimd", k)
                if self.watermark.get(key, 0) >= v:
                    continue
                e.wait_ge(sems[k], v)

        self._pending.setdefault("gpsimd", []).append(go)
        self.flush()
        self.ctx.close()


def tt(p, out, a, b, alu, reads=None, writes=None, extra=()):
    return p.op("v", lambda e: e.tensor_tensor(out=out, in0=a, in1=b, op=alu),
                reads=reads, writes=writes, extra=extra)


def _bf(x):
    return np.ascontiguousarray(x, dtype=ml_dtypes.bfloat16)


def _f32(x):
    return np.ascontiguousarray(x, dtype=np.float32)


def build_phase_a(tw=TOWN, p=None):
    if p is None:
        p = Prog("phaseA")
    xs_d = p.din("xs", [D, tw + 1])
    wab_d = p.din("wab", [2 * D, 2 * D])
    mag2_d = p.dout("mag2", [D // 2, tw])
    theta_d = p.dout("theta", [D // 2, tw])
    r4_d = p.dout("r4", [D, tw])
    p.start()

    KC = 2 * D // 128
    CC = 2 * D // 128
    NT = tw // 512

    xb = [p.sb(f"xb{i}", [128, tw + 1], BF16) for i in range(D // 128)]
    for i in range(D // 128):
        p.dma("qg", lambda e, i=i: e.dma_start(out=xb[i][:], in_=xs_d[128 * i:128 * (i + 1), :]),
              writes=[xb[i]])

    NPS = 8 if tw >= 512 else 4
    pst = [p.ps(f"psA{i}", [128, 512 if tw >= 512 else tw]) for i in range(NPS)]
    wtile = [p.sb(f"wA{i}", [128, 512], BF16) for i in range(KC)]
    xv = [p.sb(f"xv{i}", [128, tw]) for i in range(D // 128)]
    rr = [p.sb(f"rr{i}", [128, tw]) for i in range(D // 128)]

    CG = 4
    for cg in range(CC // CG):
        for k in range(KC):
            wt = wtile[k]
            p.dma("qg", lambda e, k=k, cg=cg, wt=wt: e.dma_start(
                out=wt[:], in_=wab_d[128 * k:128 * (k + 1), 512 * cg:512 * (cg + 1)]),
                writes=[wt])
        for ci in range(CG):
            c = cg * CG + ci
            for nt in range(max(NT, 1)):
                nw = 512 if tw >= 512 else tw
                psu = pst[(ci * max(NT, 1) + nt) % NPS]
                for k in range(KC):
                    wt = wtile[k]
                    xi = k % (D // 128)
                    off = 0 if k < KC // 2 else 1
                    rhs = xb[xi][:, nt * nw + off: nt * nw + off + nw]
                    p.op("t", lambda e, psu=psu, wt=wt, rhs=rhs, k=k, ci=ci: e.matmul(
                        psu[:], wt[:, 128 * ci:128 * (ci + 1)], rhs,
                        start=(k == 0), stop=(k == KC - 1)),
                        reads=[wt, xb[xi]], writes=[psu])
                if c < CC // 2:
                    dst = xv[c]
                else:
                    dst = rr[c - CC // 2]
                seg = dst[:, nt * nw: (nt + 1) * nw]
                p.op("v", lambda e, psu=psu, seg=seg: e.tensor_scalar(
                    out=seg, in0=psu[:], scalar1=0.0, scalar2=None, op0=ALU.max),
                    reads=[psu], writes=[dst])
                p.op("s", lambda e, seg=seg: e.activation(out=seg, in_=seg, func=AF.Sigmoid),
                    reads=[dst], writes=[dst])

    for i in range(D // 128):
        p.op("v", lambda e, i=i: e.tensor_tensor(out=rr[i][:], in0=rr[i][:], in1=rr[i][:], op=ALU.mult),
             reads=[rr[i]], writes=[rr[i]])
        p.op("v", lambda e, i=i: e.tensor_tensor(out=rr[i][:], in0=rr[i][:], in1=rr[i][:], op=ALU.mult),
             reads=[rr[i]], writes=[rr[i]])
        p.dma("qs", lambda e, i=i: e.dma_start(out=r4_d[128 * i:128 * (i + 1), :], in_=rr[i][:]),
              reads=[rr[i]])

    NPAIR = D // 256
    for i in range(NPAIR):
        re, im = xv[i], xv[i + NPAIR]
        m2, t1, t2 = rr[(3 * i) % 16], rr[(3 * i + 1) % 16], rr[(3 * i + 2) % 16]
        tt(p, m2[:], re[:], re[:], ALU.mult, reads=[re], writes=[m2])
        tt(p, t1[:], im[:], im[:], ALU.mult, reads=[im], writes=[t1])
        tt(p, m2[:], m2[:], t1[:], ALU.add, reads=[m2, t1], writes=[m2])
        p.dma("qs", lambda e, i=i, m2=m2: e.dma_start(
            out=mag2_d[128 * i:128 * (i + 1), :], in_=m2[:]), reads=[m2])
        tt(p, t1[:], re[:], im[:], ALU.min, reads=[re, im], writes=[t1])
        tt(p, t2[:], re[:], im[:], ALU.max, reads=[re, im], writes=[t2])
        p.op("v", lambda e, t2=t2: e.reciprocal(out=t2[:], in_=t2[:]), reads=[t2], writes=[t2])
        tt(p, t1[:], t1[:], t2[:], ALU.mult, reads=[t1, t2], writes=[t1])
        p.op("s", lambda e, t1=t1: e.activation(out=t1[:], in_=t1[:], func=AF.Arctan),
             reads=[t1], writes=[t1])
        tt(p, t2[:], im[:], re[:], ALU.is_gt, reads=[im, re], writes=[t2])
        p.op("v", lambda e, t2=t2, re=re: e.tensor_scalar(
            out=re[:], in0=t2[:], scalar1=-2.0, scalar2=1.0, op0=ALU.mult, op1=ALU.add),
            reads=[t2], writes=[re])
        tt(p, t1[:], t1[:], re[:], ALU.mult, reads=[t1, re], writes=[t1])
        p.op("v", lambda e, t2=t2, t1=t1: e.scalar_tensor_tensor(
            out=t1[:], in0=t2[:], scalar=PI / 2, in1=t1[:], op0=ALU.mult, op1=ALU.add),
            reads=[t2, t1], writes=[t1])
        p.dma("qs", lambda e, i=i, t1=t1: e.dma_start(
            out=theta_d[128 * i:128 * (i + 1), :], in_=t1[:]), reads=[t1])

    p.maybe_finish()
    return p


def build_phase_b(tn=T, p=None):
    if p is None:
        p = Prog("phaseB")
    mag2_d = p.din("mag2", [128, tn])
    theta_d = p.din("theta", [128, tn])
    kvre_d = p.dout("kvre", [128, tn], BF16)
    kvim_d = p.dout("kvim", [128, tn], BF16)
    p.start()

    m2 = p.sb("s_m2", [128, tn])
    th = p.sb("s_th", [128, tn])
    bA = p.sb("s_bA", [128, tn])
    bB = p.sb("s_bB", [128, tn])
    bC = p.sb("s_bC", [128, tn])
    zz = p.sb("s_zz", [128, 1])
    zb = zz[:, 0:1].broadcast_to([128, tn])

    p.dma("qs", lambda e: e.dma_start(out=m2[:], in_=mag2_d[:]), writes=[m2])
    p.dma("qs", lambda e: e.dma_start(out=th[:], in_=theta_d[:]), writes=[th])
    p.op("v", lambda e: e.memset(zz[:], 0.0), writes=[zz])

    def scan(dst, src_, init, op0, reads, writes):
        p.op("v", lambda e: e.tensor_tensor_scan(out=dst, data0=src_, data1=zb,
                                                 initial=init, op0=op0, op1=ALU.add),
             reads=list(reads) + [zz], writes=writes)

    scan(bA[:], m2[:], 0.0, ALU.max, [m2], [bA])
    p.op("v", lambda e: e.reciprocal(out=bA[:], in_=bA[:]), reads=[bA], writes=[bA])
    tt(p, bA[:], bA[:], m2[:], ALU.mult, reads=[bA, m2], writes=[bA])
    scan(m2[:], bA[:], 1.0, ALU.mult, [bA], [m2])
    p.op("s", lambda e: e.activation(out=m2[:], in_=m2[:], func=AF.Sqrt),
         reads=[m2], writes=[m2])

    scan(bA[:], th[:], 0.0, ALU.add, [th], [bA])
    p.op("v", lambda e: e.tensor_scalar(out=th[:], in0=bA[:], scalar1=INV_2PI,
                                        scalar2=None, op0=ALU.mult),
         reads=[bA], writes=[th])
    ki = bB[:].bitcast(I32)
    p.op("v", lambda e: e.tensor_copy(out=ki, in_=th[:]), reads=[th], writes=[bB])
    p.op("v", lambda e: e.tensor_copy(out=bC[:], in_=ki), reads=[bB], writes=[bC])
    tt(p, th[:], th[:], bC[:], ALU.subtract, reads=[th, bC], writes=[th])
    p.op("v", lambda e: e.tensor_scalar(out=bB[:], in0=th[:], scalar1=0.5, scalar2=None,
                                        op0=ALU.is_gt), reads=[th], writes=[bB])
    tt(p, bC[:], bC[:], bB[:], ALU.add, reads=[bC, bB], writes=[bC])
    p.op("v", lambda e: e.tensor_scalar(out=bB[:], in0=th[:], scalar1=-0.5, scalar2=None,
                                        op0=ALU.is_lt), reads=[th], writes=[bB])
    tt(p, bC[:], bC[:], bB[:], ALU.subtract, reads=[bC, bB], writes=[bC])
    p.op("v", lambda e: e.scalar_tensor_tensor(out=bA[:], in0=bC[:], scalar=-TWO_PI,
                                               in1=bA[:], op0=ALU.mult, op1=ALU.add),
         reads=[bC, bA], writes=[bA])
    p.op("v", lambda e: e.tensor_scalar(out=bA[:], in0=bA[:], scalar1=PI, scalar2=-PI,
                                        op0=ALU.min, op1=ALU.max), reads=[bA], writes=[bA])
    p.op("s", lambda e: e.activation(out=bB[:], in_=bA[:], func=AF.Sin),
         reads=[bA], writes=[bB])
    p.op("s", lambda e: e.activation(out=bC[:], in_=bA[:], func=AF.Abs),
         reads=[bA], writes=[bC])
    p.op("s", lambda e: e.activation(out=bC[:], in_=bC[:], func=AF.Sin, scale=-1.0, bias=p.constcell(PI / 2)),
         reads=[bC, p._cst_s], writes=[bC])
    obre = th[:].bitcast(BF16)[:, 0:tn]
    obim = th[:].bitcast(BF16)[:, tn:2 * tn]
    tt(p, obre, m2[:], bC[:], ALU.mult, reads=[m2, bC], writes=[th])
    tt(p, obim, m2[:], bB[:], ALU.mult, reads=[m2, bB, th], writes=[th])
    p.dma("qs", lambda e: e.dma_start(out=kvre_d[:], in_=obre), reads=[th])
    p.dma("qs", lambda e: e.dma_start(out=kvim_d[:], in_=obim), reads=[th])

    p.maybe_finish()
    return p


def emit_matmul_fm(p, out_chunks, lhsT_tiles, rhs_tiles, nK, nM, nN, nw,
                   psum_pool, evac):
    for m in range(nM):
        for n in range(nN):
            psu = psum_pool[(m * nN + n) % len(psum_pool)]
            for k in range(nK):
                lt, lreads = lhsT_tiles(k, m)
                rt, rreads = rhs_tiles(k, n)
                p.op("t", lambda e, psu=psu, lt=lt, rt=rt, k=k: e.matmul(
                    psu[:], lt, rt, start=(k == 0), stop=(k == nK - 1)),
                    reads=list(lreads) + list(rreads), writes=[psu])
            evac(m, n, psu)


def emit_ln_fm(p, h_chunks, hb_chunks, ones_col, ones_b, psum_small, psum_bcast, scratch_rows,
               tw=TOWN, eps=1e-5, extra_first=()):
    nch = len(h_chunks)
    nd = nch * 128
    psum_s1, psum_s2 = psum_small
    psum_b1, psum_b2 = psum_bcast
    r_mu, r_ms, r_a, r_b = scratch_rows
    nw = min(tw, 512)
    ntt = (tw + nw - 1) // nw
    for h, hb in zip(h_chunks, hb_chunks):
        p.op("s", lambda e, h=h, hb=hb: e.activation(out=hb[:], in_=h[:], func=AF.Square),
             reads=[h], writes=[hb], extra=extra_first)
    for j in range(ntt):
        sl = slice(j * nw, (j + 1) * nw)
        for i, h in enumerate(h_chunks):
            p.op("t", lambda e, h=h, i=i, sl=sl: e.matmul(
                psum_s1[:], ones_col[:, 0:1], h[:, sl], start=(i == 0), stop=(i == nch - 1)),
                reads=[h, ones_col], writes=[psum_s1])
        for i, hb in enumerate(hb_chunks):
            p.op("t", lambda e, hb=hb, i=i, sl=sl: e.matmul(
                psum_s2[:], ones_b[:, 0:1], hb[:, sl], start=(i == 0), stop=(i == nch - 1)),
                reads=[hb, ones_b], writes=[psum_s2])
        p.op("v", lambda e, sl=sl: e.tensor_scalar(out=r_mu[0:1, sl], in0=psum_s1[:],
                                                   scalar1=1.0 / nd, scalar2=None, op0=ALU.mult),
             reads=[psum_s1], writes=[r_mu])
        p.op("v", lambda e, sl=sl: e.tensor_scalar(out=r_ms[0:1, sl], in0=psum_s2[:],
                                                   scalar1=1.0 / nd, scalar2=None, op0=ALU.mult),
             reads=[psum_s2], writes=[r_ms])
    tt(p, r_a[:], r_mu[:], r_mu[:], ALU.mult, reads=[r_mu], writes=[r_a])
    tt(p, r_ms[:], r_ms[:], r_a[:], ALU.subtract, reads=[r_ms, r_a], writes=[r_ms])
    p.op("v", lambda e: e.tensor_scalar(out=r_ms[:], in0=r_ms[:], scalar1=eps, scalar2=None,
                                        op0=ALU.add), reads=[r_ms], writes=[r_ms])
    p.op("s", lambda e: e.activation(out=r_ms[:], in_=r_ms[:], func=AF.Sqrt),
         reads=[r_ms], writes=[r_ms])
    p.op("v", lambda e: e.reciprocal(out=r_a[:], in_=r_ms[:]), reads=[r_ms], writes=[r_a])
    tt(p, r_b[:], r_mu[:], r_a[:], ALU.mult, reads=[r_mu, r_a], writes=[r_b])
    p.op("v", lambda e: e.tensor_scalar(out=r_b[:], in0=r_b[:], scalar1=-1.0, scalar2=None,
                                        op0=ALU.mult), reads=[r_b], writes=[r_b])
    ab = p.sb(f"ln_ab_{p.nops}", [128, tw])
    bb = p.sb(f"ln_bb_{p.nops}", [128, tw])
    onesrow = ones_col[0:1, 0:1].broadcast_to([1, 128])
    for j in range(ntt):
        sl = slice(j * nw, (j + 1) * nw)
        p.op("t", lambda e, sl=sl: e.matmul(psum_b1[:], onesrow, r_a[0:1, sl],
                                            start=True, stop=True),
             reads=[r_a, ones_col], writes=[psum_b1])
        p.op("t", lambda e, sl=sl: e.matmul(psum_b2[:], onesrow, r_b[0:1, sl],
                                            start=True, stop=True),
             reads=[r_b, ones_col], writes=[psum_b2])
        p.op("v", lambda e, sl=sl: e.tensor_copy(out=ab[:, sl], in_=psum_b1[:]),
             reads=[psum_b1], writes=[ab])
        p.op("v", lambda e, sl=sl: e.tensor_copy(out=bb[:, sl], in_=psum_b2[:]),
             reads=[psum_b2], writes=[bb])
    for h, hb in zip(h_chunks, hb_chunks):
        tt(p, h[:], h[:], ab[:], ALU.mult, reads=[h, ab], writes=[h])
        tt(p, h[:], h[:], bb[:], ALU.add, reads=[h, bb], writes=[h])
        p.op("v", lambda e, h=h, hb=hb: e.tensor_copy(out=hb[:], in_=h[:]),
             reads=[h], writes=[hb])


def build_phase_c(tw=TOWN, p=None):
    if p is None:
        p = Prog("phaseC")
    kv_d = p.din("kv", [D, tw], BF16)
    r4_d = p.din("r4", [D, tw])
    xs_d = p.din("xs", [D, tw])
    wv_d = p.din("wv", [D, D])
    ones_d = p.din("ones", [128, 1])
    h1_d = p.dout("h1", [D, tw])
    h1b_d = p.dout("h1b", [D, tw], BF16)
    p.start()

    NCH = D // 128
    kv = [p.sb(f"c_kv{i}", [128, tw], BF16) for i in range(NCH)]
    r4 = [p.sb(f"c_r4_{i}", [128, tw], BF16) for i in range(NCH)]
    h = [p.sb(f"c_h{i}", [128, tw]) for i in range(NCH)]
    hb = [p.sb(f"c_hb{i}", [128, tw], BF16) for i in range(NCH)]
    ones = p.sb("c_ones", [128, 1])
    p.dma("qs", lambda e: e.dma_start(out=ones[:], in_=ones_d[:]), writes=[ones])
    for i in range(NCH):
        p.dma("qs", lambda e, i=i: e.dma_start(out=kv[i][:], in_=kv_d[128 * i:128 * (i + 1), :]),
              writes=[kv[i]])
        p.dma("qg", lambda e, i=i: e.dma_start(out=r4[i][:], in_=r4_d[128 * i:128 * (i + 1), :]),
              writes=[r4[i]])
        p.dma("qs", lambda e, i=i: e.dma_start(out=h[i][:], in_=xs_d[128 * i:128 * (i + 1), :]),
              writes=[h[i]])

    NT = max(tw // 512, 1)
    nw = 512 if tw >= 512 else tw
    pst = [p.ps(f"psC{i}", [128, nw]) for i in range(4)]
    wtl = [p.sb(f"wC{i}", [128, 512], BF16) for i in range(D // 128)]
    CG = 4
    for cg in range(NCH // CG):
        for k in range(NCH):
            wt = wtl[k]
            p.dma("qg", lambda e, k=k, cg=cg, wt=wt: e.dma_start(
                out=wt[:], in_=wv_d[128 * k:128 * (k + 1), 512 * cg:512 * (cg + 1)]),
                writes=[wt])
        for ci in range(CG):
            c = cg * CG + ci
            for nt in range(NT):
                psu = pst[(ci * NT + nt) % 4]
                for k in range(NCH):
                    wt = wtl[k]
                    rhs = kv[k][:, nt * nw:(nt + 1) * nw]
                    p.op("t", lambda e, psu=psu, wt=wt, rhs=rhs, k=k, ci=ci: e.matmul(
                        psu[:], wt[:, 128 * ci:128 * (ci + 1)], rhs,
                        start=(k == 0), stop=(k == NCH - 1)),
                        reads=[wt, kv[k]], writes=[psu])
                seg_h = h[c][:, nt * nw:(nt + 1) * nw]
                seg_r = r4[c][:, nt * nw:(nt + 1) * nw]
                p.op("v", lambda e, psu=psu, seg_r=seg_r: e.tensor_tensor(
                    out=psu[:], in0=psu[:], in1=seg_r, op=ALU.mult),
                    reads=[psu, r4[c]], writes=[psu])
                p.op("v", lambda e, psu=psu, seg_h=seg_h: e.tensor_tensor(
                    out=seg_h, in0=seg_h, in1=psu[:], op=ALU.add),
                    reads=[psu, h[c]], writes=[h[c]])

    ones_b = p.sb("c_ones_b", [128, 1], BF16)
    p.op("v", lambda e: e.tensor_copy(out=ones_b[:], in_=ones[:]), reads=[ones], writes=[ones_b])
    ps_s = [p.ps("psLs1", [1, min(tw, 512)]), p.ps("psLs2", [1, min(tw, 512)])]
    ps_b = [p.ps("psLb1", [128, min(tw, 512)]), p.ps("psLb2", [128, min(tw, 512)])]
    rows = [p.sb(f"lnrow{i}", [1, tw]) for i in range(4)]
    emit_ln_fm(p, h, hb, ones, ones_b, ps_s, ps_b, rows, tw=tw)

    for i in range(NCH):
        p.dma("qs", lambda e, i=i: e.dma_start(out=h1_d[128 * i:128 * (i + 1), :], in_=h[i][:]),
              reads=[h[i]])
        p.dma("qs", lambda e, i=i: e.dma_start(out=h1b_d[128 * i:128 * (i + 1), :], in_=hb[i][:]),
              reads=[hb[i]])

    p.maybe_finish()
    return p


NEXP = 32
TOPK = 8
CAP = 2 * (T * TOPK // NEXP)


def build_phase_d(tw=TOWN, p=None):
    if p is None:
        p = Prog("phaseD")
    h1_d = p.din("h1", [D, tw])
    gw_d = p.din("gw", [D, NEXP])
    tri_d = p.din("tri", [128, 128])
    ones_d = p.din("ones", [128, 1])
    wmat_d = p.dout("wmat", [tw, NEXP])
    posl_d = p.dout("posl", [tw, NEXP])
    cnt_d = p.dout("cnt", [1, NEXP])
    p.start()

    NCH = D // 128
    NTCH = tw // 128
    h = [p.sb(f"d_h{i}", [128, tw]) for i in range(NCH)]
    gw = [p.sb(f"d_gw{i}", [128, NEXP]) for i in range(NCH)]
    tri = p.sb("d_tri", [128, 128])
    for i in range(NCH):
        p.dma("qs", lambda e, i=i: e.dma_start(out=h[i][:], in_=h1_d[128 * i:128 * (i + 1), :]), writes=[h[i]])
        p.dma("qs", lambda e, i=i: e.dma_start(out=gw[i][:], in_=gw_d[128 * i:128 * (i + 1), :]), writes=[gw[i]])
    p.dma("qs", lambda e: e.dma_start(out=tri[:], in_=tri_d[:]), writes=[tri])
    onesD = p.sb("d_ones", [128, 1])
    p.dma("qs", lambda e: e.dma_start(out=onesD[:], in_=ones_d[:]), writes=[onesD])

    psl = p.ps("d_psl", [128, NEXP])
    psc = p.ps("d_psc", [128, NEXP])
    pst = p.ps("d_pst", [1, NEXP])
    z = p.sb("d_z", [128, NEXP])
    mx = p.sb("d_mx", [128, TOPK])
    zap = p.sb("d_zap", [128, NEXP])
    wch = p.sb("d_wch", [128, NEXP])
    sel = p.sb("d_sel", [128, NEXP])
    ssum = p.sb("d_ssum", [128, 1])
    run = p.sb("d_run", [1, NEXP])
    posb = p.sb("d_posb", [128, NEXP])
    p.op("v", lambda e: e.memset(run[:], 0.0), writes=[run])

    for j in range(NTCH):
        for i in range(NCH):
            p.op("t", lambda e, i=i, j=j: e.matmul(
                psl[:], h[i][:, 128 * j:128 * (j + 1)], gw[i][:],
                start=(i == 0), stop=(i == NCH - 1)),
                reads=[h[i], gw[i]], writes=[psl])
        p.op("v", lambda e: e.reduce_max(out=ssum[:], in_=psl[:], axis=AX.X),
             reads=[psl], writes=[ssum])
        p.op("v", lambda e: e.tensor_scalar(out=z[:], in0=psl[:], scalar1=ssum[:, 0:1],
                                            scalar2=None, op0=ALU.subtract),
             reads=[psl, ssum], writes=[z])
        p.op("s", lambda e: e.activation(out=z[:], in_=z[:], func=AF.Exp),
             reads=[z], writes=[z])
        p.op("v", lambda e: e.max(out=mx[:], in_=z[:]), reads=[z], writes=[mx])
        p.op("v", lambda e: e.match_replace(out=zap[:], in_to_replace=mx[:], in_values=z[:],
                                            imm_value=0.0), reads=[z, mx], writes=[zap])
        tt(p, wch[:], z[:], zap[:], ALU.subtract, reads=[z, zap], writes=[wch])
        p.op("v", lambda e: e.reduce_sum(out=ssum[:], in_=wch[:], axis=AX.X),
             reads=[wch], writes=[ssum])
        p.op("v", lambda e: e.reciprocal(out=ssum[:], in_=ssum[:]), reads=[ssum], writes=[ssum])
        p.op("v", lambda e: e.tensor_scalar(out=wch[:], in0=wch[:], scalar1=ssum[:, 0:1],
                                            scalar2=None, op0=ALU.mult),
             reads=[wch, ssum], writes=[wch])
        p.dma("qs", lambda e, j=j: e.dma_start(out=wmat_d[128 * j:128 * (j + 1), :], in_=wch[:]),
              reads=[wch])
        p.op("v", lambda e: e.tensor_scalar(out=sel[:], in0=wch[:], scalar1=0.0, scalar2=None,
                                            op0=ALU.is_gt), reads=[wch], writes=[sel])
        p.op("t", lambda e: e.matmul(psc[:], tri[:], sel[:], start=True, stop=False),
             reads=[tri, sel], writes=[psc])
        p.op("t", lambda e: e.matmul(psc[:], onesD[0:1, 0:1].broadcast_to([1, 128]), run[:],
                                     start=False, stop=True),
             reads=[onesD, run], writes=[psc])
        p.op("v", lambda e: e.tensor_copy(out=posb[:], in_=psc[:]), reads=[psc], writes=[posb])
        p.dma("qs", lambda e, j=j: e.dma_start(out=posl_d[128 * j:128 * (j + 1), :], in_=posb[:]),
              reads=[posb])
        p.op("t", lambda e: e.matmul(pst[:], onesD[:, 0:1], sel[:], start=True, stop=True),
             reads=[onesD, sel], writes=[pst])
        p.op("v", lambda e: e.tensor_tensor(out=run[:], in0=run[:], in1=pst[:], op=ALU.add),
             reads=[run, pst], writes=[run])
    p.dma("qs", lambda e: e.dma_start(out=cnt_d[:], in_=run[:]), reads=[run])
    p.maybe_finish()
    return p


def build_phase_e(tn=T, tile=512, p=None):
    if p is None:
        p = Prog("phaseE")
    xall_d = p.din("xall", [D, tn], BF16)
    wg_d = p.din("wg", [4 * D, 768])
    wu_d = p.din("wu", [4 * D, 768])
    wd_d = p.din("wd", [4 * 768, D])
    wall_d = p.din("wall", [tn, NEXP])
    posall_d = p.din("posall", [tn, NEXP])
    cnt8_d = p.din("cnt8", [8, NEXP])
    cmask_d = p.din("cmask", [8, 8])
    ones_d = p.din("ones", [128, 1])
    eye_d = p.din("eye", [128, 128])
    esel_d = p.din("esel", [4, 1], I32)
    part_d = p.dout("part", [D, tn], BF16)
    we_dram = p.dtmp("we_dram", [NEXP, tn])
    my_we = p.dtmp("my_we", [4, tn])

    p.start()
    NCH = D // 128
    NI = 768 // 128
    NTT = tn // tile

    ones = p.sb("e_ones", [128, 1])
    onesb = p.sb("e_onesb", [128, 1], BF16)
    cnt8 = p.sb("e_cnt8", [8, NEXP])
    cmask = p.sb("e_cmask", [8, 8])
    p.dma("qs", lambda e: e.dma_start(out=ones[:], in_=ones_d[:]), writes=[ones])
    p.dma("qs", lambda e: e.dma_start(out=cnt8[:], in_=cnt8_d[:]), writes=[cnt8])
    p.dma("qs", lambda e: e.dma_start(out=cmask[:], in_=cmask_d[:]), writes=[cmask])
    eye = p.sb("e_eye", [128, 128])
    p.dma("qs", lambda e: e.dma_start(out=eye[:], in_=eye_d[:]), writes=[eye])
    p.op("v", lambda e: e.tensor_copy(out=onesb[:], in_=ones[:]), reads=[ones], writes=[onesb])

    sub = ExitStack()
    pbc = sub.enter_context(p.nc.psum_tensor("e_pbc", [128, NEXP], F32))
    ptr = sub.enter_context(p.nc.psum_tensor("e_ptr", [NEXP, 128], F32))
    wch = sub.enter_context(p.nc.sbuf_tensor("e_wch", [128, NEXP], F32))
    pch = sub.enter_context(p.nc.sbuf_tensor("e_pch", [128, NEXP], F32))
    wtr = sub.enter_context(p.nc.sbuf_tensor("e_wtr", [NEXP, 128], F32))
    for j in range(tn // 128):
        core = j // max(tn // 128 // 8, 1)
        p.dma("qs", lambda e, j=j: e.dma_start(out=wch[:], in_=wall_d[128 * j:128 * (j + 1), :]),
              writes=[wch])
        p.dma("qs", lambda e, j=j: e.dma_start(out=pch[:], in_=posall_d[128 * j:128 * (j + 1), :]),
              writes=[pch])
        p.op("t", lambda e, core=core: e.matmul(
            pbc[:], cmask[:, core:core + 1].broadcast_to([8, 128]), cnt8[:],
            start=True, stop=True),
             reads=[cmask, cnt8], writes=[pbc])
        p.op("v", lambda e: e.tensor_tensor(out=pch[:], in0=pch[:], in1=pbc[:], op=ALU.add),
             reads=[pch, pbc], writes=[pch])
        p.op("v", lambda e: e.tensor_scalar(out=pch[:], in0=pch[:], scalar1=float(CAP),
                                            scalar2=None, op0=ALU.is_lt),
             reads=[pch], writes=[pch])
        tt(p, wch[:], wch[:], pch[:], ALU.mult, reads=[wch, pch], writes=[wch])
        p.op("t", lambda e: e.transpose(ptr[:], wch[:], eye[:]),
             reads=[wch, eye], writes=[ptr])
        p.op("v", lambda e: e.tensor_copy(out=wtr[:], in_=ptr[:]), reads=[ptr], writes=[wtr])
        p.dma("qs", lambda e, j=j: e.dma_start(out=we_dram[:, 128 * j:128 * (j + 1)], in_=wtr[:]),
              reads=[wtr], writes=["we_dram"])
    esel_s = sub.enter_context(p.nc.sbuf_tensor("e_esel", [4, 1], I32))
    wesel = sub.enter_context(p.nc.sbuf_tensor("e_wesel", [4, tn], F32))
    p.dma("qs", lambda e: e.dma_start(out=esel_s[:], in_=esel_d[:]), writes=[esel_s])
    p.op("v", lambda e: e.memset(wesel[:], 0.0), writes=[wesel])
    p.dma("qg", lambda e: e.indirect_dma_start(
        out=wesel[:], out_offset=None, in_=we_dram[:],
        in_offset=IndirectOffsetOnAxis(ap=esel_s[:], axis=0),
        bounds_check=NEXP - 1, oob_is_err=False),
        reads=[esel_s, "we_dram"], writes=[wesel])
    p.dma("qs", lambda e: e.dma_start(out=my_we[:], in_=wesel[:]),
          reads=[wesel], writes=["my_we"])

    sub.close()
    p.fence()

    wg = [[p.sb(f"e_wg{b}_{k}", [128, 768], BF16) for k in range(NCH)] for b in range(2)]
    wu = [[p.sb(f"e_wu{b}_{k}", [128, 768], BF16) for k in range(NCH)] for b in range(2)]
    wd = [[p.sb(f"e_wd{b}_{k}", [128, D], BF16) for k in range(NI)] for b in range(2)]
    xt = [p.sb(f"e_xt{k}", [128, tile], BF16) for k in range(NCH)]
    yt = [p.sb(f"e_yt{k}", [128, tile], BF16) for k in range(2 * NI)]
    ot = [p.sb(f"e_ot{k}", [128, tile], BF16) for k in range(3)]
    wrow = p.sb("e_wrow", [1, tile], BF16)
    psg = [p.ps(f"e_psg{k}", [128, tile]) for k in range(2)]
    psu_ = [p.ps(f"e_psu{k}", [128, tile]) for k in range(2)]
    psD = [p.ps(f"e_psD{k}", [128, tile]) for k in range(3)]
    pwb = p.ps("e_pwb", [128, tile])
    wbc = p.sb("e_wbc", [128, tile], BF16)

    st_tok = {}
    iters = [(eloc, ttn) for eloc in range(4) for ttn in range(NTT)]

    def emit_weights(eloc):
        wgb, wub = wg[eloc % 2], wu[eloc % 2]
        for k in range(NCH):
            p.dma("qg", lambda e, k=k, eloc=eloc, wgb=wgb: e.dma_start(
                out=wgb[k][:], in_=wg_d[eloc * D + 128 * k: eloc * D + 128 * (k + 1), :]),
                writes=[wgb[k]])
            p.dma("qg", lambda e, k=k, eloc=eloc, wub=wub: e.dma_start(
                out=wub[k][:], in_=wu_d[eloc * D + 128 * k: eloc * D + 128 * (k + 1), :]),
                writes=[wub[k]])
        wdb = wd[eloc % 2]
        for k in range(NI):
            p.dma("qg", lambda e, k=k, eloc=eloc, wdb=wdb: e.dma_start(
                out=wdb[k][:], in_=wd_d[eloc * 768 + 128 * k: eloc * 768 + 128 * (k + 1), :]),
                writes=[wdb[k]])

    def emit_loads(it):
        eloc, ttn = it
        tsl = slice(ttn * tile, (ttn + 1) * tile)

        def xall_sl(k, tsl=tsl):
            if len(xall_d.shape) == 3:
                c = tsl.start // TOWN
                w0 = tsl.start % TOWN
                return xall_d[128 * k:128 * (k + 1), c, w0:w0 + (tsl.stop - tsl.start)]
            return xall_d[128 * k:128 * (k + 1), tsl]
        for k in range(NCH):
            p.dma("qs", lambda e, k=k, tsl=tsl: e.dma_start(
                out=xt[k][:], in_=xall_sl(k)), writes=[xt[k]])
        p.dma("qg", lambda e, eloc=eloc, tsl=tsl: e.dma_start(
            out=wrow[:], in_=my_we[eloc:eloc + 1, tsl]),
            reads=["my_we"], writes=[wrow])
        p.op("t", lambda e: e.matmul(pwb[:], onesb[0:1, 0:1].broadcast_to([1, 128]),
                                     wrow[:], start=True, stop=True),
             reads=[wrow, onesb], writes=[pwb])
        p.op("v", lambda e: e.tensor_copy(out=wbc[:], in_=pwb[:]), reads=[pwb], writes=[wbc])

    def emit_gu_pass(it, half):
        eloc, ttn = it
        wgb, wub = wg[eloc % 2], wu[eloc % 2]
        ybase = (ttn % 2) * NI
        for ii in range(2):
            i = half * 2 + ii
            for k in range(NCH):
                p.op("t", lambda e, i=i, ii=ii, k=k, wgb=wgb: e.matmul(
                    psg[ii][:], wgb[k][:, 128 * i:128 * (i + 1)], xt[k][:],
                    start=(k == 0), stop=(k == NCH - 1)),
                    reads=[wgb[k], xt[k]], writes=[psg[ii]])
            for k in range(NCH):
                p.op("t", lambda e, i=i, ii=ii, k=k, wub=wub: e.matmul(
                    psu_[ii][:], wub[k][:, 128 * i:128 * (i + 1)], xt[k][:],
                    start=(k == 0), stop=(k == NCH - 1)),
                    reads=[wub[k], xt[k]], writes=[psu_[ii]])

    def emit_gu_evac(it, half):
        eloc, ttn = it
        ybase = (ttn % 2) * NI
        for ii in range(2):
            i = half * 2 + ii
            y = yt[ybase + i]
            p.op("s", lambda e, ii=ii, y=y: e.activation(
                out=y[:], in_=psg[ii][:], func=AF.Silu),
                reads=[psg[ii]], writes=[y])
            p.op("v", lambda e, ii=ii, y=y: e.tensor_tensor(
                out=y[:], in0=y[:], in1=psu_[ii][:], op=ALU.mult),
                reads=[y, psu_[ii]], writes=[y])
            p.op("v", lambda e, y=y: e.tensor_tensor(
                out=y[:], in0=y[:], in1=wbc[:], op=ALU.mult),
                reads=[y, wbc], writes=[y])

    def emit_down_chunks(it, c0, c1):
        eloc, ttn = it
        tsl = slice(ttn * tile, (ttn + 1) * tile)
        ybase = (ttn % 2) * NI

        def part_sl(c, tsl=tsl):
            if len(part_d.shape) == 3:
                cb = tsl.start // TOWN
                w0 = tsl.start % TOWN
                return part_d[128 * c:128 * (c + 1), cb, w0:w0 + (tsl.stop - tsl.start)]
            return part_d[128 * c:128 * (c + 1), tsl]
        wdb = wd[eloc % 2]
        for c in range(c0, c1):
            psb = psD[c % 3]
            om = ot[c % 3]
            for k in range(NI):
                p.op("t", lambda e, c=c, k=k, psb=psb, ybase=ybase, wdb=wdb: e.matmul(
                    psb[:], wdb[k][:, 128 * c:128 * (c + 1)], yt[ybase + k][:],
                    start=(k == 0), stop=(k == NI - 1)),
                    reads=[wdb[k], yt[ybase + k]], writes=[psb])
            p.op("v", lambda e, psb=psb, om=om: e.tensor_copy(out=om[:], in_=psb[:]),
                 reads=[psb], writes=[om])
            aop = ALU.add if eloc > 0 else ALU.bypass
            key = (c, ttn)
            prev = st_tok.get(key)
            tok = p.dma("qg", lambda e, c=c, om=om, aop=aop: e.dma_start(
                out=part_sl(c), in_=om[:], accum_op=aop),
                reads=[om], extra=[prev] if prev else [])
            st_tok[key] = tok

    emit_weights(0)
    for n, it in enumerate(iters):
        eloc, ttn = it
        if ttn == 1 and eloc < 3:
            emit_weights(eloc + 1)
        emit_loads(it)
        prev = iters[n - 1] if n > 0 else None
        for half in range(3):
            emit_gu_pass(it, half)
            if prev is not None:
                th = 16 // 3
                c0 = half * th
                c1 = (half + 1) * th if half < 2 else 16
                emit_down_chunks(prev, c0, c1)
            emit_gu_evac(it, half)
    emit_down_chunks(iters[-1], 0, 16)
    p.maybe_finish()
    return p


def build_phase_f(tw=TOWN, p=None, nparts=8):
    if p is None:
        p = Prog("phaseF")
    pj_d = p.din("pj", [nparts * D, tw], BF16)
    h1_d = p.din("h1", [D, tw])
    ones_d = p.din("ones", [128, 1])
    h2_d = p.dout("h2", [D, tw])
    h2b_d = p.dout("h2b", [D, tw], BF16)
    p.start()
    NCH = D // 128
    ones = p.sb("f_ones", [128, 1])
    ones_b = p.sb("f_ones_b", [128, 1], BF16)
    p.dma("qs", lambda e: e.dma_start(out=ones[:], in_=ones_d[:]), writes=[ones])
    p.op("v", lambda e: e.tensor_copy(out=ones_b[:], in_=ones[:]), reads=[ones], writes=[ones_b])
    h = [p.sb(f"f_h{i}", [128, tw]) for i in range(NCH)]
    hb = [p.sb(f"f_hb{i}", [128, tw], BF16) for i in range(NCH)]
    pj = [p.sb(f"f_pj{i}", [128, tw], BF16) for i in range(max(nparts, 2))]
    acc = p.sb("f_acc", [128, tw])
    for i in range(NCH):
        p.dma("qs", lambda e, i=i: e.dma_start(out=h[i][:], in_=h1_d[128 * i:128 * (i + 1), :]),
              writes=[h[i]])
        for j in range(nparts):
            p.dma("qs", lambda e, i=i, j=j: e.dma_start(
                out=pj[j][:], in_=pj_d[j * D + 128 * i: j * D + 128 * (i + 1), :]),
                writes=[pj[j]])
        if nparts == 1:
            tt(p, h[i][:], h[i][:], pj[0][:], ALU.add, reads=[h[i], pj[0]], writes=[h[i]])
        else:
            tt(p, acc[:], pj[0][:], pj[1][:], ALU.add, reads=[pj[0], pj[1]], writes=[acc])
            for j in range(2, nparts):
                tt(p, acc[:], acc[:], pj[j][:], ALU.add, reads=[acc, pj[j]], writes=[acc])
            tt(p, h[i][:], h[i][:], acc[:], ALU.add, reads=[h[i], acc], writes=[h[i]])
    ps_s = [p.ps("f_psLs1", [1, min(tw, 512)]), p.ps("f_psLs2", [1, min(tw, 512)])]
    ps_b = [p.ps("f_psLb1", [128, min(tw, 512)]), p.ps("f_psLb2", [128, min(tw, 512)])]
    rows = [p.sb(f"f_lnrow{i}", [1, tw]) for i in range(4)]
    emit_ln_fm(p, h, hb, ones, ones_b, ps_s, ps_b, rows, tw=tw)
    for i in range(NCH):
        p.dma("qs", lambda e, i=i: e.dma_start(out=h2_d[128 * i:128 * (i + 1), :], in_=h[i][:]),
              reads=[h[i]])
        p.dma("qs", lambda e, i=i: e.dma_start(out=h2b_d[128 * i:128 * (i + 1), :], in_=hb[i][:]),
              reads=[hb[i]])
    p.maybe_finish()
    return p


NH = 64
HP = 64
NN = 128
CS = 128
INTER = 4096
CONV_DIM = INTER + 2 * NN
NXC = CONV_DIM // 128


def build_phase_g(tw=TOWN, p=None):
    if p is None:
        p = Prog("phaseG")
    hb_d = p.din("hb", [D, tw + 3], BF16)
    wpg_d = p.din("wpg", [D, INTER])
    wpx_d = p.din("wpx", [D, CONV_DIM])
    wpd_d = p.din("wpd", [D, NH])
    convw_d = p.din("convw", [CONV_DIM, 4])
    convbt_d = p.din("convbt", [1, CONV_DIM])
    onesr_d = p.din("onesr", [1, 512])
    dtb_d = p.din("dtb", [NH, 1])
    alog_d = p.din("alog", [NH, 1])
    deye_d = p.din("deye", [128, NH])
    eye_d = p.din("eye", [128, 128])
    ones_d = p.din("ones", [128, 1])
    gsil_d = p.dout("gsil", [INTER, tw], BF16)
    yt_d = p.dout("yt", [INTER, tw], BF16)
    cw_d = p.dout("cw", [(tw // CS) * NN, NH * CS], BF16)
    sacc_d = p.dout("sacc", [NN, NH * HP], BF16)
    espan_d = p.dout("espan", [tw // CS, NH])
    espantot_d = p.dout("espantot", [1, NH])
    ac_dram = p.dtmp("ac_dram", [NH, CS])

    p.start()
    NCH = D // 128
    NCHUNK = tw // CS
    NT = tw // 512
    ones = p.sb("g_ones", [128, 1])
    onesb = p.sb("g_onesb", [128, 1], BF16)
    onesrb = p.sb("g_onesrb", [1, 512], BF16)
    eyeb = p.sb("g_eyeb", [128, 128], BF16)
    eyef = p.sb("g_eyef", [128, 128])
    p.dma("qs", lambda e: e.dma_start(out=ones[:], in_=ones_d[:]), writes=[ones])
    p.dma("qs", lambda e: e.dma_start(out=eyef[:], in_=eye_d[:]), writes=[eyef])
    p.dma("qg", lambda e: e.dma_start(out=eyeb[:], in_=eye_d[:]), writes=[eyeb])
    p.dma("qg", lambda e: e.dma_start(out=onesrb[:], in_=onesr_d[:]), writes=[onesrb])
    p.op("v", lambda e: e.tensor_copy(out=onesb[:], in_=ones[:]), reads=[ones], writes=[onesb])
    K1 = ones[0:1, 0:1].broadcast_to([1, 128])
    K1b = onesb[0:1, 0:1].broadcast_to([1, 128])

    dtm = p.sb("g_dt", [NH, tw])
    dtbc = p.sb("g_dtbc", [NH, 1])
    sc_a = p.sb("g_sca", [NH, tw])
    acol = p.sb("g_acol", [NH, 1])
    acum = p.sb("g_acum", [NH, tw])
    zz1 = p.sb("g_zz1", [NH, 1])
    xtok = [p.sb(f"g_xtok{i}", [128, 4224], BF16) for i in range(NCHUNK)]
    bt = p.sb("g_bt", [128, tw], BF16)
    ct = p.sb("g_ct", [128, tw], BF16)
    subx = ExitStack()
    xbc = [subx.enter_context(p.nc.sbuf_tensor(f"g_xbc{i}", [128, tw + 3], BF16))
           for i in range(NXC)]
    sub1 = ExitStack()
    hb = [sub1.enter_context(p.nc.sbuf_tensor(f"g_hb{i}", [128, tw + 3], BF16))
          for i in range(NCH)]
    wt = [sub1.enter_context(p.nc.sbuf_tensor(f"g_wt{i}", [128, 512], BF16))
          for i in range(NCH)]
    gso = [sub1.enter_context(p.nc.sbuf_tensor(f"g_gso{i}", [128, 512], BF16))
           for i in range(4)]
    psA = [sub1.enter_context(p.nc.psum_tensor(f"g_psA{i}", [128, 512], F32))
           for i in range(4)]
    for i in range(NCH):
        p.dma("qs", lambda e, i=i: e.dma_start(out=hb[i][:], in_=hb_d[128 * i:128 * (i + 1), :]),
              writes=[hb[i]])

    for cg in range(INTER // 512):
        for k in range(NCH):
            p.dma("qg", lambda e, k=k, cg=cg: e.dma_start(
                out=wt[k][:], in_=wpg_d[128 * k:128 * (k + 1), 512 * cg:512 * (cg + 1)]),
                writes=[wt[k]])
        for ci in range(4):
            c = cg * 4 + ci
            for nt in range(NT):
                psu = psA[(ci * NT + nt) % 4]
                for k in range(NCH):
                    p.op("t", lambda e, psu=psu, k=k, ci=ci, nt=nt: e.matmul(
                        psu[:], wt[k][:, 128 * ci:128 * (ci + 1)],
                        hb[k][:, 3 + nt * 512: 3 + (nt + 1) * 512],
                        start=(k == 0), stop=(k == NCH - 1)),
                        reads=[wt[k], hb[k]], writes=[psu])
                g_ = gso[(ci * NT + nt) % 4]
                p.op("s", lambda e, psu=psu, g_=g_: e.activation(out=g_[:], in_=psu[:], func=AF.Silu),
                     reads=[psu], writes=[g_])
                p.dma("qs", lambda e, c=c, nt=nt, g_=g_: e.dma_start(
                    out=gsil_d[128 * c:128 * (c + 1), 512 * nt:512 * (nt + 1)], in_=g_[:]),
                    reads=[g_])

    xtiles = []
    o = 0
    while o < tw + 3:
        nn_ = min(512, tw + 3 - o)
        xtiles.append((o, nn_))
        o += nn_
    for cg in range((NXC + 3) // 4):
        c0 = cg * 4
        ncc = min(4, NXC - c0)
        cwc = 128 * ncc
        for k in range(NCH):
            p.dma("qg", lambda e, k=k, cg=cg, cwc=cwc: e.dma_start(
                out=wt[k][:, 0:cwc],
                in_=wpx_d[128 * k:128 * (k + 1), 512 * cg:512 * cg + cwc]),
                writes=[wt[k]])
        for ci in range(ncc):
            c = c0 + ci
            for (o0, nn_) in xtiles:
                psu = psA[ci % 4]
                for k in range(NCH):
                    p.op("t", lambda e, psu=psu, k=k, ci=ci, o0=o0, nn_=nn_: e.matmul(
                        psu[:, 0:nn_], wt[k][:, 128 * ci:128 * (ci + 1)],
                        hb[k][:, o0:o0 + nn_],
                        start=(k == 0), stop=(k == NCH - 1)),
                        reads=[wt[k], hb[k]], writes=[psu])
                p.op("v", lambda e, psu=psu, c=c, o0=o0, nn_=nn_: e.tensor_copy(
                    out=xbc[c][:, o0:o0 + nn_], in_=psu[:, 0:nn_]),
                    reads=[psu], writes=[xbc[c]])

    for k in range(NCH):
        p.dma("qg", lambda e, k=k: e.dma_start(
            out=wt[k][:, 0:NH], in_=wpd_d[128 * k:128 * (k + 1), :]), writes=[wt[k]])
    p.dma("qs", lambda e: e.dma_start(out=dtbc[:], in_=dtb_d[:]), writes=[dtbc])
    for nt in range(NT):
        psu = psA[nt % 4]
        for k in range(NCH):
            p.op("t", lambda e, psu=psu, k=k, nt=nt: e.matmul(
                psu[0:NH, :], wt[k][:, 0:NH], hb[k][:, 3 + nt * 512: 3 + (nt + 1) * 512],
                start=(k == 0), stop=(k == NCH - 1)),
                reads=[wt[k], hb[k]], writes=[psu])
        p.op("v", lambda e, psu=psu, nt=nt: e.tensor_scalar(
            out=dtm[0:NH, 512 * nt:512 * (nt + 1)], in0=psu[0:NH, :], scalar1=dtbc[:, 0:1],
            scalar2=None, op0=ALU.add), reads=[psu, dtbc], writes=[dtm])
    sub1.close()
    p.fence()

    p.op("s", lambda e: e.activation(out=sc_a[:], in_=dtm[:], func=AF.Abs),
         reads=[dtm], writes=[sc_a])
    p.op("s", lambda e: e.activation(out=sc_a[:], in_=sc_a[:], func=AF.Exp, scale=-1.0),
         reads=[sc_a], writes=[sc_a])
    p.op("s", lambda e: e.activation(out=sc_a[:], in_=sc_a[:], func=AF.Ln,
                                     bias=p.constcell(1.0, NH)),
         reads=[sc_a, p._cst_s], writes=[sc_a])
    p.op("v", lambda e: e.tensor_scalar(out=dtm[:], in0=dtm[:], scalar1=0.0, scalar2=None,
                                        op0=ALU.max), reads=[dtm], writes=[dtm])
    tt(p, dtm[:], dtm[:], sc_a[:], ALU.add, reads=[dtm, sc_a], writes=[dtm])
    p.dma("qs", lambda e: e.dma_start(out=acol[:], in_=alog_d[:]), writes=[acol])
    p.op("s", lambda e: e.activation(out=acol[:], in_=acol[:], func=AF.Exp),
         reads=[acol], writes=[acol])
    adt = sc_a
    p.op("v", lambda e: e.tensor_scalar(out=adt[:], in0=dtm[:], scalar1=acol[:, 0:1],
                                        scalar2=-1.0, op0=ALU.mult, op1=ALU.mult),
         reads=[dtm, acol], writes=[adt])
    p.op("v", lambda e: e.memset(zz1[:], 0.0), writes=[zz1])
    for c in range(NCHUNK):
        sl = slice(c * CS, (c + 1) * CS)
        p.op("v", lambda e, sl=sl: e.tensor_tensor_scan(
            out=acum[:, sl], data0=adt[:, sl], data1=zz1[:, 0:1].broadcast_to([NH, CS]),
            initial=0.0, op0=ALU.add, op1=ALU.add),
            reads=[adt, zz1], writes=[acum])

    sub2 = ExitStack()
    cwgt = sub2.enter_context(p.nc.sbuf_tensor("g_cwgt", [128, 4], F32))
    diag = [sub2.enter_context(p.nc.sbuf_tensor(f"g_diag{j}", [128, 128], BF16))
            for j in range(4)]
    convbt = sub2.enter_context(p.nc.sbuf_tensor("g_convbt", [1, CONV_DIM], BF16))
    psC = [sub2.enter_context(p.nc.psum_tensor(f"g_psC{i}", [128, 512], F32))
           for i in range(2)]
    p.dma("qg", lambda e: e.dma_start(out=convbt[:], in_=convbt_d[:]), writes=[convbt])
    for cc in range(NXC):
        p.dma("qs", lambda e, cc=cc: e.dma_start(
            out=cwgt[:], in_=convw_d[128 * cc:128 * (cc + 1), :]), writes=[cwgt])
        for j in range(4):
            p.op("v", lambda e, j=j: e.tensor_scalar(
                out=diag[j][:], in0=eyeb[:], scalar1=cwgt[:, j:j + 1], scalar2=None,
                op0=ALU.mult), reads=[eyeb, cwgt], writes=[diag[j]])
        if cc < 33:
            for tc in range(NCHUNK):
                psu = psC[tc % 2]
                for j in range(4):
                    p.op("t", lambda e, psu=psu, j=j, tc=tc, cc=cc: e.matmul(
                        psu[:, 0:128], xbc[cc][:, 128 * tc + j:128 * tc + j + 128],
                        diag[j][:], start=(j == 0), stop=False),
                        reads=[xbc[cc], diag[j]], writes=[psu])
                p.op("t", lambda e, psu=psu, cc=cc: e.matmul(
                    psu[:, 0:128], K1b, convbt[0:1, 128 * cc:128 * (cc + 1)],
                    start=False, stop=True),
                    reads=[onesb, convbt], writes=[psu])
                p.op("s", lambda e, psu=psu, tc=tc, cc=cc: e.activation(
                    out=xtok[tc][:, 128 * cc:128 * (cc + 1)],
                    in_=psu[:, 0:128], func=AF.Silu),
                    reads=[psu], writes=[xtok[tc]])
        if cc >= 32:
            dst = bt if cc == 32 else ct
            for nt in range(NT):
                psu = psC[nt % 2]
                for j in range(4):
                    p.op("t", lambda e, psu=psu, j=j, nt=nt, cc=cc: e.matmul(
                        psu[:], diag[j][:], xbc[cc][:, j + nt * 512: j + nt * 512 + 512],
                        start=(j == 0), stop=False),
                        reads=[xbc[cc], diag[j]], writes=[psu])
                p.op("t", lambda e, psu=psu, cc=cc: e.matmul(
                    psu[:], convbt[0:1, 128 * cc:128 * (cc + 1)],
                    onesrb[:], start=False, stop=True),
                    reads=[onesrb, convbt], writes=[psu])
                p.op("s", lambda e, psu=psu, dst=dst, nt=nt: e.activation(
                    out=dst[:, 512 * nt:512 * (nt + 1)], in_=psu[:], func=AF.Silu),
                    reads=[psu], writes=[dst])
    sub2.close()
    subx.close()
    p.fence()

    eyeD = [p.sb(f"g_eyeD{h}", [128, 128], BF16) for h in range(NH)]
    deye = p.sb("g_deye", [128, NH])
    p.dma("qs", lambda e: e.dma_start(out=deye[:], in_=deye_d[:]), writes=[deye])
    for h in range(NH):
        p.op("v", lambda e, h=h: e.tensor_scalar(
            out=eyeD[h][:], in0=eyeb[:], scalar1=deye[:, h:h + 1], scalar2=None,
            op0=ALU.mult), reads=[eyeb, deye], writes=[eyeD[h]])

    triu = p.sb("g_triu", [128, 128])
    zz128 = p.sb("g_zz128", [128, 1])
    p.op("v", lambda e: e.memset(zz128[:], 0.0), writes=[zz128])
    p.op("v", lambda e: e.tensor_tensor_scan(
        out=triu[:], data0=eyef[:], data1=zz128[:, 0:1].broadcast_to([128, 128]),
        initial=0.0, op0=ALU.max, op1=ALU.add), reads=[eyef, zz128], writes=[triu])

    srun = p.sb("g_srun", [128, NH * HP])
    srunb = p.sb("g_srunb", [128, NH * HP], BF16)
    p.op("v", lambda e: e.memset(srun[:], 0.0), writes=[srun])
    stmask = p.sb("g_stm", [128, 128], BF16)
    rowb2 = [p.sb(f"g_rowb{b}", [1, 512]) for b in range(2)]
    espraw = p.sb("g_espraw", [1, NH])
    acT = p.sb("g_acT", [128, NH])
    wT = p.sb("g_wT", [128, NH], BF16)
    dtT = p.sb("g_dtT", [128, NH], BF16)
    xdt = p.sb("g_xdt", [128, NH * HP], BF16)
    xw = p.sb("g_xw", [128, NH * HP], BF16)
    mt = p.sb("g_mt", [128, NH * CS], BF16)
    cwb = [p.sb(f"g_cwb{i}", [128, NH * CS], BF16) for i in range(1)]
    expd2 = [p.sb(f"g_expd{b}", [128, 512]) for b in range(2)]
    wrow = p.sb("g_wrow", [NH, CS])
    espT = p.sb("g_espT", [128, NH])
    esprow = p.sb("g_esprow", [1, NH])
    esptot = p.sb("g_esptot", [1, NH])
    yest = [p.sb(f"g_ye{i}", [64, 512], BF16) for i in range(2)]
    p.op("v", lambda e: e.memset(esptot[:], 1.0), writes=[esptot])

    psU2 = [p.ps(f"g_psU{b}", [128, 512]) for b in range(2)]
    psU = psU2[0]
    psT = p.ps("g_psT", [128, 128])
    psY = [p.ps(f"g_psY{i}", [64, 512]) for i in range(2)]
    psS = [p.ps(f"g_psS{i}", [128, 512]) for i in range(2)]

    for c in range(NCHUNK):
        sl = slice(c * CS, (c + 1) * CS)
        p.op("t", lambda e, sl=sl: e.matmul(psU[:, 0:128], bt[:, sl], ct[:, sl],
                                            start=True, stop=True),
             reads=[bt, ct], writes=[psU])
        p.op("v", lambda e: e.tensor_tensor(out=stmask[:], in0=psU[:, 0:128], in1=triu[:],
                                            op=ALU.mult), reads=[psU, triu], writes=[stmask])
        p.dma("qs", lambda e, sl=sl: e.dma_start(out=ac_dram[:], in_=acum[:, sl]),
              reads=[acum], writes=["ac_dram"])
        p.op("t", lambda e, sl=sl: e.transpose(psT[:, 0:NH], acum[:, sl], eyef[0:NH, 0:NH]),
             reads=[acum, eyef], writes=[psT])
        p.op("v", lambda e: e.tensor_copy(out=acT[:], in_=psT[:, 0:NH]),
             reads=[psT], writes=[acT])
        p.op("v", lambda e, sl=sl: e.tensor_scalar(
            out=wrow[:], in0=acum[:, sl], scalar1=acum[:, sl.stop - 1:sl.stop],
            scalar2=-1.0, op0=ALU.subtract, op1=ALU.mult), reads=[acum], writes=[wrow])
        p.op("s", lambda e: e.activation(out=wrow[:], in_=wrow[:], func=AF.Exp),
             reads=[wrow], writes=[wrow])
        p.op("t", lambda e: e.transpose(psT[:, 0:NH], wrow[:], eyef[0:NH, 0:NH]),
             reads=[wrow, eyef], writes=[psT])
        p.op("v", lambda e: e.tensor_copy(out=wT[:], in_=psT[:, 0:NH]),
             reads=[psT], writes=[wT])
        p.op("t", lambda e, sl=sl: e.transpose(psT[:, 0:NH], dtm[:, sl], eyef[0:NH, 0:NH]),
             reads=[dtm, eyef], writes=[psT])
        p.op("v", lambda e: e.tensor_copy(out=dtT[:], in_=psT[:, 0:NH]),
             reads=[psT], writes=[dtT])
        p.dma("qg", lambda e: e.dma_start(
            out=espraw[:], in_=ac_dram[:, CS - 1:CS].rearrange("h o -> o h"),
            allow_slow_non_contiguous=True),
            reads=["ac_dram"], writes=[espraw])
        p.op("s", lambda e: e.activation(out=esprow[:], in_=espraw[:], func=AF.Exp),
             reads=[espraw], writes=[esprow])
        p.dma("qs", lambda e, c=c: e.dma_start(out=espan_d[c:c + 1, :], in_=esprow[:]),
              reads=[esprow])
        tt(p, esptot[:], esptot[:], esprow[:], ALU.mult, reads=[esptot, esprow], writes=[esptot])
        p.op("t", lambda e: e.matmul(psT[:, 0:NH], K1, esprow[:], start=True, stop=True),
             reads=[ones, esprow], writes=[psT])
        p.op("v", lambda e: e.tensor_copy(out=espT[:], in_=psT[:, 0:NH]),
             reads=[psT], writes=[espT])
        p.op("v", lambda e, c=c: e.tensor_tensor(
            out=xdt[:].rearrange("t (h q) -> t h q", h=NH),
            in0=xtok[c][:, 0:INTER].rearrange("t (h q) -> t h q", h=NH),
            in1=dtT[:].unsqueeze(-1).broadcast_to([128, NH, HP]), op=ALU.mult),
            reads=[xtok[c], dtT], writes=[xdt])
        p.op("v", lambda e: e.tensor_tensor(
            out=xw[:].rearrange("t (h q) -> t h q", h=NH),
            in0=xdt[:].rearrange("t (h q) -> t h q", h=NH),
            in1=wT[:].unsqueeze(-1).broadcast_to([128, NH, HP]), op=ALU.mult),
            reads=[xdt, wT], writes=[xw])
        p.op("v", lambda e: e.tensor_copy(out=srunb[:], in_=srun[:]),
             reads=[srun], writes=[srunb])
        cw = cwb[0]
        for g4 in range(NH // 4):
            hs = slice(g4 * 4 * CS, (g4 + 1) * 4 * CS)
            rowb = rowb2[g4 % 2]
            expd = expd2[g4 % 2]
            psU = psU2[g4 % 2]
            p.dma("qs", lambda e, g4=g4, rowb=rowb: e.dma_start(
                out=rowb[:],
                in_=ac_dram[4 * g4:4 * (g4 + 1), :].rearrange("h l -> (h l)").unsqueeze(0)),
                reads=["ac_dram"], writes=[rowb])
            p.op("t", lambda e, rowb=rowb, psU=psU: e.matmul(
                psU[:], K1, rowb[:], start=True, stop=True),
                 reads=[ones, rowb], writes=[psU])
            p.op("v", lambda e, g4=g4, psU=psU, expd=expd: e.tensor_tensor(
                out=expd[:].rearrange("s (h l) -> s h l", h=4),
                in0=psU[:].rearrange("s (h l) -> s h l", h=4),
                in1=acT[:, 4 * g4:4 * g4 + 4].unsqueeze(-1).broadcast_to([128, 4, CS]),
                op=ALU.subtract), reads=[psU, acT], writes=[expd])
            p.op("v", lambda e, expd=expd: e.tensor_scalar(
                out=expd[:], in0=expd[:], scalar1=0.0,
                scalar2=None, op0=ALU.min),
                 reads=[expd], writes=[expd])
            p.op("s", lambda e, hs=hs, expd=expd: e.activation(
                out=mt[:, hs], in_=expd[:], func=AF.Exp),
                 reads=[expd], writes=[mt])
            p.op("v", lambda e, hs=hs: e.tensor_tensor(
                out=mt[:, hs].rearrange("s (h l) -> s h l", h=4),
                in0=mt[:, hs].rearrange("s (h l) -> s h l", h=4),
                in1=stmask[:].unsqueeze(1).broadcast_to([128, 4, 128]), op=ALU.mult),
                reads=[mt, stmask], writes=[mt])
            p.op("s", lambda e, psU=psU, expd=expd: e.activation(
                out=expd[:], in_=psU[:], func=AF.Exp),
                 reads=[psU], writes=[expd])
            p.op("v", lambda e, hs=hs, sl=sl, cw=cw, expd=expd: e.tensor_tensor(
                out=cw[:, hs].rearrange("n (h l) -> n h l", h=4),
                in0=expd[:].rearrange("n (h l) -> n h l", h=4),
                in1=ct[:, sl].unsqueeze(1).broadcast_to([128, 4, 128]), op=ALU.mult),
                reads=[expd, ct], writes=[cw])
        p.dma("qs", lambda e, c=c, cw=cw: e.dma_start(
            out=cw_d[NN * c:NN * (c + 1), :], in_=cw[:]), reads=[cw])
        for wv in range(16):
            psu = psY[wv % 2]
            ye = yest[wv % 2]
            for hh in range(4):
                h = wv * 4 + hh
                osl = slice(hh * 128, (hh + 1) * 128)
                p.op("t", lambda e, h=h, osl=osl, psu=psu: e.matmul(
                    psu[:, osl], xdt[:, HP * h:HP * (h + 1)], mt[:, CS * h:CS * (h + 1)],
                    start=True, stop=False), reads=[xdt, mt], writes=[psu])
                p.op("t", lambda e, h=h, osl=osl, psu=psu, c=c: e.matmul(
                    psu[:, osl], xtok[c][:, HP * h:HP * (h + 1)], eyeD[h][:],
                    start=False, stop=False), reads=[xtok[c], eyeD[h]], writes=[psu])
                p.op("t", lambda e, h=h, osl=osl, psu=psu: e.matmul(
                    psu[:, osl], srunb[:, HP * h:HP * (h + 1)], cw[:, CS * h:CS * (h + 1)],
                    start=False, stop=True), reads=[srunb, cw], writes=[psu])
            p.op("v", lambda e, psu=psu, ye=ye: e.tensor_copy(out=ye[:], in_=psu[:]),
                 reads=[psu], writes=[ye])
            p.dma("qs", lambda e, c=c, wv=wv, ye=ye: e.dma_start(
                out=yt_d[HP * 4 * wv: HP * 4 * (wv + 1), CS * c:CS * (c + 1)]
                    .rearrange("(h q) l -> q h l", q=HP),
                in_=ye[:].rearrange("q (h l) -> q h l", h=4)), reads=[ye])
        for wv in range(8):
            psu = psS[wv % 2]
            for hh in range(8):
                h = wv * 8 + hh
                osl = slice(hh * HP, (hh + 1) * HP)
                p.op("t", lambda e, h=h, osl=osl, psu=psu, c=c: e.matmul(
                    psu[:, osl], xtok[c][:, INTER:INTER + NN],
                    xw[:, HP * h:HP * (h + 1)],
                    start=True, stop=True), reads=[xtok[c], xw], writes=[psu])
            ssl = slice(wv * 512, (wv + 1) * 512)
            p.op("v", lambda e, ssl=ssl, wv=wv: e.tensor_tensor(
                out=srun[:, ssl].rearrange("n (h q) -> n h q", h=8),
                in0=srun[:, ssl].rearrange("n (h q) -> n h q", h=8),
                in1=espT[:, 8 * wv:8 * (wv + 1)].unsqueeze(-1).broadcast_to([128, 8, HP]),
                op=ALU.mult), reads=[srun, espT], writes=[srun])
            p.op("v", lambda e, ssl=ssl, psu=psu: e.tensor_tensor(
                out=srun[:, ssl], in0=srun[:, ssl], in1=psu[:], op=ALU.add),
                reads=[srun, psu], writes=[srun])

    p.op("v", lambda e: e.tensor_copy(out=srunb[:], in_=srun[:]), reads=[srun], writes=[srunb])
    p.dma("qs", lambda e: e.dma_start(out=sacc_d[:], in_=srunb[:]), reads=[srunb])
    p.dma("qs", lambda e: e.dma_start(out=espantot_d[:], in_=esptot[:]), reads=[esptot])
    p.maybe_finish()
    return p


def build_phase_h(tw=TOWN, p=None):
    if p is None:
        p = Prog("phaseH")
    yt_dio = p.din("ytin", [INTER, tw], BF16)
    cw_d = p.din("cw", [(tw // CS) * NN, NH * CS], BF16)
    gsil_d = p.din("gsil", [INTER, tw], BF16)
    h2_d = p.din("h2", [D, tw])
    sacc8_d = p.din("sacc8", [8 * NN, NH * HP], BF16)
    esptot8_d = p.din("esptot8", [1, 8 * NH])
    espmy_d = p.din("espmy", [1, (tw // CS) * NH])
    jmask_d = p.din("jmask", [1, 8])
    nw_d = p.din("nw", [128, INTER // 128])
    wop_d = p.din("wop", [INTER, D])
    ones_d = p.din("ones", [128, 1])
    h3_d = p.dout("h3", [D, tw])
    h3b_d = p.dout("h3b", [D, tw], BF16)

    p.start()
    NCH = D // 128
    NYC = INTER // 128
    NCHUNK = tw // CS
    NT = tw // 512
    ones = p.sb("h_ones", [128, 1])
    p.dma("qs", lambda e: e.dma_start(out=ones[:], in_=ones_d[:]), writes=[ones])
    ones_b = p.sb("h_onesb", [128, 1], BF16)
    p.op("v", lambda e: e.tensor_copy(out=ones_b[:], in_=ones[:]), reads=[ones], writes=[ones_b])
    K1 = ones[0:1, 0:1].broadcast_to([1, 128])

    jrow = p.sb("h_jrow", [1, 8])
    invr = p.sb("h_invr", [1, 8])
    erow = p.sb("h_erow", [1, NH])
    etot = p.sb("h_etot", [1, 8 * NH])
    mcol8 = p.sb("h_mcol8", [128, 8])
    espmy = p.sb("h_espmy", [1, (tw // CS) * NH])
    vrow = p.sb("h_vrow", [1, tw])
    rstdb = p.sb("h_rstdb", [128, tw])
    nwc = p.sb("h_nwc", [128, INTER // 128])
    sub0 = ExitStack()
    scur = sub0.enter_context(p.nc.sbuf_tensor("h_scur", [128, NH * HP], F32))
    scb = sub0.enter_context(p.nc.sbuf_tensor("h_scb", [128, NH * HP], BF16))
    ebc = sub0.enter_context(p.nc.sbuf_tensor("h_ebc", [128, NH], F32))
    p.dma("qs", lambda e: e.dma_start(out=jrow[:], in_=jmask_d[:]), writes=[jrow])
    p.dma("qs", lambda e: e.dma_start(out=etot[:], in_=esptot8_d[:]), writes=[etot])
    p.op("v", lambda e: e.tensor_scalar(out=invr[:], in0=jrow[:], scalar1=-1.0,
                                        scalar2=1.0, op0=ALU.mult, op1=ALU.add),
         reads=[jrow], writes=[invr])
    mps = sub0.enter_context(p.nc.psum_tensor("h_mps", [128, 128], F32))
    p.op("t", lambda e: e.matmul(mps[:, 0:8], K1, jrow[:], start=True, stop=True),
         reads=[ones, jrow], writes=[mps])
    p.op("v", lambda e: e.tensor_copy(out=mcol8[:], in_=mps[:, 0:8]), reads=[mps], writes=[mcol8])

    saccj = [sub0.enter_context(p.nc.sbuf_tensor(f"h_sacc{i}", [128, NH * HP], BF16))
             for i in range(2)]
    p.op("v", lambda e: e.memset(scur[:], 0.0), writes=[scur])
    for j in range(7):
        sj = saccj[j % 2]
        p.dma("qs", lambda e, j=j, sj=sj: e.dma_start(
            out=sj[:], in_=sacc8_d[NN * j:NN * (j + 1), :]), writes=[sj])
        p.op("v", lambda e, j=j: e.tensor_scalar(
            out=erow[:], in0=etot[0:1, NH * j:NH * (j + 1)], scalar1=jrow[0:1, j:j + 1],
            scalar2=None, op0=ALU.mult), reads=[etot, jrow], writes=[erow])
        p.op("v", lambda e, j=j: e.tensor_scalar(
            out=erow[:], in0=erow[:], scalar1=invr[0:1, j:j + 1],
            scalar2=None, op0=ALU.add), reads=[erow, invr], writes=[erow])
        p.op("t", lambda e: e.matmul(mps[:, 0:NH], K1, erow[:], start=True, stop=True),
             reads=[ones, erow], writes=[mps])
        p.op("v", lambda e: e.tensor_copy(out=ebc[:], in_=mps[:, 0:NH]),
             reads=[mps], writes=[ebc])
        p.op("v", lambda e: e.tensor_tensor(
            out=scur[:].rearrange("n (h q) -> n h q", h=NH),
            in0=scur[:].rearrange("n (h q) -> n h q", h=NH),
            in1=ebc[:].unsqueeze(-1).broadcast_to([128, NH, HP]), op=ALU.mult),
            reads=[scur, ebc], writes=[scur])
        p.op("v", lambda e, sj=sj, j=j: e.scalar_tensor_tensor(
            out=scur[:], in0=sj[:], scalar=mcol8[:, j:j + 1], in1=scur[:],
            op0=ALU.mult, op1=ALU.add), reads=[sj, mcol8, scur], writes=[scur])

    p.dma("qs", lambda e: e.dma_start(out=espmy[:], in_=espmy_d[:]), writes=[espmy])
    cwch = [sub0.enter_context(p.nc.sbuf_tensor(f"h_cw{i}", [128, NH * CS], BF16))
            for i in range(2)]
    psY = [sub0.enter_context(p.nc.psum_tensor(f"h_psY{i}", [64, 512], F32))
           for i in range(2)]
    yez = [sub0.enter_context(p.nc.sbuf_tensor(f"h_ye{i}", [64, 512], BF16))
           for i in range(2)]
    for c in range(NCHUNK):
        cwc = cwch[c % 2]
        p.dma("qs", lambda e, c=c, cwc=cwc: e.dma_start(
            out=cwc[:], in_=cw_d[NN * c:NN * (c + 1), :]), writes=[cwc])
        p.op("v", lambda e: e.tensor_copy(out=scb[:], in_=scur[:]), reads=[scur], writes=[scb])
        for wv in range(16):
            psu = psY[wv % 2]
            ye = yez[wv % 2]
            for hh in range(4):
                h = wv * 4 + hh
                osl = slice(hh * 128, (hh + 1) * 128)
                p.op("t", lambda e, h=h, osl=osl, psu=psu, cwc=cwc: e.matmul(
                    psu[:, osl], scb[:, HP * h:HP * (h + 1)], cwc[:, CS * h:CS * (h + 1)],
                    start=True, stop=True), reads=[scb, cwc], writes=[psu])
            p.op("v", lambda e, psu=psu, ye=ye: e.tensor_copy(out=ye[:], in_=psu[:]),
                 reads=[psu], writes=[ye])
            p.dma("qg", lambda e, c=c, wv=wv, ye=ye: e.dma_start(
                out=yt_dio[HP * 4 * wv: HP * 4 * (wv + 1), CS * c:CS * (c + 1)]
                    .rearrange("(h q) l -> q h l", q=HP),
                in_=ye[:].rearrange("q (h l) -> q h l", h=4),
                accum_op=ALU.add), reads=[ye], writes=[f"ya_{wv}_{c}"])
        p.op("v", lambda e, c=c: e.tensor_scalar(
            out=erow[:], in0=espmy[0:1, NH * c:NH * (c + 1)], scalar1=1.0, scalar2=None,
            op0=ALU.mult), reads=[espmy], writes=[erow])
        p.op("t", lambda e: e.matmul(mps[:, 0:NH], K1, erow[:], start=True, stop=True),
             reads=[ones, erow], writes=[mps])
        p.op("v", lambda e: e.tensor_copy(out=ebc[:], in_=mps[:, 0:NH]),
             reads=[mps], writes=[ebc])
        p.op("v", lambda e: e.tensor_tensor(
            out=scur[:].rearrange("n (h q) -> n h q", h=NH),
            in0=scur[:].rearrange("n (h q) -> n h q", h=NH),
            in1=ebc[:].unsqueeze(-1).broadcast_to([128, NH, HP]), op=ALU.mult),
            reads=[scur, ebc], writes=[scur])
    sub0.close()
    p.fence()

    yf = [p.sb(f"h_yf{i}", [128, tw], BF16) for i in range(INTER // 128)]
    sub1 = ExitStack()
    ytb = [sub1.enter_context(p.nc.sbuf_tensor(f"h_ytb{i}", [128, tw], BF16))
           for i in range(2)]
    ysq = sub1.enter_context(p.nc.sbuf_tensor("h_ysq", [128, tw], BF16))
    psV = [sub1.enter_context(p.nc.psum_tensor(f"h_psV{i}", [1, 512], F32))
           for i in range(NT)]
    for r in range(NYC):
        yb = ytb[r % 2]
        p.dma("qs", lambda e, r=r, yb=yb: e.dma_start(
            out=yb[:], in_=yt_dio[128 * r:128 * (r + 1), :]), writes=[yb])
        p.op("s", lambda e, yb=yb: e.activation(out=ysq[:], in_=yb[:], func=AF.Square),
             reads=[yb], writes=[ysq])
        for nt in range(NT):
            p.op("t", lambda e, nt=nt, r=r: e.matmul(
                psV[nt][:], ones_b[:, 0:1], ysq[:, 512 * nt:512 * (nt + 1)],
                start=(r == 0), stop=(r == NYC - 1)),
                reads=[ysq, ones_b], writes=[psV[nt]])
    for nt in range(NT):
        p.op("v", lambda e, nt=nt: e.tensor_scalar(
            out=vrow[0:1, 512 * nt:512 * (nt + 1)], in0=psV[nt][:],
            scalar1=1.0 / INTER, scalar2=1e-5, op0=ALU.mult, op1=ALU.add),
            reads=[psV[nt]], writes=[vrow])
    p.op("s", lambda e: e.activation(out=vrow[:], in_=vrow[:], func=AF.Sqrt),
         reads=[vrow], writes=[vrow])
    p.op("v", lambda e: e.reciprocal(out=vrow[:], in_=vrow[:]), reads=[vrow], writes=[vrow])
    psB = sub1.enter_context(p.nc.psum_tensor("h_psB", [128, 512], F32))
    for nt in range(NT):
        p.op("t", lambda e, nt=nt: e.matmul(
            psB[:], K1, vrow[0:1, 512 * nt:512 * (nt + 1)], start=True, stop=True),
            reads=[ones, vrow], writes=[psB])
        p.op("v", lambda e, nt=nt: e.tensor_copy(
            out=rstdb[:, 512 * nt:512 * (nt + 1)], in_=psB[:]), reads=[psB], writes=[rstdb])

    p.dma("qs", lambda e: e.dma_start(out=nwc[:], in_=nw_d[:]), writes=[nwc])
    gsb = [sub1.enter_context(p.nc.sbuf_tensor(f"h_gs{i}", [128, tw], BF16))
           for i in range(2)]
    for r in range(NYC):
        yb = ytb[r % 2]
        gs = gsb[r % 2]
        p.dma("qs", lambda e, r=r, yb=yb: e.dma_start(
            out=yb[:], in_=yt_dio[128 * r:128 * (r + 1), :]), writes=[yb])
        p.dma("qs", lambda e, r=r, gs=gs: e.dma_start(
            out=gs[:], in_=gsil_d[128 * r:128 * (r + 1), :]), writes=[gs])
        p.op("v", lambda e, r=r, yb=yb: e.scalar_tensor_tensor(
            out=yf[r][:], in0=yb[:], scalar=nwc[:, r:r + 1], in1=rstdb[:],
            op0=ALU.mult, op1=ALU.mult), reads=[yb, nwc, rstdb], writes=[yf[r]])
        p.op("v", lambda e, r=r, gs=gs: e.tensor_tensor(
            out=yf[r][:], in0=yf[r][:], in1=gs[:], op=ALU.mult),
            reads=[yf[r], gs], writes=[yf[r]])
    sub1.close()
    p.fence()

    h = [p.sb(f"h_h{i}", [128, tw]) for i in range(NCH)]
    hb = [p.sb(f"h_hb{i}", [128, tw], BF16) for i in range(NCH)]
    sub2 = ExitStack()
    wtl = [sub2.enter_context(p.nc.sbuf_tensor(f"h_wtl{i}", [128, 512], BF16))
           for i in range(16)]
    pst = [sub2.enter_context(p.nc.psum_tensor(f"h_pst{i}", [128, 512], F32))
           for i in range(8)]
    for i in range(NCH):
        p.dma("qs", lambda e, i=i: e.dma_start(out=h[i][:], in_=h2_d[128 * i:128 * (i + 1), :]),
              writes=[h[i]])
    CG = 4
    for cg in range(NCH // CG):
        for khalf in range(2):
            for kk in range(16):
                k = khalf * 16 + kk
                p.dma("qg", lambda e, k=k, kk=kk, cg=cg: e.dma_start(
                    out=wtl[kk][:], in_=wop_d[128 * k:128 * (k + 1), 512 * cg:512 * (cg + 1)]),
                    writes=[wtl[kk]])
            for ci in range(CG):
                for nt in range(NT):
                    psu = pst[ci * NT + nt]
                    for kk in range(16):
                        k = khalf * 16 + kk
                        p.op("t", lambda e, psu=psu, kk=kk, ci=ci, nt=nt, k=k, khalf=khalf: e.matmul(
                            psu[:], wtl[kk][:, 128 * ci:128 * (ci + 1)],
                            yf[k][:, 512 * nt:512 * (nt + 1)],
                            start=(khalf == 0 and kk == 0), stop=(khalf == 1 and kk == 15)),
                            reads=[wtl[kk], yf[k]], writes=[psu])
        for ci in range(CG):
            c = cg * CG + ci
            for nt in range(NT):
                psu = pst[ci * NT + nt]
                p.op("v", lambda e, psu=psu, c=c, nt=nt: e.tensor_tensor(
                    out=h[c][:, 512 * nt:512 * (nt + 1)],
                    in0=h[c][:, 512 * nt:512 * (nt + 1)], in1=psu[:], op=ALU.add),
                    reads=[psu, h[c]], writes=[h[c]])
    sub2.close()
    p.fence()
    ones_bb = ones_b
    ps_s = [p.ps("h_psLs1", [1, min(tw, 512)]), p.ps("h_psLs2", [1, min(tw, 512)])]
    ps_b = [p.ps("h_psLb1", [128, min(tw, 512)]), p.ps("h_psLb2", [128, min(tw, 512)])]
    rows = [p.sb(f"h_lnrow{i}", [1, tw]) for i in range(4)]
    emit_ln_fm(p, h, hb, ones, ones_bb, ps_s, ps_b, rows, tw=tw)
    for i in range(NCH):
        p.dma("qs", lambda e, i=i: e.dma_start(out=h3_d[128 * i:128 * (i + 1), :], in_=h[i][:]),
              reads=[h[i]])
        p.dma("qs", lambda e, i=i: e.dma_start(out=h3b_d[128 * i:128 * (i + 1), :], in_=hb[i][:]),
              reads=[hb[i]])
    p.maybe_finish()
    return p


NHEAD = 16
HD = 128


def build_phase_i(tw=TOWN, p=None):
    if p is None:
        p = Prog("phaseI")
    h3_d = p.din("h3", [D, tw])
    h3b_d = p.din("h3b", [D, tw], BF16)
    wgt_d = p.din("wgt", [D, NHEAD])
    wqkvt_d = p.din("wqkvt", [D, 3 * D])
    ones_d = p.din("ones", [128, 1])
    qt_d = p.dout("qt", [D, tw], BF16)
    ksh_d = p.dout("ksh", [tw, HD], BF16)
    vsh_d = p.dout("vsh", [tw, HD], BF16)
    kraw_d = p.dout("kraw", [NHEAD, tw])
    p.start()
    NCH = D // 128
    NT = tw // 512
    NTC = tw // 128
    ones = p.sb("i_ones", [128, 1])
    onesb = p.sb("i_onesb", [128, 1], BF16)
    p.dma("qs", lambda e: e.dma_start(out=ones[:], in_=ones_d[:]), writes=[ones])
    p.op("v", lambda e: e.tensor_copy(out=onesb[:], in_=ones[:]), reads=[ones], writes=[onesb])

    h3 = [p.sb(f"i_h3_{i}", [128, tw]) for i in range(NCH)]
    h3b = [p.sb(f"i_h3b{i}", [128, tw], BF16) for i in range(NCH)]
    for i in range(NCH):
        p.dma("qs", lambda e, i=i: e.dma_start(out=h3[i][:], in_=h3_d[128 * i:128 * (i + 1), :]),
              writes=[h3[i]])
        p.dma("qg", lambda e, i=i: e.dma_start(out=h3b[i][:], in_=h3b_d[128 * i:128 * (i + 1), :]),
              writes=[h3b[i]])

    wgt = p.sb("i_wgt", [128, NHEAD * NCH])
    for k in range(NCH):
        p.dma("qs", lambda e, k=k: e.dma_start(
            out=wgt[:, NHEAD * k:NHEAD * (k + 1)], in_=wgt_d[128 * k:128 * (k + 1), :]),
            writes=[wgt])
    ind = p.sb("i_ind", [128, NHEAD * NTC])
    inds = p.sb("i_inds", [128, NHEAD * NTC])
    psg = p.ps("i_psg", [128, NHEAD])
    mx = p.sb("i_mx", [128, 1])
    zs = p.sb("i_zs", [128, 1])
    ztmp = p.sb("i_ztmp", [128, NHEAD])
    for j in range(NTC):
        for k in range(NCH):
            p.op("t", lambda e, j=j, k=k: e.matmul(
                psg[:], h3[k][:, 128 * j:128 * (j + 1)], wgt[:, NHEAD * k:NHEAD * (k + 1)],
                start=(k == 0), stop=(k == NCH - 1)),
                reads=[h3[k], wgt], writes=[psg])
        p.op("v", lambda e: e.reduce_max(out=mx[:], in_=psg[:], axis=AX.X),
             reads=[psg], writes=[mx])
        isl = slice(NHEAD * j, NHEAD * (j + 1))
        p.op("v", lambda e, isl=isl: e.tensor_scalar(
            out=ind[:, isl], in0=psg[:], scalar1=mx[:, 0:1], scalar2=None, op0=ALU.is_ge),
            reads=[psg, mx], writes=[ind])
        p.op("v", lambda e: e.tensor_scalar(
            out=ztmp[:], in0=psg[:], scalar1=mx[:, 0:1], scalar2=None, op0=ALU.subtract),
            reads=[psg, mx], writes=[ztmp])
        p.op("s", lambda e: e.activation(out=ztmp[:], in_=ztmp[:], func=AF.Exp,
                                         accum_out=zs[:]),
             reads=[ztmp], writes=[ztmp, zs])
        p.op("v", lambda e: e.reciprocal(out=zs[:], in_=zs[:]), reads=[zs], writes=[zs])
        p.op("v", lambda e, isl=isl: e.tensor_scalar(
            out=inds[:, isl], in0=ind[:, isl], scalar1=zs[:, 0:1], scalar2=None, op0=ALU.mult),
            reads=[ind, zs], writes=[inds])

    wtk = [p.sb(f"i_wtk{i}", [128, 512], BF16) for i in range(NCH)]
    ps4 = [p.ps(f"i_ps4{i}", [128, 512]) for i in range(4)]
    qtile = [p.sb(f"i_qt{i}", [128, tw], BF16) for i in range(NCH)]
    for cg in range(D // 512):
        for k in range(NCH):
            p.dma("qg", lambda e, k=k, cg=cg: e.dma_start(
                out=wtk[k][:], in_=wqkvt_d[128 * k:128 * (k + 1), 512 * cg:512 * (cg + 1)]),
                writes=[wtk[k]])
        for ci in range(4):
            c = cg * 4 + ci
            for nt in range(NT):
                psu = ps4[(ci * NT + nt) % 4]
                for k in range(NCH):
                    p.op("t", lambda e, psu=psu, k=k, ci=ci, nt=nt: e.matmul(
                        psu[:], wtk[k][:, 128 * ci:128 * (ci + 1)],
                        h3b[k][:, 512 * nt:512 * (nt + 1)],
                        start=(k == 0), stop=(k == NCH - 1)),
                        reads=[wtk[k], h3b[k]], writes=[psu])
                p.op("v", lambda e, psu=psu, c=c, nt=nt: e.tensor_copy(
                    out=qtile[c][:, 512 * nt:512 * (nt + 1)], in_=psu[:]),
                    reads=[psu], writes=[qtile[c]])
                p.dma("qs", lambda e, c=c, nt=nt: e.dma_start(
                    out=qt_d[128 * c:128 * (c + 1), 512 * nt:512 * (nt + 1)],
                    in_=qtile[c][:, 512 * nt:512 * (nt + 1)]),
                    reads=[qtile[c]])
    qsq = p.sb("i_qsq", [128, tw], BF16)
    psr = p.ps("i_psr", [1, 512])
    rowtmp = p.sb("i_rowtmp", [1, 512])
    for h in range(NHEAD):
        p.op("s", lambda e, h=h: e.activation(out=qsq[:], in_=qtile[h][:], func=AF.Square),
             reads=[qtile[h]], writes=[qsq])
        for nt in range(NT):
            p.op("t", lambda e, nt=nt: e.matmul(
                psr[:], onesb[:, 0:1], qsq[:, 512 * nt:512 * (nt + 1)],
                start=True, stop=True), reads=[qsq, onesb], writes=[psr])
            p.op("s", lambda e: e.activation(out=rowtmp[:], in_=psr[:], func=AF.Sqrt),
                 reads=[psr], writes=[rowtmp])
            p.dma("qs", lambda e, h=h, nt=nt: e.dma_start(
                out=kraw_d[h:h + 1, 512 * nt:512 * (nt + 1)], in_=rowtmp[:]),
                reads=[rowtmp])
    tmp = p.sb("i_tmp", [128, HD], BF16)
    accs = [p.sb(f"i_acc{j}", [128, HD]) for j in range(NTC)]
    for part, seltile, outd in ((1, ind, ksh_d), (2, inds, vsh_d)):
        for cg in range(D // 512):
            for k in range(NCH):
                p.dma("qg", lambda e, k=k, cg=cg, part=part: e.dma_start(
                    out=wtk[k][:],
                    in_=wqkvt_d[128 * k:128 * (k + 1),
                                part * D + 512 * cg: part * D + 512 * (cg + 1)]),
                    writes=[wtk[k]])
            for j in range(NTC):
                psu = ps4[j % 4]
                for k in range(NCH):
                    p.op("t", lambda e, psu=psu, k=k, j=j: e.matmul(
                        psu[:], h3b[k][:, 128 * j:128 * (j + 1)],
                        wtk[k][:, 0:512], start=(k == 0), stop=(k == NCH - 1)),
                        reads=[wtk[k], h3b[k]], writes=[psu])
                acc = accs[j]
                for hh in range(4):
                    h = cg * 4 + hh
                    if cg == 0 and hh == 0:
                        p.op("v", lambda e, psu=psu, j=j, h=h, acc=acc,
                             seltile=seltile: e.tensor_scalar(
                            out=acc[:], in0=psu[:, 0:HD],
                            scalar1=seltile[:, NHEAD * j + h:NHEAD * j + h + 1],
                            scalar2=None, op0=ALU.mult),
                            reads=[psu, seltile], writes=[acc])
                    else:
                        p.op("v", lambda e, psu=psu, j=j, h=h, hh=hh, acc=acc,
                             seltile=seltile: e.tensor_scalar(
                            out=tmp[:], in0=psu[:, HD * hh:HD * (hh + 1)],
                            scalar1=seltile[:, NHEAD * j + h:NHEAD * j + h + 1],
                            scalar2=None, op0=ALU.mult),
                            reads=[psu, seltile], writes=[tmp])
                        p.op("v", lambda e, acc=acc: e.tensor_tensor(
                            out=acc[:], in0=acc[:], in1=tmp[:], op=ALU.add),
                            reads=[acc, tmp], writes=[acc])
        for j in range(NTC):
            acc = accs[j]
            p.op("v", lambda e, acc=acc: e.tensor_copy(out=tmp[:], in_=acc[:]),
                 reads=[acc], writes=[tmp])
            p.dma("qs", lambda e, j=j, outd=outd: e.dma_start(
                out=outd[128 * j:128 * (j + 1), :], in_=tmp[:]), reads=[tmp])
    p.maybe_finish()
    return p


def build_phase_j(tn=T, p=None):
    if p is None:
        p = Prog("phaseJ")
    q2_d = p.din("q2", [2 * HD, tn], BF16)
    ksh_d = p.din("ksh", [tn, HD], BF16)
    vsh_d = p.din("vsh", [tn, HD], BF16)
    kraw_d = p.din("kraw", [2, tn])
    dmask_d = p.din("dmask", [4 * 128, 512])
    ones_d = p.din("ones", [128, 1])
    y2_d = p.dout("y2", [2 * HD, tn], BF16)
    p.start()
    NQB = tn // 512
    ones = p.sb("j_ones", [128, 1])
    onesb = p.sb("j_onesb", [128, 1], BF16)
    p.dma("qs", lambda e: e.dma_start(out=ones[:], in_=ones_d[:]), writes=[ones])
    p.op("v", lambda e: e.tensor_copy(out=onesb[:], in_=ones[:]), reads=[ones], writes=[onesb])
    K1 = ones[0:1, 0:1].broadcast_to([1, 128])
    K1b = onesb[0:1, 0:1].broadcast_to([1, 128])

    ksht = p.sb("j_ksht", [HD, tn], BF16)
    p.dma("qs", lambda e: e.dma_start(out=ksht[:], in_=ksh_d[:], transpose=True),
          writes=[ksht])
    vtok = p.sb("j_vtok", [128, (tn // 128) * HD], BF16)
    for j in range(tn // 128):
        p.dma("qs", lambda e, j=j: e.dma_start(
            out=vtok[:, HD * j:HD * (j + 1)], in_=vsh_d[128 * j:128 * (j + 1), :]),
            writes=[vtok])
    dmask = [p.sb(f"j_dm{i}", [128, 512], BF16) for i in range(4)]
    for i in range(4):
        p.dma("qg", lambda e, i=i: e.dma_start(out=dmask[i][:], in_=dmask_d[128 * i:128 * (i + 1), :]),
              writes=[dmask[i]])
    q2 = p.sb("j_q2", [128, 2 * tn], BF16)
    for h in range(2):
        p.dma("qs", lambda e, h=h: e.dma_start(out=q2[:, tn * h:tn * (h + 1)],
                                               in_=q2_d[HD * h:HD * (h + 1), :]),
              writes=[q2])
    ksq = p.sb("j_ksq", [128, tn], BF16)
    knrow = p.sb("j_knrow", [1, tn])
    kn1 = p.sb("j_kn1", [1, 1])
    psB = p.ps("j_psB", [128, 512])
    p.op("s", lambda e: e.activation(out=ksq[:], in_=ksht[:], func=AF.Square),
         reads=[ksht], writes=[ksq])
    for nt in range(tn // 512):
        p.op("t", lambda e, nt=nt: e.matmul(psB[0:1, :], onesb[:, 0:1],
                                            ksq[:, 512 * nt:512 * (nt + 1)],
                                            start=True, stop=True),
             reads=[ksq, onesb], writes=[psB])
        p.op("v", lambda e, nt=nt: e.tensor_copy(out=knrow[0:1, 512 * nt:512 * (nt + 1)],
                                                 in_=psB[0:1, :]), reads=[psB], writes=[knrow])
    p.op("v", lambda e: e.reduce_max(out=kn1[:], in_=knrow[:], axis=AX.X),
         reads=[knrow], writes=[kn1])
    p.op("s", lambda e: e.activation(out=kn1[:], in_=kn1[:], func=AF.Sqrt),
         reads=[kn1], writes=[kn1])
    kr2 = p.sb("j_kr2", [1, 2 * tn])
    km2 = p.sb("j_km2", [1, 1])
    negc = p.sb("j_negc", [128, 1])
    for hh in range(2):
        p.dma("qs", lambda e, hh=hh: e.dma_start(out=kr2[0:1, tn * hh:tn * (hh + 1)],
                                                 in_=kraw_d[hh:hh + 1, :]), writes=[kr2])
    p.op("v", lambda e: e.reduce_max(out=km2[:], in_=kr2[:], axis=AX.X),
         reads=[kr2], writes=[km2])
    p.op("v", lambda e: e.tensor_tensor(out=km2[:], in0=km2[:], in1=kn1[:],
                                        op=ALU.mult), reads=[km2, kn1], writes=[km2])
    p.op("v", lambda e: e.tensor_scalar(out=km2[:], in0=km2[:],
                                        scalar1=-1.0 / math.sqrt(HD), scalar2=None,
                                        op0=ALU.mult), reads=[km2], writes=[km2])
    p.op("t", lambda e: e.matmul(psB[:, 0:1], K1, km2[:], start=True, stop=True),
         reads=[km2, ones], writes=[psB])
    p.op("v", lambda e: e.tensor_copy(out=negc[:], in_=psB[:, 0:1]),
         reads=[psB], writes=[negc])

    psS = [p.ps(f"j_psS{i}", [128, 512]) for i in range(3)]
    psO = [p.ps(f"j_psO{i}", [128, 512]) for i in range(2)]
    psZ = [p.ps(f"j_psZ{i}", [1, 512]) for i in range(2)]
    expb = [p.sb(f"j_expb{i}", [128, 512], BF16) for i in range(4)]
    ebsum = p.sb("j_ebsum", [128, 512])
    ebpair = p.sb("j_ebpair", [128, 512], BF16)
    ebsb = p.sb("j_ebsb", [128, 512], BF16)
    rcpb = p.sb("j_rcpb", [128, 512])
    yout = p.sb("j_yout", [128, 512], BF16)
    INV_SQ = 1.0 / math.sqrt(HD)

    for h in range(2):
        for qb in range(NQB):
            qsl = slice(512 * qb, 512 * (qb + 1))
            pO = psO[qb % 2]
            pZ = psZ[qb % 2]
            nkb = 4 * qb + 4
            def emit_qk(kb):
                pS = psS[kb % 3]
                eb = expb[kb % 4]
                p.op("t", lambda e, h=h, kb=kb, qsl=qsl, pS=pS: e.matmul(
                    pS[:], ksht[:, 128 * kb:128 * (kb + 1)],
                    q2[:, tn * h + qsl.start:tn * h + qsl.stop],
                    start=True, stop=True), reads=[ksht, q2], writes=[pS])
                p.op("s", lambda e, pS=pS, eb=eb: e.activation(
                    out=eb[:], in_=pS[:], func=AF.Exp, scale=INV_SQ,
                    bias=negc[:, 0:1]),
                    reads=[pS, negc], writes=[eb])
                jj = kb - 4 * qb
                if 0 <= jj < 4:
                    p.op("v", lambda e, eb=eb, jj=jj: e.tensor_tensor(
                        out=eb[:], in0=eb[:], in1=dmask[jj][:], op=ALU.mult),
                        reads=[eb, dmask[jj]], writes=[eb])
                if kb % 2 == 1:
                    ebprev = expb[(kb - 1) % 4]
                    p.op("v", lambda e, eb=eb, ebprev=ebprev: e.tensor_tensor(
                        out=ebpair[:], in0=ebprev[:], in1=eb[:], op=ALU.add),
                        reads=[ebprev, eb], writes=[ebpair])
                    if kb == 1:
                        p.op("v", lambda e: e.tensor_copy(out=ebsum[:], in_=ebpair[:]),
                             reads=[ebpair], writes=[ebsum])
                    else:
                        p.op("v", lambda e: e.tensor_tensor(
                            out=ebsum[:], in0=ebsum[:], in1=ebpair[:], op=ALU.add),
                            reads=[ebsum, ebpair], writes=[ebsum])

            def emit_pv(kb):
                eb = expb[kb % 4]
                p.op("t", lambda e, kb=kb, eb=eb, pO=pO: e.matmul(
                    pO[:], vtok[:, HD * kb:HD * (kb + 1)], eb[:],
                    start=(kb == 0), stop=(kb == nkb - 1)),
                    reads=[vtok, eb], writes=[pO])

            for kb in range(nkb):
                emit_qk(kb)
                if kb > 0:
                    emit_pv(kb - 1)
            emit_pv(nkb - 1)
            p.op("v", lambda e: e.tensor_copy(out=ebsb[:], in_=ebsum[:]),
                 reads=[ebsum], writes=[ebsb])
            p.op("t", lambda e, pZ=pZ: e.matmul(pZ[:], onesb[:, 0:1], ebsb[:],
                                                start=True, stop=True),
                 reads=[onesb, ebsb], writes=[pZ])
            p.op("v", lambda e, pZ=pZ: e.reciprocal(out=rcpb[0:1, :], in_=pZ[:]),
                 reads=[pZ], writes=[rcpb])
            p.op("t", lambda e: e.matmul(psB[:], K1, rcpb[0:1, :], start=True, stop=True),
                 reads=[rcpb, ones], writes=[psB])
            p.op("v", lambda e: e.tensor_copy(out=rcpb[:], in_=psB[:]),
                 reads=[psB], writes=[rcpb])
            p.op("v", lambda e, pO=pO: e.tensor_tensor(out=yout[:], in0=pO[:], in1=rcpb[:],
                                                       op=ALU.mult),
                 reads=[pO, rcpb], writes=[yout])
            def y2_sl(h, qsl=qsl):
                if len(y2_d.shape) == 3:
                    cb = qsl.start // (tn // 8)
                    w0 = qsl.start % (tn // 8)
                    return y2_d[HD * h:HD * (h + 1), cb, w0:w0 + (qsl.stop - qsl.start)]
                return y2_d[HD * h:HD * (h + 1), qsl]
            p.dma("qs", lambda e, h=h, qsl=qsl: e.dma_start(
                out=y2_sl(h), in_=yout[:]), reads=[yout])
    p.maybe_finish()
    return p


def build_phase_k(tw=TOWN, p=None):
    if p is None:
        p = Prog("phaseK")
    y_d = p.din("y", [D, tw], BF16)
    h3_d = p.din("h3", [D, tw])
    ones_d = p.din("ones", [128, 1])
    out_d = p.dout("outT", [D, tw])
    p.start()
    NCH = D // 128
    ones = p.sb("k_ones", [128, 1])
    ones_b = p.sb("k_onesb", [128, 1], BF16)
    p.dma("qs", lambda e: e.dma_start(out=ones[:], in_=ones_d[:]), writes=[ones])
    p.op("v", lambda e: e.tensor_copy(out=ones_b[:], in_=ones[:]), reads=[ones], writes=[ones_b])
    h = [p.sb(f"k_h{i}", [128, tw]) for i in range(NCH)]
    hb = [p.sb(f"k_hb{i}", [128, tw], BF16) for i in range(NCH)]
    yb = [p.sb(f"k_yb{i}", [128, tw], BF16) for i in range(NCH)]
    for i in range(NCH):
        p.dma("qs", lambda e, i=i: e.dma_start(out=h[i][:], in_=h3_d[128 * i:128 * (i + 1), :]),
              writes=[h[i]])
        p.dma("qg", lambda e, i=i: e.dma_start(out=yb[i][:], in_=y_d[128 * i:128 * (i + 1), :]),
              writes=[yb[i]])
        tt(p, h[i][:], h[i][:], yb[i][:], ALU.add, reads=[h[i], yb[i]], writes=[h[i]])
    ps_s = [p.ps("k_psLs1", [1, min(tw, 512)]), p.ps("k_psLs2", [1, min(tw, 512)])]
    ps_b = [p.ps("k_psLb1", [128, min(tw, 512)]), p.ps("k_psLb2", [128, min(tw, 512)])]
    rows = [p.sb(f"k_lnrow{i}", [1, tw]) for i in range(4)]
    emit_ln_fm(p, h, hb, ones, ones_b, ps_s, ps_b, rows, tw=tw)
    for i in range(NCH):
        p.dma("qs", lambda e, i=i: e.dma_start(out=out_d[128 * i:128 * (i + 1), :], in_=h[i][:]),
              reads=[h[i]])
    p.maybe_finish()
    return p


class GlueIdx:
    def __init__(self):
        self.cols = []

    def reg(self, fn):
        self.cols.append(fn)
        return len(self.cols) - 1

    def host(self, core):
        assert len(self.cols) <= 96, len(self.cols)
        a = np.full((128, 96), 2 ** 30, np.int32)
        for j, fn in enumerate(self.cols):
            v = np.asarray(fn(core), np.int32)
            a[:len(v), j] = v
        return a


def build_fused(devdump=False):
    p = Prog("fused")
    gidx = GlueIdx()
    RG = [list(range(NCORES))]
    nc = p.nc
    io = {}
    p.fused_io = io

    def internal(name, shape, dtype=F32):
        return nc.dram_tensor(name, list(shape), dtype, kind="Internal").ap()

    def output(name, shape, dtype=F32):
        t = nc.dram_tensor(name, list(shape), dtype, kind="ExternalOutput").ap()
        p.outputs[name] = t
        return t

    mag2_l = internal("x_mag2", [1024, TOWN]); theta_l = internal("x_theta", [1024, TOWN])
    mag2_o = internal("x_mag2o", [1024, TOWN]); theta_o = internal("x_thetao", [1024, TOWN])
    r4_l = internal("x_r4", [D, TOWN])
    kvre_s = internal("x_kvres", [1024, TOWN], BF16)
    kvim_s = internal("x_kvims", [1024, TOWN], BF16)
    kv_c = internal("x_kvc", [D, TOWN], BF16)
    h1_l = internal("x_h1", [D, TOWN]); h1b_l = internal("x_h1b", [D, TOWN], BF16)
    h1b_ag = internal("x_h1bag", [8 * D, TOWN], BF16)
    wmat_l = internal("x_wmat", [TOWN, NEXP]); posl_l = internal("x_posl", [TOWN, NEXP])
    cnt_l = internal("x_cnt", [1, NEXP])
    wall_ag = internal("x_wallag", [T, NEXP]); posall_ag = internal("x_posallag", [T, NEXP])
    cnt_ag = internal("x_cntag", [8, NEXP])
    rs_in = internal("x_rsin", [8 * D, TOWN], BF16)
    rs_out = internal("x_rsout", [D, TOWN], BF16)
    h2_l = internal("x_h2", [D, TOWN])
    hb_f = internal("x_hbf", [D, TOWN + 3], BF16)
    gsil_l = internal("x_gsil", [INTER, TOWN], BF16)
    yt_l = internal("x_yt", [INTER, TOWN], BF16)
    cwd_l = internal("x_cwd", [(TOWN // CS) * NN, NH * CS], BF16)
    sacc_l = internal("x_sacc", [NN, NH * HP], BF16)
    sacc_ag = internal("x_saccag", [8 * NN, NH * HP], BF16)
    espan_l = internal("x_espan", [TOWN // CS, NH])
    esptot_l = internal("x_esptot", [1, NH])
    esptot_ag = internal("x_esptotag", [8, NH])
    bnd_ag = internal("x_bndag", [8 * D, 4], BF16)
    h3_l = internal("x_h3", [D, TOWN]); h3b_l = internal("x_h3b", [D, TOWN], BF16)
    qt_l = internal("x_qt", [D, TOWN], BF16)
    q2_o = internal("x_q2o", [D, TOWN], BF16)
    ksh_ag = internal("x_kshag", [T, HD], BF16); vsh_ag = internal("x_vshag", [T, HD], BF16)
    ksh_l = internal("x_kshl", [TOWN, HD], BF16); vsh_l = internal("x_vshl", [TOWN, HD], BF16)
    kraw_l = internal("x_krawl", [NHEAD, TOWN])
    kraw_ag = internal("x_krawag", [8 * NHEAD, TOWN])
    kraw_m = internal("x_krawm", [2, T])
    y2_s = internal("x_y2s", [D, TOWN], BF16)
    ky_m = internal("x_kym", [D, TOWN], BF16)
    out_f = output("outT", [D, TOWN])

    io[("a", "mag2")] = mag2_l; io[("a", "theta")] = theta_l; io[("a", "r4")] = r4_l
    io[("b", "mag2")] = mag2_o[:].rearrange("(c r) t -> r c t", c=8)
    io[("b", "theta")] = theta_o[:].rearrange("(c r) t -> r c t", c=8)
    io[("b", "kvre")] = kvre_s[:].rearrange("(c r) t -> r c t", c=8)
    io[("b", "kvim")] = kvim_s[:].rearrange("(c r) t -> r c t", c=8)
    io[("c", "kv")] = kv_c; io[("c", "r4")] = r4_l
    io[("c", "h1")] = h1_l; io[("c", "h1b")] = h1b_l
    io[("d", "h1")] = h1_l
    io[("d", "wmat")] = wmat_l; io[("d", "posl")] = posl_l; io[("d", "cnt")] = cnt_l
    io[("e", "xall")] = h1b_ag[:].rearrange("(c r) t -> r c t", c=8)
    io[("e", "wall")] = wall_ag; io[("e", "posall")] = posall_ag; io[("e", "cnt8")] = cnt_ag
    io[("e", "part")] = rs_in[:].rearrange("(c r) t -> r c t", c=8)
    io[("f", "pj")] = rs_out; io[("f", "h1")] = h1_l
    io[("f", "h2")] = h2_l; io[("f", "h2b")] = hb_f[:, 3:TOWN + 3]
    io[("g", "hb")] = hb_f
    io[("g", "gsil")] = gsil_l; io[("g", "yt")] = yt_l; io[("g", "cw")] = cwd_l
    io[("g", "sacc")] = sacc_l; io[("g", "espan")] = espan_l; io[("g", "espantot")] = esptot_l
    io[("h", "ytin")] = yt_l; io[("h", "cw")] = cwd_l; io[("h", "gsil")] = gsil_l
    io[("h", "h2")] = h2_l; io[("h", "sacc8")] = sacc_ag
    io[("h", "esptot8")] = esptot_ag[:].rearrange("a b -> (a b)").unsqueeze(0)
    io[("h", "espmy")] = espan_l[:].rearrange("a b -> (a b)").unsqueeze(0)
    io[("h", "h3")] = h3_l; io[("h", "h3b")] = h3b_l
    io[("i", "h3")] = h3_l; io[("i", "h3b")] = h3b_l
    io[("i", "qt")] = qt_l; io[("i", "ksh")] = ksh_l; io[("i", "vsh")] = vsh_l
    io[("i", "kraw")] = kraw_l
    io[("j", "q2")] = q2_o[:].rearrange("(c r) t -> r c t", c=8)
    io[("j", "ksh")] = ksh_ag; io[("j", "vsh")] = vsh_ag
    io[("j", "kraw")] = kraw_m
    io[("j", "y2")] = y2_s[:].rearrange("(c r) t -> r c t", c=8)
    io[("k", "y")] = ky_m; io[("k", "h3")] = h3_l
    io[("k", "outT")] = out_f

    p.tag = "z"
    p.start()
    gidx_d = nc.dram_tensor("gidx", [128, 96], I32, kind="ExternalInput").ap()
    p.inputs["gidx"] = gidx_d
    gix = p.sb("z_gix", [128, 96], I32)
    p.dma("qs", lambda e: e.dma_start(out=gix[:], in_=gidx_d[:]), writes=[gix])
    gbuf = p.sb("z_gbuf", [128, 8], BF16)
    ixb = p.sb("z_ixb", [128, 1], I32)

    def ag(src_ap, out_ap, note):
        p.collective(lambda e: e.collective_compute(
            "AllGather", ALU.bypass, replica_groups=RG, ins=[src_ap], outs=[out_ap]))
        p.fence()

    def a2a(src_ap, out_ap, note):
        def go(e):
            with allow_ici_all_to_all():
                return e.collective_compute(
                    "AllToAll", ALU.bypass, replica_groups=RG, ins=[src_ap], outs=[out_ap])
        p.collective(go)
        p.fence()

    with p.phase("a"):
        build_phase_a(p=p)
    a2a(mag2_l, mag2_o, "mag2")
    a2a(theta_l, theta_o, "theta")
    with p.phase("b"):
        build_phase_b(p=p)
    a2a(kvre_s, kv_c[0:1024, :], "kvre")
    a2a(kvim_s, kv_c[1024:2048, :], "kvim")
    with p.phase("c"):
        build_phase_c(p=p)
    p.collective(lambda e: e.collective_compute(
        "AllGather", ALU.bypass, replica_groups=RG, ins=[h1b_l], outs=[h1b_ag]))
    with p.phase("d"):
        build_phase_d(p=p)
    ag(wmat_l, wall_ag, "wmat")
    ag(posl_l, posall_ag, "posl")
    ag(cnt_l, cnt_ag, "cnt")
    with p.phase("e"):
        build_phase_e(p=p)
    p.collective(lambda e: e.collective_compute(
        "ReduceScatter", ALU.add, replica_groups=RG, ins=[rs_in], outs=[rs_out]))
    p.fence()
    with p.phase("f"):
        build_phase_f(p=p, nparts=1)
    with p.phase("x3"):
        bl = p.sb("x3_bl", [128, 64], BF16)
        bnd_l = internal("x_bndl", [D, 4], BF16)
        mtok = p.op("v", lambda e: e.memset(bl[:], 0.0), writes=[bl])
        for ch in range(16):
            p.dma("qs", lambda e, ch=ch: e.dma_start(
                out=bl[:, 4 * ch:4 * ch + 3], in_=hb_f[128 * ch:128 * (ch + 1), TOWN:TOWN + 3]),
                writes=[f"x3bl{ch}"], extra=[mtok])
        for ch in range(16):
            p.dma("qs", lambda e, ch=ch: e.dma_start(
                out=bnd_l[128 * ch:128 * (ch + 1), :], in_=bl[:, 4 * ch:4 * (ch + 1)]),
                reads=[f"x3bl{ch}"])
        p.fence()
    ag(bnd_l, bnd_ag, "bnd")
    with p.phase("x4"):
        for ch in range(16):
            col = gidx.reg(lambda core, ch=ch: ((core - 1) * D + 128 * ch + np.arange(128))
                           if core > 0 else np.full(128, 2 ** 30))
            p.op("v", lambda e, col=col: e.tensor_copy(out=ixb[:], in_=gix[:, col:col + 1]),
                 reads=[gix], writes=[ixb])
            p.op("v", lambda e: e.memset(gbuf[0:128, 0:4], 0.0), writes=[gbuf])
            p.dma("qg", lambda e: e.indirect_dma_start(
                out=gbuf[0:128, 0:4],
                out_offset=None, in_=bnd_ag[:],
                in_offset=IndirectOffsetOnAxis(ap=ixb[:], axis=0),
                bounds_check=8 * D - 1, oob_is_err=False),
                reads=[ixb], writes=[gbuf])
            p.dma("qs", lambda e, ch=ch: e.dma_start(
                out=hb_f[128 * ch:128 * (ch + 1), 0:3], in_=gbuf[0:128, 0:3]),
                reads=[gbuf])
            p.fence()
    with p.phase("g"):
        build_phase_g(p=p)
    ag(sacc_l, sacc_ag, "sacc")
    ag(esptot_l, esptot_ag, "esptot")
    with p.phase("h"):
        build_phase_h(p=p)
    with p.phase("i"):
        build_phase_i(p=p)
    a2a(qt_l, q2_o, "qt")
    ag(ksh_l, ksh_ag, "ksh")
    ag(vsh_l, vsh_ag, "vsh")
    ag(kraw_l, kraw_ag, "kraw")
    with p.phase("x5"):
        krb = p.sb("x5_krb", [16, 1024])
        colk = gidx.reg(lambda core: np.array(
            [c * NHEAD + 2 * core + hloc for hloc in range(2) for c in range(8)], np.int32))
        p.op("v", lambda e: e.tensor_copy(out=ixb[0:16, :], in_=gix[0:16, colk:colk + 1]),
             reads=[gix], writes=[ixb])
        p.op("v", lambda e: e.memset(krb[:], 0.0), writes=[krb])
        p.dma("qg", lambda e: e.indirect_dma_start(
            out=krb[:], out_offset=None, in_=kraw_ag[:],
            in_offset=IndirectOffsetOnAxis(ap=ixb[0:16, :], axis=0),
            bounds_check=8 * NHEAD - 1, oob_is_err=False),
            reads=[ixb], writes=[krb])
        p.dma("qs", lambda e: e.dma_start(
            out=kraw_m[:].rearrange("h (c t) -> (h c) t", c=8), in_=krb[:]), reads=[krb])
        p.fence()
    with p.phase("j"):
        build_phase_j(p=p)
    a2a(y2_s, ky_m, "y2")
    with p.phase("k"):
        build_phase_k(p=p)
    p.finish()
    return p, gidx


FUSED = True


def _host_prep(inputs):
    x = _f32(inputs["L0_attn_x"][0])
    xT = _f32(x.T)
    xT_halo = np.concatenate([np.zeros((D, 1), np.float32), xT], axis=1)
    perm = np.concatenate([np.arange(0, D, 2), np.arange(1, D, 2)])
    WA = _f32(inputs["L0_attn_W_act"].T[:, perm])
    WR = _f32(inputs["L0_attn_W_rec"].T)
    WAB = np.concatenate([WA, WR], axis=1)
    WV = _f32(inputs["L0_attn_W_val"].T[perm, :])
    ones_col = np.ones((128, 1), np.float32)
    TRI = np.triu(np.ones((128, 128), np.float32), 1)
    CMASK = np.triu(np.ones((8, 8), np.float32), 1)
    EYE = np.eye(128, dtype=np.float32)
    ONESR = np.ones((1, 512), np.float32)
    ipw = _f32(inputs["L1_attn_in_proj_w"])
    DMASK = np.zeros((4 * 128, 512), np.float32)
    for jj in range(4):
        tk = np.arange(128)[:, None] + 128 * jj
        tq = np.arange(512)[None, :]
        DMASK[128 * jj:128 * (jj + 1)] = (tk <= tq).astype(np.float32)
    return dict(
        xT=xT, xT_halo=xT_halo, WAB=WAB, WV=WV, ones=ones_col, TRI=TRI,
        CMASK=CMASK, EYE=EYE, ONESR=ONESR, DMASK=DMASK,
        gw=_f32(inputs["L0_ffn_gate_w"]),
        wg_all=_f32(inputs["L0_ffn_w_gate"]), wu_all=_f32(inputs["L0_ffn_w_up"]),
        wd_all=_f32(inputs["L0_ffn_w_down"]),
        wpg=ipw[:, :INTER], wpx=ipw[:, INTER:INTER + CONV_DIM],
        wpd=ipw[:, INTER + CONV_DIM:],
        convw=_f32(inputs["L1_attn_conv_w"]),
        convbt=_f32(inputs["L1_attn_conv_b"])[None, :],
        dtb=_f32(inputs["L1_attn_dt_bias"])[:, None],
        alog=_f32(inputs["L1_attn_A_log"])[:, None],
        deye=_f32(np.repeat(inputs["L1_attn_D"][None, :], 128, axis=0)),
        nw=_f32(inputs["L1_attn_norm_weight"].reshape(INTER // 128, 128).T),
        wop=_f32(inputs["L1_attn_out_proj_w"]),
        wgt=_f32(inputs["L1_ffn_W_gate"].T),
        wqkvt=_f32(inputs["L1_ffn_W_qkv"].T))


def kernel_fused(**inputs):
    hp = _host_prep(inputs)
    p, gidx = build_fused()
    in_maps = []
    for i in range(NCORES):
        jm = np.zeros((1, 8), np.float32)
        jm[0, :i] = 1.0
        m = {
            "a_in_xs": np.ascontiguousarray(hp["xT_halo"][:, TOWN * i:TOWN * (i + 1) + 1]),
            "a_in_wab": hp["WAB"],
            "c_in_xs": np.ascontiguousarray(hp["xT"][:, TOWN * i:TOWN * (i + 1)]),
            "c_in_wv": hp["WV"], "c_in_ones": hp["ones"],
            "d_in_gw": hp["gw"], "d_in_tri": hp["TRI"], "d_in_ones": hp["ones"],
            "e_in_wg": hp["wg_all"][4 * i:4 * i + 4].reshape(4 * D, 768),
            "e_in_wu": hp["wu_all"][4 * i:4 * i + 4].reshape(4 * D, 768),
            "e_in_wd": hp["wd_all"][4 * i:4 * i + 4].reshape(4 * 768, D),
            "e_in_cmask": hp["CMASK"], "e_in_ones": hp["ones"], "e_in_eye": hp["EYE"],
            "e_in_esel": np.arange(4 * i, 4 * i + 4, dtype=np.int32)[:, None],
            "f_in_ones": hp["ones"],
            "g_in_wpg": hp["wpg"], "g_in_wpx": hp["wpx"], "g_in_wpd": hp["wpd"],
            "g_in_convw": hp["convw"], "g_in_convbt": hp["convbt"],
            "g_in_onesr": hp["ONESR"], "g_in_dtb": hp["dtb"], "g_in_alog": hp["alog"],
            "g_in_deye": hp["deye"], "g_in_eye": hp["EYE"], "g_in_ones": hp["ones"],
            "h_in_jmask": jm, "h_in_nw": hp["nw"], "h_in_wop": hp["wop"],
            "h_in_ones": hp["ones"],
            "i_in_wgt": hp["wgt"], "i_in_wqkvt": hp["wqkvt"], "i_in_ones": hp["ones"],
            "j_in_dmask": hp["DMASK"], "j_in_ones": hp["ones"],
            "k_in_ones": hp["ones"],
            "cst": p.cst_host(),
            "gidx": gidx.host(i),
        }
        in_maps.append(m)
    res = run_bass_kernel_spmd(p.nc, in_maps, core_ids=list(range(NCORES))).results
    outT = np.concatenate([np.asarray(r["outT"]) for r in res], axis=1)
    return np.ascontiguousarray(outT.T)[None]


def kernel(**inputs):
    if FUSED:
        return kernel_fused(**inputs)
    return kernel_unfused(**inputs)


def kernel_unfused(**inputs):
    x = _f32(inputs["L0_attn_x"][0])
    xT = _f32(x.T)
    xT_halo = np.concatenate([np.zeros((D, 1), np.float32), xT], axis=1)

    perm = np.concatenate([np.arange(0, D, 2), np.arange(1, D, 2)])
    WA = _f32(inputs["L0_attn_W_act"].T[:, perm])
    WR = _f32(inputs["L0_attn_W_rec"].T)
    WAB = np.concatenate([WA, WR], axis=1)
    WV = _f32(inputs["L0_attn_W_val"].T[perm, :])
    ones_col = np.ones((128, 1), np.float32)

    pa = build_phase_a()
    in_maps = [dict(xs=xT_halo[:, TOWN * i: TOWN * (i + 1) + 1], wab=WAB, cst=pa.cst_host())
               for i in range(NCORES)]
    ra = run_bass_kernel_spmd(pa.nc, in_maps, core_ids=list(range(NCORES))).results

    mag2 = np.concatenate([np.asarray(r["mag2"]) for r in ra], axis=1)
    theta = np.concatenate([np.asarray(r["theta"]) for r in ra], axis=1)
    r4 = np.concatenate([np.asarray(r["r4"]) for r in ra], axis=1)

    pb = build_phase_b()
    in_maps = [dict(mag2=mag2[128 * i:128 * (i + 1)], theta=theta[128 * i:128 * (i + 1)],
                    cst=pb.cst_host()) for i in range(NCORES)]
    rb = run_bass_kernel_spmd(pb.nc, in_maps, core_ids=list(range(NCORES))).results
    kvre = np.concatenate([np.asarray(r["kvre"]) for r in rb], axis=0)
    kvim = np.concatenate([np.asarray(r["kvim"]) for r in rb], axis=0)
    kv_perm = np.concatenate([kvre, kvim], axis=0)

    pc = build_phase_c()
    in_maps = [dict(kv=kv_perm[:, TOWN * i:TOWN * (i + 1)],
                    r4=r4[:, TOWN * i:TOWN * (i + 1)],
                    xs=xT[:, TOWN * i:TOWN * (i + 1)],
                    wv=WV, ones=ones_col, cst=pc.cst_host()) for i in range(NCORES)]
    rc = run_bass_kernel_spmd(pc.nc, in_maps, core_ids=list(range(NCORES))).results
    h1 = np.concatenate([np.asarray(r["h1"]) for r in rc], axis=1)
    h1b = np.concatenate([np.asarray(r["h1b"]) for r in rc], axis=1)

    TRI = np.triu(np.ones((128, 128), np.float32), 1)
    pd = build_phase_d()
    in_maps = [dict(h1=h1[:, TOWN * i:TOWN * (i + 1)], gw=_f32(inputs["L0_ffn_gate_w"]),
                    tri=TRI, ones=ones_col, cst=pd.cst_host()) for i in range(NCORES)]
    rd = run_bass_kernel_spmd(pd.nc, in_maps, core_ids=list(range(NCORES))).results
    wall = np.concatenate([np.asarray(r["wmat"]) for r in rd], axis=0)
    posall = np.concatenate([np.asarray(r["posl"]) for r in rd], axis=0)
    cnt8 = np.concatenate([np.asarray(r["cnt"]) for r in rd], axis=0)

    pe = build_phase_e()
    CMASK = np.triu(np.ones((8, 8), np.float32), 1)
    EYE = np.eye(128, dtype=np.float32)
    wg_all = _f32(inputs["L0_ffn_w_gate"])
    wu_all = _f32(inputs["L0_ffn_w_up"])
    wd_all = _f32(inputs["L0_ffn_w_down"])
    in_maps = []
    for i in range(NCORES):
        in_maps.append(dict(
            xall=np.ascontiguousarray(h1b), wg=wg_all[4 * i:4 * i + 4].reshape(4 * D, 768),
            wu=wu_all[4 * i:4 * i + 4].reshape(4 * D, 768),
            wd=wd_all[4 * i:4 * i + 4].reshape(4 * 768, D),
            wall=wall, posall=posall, cnt8=cnt8,
            esel=np.arange(4 * i, 4 * i + 4, dtype=np.int32)[:, None],
            cmask=CMASK, ones=ones_col, eye=EYE, cst=pe.cst_host()))
    re_ = run_bass_kernel_spmd(pe.nc, in_maps, core_ids=list(range(NCORES))).results
    parts = [np.asarray(r["part"]) for r in re_]

    pf = build_phase_f()
    in_maps = []
    for i in range(NCORES):
        pj = np.concatenate([parts[j][:, TOWN * i:TOWN * (i + 1)] for j in range(8)], axis=0)
        in_maps.append(dict(pj=pj, h1=h1[:, TOWN * i:TOWN * (i + 1)], ones=ones_col,
                            cst=pf.cst_host()))
    rf = run_bass_kernel_spmd(pf.nc, in_maps, core_ids=list(range(NCORES))).results
    h2 = np.concatenate([np.asarray(r["h2"]) for r in rf], axis=1)
    h2b = np.concatenate([np.asarray(r["h2b"]) for r in rf], axis=1)

    pg = build_phase_g()
    ipw = _f32(inputs["L1_attn_in_proj_w"])
    wpg_h, wpx_h, wpd_h = ipw[:, :INTER], ipw[:, INTER:INTER + CONV_DIM], ipw[:, INTER + CONV_DIM:]
    convw_h = _f32(inputs["L1_attn_conv_w"])
    convbt_h = _f32(inputs["L1_attn_conv_b"])[None, :]
    dtb_h = _f32(inputs["L1_attn_dt_bias"])[:, None]
    alog_h = _f32(inputs["L1_attn_A_log"])[:, None]
    deye_h = _f32(np.repeat(inputs["L1_attn_D"][None, :], 128, axis=0))
    EYE = np.eye(128, dtype=np.float32)
    ONESR = np.ones((1, 512), np.float32)
    h2b_halo = np.concatenate([np.zeros((D, 3), ml_dtypes.bfloat16), h2b], axis=1)
    in_maps = []
    for i in range(NCORES):
        in_maps.append(dict(
            hb=np.ascontiguousarray(h2b_halo[:, TOWN * i:TOWN * (i + 1) + 3]),
            wpg=wpg_h, wpx=wpx_h, wpd=wpd_h, convw=convw_h, convbt=convbt_h,
            onesr=ONESR, dtb=dtb_h, alog=alog_h, deye=deye_h, eye=EYE,
            ones=ones_col, cst=pg.cst_host()))
    rg = run_bass_kernel_spmd(pg.nc, in_maps, core_ids=list(range(NCORES))).results

    ph = build_phase_h()
    sacc8 = np.concatenate([np.asarray(r["sacc"]) for r in rg], axis=0)
    esptot8 = np.concatenate([np.asarray(r["espantot"]) for r in rg], axis=1)
    nw_h = _f32(inputs["L1_attn_norm_weight"].reshape(INTER // 128, 128).T)
    wop_h = _f32(inputs["L1_attn_out_proj_w"])
    in_maps = []
    for i in range(NCORES):
        jm = np.zeros((1, 8), np.float32)
        jm[0, :i] = 1.0
        in_maps.append(dict(
            ytin=np.asarray(rg[i]["yt"]), cw=np.asarray(rg[i]["cw"]),
            gsil=np.asarray(rg[i]["gsil"]), h2=h2[:, TOWN * i:TOWN * (i + 1)],
            sacc8=sacc8, esptot8=esptot8,
            espmy=np.asarray(rg[i]["espan"]).reshape(1, -1),
            jmask=jm, nw=nw_h, wop=wop_h, ones=ones_col, cst=ph.cst_host()))
    rh = run_bass_kernel_spmd(ph.nc, in_maps, core_ids=list(range(NCORES))).results
    h3 = np.concatenate([np.asarray(r["h3"]) for r in rh], axis=1)
    h3b = np.concatenate([np.asarray(r["h3b"]) for r in rh], axis=1)

    pi = build_phase_i()
    WGT = _f32(inputs["L1_ffn_W_gate"].T)
    WQKVT = _f32(inputs["L1_ffn_W_qkv"].T)
    in_maps = [dict(h3=h3[:, TOWN * i:TOWN * (i + 1)], h3b=h3b[:, TOWN * i:TOWN * (i + 1)],
                    wgt=WGT, wqkvt=WQKVT, ones=ones_col, cst=pi.cst_host())
               for i in range(NCORES)]
    ri = run_bass_kernel_spmd(pi.nc, in_maps, core_ids=list(range(NCORES))).results
    qt_full = np.concatenate([np.asarray(r["qt"]) for r in ri], axis=1)
    ksh_full = np.concatenate([np.asarray(r["ksh"]) for r in ri], axis=0)
    vsh_full = np.concatenate([np.asarray(r["vsh"]) for r in ri], axis=0)
    kraw_full = np.concatenate([np.asarray(r["kraw"]) for r in ri], axis=1)

    pj = build_phase_j()
    DMASK = np.zeros((4 * 128, 512), np.float32)
    for jj in range(4):
        tk = np.arange(128)[:, None] + 128 * jj
        tq = np.arange(512)[None, :]
        DMASK[128 * jj:128 * (jj + 1)] = (tk <= tq).astype(np.float32)
    in_maps = []
    for i in range(NCORES):
        q2 = np.concatenate([qt_full[128 * (2 * i):128 * (2 * i + 1)],
                             qt_full[128 * (2 * i + 1):128 * (2 * i + 2)]], axis=0)
        in_maps.append(dict(q2=np.ascontiguousarray(q2), ksh=ksh_full, vsh=vsh_full,
                            kraw=np.ascontiguousarray(kraw_full[2 * i:2 * i + 2]),
                            dmask=DMASK, ones=ones_col, cst=pj.cst_host()))
    rj = run_bass_kernel_spmd(pj.nc, in_maps, core_ids=list(range(NCORES))).results
    ycat = np.concatenate([np.asarray(r["y2"]) for r in rj], axis=0)

    pk = build_phase_k()
    in_maps = [dict(y=np.ascontiguousarray(ycat[:, TOWN * i:TOWN * (i + 1)]),
                    h3=h3[:, TOWN * i:TOWN * (i + 1)], ones=ones_col, cst=pk.cst_host())
               for i in range(NCORES)]
    rk = run_bass_kernel_spmd(pk.nc, in_maps, core_ids=list(range(NCORES))).results
    outT = np.concatenate([np.asarray(r["outT"]) for r in rk], axis=1)
    return np.ascontiguousarray(outT.T)[None]


if __name__ == "__main__":
    pass


# version 11
# speedup vs baseline: 1.2847x; 1.0042x over previous
import math
import types as _types
from contextlib import ExitStack


def _freeze(fn):
    if fn.__closure__ is None:
        return fn
    frozen = tuple(_types.CellType(c.cell_contents) for c in fn.__closure__)
    g = _types.FunctionType(fn.__code__, fn.__globals__, fn.__name__,
                            fn.__defaults__, frozen)
    g.__kwdefaults__ = fn.__kwdefaults__
    return g

import numpy as np
import ml_dtypes

import concourse.bass as bass
import concourse.mybir as mybir
from concourse.bass_utils import run_bass_kernel_spmd
from concourse.bass import IndirectOffsetOnAxis
from concourse.aot_env import allow_ici_all_to_all

F32 = mybir.dt.float32
BF16 = mybir.dt.bfloat16
I32 = mybir.dt.int32
AF = mybir.ActivationFunctionType
ALU = mybir.AluOpType
AX = mybir.AxisListType

NCORES = 8
T = 8192
D = 2048
TOWN = T // NCORES

PI = math.pi
TWO_PI = 2.0 * math.pi
INV_2PI = 1.0 / TWO_PI


class Prog:
    NQSEM = 8

    def __init__(self, name="prog"):
        self.nc = bass.Bass("TRN2", target_bir_lowering=False, debug=False)
        self.ctx = ExitStack()
        self.started = False
        self.inputs = {}
        self.outputs = {}
        self.name = name
        self.fused_io = None
        self.tag = ""
        self._scope_stack = []

    def din(self, name, shape, dtype=F32):
        if self.fused_io is not None:
            ap = self.fused_io.get((self.tag, name))
            if ap is not None:
                return ap
            qn = self.tag + "_in_" + name
            t = self.nc.dram_tensor(qn, list(shape), dtype, kind="ExternalInput").ap()
            self.inputs[qn] = t
            return t
        t = self.nc.dram_tensor(name, list(shape), dtype, kind="ExternalInput").ap()
        self.inputs[name] = t
        return t

    def dout(self, name, shape, dtype=F32):
        if self.fused_io is not None:
            ap = self.fused_io.get((self.tag, name))
            assert ap is not None, f"fused dout {self.tag}/{name} not wired"
            return ap
        t = self.nc.dram_tensor(name, list(shape), dtype, kind="ExternalOutput").ap()
        self.outputs[name] = t
        return t

    def dtmp(self, name, shape, dtype=F32):
        return self.nc.dram_tensor(self.tag + "_" + name if self.fused_io is not None
                                   else name, list(shape), dtype, kind="Internal").ap()

    def start(self):
        if self.started:
            return
        self.started = True
        self.const_values = []
        self._cst_d = self.nc.dram_tensor("cst", [128, 16], F32, kind="ExternalInput").ap()
        self.inputs["cst"] = self._cst_d
        E = self.ctx.enter_context
        self.sems = {}
        self.cnt = {}
        for k in ("v", "s", "t", "g", "cc"):
            self.sems[k] = E(self.nc.semaphore("sem_" + k))
            self.cnt[k] = 0
        for q in ("qs", "qg"):
            for i in range(self.NQSEM):
                k = f"{q}{i}"
                self.sems[k] = E(self.nc.semaphore("sem_" + k))
                self.cnt[k] = 0
        self.qrr = {"qs": 0, "qg": 0}
        self._pending = {}
        self._tcells = []
        self._t_last_inc = -1
        self.block = E(self.nc.Block())
        self.engmap = {"v": "vector", "s": "scalar", "t": "tensor", "g": "gpsimd",
                       "qs": "sync", "qg": "gpsimd", "cc": "gpsimd"}
        self.watermark = {}
        self.writer = {}
        self.readers = {}
        self.nops = 0
        self.global_extra = []
        self._cst_s = self.sb("cst_s", [128, 16])
        self.dma("qs", lambda e: e.dma_start(out=self._cst_s[:], in_=self._cst_d[:]),
                 writes=[self._cst_s])

    def constcell(self, value, parts=128):
        value = float(value)
        if value in self.const_values:
            j = self.const_values.index(value)
        else:
            j = len(self.const_values)
            assert j < 16
            self.const_values.append(value)
        return self._cst_s[0:parts, j:j + 1]

    def cst_host(self):
        a = np.zeros((128, 16), np.float32)
        for j, v in enumerate(self.const_values):
            a[:, j] = v
        return a

    def sb(self, name, shape, dtype=F32):
        ctx = self._scope_stack[-1] if self._scope_stack else self.ctx
        return ctx.enter_context(self.nc.sbuf_tensor(name, list(shape), dtype))

    def ps(self, name, shape, dtype=F32):
        ctx = self._scope_stack[-1] if self._scope_stack else self.ctx
        return ctx.enter_context(self.nc.psum_tensor(name, list(shape), dtype))

    def phase(self, tag):
        from contextlib import contextmanager

        @contextmanager
        def _cm():
            self.tag = tag
            sc = ExitStack()
            self._scope_stack.append(sc)
            try:
                yield
            except BaseException:
                self._scope_stack.pop()
                self.tag = ""
                raise
            self._scope_stack.pop()
            sc.close()
            self.fence()
            self.tag = ""
        return _cm()

    def maybe_finish(self):
        if self.fused_io is None:
            self.finish()

    def _resolve_t(self, idx):
        if idx > self._t_last_inc:
            cell = self._tcells[-1]
            if not cell["inc"]:
                cell["inc"] = True
                self.cnt["t"] += 1
            self._t_last_inc = len(self._tcells) - 1
        return ("t", self.cnt["t"])

    def _deps(self, reads, writes, extra, consumer_ename=None):
        need = {}

        def add(tok):
            if tok is None:
                return
            if tok[0] == "t#":
                if consumer_ename == "tensor":
                    return
                tok = self._resolve_t(tok[1])
            k, v = tok
            if need.get(k, 0) < v:
                need[k] = v

        def key(b):
            return b if isinstance(b, str) else id(b)
        for b in reads:
            add(self.writer.get(key(b)))
        for b in writes:
            add(self.writer.get(key(b)))
            for tok in self.readers.get(key(b), ()):
                add(tok)
        for tok in extra:
            add(tok)
        return need

    def fence(self):
        self.global_extra = self.barrier_token()

    def _emit(self, ek, semkey, inc, build, reads, writes, extra):
        build = _freeze(build)
        ename = self.engmap[ek]
        need = self._deps(reads, writes, list(extra) + self.global_extra,
                          consumer_ename=ename)
        sems = self.sems
        wm = self.watermark

        waits = []
        for k, v in need.items():
            key = (ename, k)
            if wm.get(key, 0) >= v:
                continue
            wm[key] = v
            waits.append((k, v))

        if ek == "t":
            cell = {"inc": False}
            self._tcells.append(cell)
            tsem = sems["t"]

            def go(e, cell=cell, waits=waits):
                for k, v in waits:
                    e.wait_ge(sems[k], v)
                ins = build(e)
                if cell["inc"]:
                    ins.then_inc(tsem, 1)

            self._pending.setdefault(ename, []).append(go)
            self.nops += 1
            tok = ("t#", len(self._tcells) - 1)
        else:
            def go(e, waits=waits):
                for k, v in waits:
                    e.wait_ge(sems[k], v)
                ins = build(e)
                ins.then_inc(sems[semkey], inc)

            self._pending.setdefault(ename, []).append(go)
            self.cnt[semkey] += inc
            self.nops += 1
            tok = (semkey, self.cnt[semkey])
        for b in writes:
            kb = b if isinstance(b, str) else id(b)
            self.writer[kb] = tok
            self.readers[kb] = []
        for b in reads:
            kb = b if isinstance(b, str) else id(b)
            self.readers.setdefault(kb, []).append(tok)
        return tok

    def op(self, ek, build, reads=(), writes=(), extra=()):
        assert ek in ("v", "s", "t", "g")
        return self._emit(ek, ek, 1, build, reads, writes, extra)

    def dma(self, q, build, reads=(), writes=(), extra=()):
        assert q in ("qs", "qg")
        i = self.qrr[q]
        self.qrr[q] = (i + 1) % self.NQSEM
        semkey = f"{q}{i}"
        extra = list(extra)
        if self.cnt[semkey] > 0:
            extra.append((semkey, self.cnt[semkey]))
        return self._emit(q, semkey, 16, build, reads, writes, extra)

    def collective(self, build, reads=(), writes=(), extra=()):
        extra = list(extra)
        if self.cnt["cc"] > 0:
            extra.append(("cc", self.cnt["cc"]))
        return self._emit("cc", "cc", 1, build, reads, writes, extra)

    def barrier_token(self, keys=None):
        if self._tcells and self._t_last_inc < len(self._tcells) - 1:
            self._resolve_t(len(self._tcells) - 1)
        keys = keys or list(self.cnt)
        return [(k, self.cnt[k]) for k in keys if self.cnt[k] > 0]

    def flush(self):
        for ename, closures in self._pending.items():
            def run_all(e, closures=closures):
                for c in closures:
                    c(e)
            getattr(self.block, ename)(run_all)
        self._pending = {}

    def finish(self):
        toks = self.barrier_token()
        sems = self.sems

        def go(e):
            for k, v in toks:
                key = ("gpsimd", k)
                if self.watermark.get(key, 0) >= v:
                    continue
                e.wait_ge(sems[k], v)

        self._pending.setdefault("gpsimd", []).append(go)
        self.flush()
        self.ctx.close()


def tt(p, out, a, b, alu, reads=None, writes=None, extra=()):
    return p.op("v", lambda e: e.tensor_tensor(out=out, in0=a, in1=b, op=alu),
                reads=reads, writes=writes, extra=extra)


def _bf(x):
    return np.ascontiguousarray(x, dtype=ml_dtypes.bfloat16)


def _f32(x):
    return np.ascontiguousarray(x, dtype=np.float32)


def build_phase_a(tw=TOWN, p=None):
    if p is None:
        p = Prog("phaseA")
    xs_d = p.din("xs", [D, tw + 1])
    wab_d = p.din("wab", [2 * D, 2 * D])
    mag2_d = p.dout("mag2", [D // 2, tw])
    theta_d = p.dout("theta", [D // 2, tw])
    r4_d = p.dout("r4", [D, tw])
    p.start()

    KC = 2 * D // 128
    CC = 2 * D // 128
    NT = tw // 512

    xb = [p.sb(f"xb{i}", [128, tw + 1], BF16) for i in range(D // 128)]
    for i in range(D // 128):
        p.dma("qg", lambda e, i=i: e.dma_start(out=xb[i][:], in_=xs_d[128 * i:128 * (i + 1), :]),
              writes=[xb[i]])

    NPS = 8 if tw >= 512 else 4
    pst = [p.ps(f"psA{i}", [128, 512 if tw >= 512 else tw]) for i in range(NPS)]
    wtile = [p.sb(f"wA{i}", [128, 512], BF16) for i in range(KC)]
    xv = [p.sb(f"xv{i}", [128, tw]) for i in range(D // 128)]
    rr = [p.sb(f"rr{i}", [128, tw]) for i in range(D // 128)]

    CG = 4
    for cg in range(CC // CG):
        for k in range(KC):
            wt = wtile[k]
            p.dma("qg", lambda e, k=k, cg=cg, wt=wt: e.dma_start(
                out=wt[:], in_=wab_d[128 * k:128 * (k + 1), 512 * cg:512 * (cg + 1)]),
                writes=[wt])
        for ci in range(CG):
            c = cg * CG + ci
            for nt in range(max(NT, 1)):
                nw = 512 if tw >= 512 else tw
                psu = pst[(ci * max(NT, 1) + nt) % NPS]
                for k in range(KC):
                    wt = wtile[k]
                    xi = k % (D // 128)
                    off = 0 if k < KC // 2 else 1
                    rhs = xb[xi][:, nt * nw + off: nt * nw + off + nw]
                    p.op("t", lambda e, psu=psu, wt=wt, rhs=rhs, k=k, ci=ci: e.matmul(
                        psu[:], wt[:, 128 * ci:128 * (ci + 1)], rhs,
                        start=(k == 0), stop=(k == KC - 1)),
                        reads=[wt, xb[xi]], writes=[psu])
                if c < CC // 2:
                    dst = xv[c]
                else:
                    dst = rr[c - CC // 2]
                seg = dst[:, nt * nw: (nt + 1) * nw]
                p.op("v", lambda e, psu=psu, seg=seg: e.tensor_scalar(
                    out=seg, in0=psu[:], scalar1=0.0, scalar2=None, op0=ALU.max),
                    reads=[psu], writes=[dst])
                p.op("s", lambda e, seg=seg: e.activation(out=seg, in_=seg, func=AF.Sigmoid),
                    reads=[dst], writes=[dst])

    for i in range(D // 128):
        p.op("v", lambda e, i=i: e.tensor_tensor(out=rr[i][:], in0=rr[i][:], in1=rr[i][:], op=ALU.mult),
             reads=[rr[i]], writes=[rr[i]])
        p.op("v", lambda e, i=i: e.tensor_tensor(out=rr[i][:], in0=rr[i][:], in1=rr[i][:], op=ALU.mult),
             reads=[rr[i]], writes=[rr[i]])
        p.dma("qs", lambda e, i=i: e.dma_start(out=r4_d[128 * i:128 * (i + 1), :], in_=rr[i][:]),
              reads=[rr[i]])

    NPAIR = D // 256
    for i in range(NPAIR):
        re, im = xv[i], xv[i + NPAIR]
        m2, t1, t2 = rr[(3 * i) % 16], rr[(3 * i + 1) % 16], rr[(3 * i + 2) % 16]
        tt(p, m2[:], re[:], re[:], ALU.mult, reads=[re], writes=[m2])
        tt(p, t1[:], im[:], im[:], ALU.mult, reads=[im], writes=[t1])
        tt(p, m2[:], m2[:], t1[:], ALU.add, reads=[m2, t1], writes=[m2])
        p.dma("qs", lambda e, i=i, m2=m2: e.dma_start(
            out=mag2_d[128 * i:128 * (i + 1), :], in_=m2[:]), reads=[m2])
        tt(p, t1[:], re[:], im[:], ALU.min, reads=[re, im], writes=[t1])
        tt(p, t2[:], re[:], im[:], ALU.max, reads=[re, im], writes=[t2])
        p.op("v", lambda e, t2=t2: e.reciprocal(out=t2[:], in_=t2[:]), reads=[t2], writes=[t2])
        tt(p, t1[:], t1[:], t2[:], ALU.mult, reads=[t1, t2], writes=[t1])
        p.op("s", lambda e, t1=t1: e.activation(out=t1[:], in_=t1[:], func=AF.Arctan),
             reads=[t1], writes=[t1])
        tt(p, t2[:], im[:], re[:], ALU.is_gt, reads=[im, re], writes=[t2])
        p.op("v", lambda e, t2=t2, re=re: e.tensor_scalar(
            out=re[:], in0=t2[:], scalar1=-2.0, scalar2=1.0, op0=ALU.mult, op1=ALU.add),
            reads=[t2], writes=[re])
        tt(p, t1[:], t1[:], re[:], ALU.mult, reads=[t1, re], writes=[t1])
        p.op("v", lambda e, t2=t2, t1=t1: e.scalar_tensor_tensor(
            out=t1[:], in0=t2[:], scalar=PI / 2, in1=t1[:], op0=ALU.mult, op1=ALU.add),
            reads=[t2, t1], writes=[t1])
        p.dma("qs", lambda e, i=i, t1=t1: e.dma_start(
            out=theta_d[128 * i:128 * (i + 1), :], in_=t1[:]), reads=[t1])

    p.maybe_finish()
    return p


def build_phase_b(tn=T, p=None):
    if p is None:
        p = Prog("phaseB")
    mag2_d = p.din("mag2", [128, tn])
    theta_d = p.din("theta", [128, tn])
    kvre_d = p.dout("kvre", [128, tn], BF16)
    kvim_d = p.dout("kvim", [128, tn], BF16)
    p.start()

    m2 = p.sb("s_m2", [128, tn])
    th = p.sb("s_th", [128, tn])
    bA = p.sb("s_bA", [128, tn])
    bB = p.sb("s_bB", [128, tn])
    bC = p.sb("s_bC", [128, tn])
    zz = p.sb("s_zz", [128, 1])
    zb = zz[:, 0:1].broadcast_to([128, tn])

    p.dma("qs", lambda e: e.dma_start(out=m2[:], in_=mag2_d[:]), writes=[m2])
    p.dma("qs", lambda e: e.dma_start(out=th[:], in_=theta_d[:]), writes=[th])
    p.op("v", lambda e: e.memset(zz[:], 0.0), writes=[zz])

    def scan(dst, src_, init, op0, reads, writes):
        p.op("v", lambda e: e.tensor_tensor_scan(out=dst, data0=src_, data1=zb,
                                                 initial=init, op0=op0, op1=ALU.add),
             reads=list(reads) + [zz], writes=writes)

    scan(bA[:], m2[:], 0.0, ALU.max, [m2], [bA])
    p.op("v", lambda e: e.reciprocal(out=bA[:], in_=bA[:]), reads=[bA], writes=[bA])
    tt(p, bA[:], bA[:], m2[:], ALU.mult, reads=[bA, m2], writes=[bA])
    scan(m2[:], bA[:], 1.0, ALU.mult, [bA], [m2])
    p.op("s", lambda e: e.activation(out=m2[:], in_=m2[:], func=AF.Sqrt),
         reads=[m2], writes=[m2])

    scan(bA[:], th[:], 0.0, ALU.add, [th], [bA])
    p.op("v", lambda e: e.tensor_scalar(out=th[:], in0=bA[:], scalar1=INV_2PI,
                                        scalar2=None, op0=ALU.mult),
         reads=[bA], writes=[th])
    ki = bB[:].bitcast(I32)
    p.op("v", lambda e: e.tensor_copy(out=ki, in_=th[:]), reads=[th], writes=[bB])
    p.op("v", lambda e: e.tensor_copy(out=bC[:], in_=ki), reads=[bB], writes=[bC])
    tt(p, th[:], th[:], bC[:], ALU.subtract, reads=[th, bC], writes=[th])
    p.op("v", lambda e: e.tensor_scalar(out=bB[:], in0=th[:], scalar1=0.5, scalar2=None,
                                        op0=ALU.is_gt), reads=[th], writes=[bB])
    tt(p, bC[:], bC[:], bB[:], ALU.add, reads=[bC, bB], writes=[bC])
    p.op("v", lambda e: e.tensor_scalar(out=bB[:], in0=th[:], scalar1=-0.5, scalar2=None,
                                        op0=ALU.is_lt), reads=[th], writes=[bB])
    tt(p, bC[:], bC[:], bB[:], ALU.subtract, reads=[bC, bB], writes=[bC])
    p.op("v", lambda e: e.scalar_tensor_tensor(out=bA[:], in0=bC[:], scalar=-TWO_PI,
                                               in1=bA[:], op0=ALU.mult, op1=ALU.add),
         reads=[bC, bA], writes=[bA])
    p.op("v", lambda e: e.tensor_scalar(out=bA[:], in0=bA[:], scalar1=PI, scalar2=-PI,
                                        op0=ALU.min, op1=ALU.max), reads=[bA], writes=[bA])
    p.op("s", lambda e: e.activation(out=bB[:], in_=bA[:], func=AF.Sin),
         reads=[bA], writes=[bB])
    p.op("s", lambda e: e.activation(out=bC[:], in_=bA[:], func=AF.Abs),
         reads=[bA], writes=[bC])
    p.op("s", lambda e: e.activation(out=bC[:], in_=bC[:], func=AF.Sin, scale=-1.0, bias=p.constcell(PI / 2)),
         reads=[bC, p._cst_s], writes=[bC])
    obre = th[:].bitcast(BF16)[:, 0:tn]
    obim = th[:].bitcast(BF16)[:, tn:2 * tn]
    tt(p, obre, m2[:], bC[:], ALU.mult, reads=[m2, bC], writes=[th])
    tt(p, obim, m2[:], bB[:], ALU.mult, reads=[m2, bB, th], writes=[th])
    p.dma("qs", lambda e: e.dma_start(out=kvre_d[:], in_=obre), reads=[th])
    p.dma("qs", lambda e: e.dma_start(out=kvim_d[:], in_=obim), reads=[th])

    p.maybe_finish()
    return p


def emit_matmul_fm(p, out_chunks, lhsT_tiles, rhs_tiles, nK, nM, nN, nw,
                   psum_pool, evac):
    for m in range(nM):
        for n in range(nN):
            psu = psum_pool[(m * nN + n) % len(psum_pool)]
            for k in range(nK):
                lt, lreads = lhsT_tiles(k, m)
                rt, rreads = rhs_tiles(k, n)
                p.op("t", lambda e, psu=psu, lt=lt, rt=rt, k=k: e.matmul(
                    psu[:], lt, rt, start=(k == 0), stop=(k == nK - 1)),
                    reads=list(lreads) + list(rreads), writes=[psu])
            evac(m, n, psu)


def emit_ln_fm(p, h_chunks, hb_chunks, ones_col, ones_b, psum_small, psum_bcast, scratch_rows,
               tw=TOWN, eps=1e-5, extra_first=()):
    nch = len(h_chunks)
    nd = nch * 128
    psum_s1, psum_s2 = psum_small
    psum_b1, psum_b2 = psum_bcast
    r_mu, r_ms, r_a, r_b = scratch_rows
    nw = min(tw, 512)
    ntt = (tw + nw - 1) // nw
    for h, hb in zip(h_chunks, hb_chunks):
        p.op("s", lambda e, h=h, hb=hb: e.activation(out=hb[:], in_=h[:], func=AF.Square),
             reads=[h], writes=[hb], extra=extra_first)
    for j in range(ntt):
        sl = slice(j * nw, (j + 1) * nw)
        for i, h in enumerate(h_chunks):
            p.op("t", lambda e, h=h, i=i, sl=sl: e.matmul(
                psum_s1[:], ones_col[:, 0:1], h[:, sl], start=(i == 0), stop=(i == nch - 1)),
                reads=[h, ones_col], writes=[psum_s1])
        for i, hb in enumerate(hb_chunks):
            p.op("t", lambda e, hb=hb, i=i, sl=sl: e.matmul(
                psum_s2[:], ones_b[:, 0:1], hb[:, sl], start=(i == 0), stop=(i == nch - 1)),
                reads=[hb, ones_b], writes=[psum_s2])
        p.op("v", lambda e, sl=sl: e.tensor_scalar(out=r_mu[0:1, sl], in0=psum_s1[:],
                                                   scalar1=1.0 / nd, scalar2=None, op0=ALU.mult),
             reads=[psum_s1], writes=[r_mu])
        p.op("v", lambda e, sl=sl: e.tensor_scalar(out=r_ms[0:1, sl], in0=psum_s2[:],
                                                   scalar1=1.0 / nd, scalar2=None, op0=ALU.mult),
             reads=[psum_s2], writes=[r_ms])
    tt(p, r_a[:], r_mu[:], r_mu[:], ALU.mult, reads=[r_mu], writes=[r_a])
    tt(p, r_ms[:], r_ms[:], r_a[:], ALU.subtract, reads=[r_ms, r_a], writes=[r_ms])
    p.op("v", lambda e: e.tensor_scalar(out=r_ms[:], in0=r_ms[:], scalar1=eps, scalar2=None,
                                        op0=ALU.add), reads=[r_ms], writes=[r_ms])
    p.op("s", lambda e: e.activation(out=r_ms[:], in_=r_ms[:], func=AF.Sqrt),
         reads=[r_ms], writes=[r_ms])
    p.op("v", lambda e: e.reciprocal(out=r_a[:], in_=r_ms[:]), reads=[r_ms], writes=[r_a])
    tt(p, r_b[:], r_mu[:], r_a[:], ALU.mult, reads=[r_mu, r_a], writes=[r_b])
    p.op("v", lambda e: e.tensor_scalar(out=r_b[:], in0=r_b[:], scalar1=-1.0, scalar2=None,
                                        op0=ALU.mult), reads=[r_b], writes=[r_b])
    ab = p.sb(f"ln_ab_{p.nops}", [128, tw])
    bb = p.sb(f"ln_bb_{p.nops}", [128, tw])
    onesrow = ones_col[0:1, 0:1].broadcast_to([1, 128])
    for j in range(ntt):
        sl = slice(j * nw, (j + 1) * nw)
        p.op("t", lambda e, sl=sl: e.matmul(psum_b1[:], onesrow, r_a[0:1, sl],
                                            start=True, stop=True),
             reads=[r_a, ones_col], writes=[psum_b1])
        p.op("t", lambda e, sl=sl: e.matmul(psum_b2[:], onesrow, r_b[0:1, sl],
                                            start=True, stop=True),
             reads=[r_b, ones_col], writes=[psum_b2])
        p.op("v", lambda e, sl=sl: e.tensor_copy(out=ab[:, sl], in_=psum_b1[:]),
             reads=[psum_b1], writes=[ab])
        p.op("v", lambda e, sl=sl: e.tensor_copy(out=bb[:, sl], in_=psum_b2[:]),
             reads=[psum_b2], writes=[bb])
    for h, hb in zip(h_chunks, hb_chunks):
        tt(p, h[:], h[:], ab[:], ALU.mult, reads=[h, ab], writes=[h])
        tt(p, h[:], h[:], bb[:], ALU.add, reads=[h, bb], writes=[h])
        p.op("v", lambda e, h=h, hb=hb: e.tensor_copy(out=hb[:], in_=h[:]),
             reads=[h], writes=[hb])


def build_phase_c(tw=TOWN, p=None):
    if p is None:
        p = Prog("phaseC")
    kv_d = p.din("kv", [D, tw], BF16)
    r4_d = p.din("r4", [D, tw])
    xs_d = p.din("xs", [D, tw])
    wv_d = p.din("wv", [D, D])
    ones_d = p.din("ones", [128, 1])
    h1_d = p.dout("h1", [D, tw])
    h1b_d = p.dout("h1b", [D, tw], BF16)
    p.start()

    NCH = D // 128
    kv = [p.sb(f"c_kv{i}", [128, tw], BF16) for i in range(NCH)]
    r4 = [p.sb(f"c_r4_{i}", [128, tw], BF16) for i in range(NCH)]
    h = [p.sb(f"c_h{i}", [128, tw]) for i in range(NCH)]
    hb = [p.sb(f"c_hb{i}", [128, tw], BF16) for i in range(NCH)]
    ones = p.sb("c_ones", [128, 1])
    p.dma("qs", lambda e: e.dma_start(out=ones[:], in_=ones_d[:]), writes=[ones])
    for i in range(NCH):
        p.dma("qs", lambda e, i=i: e.dma_start(out=kv[i][:], in_=kv_d[128 * i:128 * (i + 1), :]),
              writes=[kv[i]])
        p.dma("qg", lambda e, i=i: e.dma_start(out=r4[i][:], in_=r4_d[128 * i:128 * (i + 1), :]),
              writes=[r4[i]])
        p.dma("qs", lambda e, i=i: e.dma_start(out=h[i][:], in_=xs_d[128 * i:128 * (i + 1), :]),
              writes=[h[i]])

    NT = max(tw // 512, 1)
    nw = 512 if tw >= 512 else tw
    pst = [p.ps(f"psC{i}", [128, nw]) for i in range(4)]
    wtl = [p.sb(f"wC{i}", [128, 512], BF16) for i in range(D // 128)]
    CG = 4
    for cg in range(NCH // CG):
        for k in range(NCH):
            wt = wtl[k]
            p.dma("qg", lambda e, k=k, cg=cg, wt=wt: e.dma_start(
                out=wt[:], in_=wv_d[128 * k:128 * (k + 1), 512 * cg:512 * (cg + 1)]),
                writes=[wt])
        for ci in range(CG):
            c = cg * CG + ci
            for nt in range(NT):
                psu = pst[(ci * NT + nt) % 4]
                for k in range(NCH):
                    wt = wtl[k]
                    rhs = kv[k][:, nt * nw:(nt + 1) * nw]
                    p.op("t", lambda e, psu=psu, wt=wt, rhs=rhs, k=k, ci=ci: e.matmul(
                        psu[:], wt[:, 128 * ci:128 * (ci + 1)], rhs,
                        start=(k == 0), stop=(k == NCH - 1)),
                        reads=[wt, kv[k]], writes=[psu])
                seg_h = h[c][:, nt * nw:(nt + 1) * nw]
                seg_r = r4[c][:, nt * nw:(nt + 1) * nw]
                p.op("v", lambda e, psu=psu, seg_r=seg_r: e.tensor_tensor(
                    out=psu[:], in0=psu[:], in1=seg_r, op=ALU.mult),
                    reads=[psu, r4[c]], writes=[psu])
                p.op("v", lambda e, psu=psu, seg_h=seg_h: e.tensor_tensor(
                    out=seg_h, in0=seg_h, in1=psu[:], op=ALU.add),
                    reads=[psu, h[c]], writes=[h[c]])

    ones_b = p.sb("c_ones_b", [128, 1], BF16)
    p.op("v", lambda e: e.tensor_copy(out=ones_b[:], in_=ones[:]), reads=[ones], writes=[ones_b])
    ps_s = [p.ps("psLs1", [1, min(tw, 512)]), p.ps("psLs2", [1, min(tw, 512)])]
    ps_b = [p.ps("psLb1", [128, min(tw, 512)]), p.ps("psLb2", [128, min(tw, 512)])]
    rows = [p.sb(f"lnrow{i}", [1, tw]) for i in range(4)]
    emit_ln_fm(p, h, hb, ones, ones_b, ps_s, ps_b, rows, tw=tw)

    for i in range(NCH):
        p.dma("qs", lambda e, i=i: e.dma_start(out=h1_d[128 * i:128 * (i + 1), :], in_=h[i][:]),
              reads=[h[i]])
        p.dma("qs", lambda e, i=i: e.dma_start(out=h1b_d[128 * i:128 * (i + 1), :], in_=hb[i][:]),
              reads=[hb[i]])

    p.maybe_finish()
    return p


NEXP = 32
TOPK = 8
CAP = 2 * (T * TOPK // NEXP)


def build_phase_d(tw=TOWN, p=None):
    if p is None:
        p = Prog("phaseD")
    h1_d = p.din("h1", [D, tw])
    gw_d = p.din("gw", [D, NEXP])
    tri_d = p.din("tri", [128, 128])
    ones_d = p.din("ones", [128, 1])
    wmat_d = p.dout("wmat", [tw, NEXP])
    posl_d = p.dout("posl", [tw, NEXP])
    cnt_d = p.dout("cnt", [1, NEXP])
    p.start()

    NCH = D // 128
    NTCH = tw // 128
    h = [p.sb(f"d_h{i}", [128, tw]) for i in range(NCH)]
    gw = [p.sb(f"d_gw{i}", [128, NEXP]) for i in range(NCH)]
    tri = p.sb("d_tri", [128, 128])
    for i in range(NCH):
        p.dma("qs", lambda e, i=i: e.dma_start(out=h[i][:], in_=h1_d[128 * i:128 * (i + 1), :]), writes=[h[i]])
        p.dma("qs", lambda e, i=i: e.dma_start(out=gw[i][:], in_=gw_d[128 * i:128 * (i + 1), :]), writes=[gw[i]])
    p.dma("qs", lambda e: e.dma_start(out=tri[:], in_=tri_d[:]), writes=[tri])
    onesD = p.sb("d_ones", [128, 1])
    p.dma("qs", lambda e: e.dma_start(out=onesD[:], in_=ones_d[:]), writes=[onesD])

    psl = p.ps("d_psl", [128, NEXP])
    psc = p.ps("d_psc", [128, NEXP])
    pst = p.ps("d_pst", [1, NEXP])
    z = p.sb("d_z", [128, NEXP])
    mx = p.sb("d_mx", [128, TOPK])
    zap = p.sb("d_zap", [128, NEXP])
    wch = p.sb("d_wch", [128, NEXP])
    sel = p.sb("d_sel", [128, NEXP])
    ssum = p.sb("d_ssum", [128, 1])
    run = p.sb("d_run", [1, NEXP])
    posb = p.sb("d_posb", [128, NEXP])
    p.op("v", lambda e: e.memset(run[:], 0.0), writes=[run])

    for j in range(NTCH):
        for i in range(NCH):
            p.op("t", lambda e, i=i, j=j: e.matmul(
                psl[:], h[i][:, 128 * j:128 * (j + 1)], gw[i][:],
                start=(i == 0), stop=(i == NCH - 1)),
                reads=[h[i], gw[i]], writes=[psl])
        p.op("v", lambda e: e.reduce_max(out=ssum[:], in_=psl[:], axis=AX.X),
             reads=[psl], writes=[ssum])
        p.op("v", lambda e: e.tensor_scalar(out=z[:], in0=psl[:], scalar1=ssum[:, 0:1],
                                            scalar2=None, op0=ALU.subtract),
             reads=[psl, ssum], writes=[z])
        p.op("s", lambda e: e.activation(out=z[:], in_=z[:], func=AF.Exp),
             reads=[z], writes=[z])
        p.op("v", lambda e: e.max(out=mx[:], in_=z[:]), reads=[z], writes=[mx])
        p.op("v", lambda e: e.match_replace(out=zap[:], in_to_replace=mx[:], in_values=z[:],
                                            imm_value=0.0), reads=[z, mx], writes=[zap])
        tt(p, wch[:], z[:], zap[:], ALU.subtract, reads=[z, zap], writes=[wch])
        p.op("v", lambda e: e.reduce_sum(out=ssum[:], in_=wch[:], axis=AX.X),
             reads=[wch], writes=[ssum])
        p.op("v", lambda e: e.reciprocal(out=ssum[:], in_=ssum[:]), reads=[ssum], writes=[ssum])
        p.op("v", lambda e: e.tensor_scalar(out=wch[:], in0=wch[:], scalar1=ssum[:, 0:1],
                                            scalar2=None, op0=ALU.mult),
             reads=[wch, ssum], writes=[wch])
        p.dma("qs", lambda e, j=j: e.dma_start(out=wmat_d[128 * j:128 * (j + 1), :], in_=wch[:]),
              reads=[wch])
        p.op("v", lambda e: e.tensor_scalar(out=sel[:], in0=wch[:], scalar1=0.0, scalar2=None,
                                            op0=ALU.is_gt), reads=[wch], writes=[sel])
        p.op("t", lambda e: e.matmul(psc[:], tri[:], sel[:], start=True, stop=False),
             reads=[tri, sel], writes=[psc])
        p.op("t", lambda e: e.matmul(psc[:], onesD[0:1, 0:1].broadcast_to([1, 128]), run[:],
                                     start=False, stop=True),
             reads=[onesD, run], writes=[psc])
        p.op("v", lambda e: e.tensor_copy(out=posb[:], in_=psc[:]), reads=[psc], writes=[posb])
        p.dma("qs", lambda e, j=j: e.dma_start(out=posl_d[128 * j:128 * (j + 1), :], in_=posb[:]),
              reads=[posb])
        p.op("t", lambda e: e.matmul(pst[:], onesD[:, 0:1], sel[:], start=True, stop=True),
             reads=[onesD, sel], writes=[pst])
        p.op("v", lambda e: e.tensor_tensor(out=run[:], in0=run[:], in1=pst[:], op=ALU.add),
             reads=[run, pst], writes=[run])
    p.dma("qs", lambda e: e.dma_start(out=cnt_d[:], in_=run[:]), reads=[run])
    p.maybe_finish()
    return p


def build_phase_e(tn=T, tile=512, p=None):
    if p is None:
        p = Prog("phaseE")
    xall_d = p.din("xall", [D, tn], BF16)
    wg_d = p.din("wg", [4 * D, 768])
    wu_d = p.din("wu", [4 * D, 768])
    wd_d = p.din("wd", [4 * 768, D])
    wall_d = p.din("wall", [tn, NEXP])
    posall_d = p.din("posall", [tn, NEXP])
    cnt8_d = p.din("cnt8", [8, NEXP])
    cmask_d = p.din("cmask", [8, 8])
    ones_d = p.din("ones", [128, 1])
    eye_d = p.din("eye", [128, 128])
    esel_d = p.din("esel", [4, 1], I32)
    part_d = p.dout("part", [D, tn], BF16)
    we_dram = p.dtmp("we_dram", [NEXP, tn])
    my_we = p.dtmp("my_we", [4, tn])

    p.start()
    NCH = D // 128
    NI = 768 // 128
    NTT = tn // tile

    ones = p.sb("e_ones", [128, 1])
    onesb = p.sb("e_onesb", [128, 1], BF16)
    cnt8 = p.sb("e_cnt8", [8, NEXP])
    cmask = p.sb("e_cmask", [8, 8])
    p.dma("qs", lambda e: e.dma_start(out=ones[:], in_=ones_d[:]), writes=[ones])
    p.dma("qs", lambda e: e.dma_start(out=cnt8[:], in_=cnt8_d[:]), writes=[cnt8])
    p.dma("qs", lambda e: e.dma_start(out=cmask[:], in_=cmask_d[:]), writes=[cmask])
    eye = p.sb("e_eye", [128, 128])
    p.dma("qs", lambda e: e.dma_start(out=eye[:], in_=eye_d[:]), writes=[eye])
    p.op("v", lambda e: e.tensor_copy(out=onesb[:], in_=ones[:]), reads=[ones], writes=[onesb])

    sub = ExitStack()
    pbc = sub.enter_context(p.nc.psum_tensor("e_pbc", [128, NEXP], F32))
    ptr = sub.enter_context(p.nc.psum_tensor("e_ptr", [NEXP, 128], F32))
    wch = sub.enter_context(p.nc.sbuf_tensor("e_wch", [128, NEXP], F32))
    pch = sub.enter_context(p.nc.sbuf_tensor("e_pch", [128, NEXP], F32))
    wtr = sub.enter_context(p.nc.sbuf_tensor("e_wtr", [NEXP, 128], F32))
    for j in range(tn // 128):
        core = j // max(tn // 128 // 8, 1)
        p.dma("qs", lambda e, j=j: e.dma_start(out=wch[:], in_=wall_d[128 * j:128 * (j + 1), :]),
              writes=[wch])
        p.dma("qs", lambda e, j=j: e.dma_start(out=pch[:], in_=posall_d[128 * j:128 * (j + 1), :]),
              writes=[pch])
        p.op("t", lambda e, core=core: e.matmul(
            pbc[:], cmask[:, core:core + 1].broadcast_to([8, 128]), cnt8[:],
            start=True, stop=True),
             reads=[cmask, cnt8], writes=[pbc])
        p.op("v", lambda e: e.tensor_tensor(out=pch[:], in0=pch[:], in1=pbc[:], op=ALU.add),
             reads=[pch, pbc], writes=[pch])
        p.op("v", lambda e: e.tensor_scalar(out=pch[:], in0=pch[:], scalar1=float(CAP),
                                            scalar2=None, op0=ALU.is_lt),
             reads=[pch], writes=[pch])
        tt(p, wch[:], wch[:], pch[:], ALU.mult, reads=[wch, pch], writes=[wch])
        p.op("t", lambda e: e.transpose(ptr[:], wch[:], eye[:]),
             reads=[wch, eye], writes=[ptr])
        p.op("v", lambda e: e.tensor_copy(out=wtr[:], in_=ptr[:]), reads=[ptr], writes=[wtr])
        p.dma("qs", lambda e, j=j: e.dma_start(out=we_dram[:, 128 * j:128 * (j + 1)], in_=wtr[:]),
              reads=[wtr], writes=["we_dram"])
    esel_s = sub.enter_context(p.nc.sbuf_tensor("e_esel", [4, 1], I32))
    wesel = sub.enter_context(p.nc.sbuf_tensor("e_wesel", [4, tn], F32))
    p.dma("qs", lambda e: e.dma_start(out=esel_s[:], in_=esel_d[:]), writes=[esel_s])
    p.op("v", lambda e: e.memset(wesel[:], 0.0), writes=[wesel])
    p.dma("qg", lambda e: e.indirect_dma_start(
        out=wesel[:], out_offset=None, in_=we_dram[:],
        in_offset=IndirectOffsetOnAxis(ap=esel_s[:], axis=0),
        bounds_check=NEXP - 1, oob_is_err=False),
        reads=[esel_s, "we_dram"], writes=[wesel])
    p.dma("qs", lambda e: e.dma_start(out=my_we[:], in_=wesel[:]),
          reads=[wesel], writes=["my_we"])

    sub.close()
    p.fence()

    wg = [[p.sb(f"e_wg{b}_{k}", [128, 768], BF16) for k in range(NCH)] for b in range(2)]
    wu = [[p.sb(f"e_wu{b}_{k}", [128, 768], BF16) for k in range(NCH)] for b in range(2)]
    wd = [[p.sb(f"e_wd{b}_{k}", [128, D], BF16) for k in range(NI)] for b in range(2)]
    xt = [p.sb(f"e_xt{k}", [128, tile], BF16) for k in range(NCH)]
    yt = [p.sb(f"e_yt{k}", [128, tile], BF16) for k in range(2 * NI)]
    ot = [p.sb(f"e_ot{k}", [128, tile], BF16) for k in range(3)]
    wrow = p.sb("e_wrow", [1, tile], BF16)
    psg = [p.ps(f"e_psg{k}", [128, tile]) for k in range(2)]
    psu_ = [p.ps(f"e_psu{k}", [128, tile]) for k in range(2)]
    psD = [p.ps(f"e_psD{k}", [128, tile]) for k in range(3)]
    pwb = p.ps("e_pwb", [128, tile])
    wbc = p.sb("e_wbc", [128, tile], BF16)

    st_tok = {}
    iters = [(eloc, ttn) for eloc in range(4) for ttn in range(NTT)]

    def emit_weights(eloc):
        wgb, wub = wg[eloc % 2], wu[eloc % 2]
        for k in range(NCH):
            p.dma("qg", lambda e, k=k, eloc=eloc, wgb=wgb: e.dma_start(
                out=wgb[k][:], in_=wg_d[eloc * D + 128 * k: eloc * D + 128 * (k + 1), :]),
                writes=[wgb[k]])
            p.dma("qg", lambda e, k=k, eloc=eloc, wub=wub: e.dma_start(
                out=wub[k][:], in_=wu_d[eloc * D + 128 * k: eloc * D + 128 * (k + 1), :]),
                writes=[wub[k]])
        wdb = wd[eloc % 2]
        for k in range(NI):
            p.dma("qg", lambda e, k=k, eloc=eloc, wdb=wdb: e.dma_start(
                out=wdb[k][:], in_=wd_d[eloc * 768 + 128 * k: eloc * 768 + 128 * (k + 1), :]),
                writes=[wdb[k]])

    def emit_loads(it):
        eloc, ttn = it
        tsl = slice(ttn * tile, (ttn + 1) * tile)

        def xall_sl(k, tsl=tsl):
            if len(xall_d.shape) == 3:
                c = tsl.start // TOWN
                w0 = tsl.start % TOWN
                return xall_d[128 * k:128 * (k + 1), c, w0:w0 + (tsl.stop - tsl.start)]
            return xall_d[128 * k:128 * (k + 1), tsl]
        for k in range(NCH):
            p.dma("qs", lambda e, k=k, tsl=tsl: e.dma_start(
                out=xt[k][:], in_=xall_sl(k)), writes=[xt[k]])
        p.dma("qg", lambda e, eloc=eloc, tsl=tsl: e.dma_start(
            out=wrow[:], in_=my_we[eloc:eloc + 1, tsl]),
            reads=["my_we"], writes=[wrow])
        p.op("t", lambda e: e.matmul(pwb[:], onesb[0:1, 0:1].broadcast_to([1, 128]),
                                     wrow[:], start=True, stop=True),
             reads=[wrow, onesb], writes=[pwb])
        p.op("v", lambda e: e.tensor_copy(out=wbc[:], in_=pwb[:]), reads=[pwb], writes=[wbc])

    def emit_gu_pass(it, half):
        eloc, ttn = it
        wgb, wub = wg[eloc % 2], wu[eloc % 2]
        ybase = (ttn % 2) * NI
        for ii in range(2):
            i = half * 2 + ii
            for k in range(NCH):
                p.op("t", lambda e, i=i, ii=ii, k=k, wgb=wgb: e.matmul(
                    psg[ii][:], wgb[k][:, 128 * i:128 * (i + 1)], xt[k][:],
                    start=(k == 0), stop=(k == NCH - 1)),
                    reads=[wgb[k], xt[k]], writes=[psg[ii]])
            for k in range(NCH):
                p.op("t", lambda e, i=i, ii=ii, k=k, wub=wub: e.matmul(
                    psu_[ii][:], wub[k][:, 128 * i:128 * (i + 1)], xt[k][:],
                    start=(k == 0), stop=(k == NCH - 1)),
                    reads=[wub[k], xt[k]], writes=[psu_[ii]])

    def emit_gu_evac(it, half):
        eloc, ttn = it
        ybase = (ttn % 2) * NI
        for ii in range(2):
            i = half * 2 + ii
            y = yt[ybase + i]
            p.op("s", lambda e, ii=ii, y=y: e.activation(
                out=y[:], in_=psg[ii][:], func=AF.Silu),
                reads=[psg[ii]], writes=[y])
            p.op("v", lambda e, ii=ii, y=y: e.tensor_tensor(
                out=y[:], in0=y[:], in1=psu_[ii][:], op=ALU.mult),
                reads=[y, psu_[ii]], writes=[y])
            p.op("v", lambda e, y=y: e.tensor_tensor(
                out=y[:], in0=y[:], in1=wbc[:], op=ALU.mult),
                reads=[y, wbc], writes=[y])

    def emit_down_chunks(it, c0, c1):
        eloc, ttn = it
        tsl = slice(ttn * tile, (ttn + 1) * tile)
        ybase = (ttn % 2) * NI

        def part_sl(c, tsl=tsl):
            if len(part_d.shape) == 3:
                cb = tsl.start // TOWN
                w0 = tsl.start % TOWN
                return part_d[128 * c:128 * (c + 1), cb, w0:w0 + (tsl.stop - tsl.start)]
            return part_d[128 * c:128 * (c + 1), tsl]
        wdb = wd[eloc % 2]
        for c in range(c0, c1):
            psb = psD[c % 3]
            om = ot[c % 3]
            for k in range(NI):
                p.op("t", lambda e, c=c, k=k, psb=psb, ybase=ybase, wdb=wdb: e.matmul(
                    psb[:], wdb[k][:, 128 * c:128 * (c + 1)], yt[ybase + k][:],
                    start=(k == 0), stop=(k == NI - 1)),
                    reads=[wdb[k], yt[ybase + k]], writes=[psb])
            p.op("v", lambda e, psb=psb, om=om: e.tensor_copy(out=om[:], in_=psb[:]),
                 reads=[psb], writes=[om])
            aop = ALU.add if eloc > 0 else ALU.bypass
            key = (c, ttn)
            prev = st_tok.get(key)
            tok = p.dma("qg", lambda e, c=c, om=om, aop=aop: e.dma_start(
                out=part_sl(c), in_=om[:], accum_op=aop),
                reads=[om], extra=[prev] if prev else [])
            st_tok[key] = tok

    emit_weights(0)
    for n, it in enumerate(iters):
        eloc, ttn = it
        if ttn == 1 and eloc < 3:
            emit_weights(eloc + 1)
        emit_loads(it)
        prev = iters[n - 1] if n > 0 else None
        for half in range(3):
            emit_gu_pass(it, half)
            if prev is not None:
                th = 16 // 3
                c0 = half * th
                c1 = (half + 1) * th if half < 2 else 16
                emit_down_chunks(prev, c0, c1)
            emit_gu_evac(it, half)
    emit_down_chunks(iters[-1], 0, 16)
    p._e_st = st_tok
    p.maybe_finish()
    return p


def build_phase_f(tw=TOWN, p=None, nparts=8):
    if p is None:
        p = Prog("phaseF")
    pj_d = p.din("pj", [nparts * D, tw], BF16)
    h1_d = p.din("h1", [D, tw])
    ones_d = p.din("ones", [128, 1])
    h2_d = p.dout("h2", [D, tw])
    h2b_d = p.dout("h2b", [D, tw], BF16)
    p.start()
    NCH = D // 128
    ones = p.sb("f_ones", [128, 1])
    ones_b = p.sb("f_ones_b", [128, 1], BF16)
    p.dma("qs", lambda e: e.dma_start(out=ones[:], in_=ones_d[:]), writes=[ones])
    p.op("v", lambda e: e.tensor_copy(out=ones_b[:], in_=ones[:]), reads=[ones], writes=[ones_b])
    h = [p.sb(f"f_h{i}", [128, tw]) for i in range(NCH)]
    hb = [p.sb(f"f_hb{i}", [128, tw], BF16) for i in range(NCH)]
    pj = [p.sb(f"f_pj{i}", [128, tw], BF16) for i in range(max(nparts, 2))]
    acc = p.sb("f_acc", [128, tw])
    for i in range(NCH):
        p.dma("qs", lambda e, i=i: e.dma_start(out=h[i][:], in_=h1_d[128 * i:128 * (i + 1), :]),
              writes=[h[i]])
        for j in range(nparts):
            p.dma("qs", lambda e, i=i, j=j: e.dma_start(
                out=pj[j][:], in_=pj_d[j * D + 128 * i: j * D + 128 * (i + 1), :]),
                writes=[pj[j]])
        if nparts == 1:
            tt(p, h[i][:], h[i][:], pj[0][:], ALU.add, reads=[h[i], pj[0]], writes=[h[i]])
        else:
            tt(p, acc[:], pj[0][:], pj[1][:], ALU.add, reads=[pj[0], pj[1]], writes=[acc])
            for j in range(2, nparts):
                tt(p, acc[:], acc[:], pj[j][:], ALU.add, reads=[acc, pj[j]], writes=[acc])
            tt(p, h[i][:], h[i][:], acc[:], ALU.add, reads=[h[i], acc], writes=[h[i]])
    ps_s = [p.ps("f_psLs1", [1, min(tw, 512)]), p.ps("f_psLs2", [1, min(tw, 512)])]
    ps_b = [p.ps("f_psLb1", [128, min(tw, 512)]), p.ps("f_psLb2", [128, min(tw, 512)])]
    rows = [p.sb(f"f_lnrow{i}", [1, tw]) for i in range(4)]
    emit_ln_fm(p, h, hb, ones, ones_b, ps_s, ps_b, rows, tw=tw)
    for i in range(NCH):
        p.dma("qs", lambda e, i=i: e.dma_start(out=h2_d[128 * i:128 * (i + 1), :], in_=h[i][:]),
              reads=[h[i]])
        p.dma("qs", lambda e, i=i: e.dma_start(out=h2b_d[128 * i:128 * (i + 1), :], in_=hb[i][:]),
              reads=[hb[i]])
    p.maybe_finish()
    return p


NH = 64
HP = 64
NN = 128
CS = 128
INTER = 4096
CONV_DIM = INTER + 2 * NN
NXC = CONV_DIM // 128


def build_phase_g(tw=TOWN, p=None):
    if p is None:
        p = Prog("phaseG")
    hb_d = p.din("hb", [D, tw + 3], BF16)
    wpg_d = p.din("wpg", [D, INTER])
    wpx_d = p.din("wpx", [D, CONV_DIM])
    wpd_d = p.din("wpd", [D, NH])
    convw_d = p.din("convw", [CONV_DIM, 4])
    convbt_d = p.din("convbt", [1, CONV_DIM])
    onesr_d = p.din("onesr", [1, 512])
    dtb_d = p.din("dtb", [NH, 1])
    alog_d = p.din("alog", [NH, 1])
    deye_d = p.din("deye", [128, NH])
    eye_d = p.din("eye", [128, 128])
    ones_d = p.din("ones", [128, 1])
    gsil_d = p.dout("gsil", [INTER, tw], BF16)
    yt_d = p.dout("yt", [INTER, tw], BF16)
    cw_d = p.dout("cw", [(tw // CS) * NN, NH * CS], BF16)
    sacc_d = p.dout("sacc", [NN, NH * HP], BF16)
    espan_d = p.dout("espan", [tw // CS, NH])
    espantot_d = p.dout("espantot", [1, NH])
    ac_dram = p.dtmp("ac_dram", [NH, CS])

    p.start()
    NCH = D // 128
    NCHUNK = tw // CS
    NT = tw // 512
    ones = p.sb("g_ones", [128, 1])
    onesb = p.sb("g_onesb", [128, 1], BF16)
    onesrb = p.sb("g_onesrb", [1, 512], BF16)
    eyeb = p.sb("g_eyeb", [128, 128], BF16)
    eyef = p.sb("g_eyef", [128, 128])
    p.dma("qs", lambda e: e.dma_start(out=ones[:], in_=ones_d[:]), writes=[ones])
    p.dma("qs", lambda e: e.dma_start(out=eyef[:], in_=eye_d[:]), writes=[eyef])
    p.dma("qg", lambda e: e.dma_start(out=eyeb[:], in_=eye_d[:]), writes=[eyeb])
    p.dma("qg", lambda e: e.dma_start(out=onesrb[:], in_=onesr_d[:]), writes=[onesrb])
    p.op("v", lambda e: e.tensor_copy(out=onesb[:], in_=ones[:]), reads=[ones], writes=[onesb])
    K1 = ones[0:1, 0:1].broadcast_to([1, 128])
    K1b = onesb[0:1, 0:1].broadcast_to([1, 128])

    dtm = p.sb("g_dt", [NH, tw])
    dtbc = p.sb("g_dtbc", [NH, 1])
    sc_a = p.sb("g_sca", [NH, tw])
    acol = p.sb("g_acol", [NH, 1])
    acum = p.sb("g_acum", [NH, tw])
    zz1 = p.sb("g_zz1", [NH, 1])
    xtok = [p.sb(f"g_xtok{i}", [128, 4224], BF16) for i in range(NCHUNK)]
    bt = p.sb("g_bt", [128, tw], BF16)
    ct = p.sb("g_ct", [128, tw], BF16)
    subx = ExitStack()
    xbc = [subx.enter_context(p.nc.sbuf_tensor(f"g_xbc{i}", [128, tw + 3], BF16))
           for i in range(NXC)]
    sub1 = ExitStack()
    hb = [sub1.enter_context(p.nc.sbuf_tensor(f"g_hb{i}", [128, tw + 3], BF16))
          for i in range(NCH)]
    wt = [sub1.enter_context(p.nc.sbuf_tensor(f"g_wt{i}", [128, 512], BF16))
          for i in range(NCH)]
    gso = [sub1.enter_context(p.nc.sbuf_tensor(f"g_gso{i}", [128, 512], BF16))
           for i in range(4)]
    psA = [sub1.enter_context(p.nc.psum_tensor(f"g_psA{i}", [128, 512], F32))
           for i in range(4)]
    for i in range(NCH):
        p.dma("qs", lambda e, i=i: e.dma_start(out=hb[i][:], in_=hb_d[128 * i:128 * (i + 1), :]),
              writes=[hb[i]])

    for cg in range(INTER // 512):
        for k in range(NCH):
            p.dma("qg", lambda e, k=k, cg=cg: e.dma_start(
                out=wt[k][:], in_=wpg_d[128 * k:128 * (k + 1), 512 * cg:512 * (cg + 1)]),
                writes=[wt[k]])
        for ci in range(4):
            c = cg * 4 + ci
            for nt in range(NT):
                psu = psA[(ci * NT + nt) % 4]
                for k in range(NCH):
                    p.op("t", lambda e, psu=psu, k=k, ci=ci, nt=nt: e.matmul(
                        psu[:], wt[k][:, 128 * ci:128 * (ci + 1)],
                        hb[k][:, 3 + nt * 512: 3 + (nt + 1) * 512],
                        start=(k == 0), stop=(k == NCH - 1)),
                        reads=[wt[k], hb[k]], writes=[psu])
                g_ = gso[(ci * NT + nt) % 4]
                p.op("s", lambda e, psu=psu, g_=g_: e.activation(out=g_[:], in_=psu[:], func=AF.Silu),
                     reads=[psu], writes=[g_])
                p.dma("qs", lambda e, c=c, nt=nt, g_=g_: e.dma_start(
                    out=gsil_d[128 * c:128 * (c + 1), 512 * nt:512 * (nt + 1)], in_=g_[:]),
                    reads=[g_])

    xtiles = []
    o = 0
    while o < tw + 3:
        nn_ = min(512, tw + 3 - o)
        xtiles.append((o, nn_))
        o += nn_
    for cg in range((NXC + 3) // 4):
        c0 = cg * 4
        ncc = min(4, NXC - c0)
        cwc = 128 * ncc
        for k in range(NCH):
            p.dma("qg", lambda e, k=k, cg=cg, cwc=cwc: e.dma_start(
                out=wt[k][:, 0:cwc],
                in_=wpx_d[128 * k:128 * (k + 1), 512 * cg:512 * cg + cwc]),
                writes=[wt[k]])
        for ci in range(ncc):
            c = c0 + ci
            for (o0, nn_) in xtiles:
                psu = psA[ci % 4]
                for k in range(NCH):
                    p.op("t", lambda e, psu=psu, k=k, ci=ci, o0=o0, nn_=nn_: e.matmul(
                        psu[:, 0:nn_], wt[k][:, 128 * ci:128 * (ci + 1)],
                        hb[k][:, o0:o0 + nn_],
                        start=(k == 0), stop=(k == NCH - 1)),
                        reads=[wt[k], hb[k]], writes=[psu])
                p.op("v", lambda e, psu=psu, c=c, o0=o0, nn_=nn_: e.tensor_copy(
                    out=xbc[c][:, o0:o0 + nn_], in_=psu[:, 0:nn_]),
                    reads=[psu], writes=[xbc[c]])

    for k in range(NCH):
        p.dma("qg", lambda e, k=k: e.dma_start(
            out=wt[k][:, 0:NH], in_=wpd_d[128 * k:128 * (k + 1), :]), writes=[wt[k]])
    p.dma("qs", lambda e: e.dma_start(out=dtbc[:], in_=dtb_d[:]), writes=[dtbc])
    for nt in range(NT):
        psu = psA[nt % 4]
        for k in range(NCH):
            p.op("t", lambda e, psu=psu, k=k, nt=nt: e.matmul(
                psu[0:NH, :], wt[k][:, 0:NH], hb[k][:, 3 + nt * 512: 3 + (nt + 1) * 512],
                start=(k == 0), stop=(k == NCH - 1)),
                reads=[wt[k], hb[k]], writes=[psu])
        p.op("v", lambda e, psu=psu, nt=nt: e.tensor_scalar(
            out=dtm[0:NH, 512 * nt:512 * (nt + 1)], in0=psu[0:NH, :], scalar1=dtbc[:, 0:1],
            scalar2=None, op0=ALU.add), reads=[psu, dtbc], writes=[dtm])
    sub1.close()
    p.fence()

    p.op("s", lambda e: e.activation(out=sc_a[:], in_=dtm[:], func=AF.Abs),
         reads=[dtm], writes=[sc_a])
    p.op("s", lambda e: e.activation(out=sc_a[:], in_=sc_a[:], func=AF.Exp, scale=-1.0),
         reads=[sc_a], writes=[sc_a])
    p.op("s", lambda e: e.activation(out=sc_a[:], in_=sc_a[:], func=AF.Ln,
                                     bias=p.constcell(1.0, NH)),
         reads=[sc_a, p._cst_s], writes=[sc_a])
    p.op("v", lambda e: e.tensor_scalar(out=dtm[:], in0=dtm[:], scalar1=0.0, scalar2=None,
                                        op0=ALU.max), reads=[dtm], writes=[dtm])
    tt(p, dtm[:], dtm[:], sc_a[:], ALU.add, reads=[dtm, sc_a], writes=[dtm])
    p.dma("qs", lambda e: e.dma_start(out=acol[:], in_=alog_d[:]), writes=[acol])
    p.op("s", lambda e: e.activation(out=acol[:], in_=acol[:], func=AF.Exp),
         reads=[acol], writes=[acol])
    adt = sc_a
    p.op("v", lambda e: e.tensor_scalar(out=adt[:], in0=dtm[:], scalar1=acol[:, 0:1],
                                        scalar2=-1.0, op0=ALU.mult, op1=ALU.mult),
         reads=[dtm, acol], writes=[adt])
    p.op("v", lambda e: e.memset(zz1[:], 0.0), writes=[zz1])
    for c in range(NCHUNK):
        sl = slice(c * CS, (c + 1) * CS)
        p.op("v", lambda e, sl=sl: e.tensor_tensor_scan(
            out=acum[:, sl], data0=adt[:, sl], data1=zz1[:, 0:1].broadcast_to([NH, CS]),
            initial=0.0, op0=ALU.add, op1=ALU.add),
            reads=[adt, zz1], writes=[acum])

    sub2 = ExitStack()
    cwgt = sub2.enter_context(p.nc.sbuf_tensor("g_cwgt", [128, 4], F32))
    diag = [sub2.enter_context(p.nc.sbuf_tensor(f"g_diag{j}", [128, 128], BF16))
            for j in range(4)]
    convbt = sub2.enter_context(p.nc.sbuf_tensor("g_convbt", [1, CONV_DIM], BF16))
    psC = [sub2.enter_context(p.nc.psum_tensor(f"g_psC{i}", [128, 512], F32))
           for i in range(2)]
    p.dma("qg", lambda e: e.dma_start(out=convbt[:], in_=convbt_d[:]), writes=[convbt])
    for cc in range(NXC):
        p.dma("qs", lambda e, cc=cc: e.dma_start(
            out=cwgt[:], in_=convw_d[128 * cc:128 * (cc + 1), :]), writes=[cwgt])
        for j in range(4):
            p.op("v", lambda e, j=j: e.tensor_scalar(
                out=diag[j][:], in0=eyeb[:], scalar1=cwgt[:, j:j + 1], scalar2=None,
                op0=ALU.mult), reads=[eyeb, cwgt], writes=[diag[j]])
        if cc < 33:
            for tc in range(NCHUNK):
                psu = psC[tc % 2]
                for j in range(4):
                    p.op("t", lambda e, psu=psu, j=j, tc=tc, cc=cc: e.matmul(
                        psu[:, 0:128], xbc[cc][:, 128 * tc + j:128 * tc + j + 128],
                        diag[j][:], start=(j == 0), stop=False),
                        reads=[xbc[cc], diag[j]], writes=[psu])
                p.op("t", lambda e, psu=psu, cc=cc: e.matmul(
                    psu[:, 0:128], K1b, convbt[0:1, 128 * cc:128 * (cc + 1)],
                    start=False, stop=True),
                    reads=[onesb, convbt], writes=[psu])
                p.op("s", lambda e, psu=psu, tc=tc, cc=cc: e.activation(
                    out=xtok[tc][:, 128 * cc:128 * (cc + 1)],
                    in_=psu[:, 0:128], func=AF.Silu),
                    reads=[psu], writes=[xtok[tc]])
        if cc >= 32:
            dst = bt if cc == 32 else ct
            for nt in range(NT):
                psu = psC[nt % 2]
                for j in range(4):
                    p.op("t", lambda e, psu=psu, j=j, nt=nt, cc=cc: e.matmul(
                        psu[:], diag[j][:], xbc[cc][:, j + nt * 512: j + nt * 512 + 512],
                        start=(j == 0), stop=False),
                        reads=[xbc[cc], diag[j]], writes=[psu])
                p.op("t", lambda e, psu=psu, cc=cc: e.matmul(
                    psu[:], convbt[0:1, 128 * cc:128 * (cc + 1)],
                    onesrb[:], start=False, stop=True),
                    reads=[onesrb, convbt], writes=[psu])
                p.op("s", lambda e, psu=psu, dst=dst, nt=nt: e.activation(
                    out=dst[:, 512 * nt:512 * (nt + 1)], in_=psu[:], func=AF.Silu),
                    reads=[psu], writes=[dst])
    sub2.close()
    subx.close()
    p.fence()

    eyeD = [p.sb(f"g_eyeD{h}", [128, 128], BF16) for h in range(NH)]
    deye = p.sb("g_deye", [128, NH])
    p.dma("qs", lambda e: e.dma_start(out=deye[:], in_=deye_d[:]), writes=[deye])
    for h in range(NH):
        p.op("v", lambda e, h=h: e.tensor_scalar(
            out=eyeD[h][:], in0=eyeb[:], scalar1=deye[:, h:h + 1], scalar2=None,
            op0=ALU.mult), reads=[eyeb, deye], writes=[eyeD[h]])

    triu = p.sb("g_triu", [128, 128])
    zz128 = p.sb("g_zz128", [128, 1])
    p.op("v", lambda e: e.memset(zz128[:], 0.0), writes=[zz128])
    p.op("v", lambda e: e.tensor_tensor_scan(
        out=triu[:], data0=eyef[:], data1=zz128[:, 0:1].broadcast_to([128, 128]),
        initial=0.0, op0=ALU.max, op1=ALU.add), reads=[eyef, zz128], writes=[triu])

    srun = p.sb("g_srun", [128, NH * HP])
    srunb = p.sb("g_srunb", [128, NH * HP], BF16)
    p.op("v", lambda e: e.memset(srun[:], 0.0), writes=[srun])
    stmask = p.sb("g_stm", [128, 128], BF16)
    rowb2 = [p.sb(f"g_rowb{b}", [1, 512]) for b in range(2)]
    espraw = p.sb("g_espraw", [1, NH])
    acT = p.sb("g_acT", [128, NH])
    wT = p.sb("g_wT", [128, NH], BF16)
    dtT = p.sb("g_dtT", [128, NH], BF16)
    xdt = p.sb("g_xdt", [128, NH * HP], BF16)
    xw = p.sb("g_xw", [128, NH * HP], BF16)
    mt = p.sb("g_mt", [128, NH * CS], BF16)
    cwb = [p.sb(f"g_cwb{i}", [128, NH * CS], BF16) for i in range(1)]
    expd2 = [p.sb(f"g_expd{b}", [128, 512]) for b in range(2)]
    wrow = p.sb("g_wrow", [NH, CS])
    espT = p.sb("g_espT", [128, NH])
    esprow = p.sb("g_esprow", [1, NH])
    esptot = p.sb("g_esptot", [1, NH])
    yest = [p.sb(f"g_ye{i}", [64, 512], BF16) for i in range(2)]
    p.op("v", lambda e: e.memset(esptot[:], 1.0), writes=[esptot])

    psU2 = [p.ps(f"g_psU{b}", [128, 512]) for b in range(2)]
    psU = psU2[0]
    psT = p.ps("g_psT", [128, 128])
    psY = [p.ps(f"g_psY{i}", [64, 512]) for i in range(2)]
    psS = [p.ps(f"g_psS{i}", [128, 512]) for i in range(2)]

    for c in range(NCHUNK):
        sl = slice(c * CS, (c + 1) * CS)
        p.op("t", lambda e, sl=sl: e.matmul(psU[:, 0:128], bt[:, sl], ct[:, sl],
                                            start=True, stop=True),
             reads=[bt, ct], writes=[psU])
        p.op("v", lambda e: e.tensor_tensor(out=stmask[:], in0=psU[:, 0:128], in1=triu[:],
                                            op=ALU.mult), reads=[psU, triu], writes=[stmask])
        p.dma("qs", lambda e, sl=sl: e.dma_start(out=ac_dram[:], in_=acum[:, sl]),
              reads=[acum], writes=["ac_dram"])
        p.op("t", lambda e, sl=sl: e.transpose(psT[:, 0:NH], acum[:, sl], eyef[0:NH, 0:NH]),
             reads=[acum, eyef], writes=[psT])
        p.op("v", lambda e: e.tensor_copy(out=acT[:], in_=psT[:, 0:NH]),
             reads=[psT], writes=[acT])
        p.op("v", lambda e, sl=sl: e.tensor_scalar(
            out=wrow[:], in0=acum[:, sl], scalar1=acum[:, sl.stop - 1:sl.stop],
            scalar2=-1.0, op0=ALU.subtract, op1=ALU.mult), reads=[acum], writes=[wrow])
        p.op("s", lambda e: e.activation(out=wrow[:], in_=wrow[:], func=AF.Exp),
             reads=[wrow], writes=[wrow])
        p.op("t", lambda e: e.transpose(psT[:, 0:NH], wrow[:], eyef[0:NH, 0:NH]),
             reads=[wrow, eyef], writes=[psT])
        p.op("v", lambda e: e.tensor_copy(out=wT[:], in_=psT[:, 0:NH]),
             reads=[psT], writes=[wT])
        p.op("t", lambda e, sl=sl: e.transpose(psT[:, 0:NH], dtm[:, sl], eyef[0:NH, 0:NH]),
             reads=[dtm, eyef], writes=[psT])
        p.op("v", lambda e: e.tensor_copy(out=dtT[:], in_=psT[:, 0:NH]),
             reads=[psT], writes=[dtT])
        p.dma("qg", lambda e: e.dma_start(
            out=espraw[:], in_=ac_dram[:, CS - 1:CS].rearrange("h o -> o h"),
            allow_slow_non_contiguous=True),
            reads=["ac_dram"], writes=[espraw])
        p.op("s", lambda e: e.activation(out=esprow[:], in_=espraw[:], func=AF.Exp),
             reads=[espraw], writes=[esprow])
        p.dma("qs", lambda e, c=c: e.dma_start(out=espan_d[c:c + 1, :], in_=esprow[:]),
              reads=[esprow])
        tt(p, esptot[:], esptot[:], esprow[:], ALU.mult, reads=[esptot, esprow], writes=[esptot])
        p.op("t", lambda e: e.matmul(psT[:, 0:NH], K1, esprow[:], start=True, stop=True),
             reads=[ones, esprow], writes=[psT])
        p.op("v", lambda e: e.tensor_copy(out=espT[:], in_=psT[:, 0:NH]),
             reads=[psT], writes=[espT])
        p.op("v", lambda e, c=c: e.tensor_tensor(
            out=xdt[:].rearrange("t (h q) -> t h q", h=NH),
            in0=xtok[c][:, 0:INTER].rearrange("t (h q) -> t h q", h=NH),
            in1=dtT[:].unsqueeze(-1).broadcast_to([128, NH, HP]), op=ALU.mult),
            reads=[xtok[c], dtT], writes=[xdt])
        p.op("v", lambda e: e.tensor_tensor(
            out=xw[:].rearrange("t (h q) -> t h q", h=NH),
            in0=xdt[:].rearrange("t (h q) -> t h q", h=NH),
            in1=wT[:].unsqueeze(-1).broadcast_to([128, NH, HP]), op=ALU.mult),
            reads=[xdt, wT], writes=[xw])
        p.op("v", lambda e: e.tensor_copy(out=srunb[:], in_=srun[:]),
             reads=[srun], writes=[srunb])
        cw = cwb[0]
        for g4 in range(NH // 4):
            hs = slice(g4 * 4 * CS, (g4 + 1) * 4 * CS)
            rowb = rowb2[g4 % 2]
            expd = expd2[g4 % 2]
            psU = psU2[g4 % 2]
            p.dma("qs", lambda e, g4=g4, rowb=rowb: e.dma_start(
                out=rowb[:],
                in_=ac_dram[4 * g4:4 * (g4 + 1), :].rearrange("h l -> (h l)").unsqueeze(0)),
                reads=["ac_dram"], writes=[rowb])
            p.op("t", lambda e, rowb=rowb, psU=psU: e.matmul(
                psU[:], K1, rowb[:], start=True, stop=True),
                 reads=[ones, rowb], writes=[psU])
            p.op("v", lambda e, g4=g4, psU=psU, expd=expd: e.tensor_tensor(
                out=expd[:].rearrange("s (h l) -> s h l", h=4),
                in0=psU[:].rearrange("s (h l) -> s h l", h=4),
                in1=acT[:, 4 * g4:4 * g4 + 4].unsqueeze(-1).broadcast_to([128, 4, CS]),
                op=ALU.subtract), reads=[psU, acT], writes=[expd])
            p.op("v", lambda e, expd=expd: e.tensor_scalar(
                out=expd[:], in0=expd[:], scalar1=0.0,
                scalar2=None, op0=ALU.min),
                 reads=[expd], writes=[expd])
            p.op("s", lambda e, hs=hs, expd=expd: e.activation(
                out=mt[:, hs], in_=expd[:], func=AF.Exp),
                 reads=[expd], writes=[mt])
            p.op("v", lambda e, hs=hs: e.tensor_tensor(
                out=mt[:, hs].rearrange("s (h l) -> s h l", h=4),
                in0=mt[:, hs].rearrange("s (h l) -> s h l", h=4),
                in1=stmask[:].unsqueeze(1).broadcast_to([128, 4, 128]), op=ALU.mult),
                reads=[mt, stmask], writes=[mt])
            p.op("s", lambda e, psU=psU, expd=expd: e.activation(
                out=expd[:], in_=psU[:], func=AF.Exp),
                 reads=[psU], writes=[expd])
            p.op("v", lambda e, hs=hs, sl=sl, cw=cw, expd=expd: e.tensor_tensor(
                out=cw[:, hs].rearrange("n (h l) -> n h l", h=4),
                in0=expd[:].rearrange("n (h l) -> n h l", h=4),
                in1=ct[:, sl].unsqueeze(1).broadcast_to([128, 4, 128]), op=ALU.mult),
                reads=[expd, ct], writes=[cw])
        p.dma("qs", lambda e, c=c, cw=cw: e.dma_start(
            out=cw_d[NN * c:NN * (c + 1), :], in_=cw[:]), reads=[cw])
        for wv in range(16):
            psu = psY[wv % 2]
            ye = yest[wv % 2]
            for hh in range(4):
                h = wv * 4 + hh
                osl = slice(hh * 128, (hh + 1) * 128)
                p.op("t", lambda e, h=h, osl=osl, psu=psu: e.matmul(
                    psu[:, osl], xdt[:, HP * h:HP * (h + 1)], mt[:, CS * h:CS * (h + 1)],
                    start=True, stop=False), reads=[xdt, mt], writes=[psu])
                p.op("t", lambda e, h=h, osl=osl, psu=psu, c=c: e.matmul(
                    psu[:, osl], xtok[c][:, HP * h:HP * (h + 1)], eyeD[h][:],
                    start=False, stop=False), reads=[xtok[c], eyeD[h]], writes=[psu])
                p.op("t", lambda e, h=h, osl=osl, psu=psu: e.matmul(
                    psu[:, osl], srunb[:, HP * h:HP * (h + 1)], cw[:, CS * h:CS * (h + 1)],
                    start=False, stop=True), reads=[srunb, cw], writes=[psu])
            p.op("v", lambda e, psu=psu, ye=ye: e.tensor_copy(out=ye[:], in_=psu[:]),
                 reads=[psu], writes=[ye])
            p.dma("qs", lambda e, c=c, wv=wv, ye=ye: e.dma_start(
                out=yt_d[HP * 4 * wv: HP * 4 * (wv + 1), CS * c:CS * (c + 1)]
                    .rearrange("(h q) l -> q h l", q=HP),
                in_=ye[:].rearrange("q (h l) -> q h l", h=4)), reads=[ye])
        for wv in range(8):
            psu = psS[wv % 2]
            for hh in range(8):
                h = wv * 8 + hh
                osl = slice(hh * HP, (hh + 1) * HP)
                p.op("t", lambda e, h=h, osl=osl, psu=psu, c=c: e.matmul(
                    psu[:, osl], xtok[c][:, INTER:INTER + NN],
                    xw[:, HP * h:HP * (h + 1)],
                    start=True, stop=True), reads=[xtok[c], xw], writes=[psu])
            ssl = slice(wv * 512, (wv + 1) * 512)
            p.op("v", lambda e, ssl=ssl, wv=wv: e.tensor_tensor(
                out=srun[:, ssl].rearrange("n (h q) -> n h q", h=8),
                in0=srun[:, ssl].rearrange("n (h q) -> n h q", h=8),
                in1=espT[:, 8 * wv:8 * (wv + 1)].unsqueeze(-1).broadcast_to([128, 8, HP]),
                op=ALU.mult), reads=[srun, espT], writes=[srun])
            p.op("v", lambda e, ssl=ssl, psu=psu: e.tensor_tensor(
                out=srun[:, ssl], in0=srun[:, ssl], in1=psu[:], op=ALU.add),
                reads=[srun, psu], writes=[srun])

    p.op("v", lambda e: e.tensor_copy(out=srunb[:], in_=srun[:]), reads=[srun], writes=[srunb])
    p.dma("qs", lambda e: e.dma_start(out=sacc_d[:], in_=srunb[:]), reads=[srunb])
    p.dma("qs", lambda e: e.dma_start(out=espantot_d[:], in_=esptot[:]), reads=[esptot])
    p.maybe_finish()
    return p


def build_phase_h(tw=TOWN, p=None):
    if p is None:
        p = Prog("phaseH")
    yt_dio = p.din("ytin", [INTER, tw], BF16)
    cw_d = p.din("cw", [(tw // CS) * NN, NH * CS], BF16)
    gsil_d = p.din("gsil", [INTER, tw], BF16)
    h2_d = p.din("h2", [D, tw])
    sacc8_d = p.din("sacc8", [8 * NN, NH * HP], BF16)
    esptot8_d = p.din("esptot8", [1, 8 * NH])
    espmy_d = p.din("espmy", [1, (tw // CS) * NH])
    jmask_d = p.din("jmask", [1, 8])
    nw_d = p.din("nw", [128, INTER // 128])
    wop_d = p.din("wop", [INTER, D])
    ones_d = p.din("ones", [128, 1])
    h3_d = p.dout("h3", [D, tw])
    h3b_d = p.dout("h3b", [D, tw], BF16)

    p.start()
    NCH = D // 128
    NYC = INTER // 128
    NCHUNK = tw // CS
    NT = tw // 512
    ones = p.sb("h_ones", [128, 1])
    p.dma("qs", lambda e: e.dma_start(out=ones[:], in_=ones_d[:]), writes=[ones])
    ones_b = p.sb("h_onesb", [128, 1], BF16)
    p.op("v", lambda e: e.tensor_copy(out=ones_b[:], in_=ones[:]), reads=[ones], writes=[ones_b])
    K1 = ones[0:1, 0:1].broadcast_to([1, 128])

    jrow = p.sb("h_jrow", [1, 8])
    invr = p.sb("h_invr", [1, 8])
    erow = p.sb("h_erow", [1, NH])
    etot = p.sb("h_etot", [1, 8 * NH])
    mcol8 = p.sb("h_mcol8", [128, 8])
    espmy = p.sb("h_espmy", [1, (tw // CS) * NH])
    vrow = p.sb("h_vrow", [1, tw])
    rstdb = p.sb("h_rstdb", [128, tw])
    nwc = p.sb("h_nwc", [128, INTER // 128])
    sub0 = ExitStack()
    scur = sub0.enter_context(p.nc.sbuf_tensor("h_scur", [128, NH * HP], F32))
    scb = sub0.enter_context(p.nc.sbuf_tensor("h_scb", [128, NH * HP], BF16))
    ebc = sub0.enter_context(p.nc.sbuf_tensor("h_ebc", [128, NH], F32))
    p.dma("qs", lambda e: e.dma_start(out=jrow[:], in_=jmask_d[:]), writes=[jrow])
    p.dma("qs", lambda e: e.dma_start(out=etot[:], in_=esptot8_d[:]), writes=[etot])
    p.op("v", lambda e: e.tensor_scalar(out=invr[:], in0=jrow[:], scalar1=-1.0,
                                        scalar2=1.0, op0=ALU.mult, op1=ALU.add),
         reads=[jrow], writes=[invr])
    mps = sub0.enter_context(p.nc.psum_tensor("h_mps", [128, 128], F32))
    p.op("t", lambda e: e.matmul(mps[:, 0:8], K1, jrow[:], start=True, stop=True),
         reads=[ones, jrow], writes=[mps])
    p.op("v", lambda e: e.tensor_copy(out=mcol8[:], in_=mps[:, 0:8]), reads=[mps], writes=[mcol8])

    saccj = [sub0.enter_context(p.nc.sbuf_tensor(f"h_sacc{i}", [128, NH * HP], BF16))
             for i in range(2)]
    p.op("v", lambda e: e.memset(scur[:], 0.0), writes=[scur])
    for j in range(7):
        sj = saccj[j % 2]
        p.dma("qs", lambda e, j=j, sj=sj: e.dma_start(
            out=sj[:], in_=sacc8_d[NN * j:NN * (j + 1), :]), writes=[sj])
        p.op("v", lambda e, j=j: e.tensor_scalar(
            out=erow[:], in0=etot[0:1, NH * j:NH * (j + 1)], scalar1=jrow[0:1, j:j + 1],
            scalar2=None, op0=ALU.mult), reads=[etot, jrow], writes=[erow])
        p.op("v", lambda e, j=j: e.tensor_scalar(
            out=erow[:], in0=erow[:], scalar1=invr[0:1, j:j + 1],
            scalar2=None, op0=ALU.add), reads=[erow, invr], writes=[erow])
        p.op("t", lambda e: e.matmul(mps[:, 0:NH], K1, erow[:], start=True, stop=True),
             reads=[ones, erow], writes=[mps])
        p.op("v", lambda e: e.tensor_copy(out=ebc[:], in_=mps[:, 0:NH]),
             reads=[mps], writes=[ebc])
        p.op("v", lambda e: e.tensor_tensor(
            out=scur[:].rearrange("n (h q) -> n h q", h=NH),
            in0=scur[:].rearrange("n (h q) -> n h q", h=NH),
            in1=ebc[:].unsqueeze(-1).broadcast_to([128, NH, HP]), op=ALU.mult),
            reads=[scur, ebc], writes=[scur])
        p.op("v", lambda e, sj=sj, j=j: e.scalar_tensor_tensor(
            out=scur[:], in0=sj[:], scalar=mcol8[:, j:j + 1], in1=scur[:],
            op0=ALU.mult, op1=ALU.add), reads=[sj, mcol8, scur], writes=[scur])

    p.dma("qs", lambda e: e.dma_start(out=espmy[:], in_=espmy_d[:]), writes=[espmy])
    cwch = [sub0.enter_context(p.nc.sbuf_tensor(f"h_cw{i}", [128, NH * CS], BF16))
            for i in range(2)]
    psY = [sub0.enter_context(p.nc.psum_tensor(f"h_psY{i}", [64, 512], F32))
           for i in range(2)]
    yez = [sub0.enter_context(p.nc.sbuf_tensor(f"h_ye{i}", [64, 512], BF16))
           for i in range(2)]
    for c in range(NCHUNK):
        cwc = cwch[c % 2]
        p.dma("qs", lambda e, c=c, cwc=cwc: e.dma_start(
            out=cwc[:], in_=cw_d[NN * c:NN * (c + 1), :]), writes=[cwc])
        p.op("v", lambda e: e.tensor_copy(out=scb[:], in_=scur[:]), reads=[scur], writes=[scb])
        for wv in range(16):
            psu = psY[wv % 2]
            ye = yez[wv % 2]
            for hh in range(4):
                h = wv * 4 + hh
                osl = slice(hh * 128, (hh + 1) * 128)
                p.op("t", lambda e, h=h, osl=osl, psu=psu, cwc=cwc: e.matmul(
                    psu[:, osl], scb[:, HP * h:HP * (h + 1)], cwc[:, CS * h:CS * (h + 1)],
                    start=True, stop=True), reads=[scb, cwc], writes=[psu])
            p.op("v", lambda e, psu=psu, ye=ye: e.tensor_copy(out=ye[:], in_=psu[:]),
                 reads=[psu], writes=[ye])
            p.dma("qg", lambda e, c=c, wv=wv, ye=ye: e.dma_start(
                out=yt_dio[HP * 4 * wv: HP * 4 * (wv + 1), CS * c:CS * (c + 1)]
                    .rearrange("(h q) l -> q h l", q=HP),
                in_=ye[:].rearrange("q (h l) -> q h l", h=4),
                accum_op=ALU.add), reads=[ye], writes=[f"ya_{wv}_{c}"])
        p.op("v", lambda e, c=c: e.tensor_scalar(
            out=erow[:], in0=espmy[0:1, NH * c:NH * (c + 1)], scalar1=1.0, scalar2=None,
            op0=ALU.mult), reads=[espmy], writes=[erow])
        p.op("t", lambda e: e.matmul(mps[:, 0:NH], K1, erow[:], start=True, stop=True),
             reads=[ones, erow], writes=[mps])
        p.op("v", lambda e: e.tensor_copy(out=ebc[:], in_=mps[:, 0:NH]),
             reads=[mps], writes=[ebc])
        p.op("v", lambda e: e.tensor_tensor(
            out=scur[:].rearrange("n (h q) -> n h q", h=NH),
            in0=scur[:].rearrange("n (h q) -> n h q", h=NH),
            in1=ebc[:].unsqueeze(-1).broadcast_to([128, NH, HP]), op=ALU.mult),
            reads=[scur, ebc], writes=[scur])
    sub0.close()
    p.fence()

    yf = [p.sb(f"h_yf{i}", [128, tw], BF16) for i in range(INTER // 128)]
    sub1 = ExitStack()
    ytb = [sub1.enter_context(p.nc.sbuf_tensor(f"h_ytb{i}", [128, tw], BF16))
           for i in range(2)]
    ysq = sub1.enter_context(p.nc.sbuf_tensor("h_ysq", [128, tw], BF16))
    psV = [sub1.enter_context(p.nc.psum_tensor(f"h_psV{i}", [1, 512], F32))
           for i in range(NT)]
    for r in range(NYC):
        yb = ytb[r % 2]
        p.dma("qs", lambda e, r=r, yb=yb: e.dma_start(
            out=yb[:], in_=yt_dio[128 * r:128 * (r + 1), :]), writes=[yb])
        p.op("s", lambda e, yb=yb: e.activation(out=ysq[:], in_=yb[:], func=AF.Square),
             reads=[yb], writes=[ysq])
        for nt in range(NT):
            p.op("t", lambda e, nt=nt, r=r: e.matmul(
                psV[nt][:], ones_b[:, 0:1], ysq[:, 512 * nt:512 * (nt + 1)],
                start=(r == 0), stop=(r == NYC - 1)),
                reads=[ysq, ones_b], writes=[psV[nt]])
    for nt in range(NT):
        p.op("v", lambda e, nt=nt: e.tensor_scalar(
            out=vrow[0:1, 512 * nt:512 * (nt + 1)], in0=psV[nt][:],
            scalar1=1.0 / INTER, scalar2=1e-5, op0=ALU.mult, op1=ALU.add),
            reads=[psV[nt]], writes=[vrow])
    p.op("s", lambda e: e.activation(out=vrow[:], in_=vrow[:], func=AF.Sqrt),
         reads=[vrow], writes=[vrow])
    p.op("v", lambda e: e.reciprocal(out=vrow[:], in_=vrow[:]), reads=[vrow], writes=[vrow])
    psB = sub1.enter_context(p.nc.psum_tensor("h_psB", [128, 512], F32))
    for nt in range(NT):
        p.op("t", lambda e, nt=nt: e.matmul(
            psB[:], K1, vrow[0:1, 512 * nt:512 * (nt + 1)], start=True, stop=True),
            reads=[ones, vrow], writes=[psB])
        p.op("v", lambda e, nt=nt: e.tensor_copy(
            out=rstdb[:, 512 * nt:512 * (nt + 1)], in_=psB[:]), reads=[psB], writes=[rstdb])

    p.dma("qs", lambda e: e.dma_start(out=nwc[:], in_=nw_d[:]), writes=[nwc])
    gsb = [sub1.enter_context(p.nc.sbuf_tensor(f"h_gs{i}", [128, tw], BF16))
           for i in range(2)]
    for r in range(NYC):
        yb = ytb[r % 2]
        gs = gsb[r % 2]
        p.dma("qs", lambda e, r=r, yb=yb: e.dma_start(
            out=yb[:], in_=yt_dio[128 * r:128 * (r + 1), :]), writes=[yb])
        p.dma("qs", lambda e, r=r, gs=gs: e.dma_start(
            out=gs[:], in_=gsil_d[128 * r:128 * (r + 1), :]), writes=[gs])
        p.op("v", lambda e, r=r, yb=yb: e.scalar_tensor_tensor(
            out=yf[r][:], in0=yb[:], scalar=nwc[:, r:r + 1], in1=rstdb[:],
            op0=ALU.mult, op1=ALU.mult), reads=[yb, nwc, rstdb], writes=[yf[r]])
        p.op("v", lambda e, r=r, gs=gs: e.tensor_tensor(
            out=yf[r][:], in0=yf[r][:], in1=gs[:], op=ALU.mult),
            reads=[yf[r], gs], writes=[yf[r]])
    sub1.close()
    p.fence()

    h = [p.sb(f"h_h{i}", [128, tw]) for i in range(NCH)]
    hb = [p.sb(f"h_hb{i}", [128, tw], BF16) for i in range(NCH)]
    sub2 = ExitStack()
    wtl = [sub2.enter_context(p.nc.sbuf_tensor(f"h_wtl{i}", [128, 512], BF16))
           for i in range(16)]
    pst = [sub2.enter_context(p.nc.psum_tensor(f"h_pst{i}", [128, 512], F32))
           for i in range(8)]
    for i in range(NCH):
        p.dma("qs", lambda e, i=i: e.dma_start(out=h[i][:], in_=h2_d[128 * i:128 * (i + 1), :]),
              writes=[h[i]])
    CG = 4
    for cg in range(NCH // CG):
        for khalf in range(2):
            for kk in range(16):
                k = khalf * 16 + kk
                p.dma("qg", lambda e, k=k, kk=kk, cg=cg: e.dma_start(
                    out=wtl[kk][:], in_=wop_d[128 * k:128 * (k + 1), 512 * cg:512 * (cg + 1)]),
                    writes=[wtl[kk]])
            for ci in range(CG):
                for nt in range(NT):
                    psu = pst[ci * NT + nt]
                    for kk in range(16):
                        k = khalf * 16 + kk
                        p.op("t", lambda e, psu=psu, kk=kk, ci=ci, nt=nt, k=k, khalf=khalf: e.matmul(
                            psu[:], wtl[kk][:, 128 * ci:128 * (ci + 1)],
                            yf[k][:, 512 * nt:512 * (nt + 1)],
                            start=(khalf == 0 and kk == 0), stop=(khalf == 1 and kk == 15)),
                            reads=[wtl[kk], yf[k]], writes=[psu])
        for ci in range(CG):
            c = cg * CG + ci
            for nt in range(NT):
                psu = pst[ci * NT + nt]
                p.op("v", lambda e, psu=psu, c=c, nt=nt: e.tensor_tensor(
                    out=h[c][:, 512 * nt:512 * (nt + 1)],
                    in0=h[c][:, 512 * nt:512 * (nt + 1)], in1=psu[:], op=ALU.add),
                    reads=[psu, h[c]], writes=[h[c]])
    sub2.close()
    p.fence()
    ones_bb = ones_b
    ps_s = [p.ps("h_psLs1", [1, min(tw, 512)]), p.ps("h_psLs2", [1, min(tw, 512)])]
    ps_b = [p.ps("h_psLb1", [128, min(tw, 512)]), p.ps("h_psLb2", [128, min(tw, 512)])]
    rows = [p.sb(f"h_lnrow{i}", [1, tw]) for i in range(4)]
    emit_ln_fm(p, h, hb, ones, ones_bb, ps_s, ps_b, rows, tw=tw)
    for i in range(NCH):
        p.dma("qs", lambda e, i=i: e.dma_start(out=h3_d[128 * i:128 * (i + 1), :], in_=h[i][:]),
              reads=[h[i]])
        p.dma("qs", lambda e, i=i: e.dma_start(out=h3b_d[128 * i:128 * (i + 1), :], in_=hb[i][:]),
              reads=[hb[i]])
    p.maybe_finish()
    return p


NHEAD = 16
HD = 128


def build_phase_i(tw=TOWN, p=None):
    if p is None:
        p = Prog("phaseI")
    h3_d = p.din("h3", [D, tw])
    h3b_d = p.din("h3b", [D, tw], BF16)
    wgt_d = p.din("wgt", [D, NHEAD])
    wqkvt_d = p.din("wqkvt", [D, 3 * D])
    ones_d = p.din("ones", [128, 1])
    qt_d = p.dout("qt", [D, tw], BF16)
    ksh_d = p.dout("ksh", [tw, HD], BF16)
    vsh_d = p.dout("vsh", [tw, HD], BF16)
    kraw_d = p.dout("kraw", [NHEAD, tw])
    p.start()
    NCH = D // 128
    NT = tw // 512
    NTC = tw // 128
    ones = p.sb("i_ones", [128, 1])
    onesb = p.sb("i_onesb", [128, 1], BF16)
    p.dma("qs", lambda e: e.dma_start(out=ones[:], in_=ones_d[:]), writes=[ones])
    p.op("v", lambda e: e.tensor_copy(out=onesb[:], in_=ones[:]), reads=[ones], writes=[onesb])

    h3 = [p.sb(f"i_h3_{i}", [128, tw]) for i in range(NCH)]
    h3b = [p.sb(f"i_h3b{i}", [128, tw], BF16) for i in range(NCH)]
    for i in range(NCH):
        p.dma("qs", lambda e, i=i: e.dma_start(out=h3[i][:], in_=h3_d[128 * i:128 * (i + 1), :]),
              writes=[h3[i]])
        p.dma("qg", lambda e, i=i: e.dma_start(out=h3b[i][:], in_=h3b_d[128 * i:128 * (i + 1), :]),
              writes=[h3b[i]])

    wgt = p.sb("i_wgt", [128, NHEAD * NCH])
    for k in range(NCH):
        p.dma("qs", lambda e, k=k: e.dma_start(
            out=wgt[:, NHEAD * k:NHEAD * (k + 1)], in_=wgt_d[128 * k:128 * (k + 1), :]),
            writes=[wgt])
    ind = p.sb("i_ind", [128, NHEAD * NTC])
    inds = p.sb("i_inds", [128, NHEAD * NTC])
    psg = p.ps("i_psg", [128, NHEAD])
    mx = p.sb("i_mx", [128, 1])
    zs = p.sb("i_zs", [128, 1])
    ztmp = p.sb("i_ztmp", [128, NHEAD])
    for j in range(NTC):
        for k in range(NCH):
            p.op("t", lambda e, j=j, k=k: e.matmul(
                psg[:], h3[k][:, 128 * j:128 * (j + 1)], wgt[:, NHEAD * k:NHEAD * (k + 1)],
                start=(k == 0), stop=(k == NCH - 1)),
                reads=[h3[k], wgt], writes=[psg])
        p.op("v", lambda e: e.reduce_max(out=mx[:], in_=psg[:], axis=AX.X),
             reads=[psg], writes=[mx])
        isl = slice(NHEAD * j, NHEAD * (j + 1))
        p.op("v", lambda e, isl=isl: e.tensor_scalar(
            out=ind[:, isl], in0=psg[:], scalar1=mx[:, 0:1], scalar2=None, op0=ALU.is_ge),
            reads=[psg, mx], writes=[ind])
        p.op("v", lambda e: e.tensor_scalar(
            out=ztmp[:], in0=psg[:], scalar1=mx[:, 0:1], scalar2=None, op0=ALU.subtract),
            reads=[psg, mx], writes=[ztmp])
        p.op("s", lambda e: e.activation(out=ztmp[:], in_=ztmp[:], func=AF.Exp,
                                         accum_out=zs[:]),
             reads=[ztmp], writes=[ztmp, zs])
        p.op("v", lambda e: e.reciprocal(out=zs[:], in_=zs[:]), reads=[zs], writes=[zs])
        p.op("v", lambda e, isl=isl: e.tensor_scalar(
            out=inds[:, isl], in0=ind[:, isl], scalar1=zs[:, 0:1], scalar2=None, op0=ALU.mult),
            reads=[ind, zs], writes=[inds])

    wtk = [p.sb(f"i_wtk{i}", [128, 512], BF16) for i in range(NCH)]
    ps4 = [p.ps(f"i_ps4{i}", [128, 512]) for i in range(4)]
    qtile = [p.sb(f"i_qt{i}", [128, tw], BF16) for i in range(NCH)]
    for cg in range(D // 512):
        for k in range(NCH):
            p.dma("qg", lambda e, k=k, cg=cg: e.dma_start(
                out=wtk[k][:], in_=wqkvt_d[128 * k:128 * (k + 1), 512 * cg:512 * (cg + 1)]),
                writes=[wtk[k]])
        for ci in range(4):
            c = cg * 4 + ci
            for nt in range(NT):
                psu = ps4[(ci * NT + nt) % 4]
                for k in range(NCH):
                    p.op("t", lambda e, psu=psu, k=k, ci=ci, nt=nt: e.matmul(
                        psu[:], wtk[k][:, 128 * ci:128 * (ci + 1)],
                        h3b[k][:, 512 * nt:512 * (nt + 1)],
                        start=(k == 0), stop=(k == NCH - 1)),
                        reads=[wtk[k], h3b[k]], writes=[psu])
                p.op("v", lambda e, psu=psu, c=c, nt=nt: e.tensor_copy(
                    out=qtile[c][:, 512 * nt:512 * (nt + 1)], in_=psu[:]),
                    reads=[psu], writes=[qtile[c]])
                p.dma("qs", lambda e, c=c, nt=nt: e.dma_start(
                    out=qt_d[128 * c:128 * (c + 1), 512 * nt:512 * (nt + 1)],
                    in_=qtile[c][:, 512 * nt:512 * (nt + 1)]),
                    reads=[qtile[c]])
    qsq = p.sb("i_qsq", [128, tw], BF16)
    psr = p.ps("i_psr", [1, 512])
    rowtmp = p.sb("i_rowtmp", [1, 512])
    for h in range(NHEAD):
        p.op("s", lambda e, h=h: e.activation(out=qsq[:], in_=qtile[h][:], func=AF.Square),
             reads=[qtile[h]], writes=[qsq])
        for nt in range(NT):
            p.op("t", lambda e, nt=nt: e.matmul(
                psr[:], onesb[:, 0:1], qsq[:, 512 * nt:512 * (nt + 1)],
                start=True, stop=True), reads=[qsq, onesb], writes=[psr])
            p.op("s", lambda e: e.activation(out=rowtmp[:], in_=psr[:], func=AF.Sqrt),
                 reads=[psr], writes=[rowtmp])
            p.dma("qs", lambda e, h=h, nt=nt: e.dma_start(
                out=kraw_d[h:h + 1, 512 * nt:512 * (nt + 1)], in_=rowtmp[:]),
                reads=[rowtmp])
    tmp = p.sb("i_tmp", [128, HD], BF16)
    accs = [p.sb(f"i_acc{j}", [128, HD]) for j in range(NTC)]
    for part, seltile, outd in ((1, ind, ksh_d), (2, inds, vsh_d)):
        for cg in range(D // 512):
            for k in range(NCH):
                p.dma("qg", lambda e, k=k, cg=cg, part=part: e.dma_start(
                    out=wtk[k][:],
                    in_=wqkvt_d[128 * k:128 * (k + 1),
                                part * D + 512 * cg: part * D + 512 * (cg + 1)]),
                    writes=[wtk[k]])
            for j in range(NTC):
                psu = ps4[j % 4]
                for k in range(NCH):
                    p.op("t", lambda e, psu=psu, k=k, j=j: e.matmul(
                        psu[:], h3b[k][:, 128 * j:128 * (j + 1)],
                        wtk[k][:, 0:512], start=(k == 0), stop=(k == NCH - 1)),
                        reads=[wtk[k], h3b[k]], writes=[psu])
                acc = accs[j]
                for hh in range(4):
                    h = cg * 4 + hh
                    if cg == 0 and hh == 0:
                        p.op("v", lambda e, psu=psu, j=j, h=h, acc=acc,
                             seltile=seltile: e.tensor_scalar(
                            out=acc[:], in0=psu[:, 0:HD],
                            scalar1=seltile[:, NHEAD * j + h:NHEAD * j + h + 1],
                            scalar2=None, op0=ALU.mult),
                            reads=[psu, seltile], writes=[acc])
                    else:
                        p.op("v", lambda e, psu=psu, j=j, h=h, hh=hh, acc=acc,
                             seltile=seltile: e.tensor_scalar(
                            out=tmp[:], in0=psu[:, HD * hh:HD * (hh + 1)],
                            scalar1=seltile[:, NHEAD * j + h:NHEAD * j + h + 1],
                            scalar2=None, op0=ALU.mult),
                            reads=[psu, seltile], writes=[tmp])
                        p.op("v", lambda e, acc=acc: e.tensor_tensor(
                            out=acc[:], in0=acc[:], in1=tmp[:], op=ALU.add),
                            reads=[acc, tmp], writes=[acc])
        for j in range(NTC):
            acc = accs[j]
            p.op("v", lambda e, acc=acc: e.tensor_copy(out=tmp[:], in_=acc[:]),
                 reads=[acc], writes=[tmp])
            p.dma("qs", lambda e, j=j, outd=outd: e.dma_start(
                out=outd[128 * j:128 * (j + 1), :], in_=tmp[:]), reads=[tmp])
    p.maybe_finish()
    return p


def build_phase_j(tn=T, p=None):
    if p is None:
        p = Prog("phaseJ")
    q2_d = p.din("q2", [2 * HD, tn], BF16)
    ksh_d = p.din("ksh", [tn, HD], BF16)
    vsh_d = p.din("vsh", [tn, HD], BF16)
    kraw_d = p.din("kraw", [2, tn])
    dmask_d = p.din("dmask", [4 * 128, 512])
    ones_d = p.din("ones", [128, 1])
    y2_d = p.dout("y2", [2 * HD, tn], BF16)
    p.start()
    NQB = tn // 512
    ones = p.sb("j_ones", [128, 1])
    onesb = p.sb("j_onesb", [128, 1], BF16)
    p.dma("qs", lambda e: e.dma_start(out=ones[:], in_=ones_d[:]), writes=[ones])
    p.op("v", lambda e: e.tensor_copy(out=onesb[:], in_=ones[:]), reads=[ones], writes=[onesb])
    K1 = ones[0:1, 0:1].broadcast_to([1, 128])
    K1b = onesb[0:1, 0:1].broadcast_to([1, 128])

    ksht = p.sb("j_ksht", [HD, tn], BF16)
    q2 = p.sb("j_q2", [128, 2 * tn], BF16)
    QCH = 2048

    def q2_sl(h, cq):
        if len(q2_d.shape) == 3:
            nb = QCH // q2_d.shape[2]
            return q2_d[HD * h:HD * (h + 1), nb * cq:nb * (cq + 1), :]
        return q2_d[HD * h:HD * (h + 1), QCH * cq:QCH * (cq + 1)]
    for h in range(2):
        for cq in range(tn // QCH):
            p.dma("qs", lambda e, h=h, cq=cq: e.dma_start(
                out=q2[:, tn * h + QCH * cq:tn * h + QCH * (cq + 1)],
                in_=q2_sl(h, cq)),
                writes=[f"j_q2_{h}_{cq}"])
    p.dma("qs", lambda e: e.dma_start(out=ksht[:], in_=ksh_d[:], transpose=True),
          writes=[ksht])
    dmask = [p.sb(f"j_dm{i}", [128, 512], BF16) for i in range(4)]
    for i in range(4):
        p.dma("qg", lambda e, i=i: e.dma_start(out=dmask[i][:], in_=dmask_d[128 * i:128 * (i + 1), :]),
              writes=[dmask[i]])
    vtok = p.sb("j_vtok", [128, (tn // 128) * HD], BF16)
    for j in range(tn // 128):
        p.dma("qs", lambda e, j=j: e.dma_start(
            out=vtok[:, HD * j:HD * (j + 1)], in_=vsh_d[128 * j:128 * (j + 1), :]),
            writes=[f"j_vtok_{j}"])
    ksq = p.sb("j_ksq", [128, tn], BF16)
    knrow = p.sb("j_knrow", [1, tn])
    kn1 = p.sb("j_kn1", [1, 1])
    psB = p.ps("j_psB", [128, 512])
    p.op("s", lambda e: e.activation(out=ksq[:], in_=ksht[:], func=AF.Square),
         reads=[ksht], writes=[ksq])
    for nt in range(tn // 512):
        p.op("t", lambda e, nt=nt: e.matmul(psB[0:1, :], onesb[:, 0:1],
                                            ksq[:, 512 * nt:512 * (nt + 1)],
                                            start=True, stop=True),
             reads=[ksq, onesb], writes=[psB])
        p.op("v", lambda e, nt=nt: e.tensor_copy(out=knrow[0:1, 512 * nt:512 * (nt + 1)],
                                                 in_=psB[0:1, :]), reads=[psB], writes=[knrow])
    p.op("v", lambda e: e.reduce_max(out=kn1[:], in_=knrow[:], axis=AX.X),
         reads=[knrow], writes=[kn1])
    p.op("s", lambda e: e.activation(out=kn1[:], in_=kn1[:], func=AF.Sqrt),
         reads=[kn1], writes=[kn1])
    kr2 = p.sb("j_kr2", [1, 2 * tn])
    km2 = p.sb("j_km2", [1, 1])
    negc = p.sb("j_negc", [128, 1])
    for hh in range(2):
        p.dma("qs", lambda e, hh=hh: e.dma_start(out=kr2[0:1, tn * hh:tn * (hh + 1)],
                                                 in_=kraw_d[hh:hh + 1, :]), writes=[kr2])
    p.op("v", lambda e: e.reduce_max(out=km2[:], in_=kr2[:], axis=AX.X),
         reads=[kr2], writes=[km2])
    p.op("v", lambda e: e.tensor_tensor(out=km2[:], in0=km2[:], in1=kn1[:],
                                        op=ALU.mult), reads=[km2, kn1], writes=[km2])
    p.op("v", lambda e: e.tensor_scalar(out=km2[:], in0=km2[:],
                                        scalar1=-1.0 / math.sqrt(HD), scalar2=None,
                                        op0=ALU.mult), reads=[km2], writes=[km2])
    p.op("t", lambda e: e.matmul(psB[:, 0:1], K1, km2[:], start=True, stop=True),
         reads=[km2, ones], writes=[psB])
    p.op("v", lambda e: e.tensor_copy(out=negc[:], in_=psB[:, 0:1]),
         reads=[psB], writes=[negc])

    psS = [p.ps(f"j_psS{i}", [128, 512]) for i in range(3)]
    psO = [p.ps(f"j_psO{i}", [128, 512]) for i in range(2)]
    psZ = [p.ps(f"j_psZ{i}", [1, 512]) for i in range(2)]
    expb = [p.sb(f"j_expb{i}", [128, 512], BF16) for i in range(4)]
    ebsum = p.sb("j_ebsum", [128, 512])
    ebpair = p.sb("j_ebpair", [128, 512], BF16)
    ebsb = p.sb("j_ebsb", [128, 512], BF16)
    rcpb = p.sb("j_rcpb", [128, 512])
    yout = p.sb("j_yout", [128, 512], BF16)
    INV_SQ = 1.0 / math.sqrt(HD)

    for h in range(2):
        for qb in range(NQB):
            qsl = slice(512 * qb, 512 * (qb + 1))
            pO = psO[qb % 2]
            pZ = psZ[qb % 2]
            nkb = 4 * qb + 4
            def emit_qk(kb):
                pS = psS[kb % 3]
                eb = expb[kb % 4]
                p.op("t", lambda e, h=h, kb=kb, qsl=qsl, pS=pS: e.matmul(
                    pS[:], ksht[:, 128 * kb:128 * (kb + 1)],
                    q2[:, tn * h + qsl.start:tn * h + qsl.stop],
                    start=True, stop=True),
                    reads=[ksht, f"j_q2_{h}_{qsl.start // 2048}"], writes=[pS])
                p.op("s", lambda e, pS=pS, eb=eb: e.activation(
                    out=eb[:], in_=pS[:], func=AF.Exp, scale=INV_SQ,
                    bias=negc[:, 0:1]),
                    reads=[pS, negc], writes=[eb])
                jj = kb - 4 * qb
                if 0 <= jj < 4:
                    p.op("v", lambda e, eb=eb, jj=jj: e.tensor_tensor(
                        out=eb[:], in0=eb[:], in1=dmask[jj][:], op=ALU.mult),
                        reads=[eb, dmask[jj]], writes=[eb])
                if kb % 2 == 1:
                    ebprev = expb[(kb - 1) % 4]
                    p.op("v", lambda e, eb=eb, ebprev=ebprev: e.tensor_tensor(
                        out=ebpair[:], in0=ebprev[:], in1=eb[:], op=ALU.add),
                        reads=[ebprev, eb], writes=[ebpair])
                    if kb == 1:
                        p.op("v", lambda e: e.tensor_copy(out=ebsum[:], in_=ebpair[:]),
                             reads=[ebpair], writes=[ebsum])
                    else:
                        p.op("v", lambda e: e.tensor_tensor(
                            out=ebsum[:], in0=ebsum[:], in1=ebpair[:], op=ALU.add),
                            reads=[ebsum, ebpair], writes=[ebsum])

            def emit_pv(kb):
                eb = expb[kb % 4]
                p.op("t", lambda e, kb=kb, eb=eb, pO=pO: e.matmul(
                    pO[:], vtok[:, HD * kb:HD * (kb + 1)], eb[:],
                    start=(kb == 0), stop=(kb == nkb - 1)),
                    reads=[f"j_vtok_{kb}", eb], writes=[pO])

            for kb in range(nkb):
                emit_qk(kb)
                if kb > 0:
                    emit_pv(kb - 1)
            emit_pv(nkb - 1)
            p.op("v", lambda e: e.tensor_copy(out=ebsb[:], in_=ebsum[:]),
                 reads=[ebsum], writes=[ebsb])
            p.op("t", lambda e, pZ=pZ: e.matmul(pZ[:], onesb[:, 0:1], ebsb[:],
                                                start=True, stop=True),
                 reads=[onesb, ebsb], writes=[pZ])
            p.op("v", lambda e, pZ=pZ: e.reciprocal(out=rcpb[0:1, :], in_=pZ[:]),
                 reads=[pZ], writes=[rcpb])
            p.op("t", lambda e: e.matmul(psB[:], K1, rcpb[0:1, :], start=True, stop=True),
                 reads=[rcpb, ones], writes=[psB])
            p.op("v", lambda e: e.tensor_copy(out=rcpb[:], in_=psB[:]),
                 reads=[psB], writes=[rcpb])
            p.op("v", lambda e, pO=pO: e.tensor_tensor(out=yout[:], in0=pO[:], in1=rcpb[:],
                                                       op=ALU.mult),
                 reads=[pO, rcpb], writes=[yout])
            def y2_sl(h, qsl=qsl):
                if len(y2_d.shape) == 3:
                    cb = qsl.start // (tn // 8)
                    w0 = qsl.start % (tn // 8)
                    return y2_d[HD * h:HD * (h + 1), cb, w0:w0 + (qsl.stop - qsl.start)]
                return y2_d[HD * h:HD * (h + 1), qsl]
            p.dma("qs", lambda e, h=h, qsl=qsl: e.dma_start(
                out=y2_sl(h), in_=yout[:]), reads=[yout])
    p.maybe_finish()
    return p


def build_phase_k(tw=TOWN, p=None):
    if p is None:
        p = Prog("phaseK")
    y_d = p.din("y", [D, tw], BF16)
    h3_d = p.din("h3", [D, tw])
    ones_d = p.din("ones", [128, 1])
    out_d = p.dout("outT", [D, tw])
    p.start()
    NCH = D // 128
    ones = p.sb("k_ones", [128, 1])
    ones_b = p.sb("k_onesb", [128, 1], BF16)
    p.dma("qs", lambda e: e.dma_start(out=ones[:], in_=ones_d[:]), writes=[ones])
    p.op("v", lambda e: e.tensor_copy(out=ones_b[:], in_=ones[:]), reads=[ones], writes=[ones_b])
    h = [p.sb(f"k_h{i}", [128, tw]) for i in range(NCH)]
    hb = [p.sb(f"k_hb{i}", [128, tw], BF16) for i in range(NCH)]
    yb = [p.sb(f"k_yb{i}", [128, tw], BF16) for i in range(NCH)]
    for i in range(NCH):
        p.dma("qs", lambda e, i=i: e.dma_start(out=h[i][:], in_=h3_d[128 * i:128 * (i + 1), :]),
              writes=[h[i]])
        p.dma("qg", lambda e, i=i: e.dma_start(out=yb[i][:], in_=y_d[128 * i:128 * (i + 1), :]),
              writes=[yb[i]])
        tt(p, h[i][:], h[i][:], yb[i][:], ALU.add, reads=[h[i], yb[i]], writes=[h[i]])
    ps_s = [p.ps("k_psLs1", [1, min(tw, 512)]), p.ps("k_psLs2", [1, min(tw, 512)])]
    ps_b = [p.ps("k_psLb1", [128, min(tw, 512)]), p.ps("k_psLb2", [128, min(tw, 512)])]
    rows = [p.sb(f"k_lnrow{i}", [1, tw]) for i in range(4)]
    emit_ln_fm(p, h, hb, ones, ones_b, ps_s, ps_b, rows, tw=tw)
    for i in range(NCH):
        p.dma("qs", lambda e, i=i: e.dma_start(out=out_d[128 * i:128 * (i + 1), :], in_=h[i][:]),
              reads=[h[i]])
    p.maybe_finish()
    return p


class GlueIdx:
    def __init__(self):
        self.cols = []

    def reg(self, fn):
        self.cols.append(fn)
        return len(self.cols) - 1

    def host(self, core):
        assert len(self.cols) <= 96, len(self.cols)
        a = np.full((128, 96), 2 ** 30, np.int32)
        for j, fn in enumerate(self.cols):
            v = np.asarray(fn(core), np.int32)
            a[:len(v), j] = v
        return a


def build_fused(devdump=False):
    p = Prog("fused")
    gidx = GlueIdx()
    RG = [list(range(NCORES))]
    nc = p.nc
    io = {}
    p.fused_io = io

    def internal(name, shape, dtype=F32):
        return nc.dram_tensor(name, list(shape), dtype, kind="Internal").ap()

    def output(name, shape, dtype=F32):
        t = nc.dram_tensor(name, list(shape), dtype, kind="ExternalOutput").ap()
        p.outputs[name] = t
        return t

    mag2_l = internal("x_mag2", [1024, TOWN]); theta_l = internal("x_theta", [1024, TOWN])
    mag2_o = internal("x_mag2o", [1024, TOWN]); theta_o = internal("x_thetao", [1024, TOWN])
    r4_l = internal("x_r4", [D, TOWN])
    kvre_s = internal("x_kvres", [1024, TOWN], BF16)
    kvim_s = internal("x_kvims", [1024, TOWN], BF16)
    kv_c = internal("x_kvc", [D, TOWN], BF16)
    h1_l = internal("x_h1", [D, TOWN]); h1b_l = internal("x_h1b", [D, TOWN], BF16)
    h1b_ag = internal("x_h1bag", [8 * D, TOWN], BF16)
    wmat_l = internal("x_wmat", [TOWN, NEXP]); posl_l = internal("x_posl", [TOWN, NEXP])
    cnt_l = internal("x_cnt", [1, NEXP])
    wall_ag = internal("x_wallag", [T, NEXP]); posall_ag = internal("x_posallag", [T, NEXP])
    cnt_ag = internal("x_cntag", [8, NEXP])
    rs_in = internal("x_rsin", [8 * D, TOWN], BF16)
    rs_out = internal("x_rsout", [D, TOWN], BF16)
    h2_l = internal("x_h2", [D, TOWN])
    hb_f = internal("x_hbf", [D, TOWN + 3], BF16)
    gsil_l = internal("x_gsil", [INTER, TOWN], BF16)
    yt_l = internal("x_yt", [INTER, TOWN], BF16)
    cwd_l = internal("x_cwd", [(TOWN // CS) * NN, NH * CS], BF16)
    sacc_l = internal("x_sacc", [NN, NH * HP], BF16)
    sacc_ag = internal("x_saccag", [8 * NN, NH * HP], BF16)
    espan_l = internal("x_espan", [TOWN // CS, NH])
    esptot_l = internal("x_esptot", [1, NH])
    esptot_ag = internal("x_esptotag", [8, NH])
    bnd_ag = internal("x_bndag", [8 * D, 4], BF16)
    h3_l = internal("x_h3", [D, TOWN]); h3b_l = internal("x_h3b", [D, TOWN], BF16)
    qt_l = internal("x_qt", [D, TOWN], BF16)
    q2_o = internal("x_q2o", [D, TOWN], BF16)
    ksh_ag = internal("x_kshag", [T, HD], BF16); vsh_ag = internal("x_vshag", [T, HD], BF16)
    ksh_l = internal("x_kshl", [TOWN, HD], BF16); vsh_l = internal("x_vshl", [TOWN, HD], BF16)
    kraw_l = internal("x_krawl", [NHEAD, TOWN])
    kraw_ag = internal("x_krawag", [8 * NHEAD, TOWN])
    kraw_m = internal("x_krawm", [2, T])
    y2_s = internal("x_y2s", [D, TOWN], BF16)
    ky_m = internal("x_kym", [D, TOWN], BF16)
    out_f = output("outT", [D, TOWN])

    io[("a", "mag2")] = mag2_l; io[("a", "theta")] = theta_l; io[("a", "r4")] = r4_l
    io[("b", "mag2")] = mag2_o[:].rearrange("(c r) t -> r c t", c=8)
    io[("b", "theta")] = theta_o[:].rearrange("(c r) t -> r c t", c=8)
    io[("b", "kvre")] = kvre_s[:].rearrange("(c r) t -> r c t", c=8)
    io[("b", "kvim")] = kvim_s[:].rearrange("(c r) t -> r c t", c=8)
    io[("c", "kv")] = kv_c; io[("c", "r4")] = r4_l
    io[("c", "h1")] = h1_l; io[("c", "h1b")] = h1b_l
    io[("d", "h1")] = h1_l
    io[("d", "wmat")] = wmat_l; io[("d", "posl")] = posl_l; io[("d", "cnt")] = cnt_l
    io[("e", "xall")] = h1b_ag[:].rearrange("(c r) t -> r c t", c=8)
    io[("e", "wall")] = wall_ag; io[("e", "posall")] = posall_ag; io[("e", "cnt8")] = cnt_ag
    io[("e", "part")] = rs_in[:].rearrange("(c r) t -> r c t", c=8)
    io[("f", "pj")] = rs_out; io[("f", "h1")] = h1_l
    io[("f", "h2")] = h2_l; io[("f", "h2b")] = hb_f[:, 3:TOWN + 3]
    io[("g", "hb")] = hb_f
    io[("g", "gsil")] = gsil_l; io[("g", "yt")] = yt_l; io[("g", "cw")] = cwd_l
    io[("g", "sacc")] = sacc_l; io[("g", "espan")] = espan_l; io[("g", "espantot")] = esptot_l
    io[("h", "ytin")] = yt_l; io[("h", "cw")] = cwd_l; io[("h", "gsil")] = gsil_l
    io[("h", "h2")] = h2_l; io[("h", "sacc8")] = sacc_ag
    io[("h", "esptot8")] = esptot_ag[:].rearrange("a b -> (a b)").unsqueeze(0)
    io[("h", "espmy")] = espan_l[:].rearrange("a b -> (a b)").unsqueeze(0)
    io[("h", "h3")] = h3_l; io[("h", "h3b")] = h3b_l
    io[("i", "h3")] = h3_l; io[("i", "h3b")] = h3b_l
    io[("i", "qt")] = qt_l; io[("i", "ksh")] = ksh_l; io[("i", "vsh")] = vsh_l
    io[("i", "kraw")] = kraw_l
    io[("j", "q2")] = q2_o[:].rearrange("(c r) t -> r c t", c=8)
    io[("j", "ksh")] = ksh_ag; io[("j", "vsh")] = vsh_ag
    io[("j", "kraw")] = kraw_m
    io[("j", "y2")] = y2_s[:].rearrange("(c r) t -> r c t", c=8)
    io[("k", "y")] = ky_m; io[("k", "h3")] = h3_l
    io[("k", "outT")] = out_f

    p.tag = "z"
    p.start()
    gidx_d = nc.dram_tensor("gidx", [128, 96], I32, kind="ExternalInput").ap()
    p.inputs["gidx"] = gidx_d
    gix = p.sb("z_gix", [128, 96], I32)
    p.dma("qs", lambda e: e.dma_start(out=gix[:], in_=gidx_d[:]), writes=[gix])
    gbuf = p.sb("z_gbuf", [128, 8], BF16)
    ixb = p.sb("z_ixb", [128, 1], I32)

    def ag(src_ap, out_ap, note):
        p.collective(lambda e: e.collective_compute(
            "AllGather", ALU.bypass, replica_groups=RG, ins=[src_ap], outs=[out_ap]))
        p.fence()

    def a2a(src_ap, out_ap, note):
        def go(e):
            with allow_ici_all_to_all():
                return e.collective_compute(
                    "AllToAll", ALU.bypass, replica_groups=RG, ins=[src_ap], outs=[out_ap])
        p.collective(go)
        p.fence()

    with p.phase("a"):
        build_phase_a(p=p)
    a2a(mag2_l, mag2_o, "mag2")
    a2a(theta_l, theta_o, "theta")
    with p.phase("b"):
        build_phase_b(p=p)
    a2a(kvre_s, kv_c[0:1024, :], "kvre")
    a2a(kvim_s, kv_c[1024:2048, :], "kvim")
    with p.phase("c"):
        build_phase_c(p=p)
    p.collective(lambda e: e.collective_compute(
        "AllGather", ALU.bypass, replica_groups=RG, ins=[h1b_l], outs=[h1b_ag]))
    with p.phase("d"):
        build_phase_d(p=p)
    ag(wmat_l, wall_ag, "wmat")
    ag(posl_l, posall_ag, "posl")
    ag(cnt_l, cnt_ag, "cnt")
    with p.phase("e"):
        build_phase_e(p=p)
    p.collective(lambda e: e.collective_compute(
        "ReduceScatter", ALU.add, replica_groups=RG, ins=[rs_in], outs=[rs_out]))
    p.fence()
    with p.phase("f"):
        build_phase_f(p=p, nparts=1)
    with p.phase("x3"):
        bl = p.sb("x3_bl", [128, 64], BF16)
        bnd_l = internal("x_bndl", [D, 4], BF16)
        mtok = p.op("v", lambda e: e.memset(bl[:], 0.0), writes=[bl])
        for ch in range(16):
            p.dma("qs", lambda e, ch=ch: e.dma_start(
                out=bl[:, 4 * ch:4 * ch + 3], in_=hb_f[128 * ch:128 * (ch + 1), TOWN:TOWN + 3]),
                writes=[f"x3bl{ch}"], extra=[mtok])
        for ch in range(16):
            p.dma("qs", lambda e, ch=ch: e.dma_start(
                out=bnd_l[128 * ch:128 * (ch + 1), :], in_=bl[:, 4 * ch:4 * (ch + 1)]),
                reads=[f"x3bl{ch}"])
        p.fence()
    ag(bnd_l, bnd_ag, "bnd")
    with p.phase("x4"):
        ixb4 = [p.sb(f"x4_ix{b}", [128, 1], I32) for b in range(4)]
        gb4 = [p.sb(f"x4_gb{b}", [128, 4], BF16) for b in range(4)]
        for ch in range(16):
            ix = ixb4[ch % 4]
            gb = gb4[ch % 4]
            col = gidx.reg(lambda core, ch=ch: ((core - 1) * D + 128 * ch + np.arange(128))
                           if core > 0 else np.full(128, 2 ** 30))
            p.op("v", lambda e, col=col, ix=ix: e.tensor_copy(out=ix[:], in_=gix[:, col:col + 1]),
                 reads=[gix], writes=[ix])
            p.op("v", lambda e, gb=gb: e.memset(gb[:], 0.0), writes=[gb])
            p.dma("qg", lambda e, ix=ix, gb=gb: e.indirect_dma_start(
                out=gb[:],
                out_offset=None, in_=bnd_ag[:],
                in_offset=IndirectOffsetOnAxis(ap=ix[:], axis=0),
                bounds_check=8 * D - 1, oob_is_err=False),
                reads=[ix], writes=[gb])
            p.dma("qs", lambda e, ch=ch, gb=gb: e.dma_start(
                out=hb_f[128 * ch:128 * (ch + 1), 0:3], in_=gb[0:128, 0:3]),
                reads=[gb])
    with p.phase("g"):
        build_phase_g(p=p)
    ag(sacc_l, sacc_ag, "sacc")
    ag(esptot_l, esptot_ag, "esptot")
    with p.phase("h"):
        build_phase_h(p=p)
    with p.phase("i"):
        build_phase_i(p=p)
    a2a(qt_l, q2_o, "qt")
    ag(ksh_l, ksh_ag, "ksh")
    ag(vsh_l, vsh_ag, "vsh")
    ag(kraw_l, kraw_ag, "kraw")
    with p.phase("x5"):
        krb = p.sb("x5_krb", [16, 1024])
        colk = gidx.reg(lambda core: np.array(
            [c * NHEAD + 2 * core + hloc for hloc in range(2) for c in range(8)], np.int32))
        p.op("v", lambda e: e.tensor_copy(out=ixb[0:16, :], in_=gix[0:16, colk:colk + 1]),
             reads=[gix], writes=[ixb])
        p.op("v", lambda e: e.memset(krb[:], 0.0), writes=[krb])
        p.dma("qg", lambda e: e.indirect_dma_start(
            out=krb[:], out_offset=None, in_=kraw_ag[:],
            in_offset=IndirectOffsetOnAxis(ap=ixb[0:16, :], axis=0),
            bounds_check=8 * NHEAD - 1, oob_is_err=False),
            reads=[ixb], writes=[krb])
        p.dma("qs", lambda e: e.dma_start(
            out=kraw_m[:].rearrange("h (c t) -> (h c) t", c=8), in_=krb[:]), reads=[krb])
        p.fence()
    with p.phase("j"):
        build_phase_j(p=p)
    a2a(y2_s, ky_m, "y2")
    with p.phase("k"):
        build_phase_k(p=p)
    p.finish()
    return p, gidx


FUSED = True


def _host_prep(inputs):
    x = _f32(inputs["L0_attn_x"][0])
    xT = _f32(x.T)
    xT_halo = np.concatenate([np.zeros((D, 1), np.float32), xT], axis=1)
    perm = np.concatenate([np.arange(0, D, 2), np.arange(1, D, 2)])
    WA = _f32(inputs["L0_attn_W_act"].T[:, perm])
    WR = _f32(inputs["L0_attn_W_rec"].T)
    WAB = np.concatenate([WA, WR], axis=1)
    WV = _f32(inputs["L0_attn_W_val"].T[perm, :])
    ones_col = np.ones((128, 1), np.float32)
    TRI = np.triu(np.ones((128, 128), np.float32), 1)
    CMASK = np.triu(np.ones((8, 8), np.float32), 1)
    EYE = np.eye(128, dtype=np.float32)
    ONESR = np.ones((1, 512), np.float32)
    ipw = _f32(inputs["L1_attn_in_proj_w"])
    DMASK = np.zeros((4 * 128, 512), np.float32)
    for jj in range(4):
        tk = np.arange(128)[:, None] + 128 * jj
        tq = np.arange(512)[None, :]
        DMASK[128 * jj:128 * (jj + 1)] = (tk <= tq).astype(np.float32)
    return dict(
        xT=xT, xT_halo=xT_halo, WAB=WAB, WV=WV, ones=ones_col, TRI=TRI,
        CMASK=CMASK, EYE=EYE, ONESR=ONESR, DMASK=DMASK,
        gw=_f32(inputs["L0_ffn_gate_w"]),
        wg_all=_f32(inputs["L0_ffn_w_gate"]), wu_all=_f32(inputs["L0_ffn_w_up"]),
        wd_all=_f32(inputs["L0_ffn_w_down"]),
        wpg=ipw[:, :INTER], wpx=ipw[:, INTER:INTER + CONV_DIM],
        wpd=ipw[:, INTER + CONV_DIM:],
        convw=_f32(inputs["L1_attn_conv_w"]),
        convbt=_f32(inputs["L1_attn_conv_b"])[None, :],
        dtb=_f32(inputs["L1_attn_dt_bias"])[:, None],
        alog=_f32(inputs["L1_attn_A_log"])[:, None],
        deye=_f32(np.repeat(inputs["L1_attn_D"][None, :], 128, axis=0)),
        nw=_f32(inputs["L1_attn_norm_weight"].reshape(INTER // 128, 128).T),
        wop=_f32(inputs["L1_attn_out_proj_w"]),
        wgt=_f32(inputs["L1_ffn_W_gate"].T),
        wqkvt=_f32(inputs["L1_ffn_W_qkv"].T))


def kernel_fused(**inputs):
    hp = _host_prep(inputs)
    p, gidx = build_fused()
    in_maps = []
    for i in range(NCORES):
        jm = np.zeros((1, 8), np.float32)
        jm[0, :i] = 1.0
        m = {
            "a_in_xs": np.ascontiguousarray(hp["xT_halo"][:, TOWN * i:TOWN * (i + 1) + 1]),
            "a_in_wab": hp["WAB"],
            "c_in_xs": np.ascontiguousarray(hp["xT"][:, TOWN * i:TOWN * (i + 1)]),
            "c_in_wv": hp["WV"], "c_in_ones": hp["ones"],
            "d_in_gw": hp["gw"], "d_in_tri": hp["TRI"], "d_in_ones": hp["ones"],
            "e_in_wg": hp["wg_all"][4 * i:4 * i + 4].reshape(4 * D, 768),
            "e_in_wu": hp["wu_all"][4 * i:4 * i + 4].reshape(4 * D, 768),
            "e_in_wd": hp["wd_all"][4 * i:4 * i + 4].reshape(4 * 768, D),
            "e_in_cmask": hp["CMASK"], "e_in_ones": hp["ones"], "e_in_eye": hp["EYE"],
            "e_in_esel": np.arange(4 * i, 4 * i + 4, dtype=np.int32)[:, None],
            "f_in_ones": hp["ones"],
            "g_in_wpg": hp["wpg"], "g_in_wpx": hp["wpx"], "g_in_wpd": hp["wpd"],
            "g_in_convw": hp["convw"], "g_in_convbt": hp["convbt"],
            "g_in_onesr": hp["ONESR"], "g_in_dtb": hp["dtb"], "g_in_alog": hp["alog"],
            "g_in_deye": hp["deye"], "g_in_eye": hp["EYE"], "g_in_ones": hp["ones"],
            "h_in_jmask": jm, "h_in_nw": hp["nw"], "h_in_wop": hp["wop"],
            "h_in_ones": hp["ones"],
            "i_in_wgt": hp["wgt"], "i_in_wqkvt": hp["wqkvt"], "i_in_ones": hp["ones"],
            "j_in_dmask": hp["DMASK"], "j_in_ones": hp["ones"],
            "k_in_ones": hp["ones"],
            "cst": p.cst_host(),
            "gidx": gidx.host(i),
        }
        in_maps.append(m)
    res = run_bass_kernel_spmd(p.nc, in_maps, core_ids=list(range(NCORES))).results
    outT = np.concatenate([np.asarray(r["outT"]) for r in res], axis=1)
    return np.ascontiguousarray(outT.T)[None]


def kernel(**inputs):
    if FUSED:
        return kernel_fused(**inputs)
    return kernel_unfused(**inputs)


def kernel_unfused(**inputs):
    x = _f32(inputs["L0_attn_x"][0])
    xT = _f32(x.T)
    xT_halo = np.concatenate([np.zeros((D, 1), np.float32), xT], axis=1)

    perm = np.concatenate([np.arange(0, D, 2), np.arange(1, D, 2)])
    WA = _f32(inputs["L0_attn_W_act"].T[:, perm])
    WR = _f32(inputs["L0_attn_W_rec"].T)
    WAB = np.concatenate([WA, WR], axis=1)
    WV = _f32(inputs["L0_attn_W_val"].T[perm, :])
    ones_col = np.ones((128, 1), np.float32)

    pa = build_phase_a()
    in_maps = [dict(xs=xT_halo[:, TOWN * i: TOWN * (i + 1) + 1], wab=WAB, cst=pa.cst_host())
               for i in range(NCORES)]
    ra = run_bass_kernel_spmd(pa.nc, in_maps, core_ids=list(range(NCORES))).results

    mag2 = np.concatenate([np.asarray(r["mag2"]) for r in ra], axis=1)
    theta = np.concatenate([np.asarray(r["theta"]) for r in ra], axis=1)
    r4 = np.concatenate([np.asarray(r["r4"]) for r in ra], axis=1)

    pb = build_phase_b()
    in_maps = [dict(mag2=mag2[128 * i:128 * (i + 1)], theta=theta[128 * i:128 * (i + 1)],
                    cst=pb.cst_host()) for i in range(NCORES)]
    rb = run_bass_kernel_spmd(pb.nc, in_maps, core_ids=list(range(NCORES))).results
    kvre = np.concatenate([np.asarray(r["kvre"]) for r in rb], axis=0)
    kvim = np.concatenate([np.asarray(r["kvim"]) for r in rb], axis=0)
    kv_perm = np.concatenate([kvre, kvim], axis=0)

    pc = build_phase_c()
    in_maps = [dict(kv=kv_perm[:, TOWN * i:TOWN * (i + 1)],
                    r4=r4[:, TOWN * i:TOWN * (i + 1)],
                    xs=xT[:, TOWN * i:TOWN * (i + 1)],
                    wv=WV, ones=ones_col, cst=pc.cst_host()) for i in range(NCORES)]
    rc = run_bass_kernel_spmd(pc.nc, in_maps, core_ids=list(range(NCORES))).results
    h1 = np.concatenate([np.asarray(r["h1"]) for r in rc], axis=1)
    h1b = np.concatenate([np.asarray(r["h1b"]) for r in rc], axis=1)

    TRI = np.triu(np.ones((128, 128), np.float32), 1)
    pd = build_phase_d()
    in_maps = [dict(h1=h1[:, TOWN * i:TOWN * (i + 1)], gw=_f32(inputs["L0_ffn_gate_w"]),
                    tri=TRI, ones=ones_col, cst=pd.cst_host()) for i in range(NCORES)]
    rd = run_bass_kernel_spmd(pd.nc, in_maps, core_ids=list(range(NCORES))).results
    wall = np.concatenate([np.asarray(r["wmat"]) for r in rd], axis=0)
    posall = np.concatenate([np.asarray(r["posl"]) for r in rd], axis=0)
    cnt8 = np.concatenate([np.asarray(r["cnt"]) for r in rd], axis=0)

    pe = build_phase_e()
    CMASK = np.triu(np.ones((8, 8), np.float32), 1)
    EYE = np.eye(128, dtype=np.float32)
    wg_all = _f32(inputs["L0_ffn_w_gate"])
    wu_all = _f32(inputs["L0_ffn_w_up"])
    wd_all = _f32(inputs["L0_ffn_w_down"])
    in_maps = []
    for i in range(NCORES):
        in_maps.append(dict(
            xall=np.ascontiguousarray(h1b), wg=wg_all[4 * i:4 * i + 4].reshape(4 * D, 768),
            wu=wu_all[4 * i:4 * i + 4].reshape(4 * D, 768),
            wd=wd_all[4 * i:4 * i + 4].reshape(4 * 768, D),
            wall=wall, posall=posall, cnt8=cnt8,
            esel=np.arange(4 * i, 4 * i + 4, dtype=np.int32)[:, None],
            cmask=CMASK, ones=ones_col, eye=EYE, cst=pe.cst_host()))
    re_ = run_bass_kernel_spmd(pe.nc, in_maps, core_ids=list(range(NCORES))).results
    parts = [np.asarray(r["part"]) for r in re_]

    pf = build_phase_f()
    in_maps = []
    for i in range(NCORES):
        pj = np.concatenate([parts[j][:, TOWN * i:TOWN * (i + 1)] for j in range(8)], axis=0)
        in_maps.append(dict(pj=pj, h1=h1[:, TOWN * i:TOWN * (i + 1)], ones=ones_col,
                            cst=pf.cst_host()))
    rf = run_bass_kernel_spmd(pf.nc, in_maps, core_ids=list(range(NCORES))).results
    h2 = np.concatenate([np.asarray(r["h2"]) for r in rf], axis=1)
    h2b = np.concatenate([np.asarray(r["h2b"]) for r in rf], axis=1)

    pg = build_phase_g()
    ipw = _f32(inputs["L1_attn_in_proj_w"])
    wpg_h, wpx_h, wpd_h = ipw[:, :INTER], ipw[:, INTER:INTER + CONV_DIM], ipw[:, INTER + CONV_DIM:]
    convw_h = _f32(inputs["L1_attn_conv_w"])
    convbt_h = _f32(inputs["L1_attn_conv_b"])[None, :]
    dtb_h = _f32(inputs["L1_attn_dt_bias"])[:, None]
    alog_h = _f32(inputs["L1_attn_A_log"])[:, None]
    deye_h = _f32(np.repeat(inputs["L1_attn_D"][None, :], 128, axis=0))
    EYE = np.eye(128, dtype=np.float32)
    ONESR = np.ones((1, 512), np.float32)
    h2b_halo = np.concatenate([np.zeros((D, 3), ml_dtypes.bfloat16), h2b], axis=1)
    in_maps = []
    for i in range(NCORES):
        in_maps.append(dict(
            hb=np.ascontiguousarray(h2b_halo[:, TOWN * i:TOWN * (i + 1) + 3]),
            wpg=wpg_h, wpx=wpx_h, wpd=wpd_h, convw=convw_h, convbt=convbt_h,
            onesr=ONESR, dtb=dtb_h, alog=alog_h, deye=deye_h, eye=EYE,
            ones=ones_col, cst=pg.cst_host()))
    rg = run_bass_kernel_spmd(pg.nc, in_maps, core_ids=list(range(NCORES))).results

    ph = build_phase_h()
    sacc8 = np.concatenate([np.asarray(r["sacc"]) for r in rg], axis=0)
    esptot8 = np.concatenate([np.asarray(r["espantot"]) for r in rg], axis=1)
    nw_h = _f32(inputs["L1_attn_norm_weight"].reshape(INTER // 128, 128).T)
    wop_h = _f32(inputs["L1_attn_out_proj_w"])
    in_maps = []
    for i in range(NCORES):
        jm = np.zeros((1, 8), np.float32)
        jm[0, :i] = 1.0
        in_maps.append(dict(
            ytin=np.asarray(rg[i]["yt"]), cw=np.asarray(rg[i]["cw"]),
            gsil=np.asarray(rg[i]["gsil"]), h2=h2[:, TOWN * i:TOWN * (i + 1)],
            sacc8=sacc8, esptot8=esptot8,
            espmy=np.asarray(rg[i]["espan"]).reshape(1, -1),
            jmask=jm, nw=nw_h, wop=wop_h, ones=ones_col, cst=ph.cst_host()))
    rh = run_bass_kernel_spmd(ph.nc, in_maps, core_ids=list(range(NCORES))).results
    h3 = np.concatenate([np.asarray(r["h3"]) for r in rh], axis=1)
    h3b = np.concatenate([np.asarray(r["h3b"]) for r in rh], axis=1)

    pi = build_phase_i()
    WGT = _f32(inputs["L1_ffn_W_gate"].T)
    WQKVT = _f32(inputs["L1_ffn_W_qkv"].T)
    in_maps = [dict(h3=h3[:, TOWN * i:TOWN * (i + 1)], h3b=h3b[:, TOWN * i:TOWN * (i + 1)],
                    wgt=WGT, wqkvt=WQKVT, ones=ones_col, cst=pi.cst_host())
               for i in range(NCORES)]
    ri = run_bass_kernel_spmd(pi.nc, in_maps, core_ids=list(range(NCORES))).results
    qt_full = np.concatenate([np.asarray(r["qt"]) for r in ri], axis=1)
    ksh_full = np.concatenate([np.asarray(r["ksh"]) for r in ri], axis=0)
    vsh_full = np.concatenate([np.asarray(r["vsh"]) for r in ri], axis=0)
    kraw_full = np.concatenate([np.asarray(r["kraw"]) for r in ri], axis=1)

    pj = build_phase_j()
    DMASK = np.zeros((4 * 128, 512), np.float32)
    for jj in range(4):
        tk = np.arange(128)[:, None] + 128 * jj
        tq = np.arange(512)[None, :]
        DMASK[128 * jj:128 * (jj + 1)] = (tk <= tq).astype(np.float32)
    in_maps = []
    for i in range(NCORES):
        q2 = np.concatenate([qt_full[128 * (2 * i):128 * (2 * i + 1)],
                             qt_full[128 * (2 * i + 1):128 * (2 * i + 2)]], axis=0)
        in_maps.append(dict(q2=np.ascontiguousarray(q2), ksh=ksh_full, vsh=vsh_full,
                            kraw=np.ascontiguousarray(kraw_full[2 * i:2 * i + 2]),
                            dmask=DMASK, ones=ones_col, cst=pj.cst_host()))
    rj = run_bass_kernel_spmd(pj.nc, in_maps, core_ids=list(range(NCORES))).results
    ycat = np.concatenate([np.asarray(r["y2"]) for r in rj], axis=0)

    pk = build_phase_k()
    in_maps = [dict(y=np.ascontiguousarray(ycat[:, TOWN * i:TOWN * (i + 1)]),
                    h3=h3[:, TOWN * i:TOWN * (i + 1)], ones=ones_col, cst=pk.cst_host())
               for i in range(NCORES)]
    rk = run_bass_kernel_spmd(pk.nc, in_maps, core_ids=list(range(NCORES))).results
    outT = np.concatenate([np.asarray(r["outT"]) for r in rk], axis=1)
    return np.ascontiguousarray(outT.T)[None]


if __name__ == "__main__":
    pass
